# version 08
# speedup vs baseline: 1.3046x; 1.0073x over previous
import numpy as np
import concourse.bass as bass
import concourse.mybir as mybir
from concourse.bass_utils import run_bass_kernel_spmd

F32 = mybir.dt.float32
BF16 = mybir.dt.bfloat16
I32 = mybir.dt.int32
AF = mybir.ActivationFunctionType
ALU = mybir.AluOpType
AX = mybir.AxisListType

H, DH, D = 16, 64, 1024
C = 128
T = 2048
S = 4096
EPS = 1e-5
NB = 8


def make_perm():
    perm = np.zeros(D, dtype=np.int64)
    for h in range(H):
        for i in range(32):
            perm[32 * h + i] = 64 * h + 2 * i
            perm[512 + 32 * h + i] = 64 * h + 2 * i + 1
    return perm

def make_tables():
    t = {}
    angle = 10.0 ** (-4.0 * (np.arange(0, DH, 2, dtype=np.float64) / DH))
    pos = np.arange(S, dtype=np.float64)
    nang = pos[:, None] * angle[None, :]
    COS = np.cos(nang); SIN = np.sin(nang)
    colmap = np.arange(512) % 32
    t['COS_T'] = COS[:, colmap].T.astype(np.float32).copy()
    t['SIN_T'] = SIN[:, colmap].T.astype(np.float32).copy()
    scale_h = 1.0 - 2.0 ** (-5.0 - np.arange(H, dtype=np.float64))
    lns = np.log(scale_h)
    jj, ii = np.meshgrid(np.arange(C), np.arange(C), indexing='ij')
    DT = np.where(ii >= jj, np.exp((ii - jj)[None] * lns[:, None, None]), 0.0)
    t['DTcat'] = np.concatenate([DT[h] for h in range(H)], axis=1).astype(np.float32)
    qd = np.zeros((128, 8 * C), dtype=np.float32)
    for g in range(8):
        for hf, h in ((0, 2 * g), (1, 2 * g + 1)):
            qd[64 * hf:64 * hf + 64, C * g:C * (g + 1)] = np.exp(
                (np.arange(C, dtype=np.float64)[None, :] + 1.0) * lns[h])
    t['QDECG'] = qd
    kdec = np.exp((C - 1.0 - (pos % C))[:, None] * lns[None, :])
    hmap = np.arange(512) // 32
    t['COS_BK'] = (COS[:, colmap] * kdec[:, hmap]).astype(np.float32)
    t['SIN_BK'] = (SIN[:, colmap] * kdec[:, hmap]).astype(np.float32)
    sdec = np.exp(C * lns)
    sd = np.zeros((128, 8), dtype=np.float32)
    for g in range(8):
        sd[0:64, g] = sdec[2 * g]
        sd[64:128, g] = sdec[2 * g + 1]
    t['SDECG'] = sd
    return t


_ENG_ATTR = {'t': 'tensor', 'v': 'vector', 's': 'scalar', 'g': 'gpsimd', 'y': 'sync'}

class P:
    def __init__(self, nc, sems=('pe', 'dv', 'ac', 'gp', 'dy', 'dg', 'ds')):
        self.nc = nc
        self.q = {e: [] for e in _ENG_ATTR}
        self.sems = {}
        self.cnt = {}
        for nm in sems:
            self.sems[nm] = nc.alloc_semaphore(nm)
            self.cnt[nm] = 0

    def emit(self, eng, fn, inc=None, k=1):
        if inc is None:
            self.q[eng].append(fn)
            return None
        self.cnt[inc] += k
        sem = self.sems[inc]
        self.q[eng].append(lambda e, fn=fn, sem=sem, k=k: fn(e).then_inc(sem, k))
        return self.cnt[inc]

    def dma(self, eng, out, in_, inc=None, **kw):
        inc = inc if inc is not None else {'y': 'dy', 'g': 'dg', 's': 'ds'}[eng]
        return self.emit(eng, lambda e, out=out, in_=in_, kw=kw: e.dma_start(out=out, in_=in_, **kw),
                         inc=inc, k=16)

    def wait(self, eng, nm, val=None):
        v = self.cnt[nm] if val is None else val
        sem = self.sems[nm]
        self.q[eng].append(lambda e, sem=sem, v=v: e.wait_ge(sem, v))

    def tok(self, nm):
        return self.cnt[nm]

    def finalize(self):
        nc = self.nc
        with nc.Block() as block:
            for eng, attr in _ENG_ATTR.items():
                thunks = self.q[eng]
                if not thunks:
                    continue
                getattr(block, attr)(lambda e, thunks=thunks: [th(e) for th in thunks])


def alloc_common(nc, p):
    k = type('K', (), {})()
    k.nc, k.p = nc, p
    k.pb = [nc.alloc_psum_tensor(f"pb{i}", [128, 512], F32) for i in range(7)]
    k.pbh = nc.alloc_psum_tensor("pbh", [128, 1024], BF16)
    k.ones_bf = nc.alloc_sbuf_tensor("ones_bf", [128, 1], BF16)
    k.ones_f1 = nc.alloc_sbuf_tensor("ones_f1", [1, 128], F32)
    k.stat = nc.alloc_sbuf_tensor("stat", [1, 4 * 512], F32)
    k.stat2 = nc.alloc_sbuf_tensor("stat2", [2, 4 * 512], F32)
    k.st2m = k.stat2[0:2, 0:512]
    k.st2v = k.stat2[0:2, 512:1024]
    k.st2t = k.stat2[0:2, 1024:1536]
    k.st2r = k.stat2[0:2, 1536:2048]
    k.tscr = nc.alloc_sbuf_tensor("tscr", [128, 512], F32)
    p.emit('v', lambda e: e.memset(k.ones_bf[:], 1.0))
    p.emit('v', lambda e: e.memset(k.ones_f1[:], 1.0), inc='dv')
    return k

def rearr_fm(hbm_ap):
    return hbm_ap.rearrange("(c p) t -> p c t", p=128)

def rearr_tm(hbm_ap):
    return hbm_ap.rearrange("(n p) c -> p n c", p=128)


def emit_ln_fm(k, xb, xsq, out, *, nchunk=8, tlen=T, gamma_ap=None, beta_ap=None, m_out=None, r_out=None):
    p, nc = k.p, k.nc
    NS = 512
    nts = tlen // NS
    for c in range(nchunk):
        p.emit('v', lambda e, c=c: e.tensor_mul(
            xsq[:, c * tlen:(c + 1) * tlen], xb[:, c * tlen:(c + 1) * tlen],
            xb[:, c * tlen:(c + 1) * tlen]), inc='dv')
    sq_tok = p.tok('dv')
    p.wait('t', 'dv', sq_tok)
    stat_tok = [None, None]
    rdy_tok = [None, None]
    apply_tok = [None, None]
    for ts in range(nts):
        par = ts % 2
        sb, qb_ = k.pb[0], k.pb[1]
        bcm = k.pb[2] if par == 0 else k.pb[5]
        bcr = k.pb[3] if par == 0 else k.pb[6]
        M = k.stat[0:1, par * 1024 + 0:par * 1024 + 512]
        R = k.stat[0:1, par * 1024 + 512:par * 1024 + 1024]
        if stat_tok[0] is not None:
            p.wait('t', 'dv', stat_tok[0])
        def sums(e, ts=ts):
            for c in range(nchunk):
                sl = slice(c * tlen + ts * NS, c * tlen + ts * NS + NS)
                e.matmul(sb[0:1, 0:NS], k.ones_bf[:, 0:1], xb[:, sl],
                         start=(c == 0), stop=(c == nchunk - 1))
            for c in range(nchunk):
                sl = slice(c * tlen + ts * NS, c * tlen + ts * NS + NS)
                i2 = e.matmul(qb_[0:1, 0:NS], k.ones_bf[:, 0:1], xsq[:, sl],
                              start=(c == 0), stop=(c == nchunk - 1))
            return i2
        pe_tok = p.emit('t', sums, inc='pe')
        p.wait('v', 'pe', pe_tok)
        def stats(e, ts=ts, M=M, R=R):
            e.tensor_scalar_mul(M, sb[0:1, 0:NS], 1.0 / (nchunk * 128.0))
            e.tensor_scalar_mul(R, qb_[0:1, 0:NS], 1.0 / (nchunk * 128.0))
            if m_out is not None:
                e.tensor_copy(m_out[0:1, ts * NS:(ts + 1) * NS], M)
            e.tensor_mul(k.tscr[0:1, 0:NS], M, M)
            e.tensor_sub(R, R, k.tscr[0:1, 0:NS])
            return e.tensor_scalar_add(R, R, EPS)
        stat_tok[0] = p.emit('v', stats, inc='dv')
        p.wait('s', 'dv', stat_tok[0])
        ac_tok = p.emit('s', lambda e, R=R: e.activation(R, R, AF.Sqrt), inc='ac')
        p.wait('v', 'ac', ac_tok)
        def recf(e, ts=ts, R=R):
            i = e.reciprocal(R, R)
            if r_out is not None:
                i = e.tensor_copy(r_out[0:1, ts * NS:(ts + 1) * NS], R)
            return i
        rdy_tok[par] = p.emit('v', recf, inc='dv')
        if apply_tok[par] is not None:
            p.wait('t', 'dv', apply_tok[par])
        p.wait('t', 'dv', rdy_tok[par])
        def bcast(e, M=M, R=R, bcm=bcm, bcr=bcr):
            e.matmul(bcm[:, 0:NS], k.ones_f1[0:1, :], M, start=True, stop=True)
            return e.matmul(bcr[:, 0:NS], k.ones_f1[0:1, :], R, start=True, stop=True)
        pe_tok = p.emit('t', bcast, inc='pe')
        p.wait('v', 'pe', pe_tok)
        def apply(e, ts=ts, bcm=bcm, bcr=bcr):
            for c in range(nchunk):
                sl = slice(c * tlen + ts * NS, c * tlen + ts * NS + NS)
                e.tensor_sub(k.tscr[:, 0:NS], xb[:, sl], bcm[:, 0:NS])
                if gamma_ap is None:
                    iv = e.tensor_mul(out[:, sl], k.tscr[:, 0:NS], bcr[:, 0:NS])
                else:
                    e.tensor_mul(k.tscr[:, 0:NS], k.tscr[:, 0:NS], bcr[:, 0:NS])
                    iv = e.tensor_scalar(out[:, sl], k.tscr[:, 0:NS],
                                         gamma_ap[:, c:c + 1], beta_ap[:, c:c + 1],
                                         ALU.mult, ALU.add)
            return iv
        apply_tok[par] = p.emit('v', apply, inc='dv')
    p.wait('t', 'dv')


def emit_proj_fm(k, wfm, xfm, out, *, nchunk_out=8, nchunk_in=8, tlen=T, evac='s',
                 bank0=0, bias_ap=None, relu=False, out_tlen=None, out_col0=0):
    p = k.p
    NS = 512
    wcols = nchunk_out * 128
    if out_tlen is None:
        out_tlen = tlen
    btok = [None, None]
    for cch in range(nchunk_out):
        for ts in range(tlen // NS):
            bi = ts % 2
            bank = k.pb[bank0 + bi]
            if btok[bi] is not None:
                p.wait('t', 'ac' if evac == 's' else 'dv', btok[bi])
            def mms(e, cch=cch, ts=ts, bank=bank):
                for dc in range(nchunk_in):
                    i = e.matmul(bank[:, 0:NS],
                                 wfm[:, dc * wcols + cch * 128: dc * wcols + cch * 128 + 128],
                                 xfm[:, dc * tlen + ts * NS: dc * tlen + ts * NS + NS],
                                 start=(dc == 0), stop=(dc == nchunk_in - 1))
                return i
            pe_tok = p.emit('t', mms, inc='pe')
            osl = out[:, cch * out_tlen + out_col0 + ts * NS: cch * out_tlen + out_col0 + ts * NS + NS]
            if evac == 's':
                p.wait('s', 'pe', pe_tok)
                btok[bi] = p.emit('s', lambda e, bank=bank, osl=osl: e.activation(osl, bank[:, 0:NS], AF.Copy),
                                  inc='ac')
            else:
                p.wait('v', 'pe', pe_tok)
                if bias_ap is not None:
                    btok[bi] = p.emit('v', lambda e, bank=bank, osl=osl, cch=cch:
                                      e.tensor_scalar(osl, bank[:, 0:NS], bias_ap[:, cch:cch + 1],
                                                      0.0, ALU.add, ALU.max if relu else ALU.bypass),
                                      inc='dv')
                else:
                    btok[bi] = p.emit('v', lambda e, bank=bank, osl=osl: e.tensor_copy(osl, bank[:, 0:NS]),
                                      inc='dv')
    p.wait('t', 'ac' if evac == 's' else 'dv')


def emit_rot_fm(k, raw, cosT, sinT, scrA, scrB, scrC):
    p = k.p
    for c in range(4):
        lo = raw[:, c * T:(c + 1) * T]
        hi = raw[:, (c + 4) * T:(c + 5) * T]
        cs = cosT[:, c * T:(c + 1) * T]
        sn = sinT[:, c * T:(c + 1) * T]
        def rot(e, lo=lo, hi=hi, cs=cs, sn=sn):
            e.tensor_mul(scrA[:], lo, cs)
            e.tensor_mul(scrC[:], lo, sn)
            e.tensor_mul(scrB[:], hi, sn)
            e.tensor_sub(lo, scrA[:], scrB[:])
            e.tensor_mul(scrB[:], hi, cs)
            return e.tensor_add(hi, scrC[:], scrB[:])
        p.emit('v', rot, inc='dv')


def emit_regroup(k, src, dst):
    p = k.p
    for g in range(8):
        for hf in range(2):
            h = 2 * g + hf
            cr, qq = h // 4, h % 4
            for x1 in range(2):
                srows = slice(32 * qq, 32 * qq + 32)
                scol = (cr + 4 * x1) * T
                drows = slice(64 * hf + 32 * x1, 64 * hf + 32 * x1 + 32)
                p.dma('y', dst[drows, g * T:(g + 1) * T], src[srows, scol:scol + T])


def emit_rot_tm(k, raw_tm, out_tm, cosB, sinB, scrA, scrB, scrC, *, ntile=16):
    p = k.p
    for n in range(ntile):
        lo = raw_tm[:, n * 1024: n * 1024 + 512]
        hi = raw_tm[:, n * 1024 + 512: n * 1024 + 1024]
        cs = cosB[:, n * 512:(n + 1) * 512]
        sn = sinB[:, n * 512:(n + 1) * 512]
        oblk = out_tm[:, n * 1024:(n + 1) * 1024].rearrange("p (h i) -> p h i", h=16, i=64)
        olo = oblk[:, :, 0:32]
        ohi = oblk[:, :, 32:64]
        def rot(e, lo=lo, hi=hi, cs=cs, sn=sn, olo=olo, ohi=ohi):
            e.tensor_mul(scrA[:, 0:512], lo, cs)
            e.tensor_mul(scrC[:, 0:512], lo, sn)
            e.tensor_mul(scrB[:, 0:512], hi, sn)
            e.tensor_sub(olo, scrA[:, 0:512].rearrange("p (h i) -> p h i", i=32),
                         scrB[:, 0:512].rearrange("p (h i) -> p h i", i=32))
            e.tensor_mul(scrB[:, 0:512], hi, cs)
            return e.tensor_add(ohi, scrC[:, 0:512].rearrange("p (h i) -> p h i", i=32),
                                scrB[:, 0:512].rearrange("p (h i) -> p h i", i=32))
        p.emit('v', rot, inc='dv')

def emit_proj_tm(k, wfm, xfm, out_tm, *, nchunk_in=8, tlen=T, ncol=1024, bank0=0):
    p = k.p
    NS = 512
    btok = [None, None]
    ii_ = 0
    for tt in range(tlen // 128):
        for cs in range(ncol // NS):
            bi = ii_ % 2
            ii_ += 1
            bank = k.pb[bank0 + bi]
            if btok[bi] is not None:
                p.wait('t', 'ac', btok[bi])
            def mms(e, tt=tt, cs=cs, bank=bank):
                for dc in range(nchunk_in):
                    i = e.matmul(bank[:, 0:NS],
                                 xfm[:, dc * tlen + tt * 128: dc * tlen + tt * 128 + 128],
                                 wfm[:, dc * ncol + cs * NS: dc * ncol + cs * NS + NS],
                                 start=(dc == 0), stop=(dc == nchunk_in - 1))
                return i
            pe_tok = p.emit('t', mms, inc='pe')
            p.wait('s', 'pe', pe_tok)
            osl = out_tm[:, tt * ncol + cs * NS: tt * ncol + cs * NS + NS]
            btok[bi] = p.emit('s', lambda e, bank=bank, osl=osl: e.activation(osl, bank[:, 0:NS], AF.Copy),
                              inc='ac')
    p.wait('t', 'ac')


def emit_retention(k, Qg, Kg, ktok, Vtm, DT, QDECG, SDECG, Sf32, S2b, RTg, qscr, atA, atB):
    p = k.p
    scA, scB, rtA, rtB, ds = k.pb[0], k.pb[1], k.pb[2], k.pb[3], k.pb[4]
    for c in range(16):
        for g in range(8):
            qcol = slice(g * T + c * 128, g * T + c * 128 + 128)
            vcol = slice(c * 1024 + g * 128, c * 1024 + g * 128 + 128)
            def dve1(e, g=g, qcol=qcol):
                e.tensor_mul(qscr[:], Qg[:, qcol], QDECG[:, g * 128:(g + 1) * 128])
                e.tensor_copy(S2b[0:64, g * 128:g * 128 + 64], Sf32[0:64, g * 128:g * 128 + 64])
                return e.tensor_copy(S2b[64:128, g * 128 + 64:g * 128 + 128],
                                     Sf32[64:128, g * 128 + 64:g * 128 + 128])
            dv_tok = p.emit('v', dve1, inc='dv')
            p.wait('t', 'dv', dv_tok)
            def pe1(e, g=g, qcol=qcol):
                e.matmul(rtA[0:64, 0:128], S2b[:, g * 128:g * 128 + 64], qscr[:],
                         start=True, stop=False)
                e.matmul(rtB[64:128, 0:128], S2b[:, g * 128 + 64:g * 128 + 128], qscr[:],
                         start=True, stop=False, tile_position=(0, 64))
                e.matmul(scA[:, 0:128], Kg[0:64, qcol], Qg[0:64, qcol],
                         start=True, stop=True)
                return e.matmul(scB[:, 0:128], Kg[64:128, qcol], Qg[64:128, qcol],
                                start=True, stop=True, tile_position=(64, 0))
            pe_tok = p.emit('t', pe1, inc='pe')
            p.wait('v', 'pe', pe_tok)
            def dve2(e, g=g):
                e.tensor_mul(atA[:], scA[:, 0:128], DT[:, (2 * g) * 128:(2 * g) * 128 + 128])
                return e.tensor_mul(atB[:], scB[:, 0:128], DT[:, (2 * g + 1) * 128:(2 * g + 1) * 128 + 128])
            dv_tok = p.emit('v', dve2, inc='dv')
            p.wait('t', 'dv', dv_tok)
            def pe2(e, g=g, vcol=vcol):
                e.matmul(rtA[0:64, 0:128], Vtm[:, vcol.start:vcol.start + 64], atA[:],
                         start=False, stop=True)
                e.matmul(rtB[64:128, 0:128], Vtm[:, vcol.start + 64:vcol.start + 128], atB[:],
                         start=False, stop=True, tile_position=(0, 64))
                return e.matmul(ds[:, 0:128], ktok[:, vcol], Vtm[:, vcol],
                                start=True, stop=True)
            pe_tok = p.emit('t', pe2, inc='pe')
            p.wait('v', 'pe', pe_tok)
            def dve3(e, g=g, c=c):
                e.tensor_copy(RTg[0:64, g * T + c * 128: g * T + c * 128 + 128], rtA[0:64, 0:128])
                e.tensor_copy(RTg[64:128, g * T + c * 128: g * T + c * 128 + 128], rtB[64:128, 0:128])
                e.scalar_tensor_tensor(Sf32[0:64, g * 128:g * 128 + 64],
                                       Sf32[0:64, g * 128:g * 128 + 64],
                                       SDECG[0:64, g:g + 1], ds[0:64, 0:64],
                                       ALU.mult, ALU.add)
                return e.scalar_tensor_tensor(Sf32[64:128, g * 128 + 64:g * 128 + 128],
                                              Sf32[64:128, g * 128 + 64:g * 128 + 128],
                                              SDECG[64:128, g:g + 1], ds[64:128, 64:128],
                                              ALU.mult, ALU.add)
            dv_tok = p.emit('v', dve3, inc='dv')
            p.wait('t', 'dv', dv_tok)


def emit_gnorm(k, RTg, Rn, ones2, E2, gammaT, betaT, sqscr):
    p = k.p
    NS = 512
    for g in range(8):
        for ts in range(T // NS):
            sl = slice(g * T + ts * NS, g * T + ts * NS + NS)
            dv_tok = p.emit('v', lambda e, sl=sl: e.tensor_mul(sqscr[:], RTg[:, sl], RTg[:, sl]),
                            inc='dv')
            p.wait('t', 'dv', dv_tok)
            def pe1(e, sl=sl):
                e.matmul(k.pb[0][0:2, 0:NS], ones2[:, 0:2], RTg[:, sl], start=True, stop=True)
                return e.matmul(k.pb[1][0:2, 0:NS], ones2[:, 0:2], sqscr[:], start=True, stop=True)
            pe_tok = p.emit('t', pe1, inc='pe')
            p.wait('v', 'pe', pe_tok)
            def stats(e):
                e.tensor_scalar_mul(k.st2m[:], k.pb[0][0:2, 0:NS], 1.0 / 64.0)
                e.tensor_scalar_mul(k.st2v[:], k.pb[1][0:2, 0:NS], 1.0 / 64.0)
                e.tensor_mul(k.st2t[:], k.st2m[:], k.st2m[:])
                e.tensor_sub(k.st2v[:], k.st2v[:], k.st2t[:])
                return e.tensor_scalar_add(k.st2v[:], k.st2v[:], EPS)
            dv_tok = p.emit('v', stats, inc='dv')
            p.wait('s', 'dv', dv_tok)
            ac_tok = p.emit('s', lambda e: e.activation(k.st2t[:], k.st2v[:], AF.Sqrt), inc='ac')
            p.wait('v', 'ac', ac_tok)
            dv_tok = p.emit('v', lambda e: e.reciprocal(k.st2r[:], k.st2t[:]), inc='dv')
            p.wait('t', 'dv', dv_tok)
            def bcast(e):
                e.matmul(k.pb[2][:, 0:NS], E2[0:2, :], k.st2m[:], start=True, stop=True)
                return e.matmul(k.pb[3][:, 0:NS], E2[0:2, :], k.st2r[:], start=True, stop=True)
            pe_tok = p.emit('t', bcast, inc='pe')
            p.wait('v', 'pe', pe_tok)
            def apply(e, sl=sl, g=g):
                e.tensor_sub(k.tscr[:, 0:NS], RTg[:, sl], k.pb[2][:, 0:NS])
                e.tensor_mul(k.tscr[:, 0:NS], k.tscr[:, 0:NS], k.pb[3][:, 0:NS])
                return e.tensor_scalar(Rn[:, sl], k.tscr[:, 0:NS],
                                       gammaT[:, g:g + 1], betaT[:, g:g + 1],
                                       ALU.mult, ALU.add)
            dv_tok = p.emit('v', apply, inc='dv')
            p.wait('t', 'dv', dv_tok)


def emit_dequant(k, wq_i32_hbm, w_sb, scales_hbm, scaleB, *, nk, nn, eng='g'):
    p = k.p
    for g in range(nk):
        p.dma(eng, w_sb[:, g * nn:(g + 1) * nn],
              wq_i32_hbm.rearrange("(c p) t -> p c t", p=128)[:, g:g + 1, :])
        p.dma(eng, scaleB[:], scales_hbm[g:g + 1, :].broadcast_to((128, nn)))
        p.wait('v', 'd' + eng)
        p.emit('v', lambda e, g=g: e.scalar_tensor_tensor(
            w_sb[:, g * nn:(g + 1) * nn], w_sb[:, g * nn:(g + 1) * nn], 128.0,
            scaleB[:], ALU.subtract, ALU.mult), inc='dv')
        p.wait('g' if eng == 'g' else 'y', 'dv')


def emit_ffn_fc1(k, w1, hT, b1f, y1T, *, tc, NS=512):
    p = k.p
    btok = [None, None]
    for n1c in range(32):
        bi = n1c % 2
        bank = k.pb[bi]
        if btok[bi] is not None:
            p.wait('t', 'ac', btok[bi])
        def mms(e, n1c=n1c, bank=bank, tc=tc):
            for dc in range(8):
                i = e.matmul(bank[:, 0:NS],
                             w1[:, dc * 4096 + n1c * 128: dc * 4096 + n1c * 128 + 128],
                             hT[:, dc * T + tc * NS: dc * T + tc * NS + NS],
                             start=(dc == 0), stop=(dc == 7))
            return i
        pe_tok = p.emit('t', mms, inc='pe')
        p.wait('s', 'pe', pe_tok)
        btok[bi] = p.emit('s', lambda e, bank=bank, n1c=n1c: e.activation(
            y1T[:, n1c * NS:(n1c + 1) * NS], bank[:, 0:NS], AF.Silu,
            bias=b1f[:, n1c:n1c + 1]), inc='ac')
    p.wait('t', 'ac')


def emit_ffn_fc2(k, w2, y1T, b2h, xnT, gT, *, tc, NS=512):
    p = k.p
    btok = [None, None]
    for n2c in range(8):
        bi = n2c % 2
        bank = k.pb[bi]
        if btok[bi] is not None:
            p.wait('t', 'dv', btok[bi])
        def mms(e, n2c=n2c, bank=bank):
            for kc in range(32):
                i = e.matmul(bank[:, 0:NS],
                             w2[:, kc * 1024 + n2c * 128: kc * 1024 + n2c * 128 + 128],
                             y1T[:, kc * NS:(kc + 1) * NS],
                             start=(kc == 0), stop=(kc == 31))
            return i
        pe_tok = p.emit('t', mms, inc='pe')
        p.wait('v', 'pe', pe_tok)
        sl = slice(n2c * T + tc * NS, n2c * T + tc * NS + NS)
        p.wait('v', 'ac')
        btok[bi] = p.emit('v', lambda e, bank=bank, sl=sl: e.scalar_tensor_tensor(
            k.tscr[:, 0:NS], bank[:, 0:NS], 0.5, xnT[:, sl], ALU.mult, ALU.add),
            inc='dv')
        p.wait('s', 'dv', btok[bi])
        tk = p.emit('s', lambda e, sl=sl, n2c=n2c: e.activation(
            gT[:, sl], k.tscr[:, 0:NS], AF.Silu, bias=b2h[:, n2c:n2c + 1]), inc='ac')
    p.wait('t', 'ac'); p.wait('v', 'ac')

def emit_retention2(k, Qg, Kg, ktok, Vtm, DT, QDECG, SDECG, Sf32, S2b, RTg, qfull):
    p = k.p
    rtA, rtB, ds = k.pb[2], k.pb[3], k.pb[4]
    Qgv = Qg.rearrange("p (g t) -> p g t", g=8)
    dve3_tok = None
    for c in range(16):
        p.wait('v', 'pe')
        def dve0(e, c=c):
            for g_ in range(8):
                e.tensor_mul(qfull[:, g_ * 128:(g_ + 1) * 128],
                             Qg[:, g_ * T + c * 128: g_ * T + c * 128 + 128],
                             QDECG[:, g_ * 128:(g_ + 1) * 128])
            return e.tensor_copy(S2b[:], Sf32[:])
        dv0_tok = p.emit('v', dve0, inc='dv')
        p.wait('t', 'dv', dv0_tok)
        for g in range(8):
            par = g % 2
            scA = k.pb[0] if par == 0 else k.pb[5]
            scB = k.pb[1] if par == 0 else k.pb[6]
            qcol = slice(g * T + c * 128, g * T + c * 128 + 128)
            vcol = slice(c * 1024 + g * 128, c * 1024 + g * 128 + 128)
            if dve3_tok is not None:
                p.wait('t', 'dv', dve3_tok)
            def pe1(e, g=g, qcol=qcol, scA=scA, scB=scB):
                e.matmul(rtA[0:64, 0:128], S2b[:, g * 128:g * 128 + 64],
                         qfull[:, g * 128:(g + 1) * 128], start=True, stop=False)
                e.matmul(rtB[64:128, 0:128], S2b[:, g * 128 + 64:g * 128 + 128],
                         qfull[:, g * 128:(g + 1) * 128], start=True, stop=False,
                         tile_position=(0, 64))
                e.matmul(scA[:, 0:128], Kg[0:64, qcol], Qg[0:64, qcol],
                         start=True, stop=True)
                return e.matmul(scB[:, 0:128], Kg[64:128, qcol], Qg[64:128, qcol],
                                start=True, stop=True, tile_position=(64, 0))
            pe_tok = p.emit('t', pe1, inc='pe')
            p.wait('v', 'pe', pe_tok)
            atA = k.atA if par == 0 else k.atA2
            atB = k.atB if par == 0 else k.atB2
            def dve2(e, g=g, scA=scA, scB=scB, atA=atA, atB=atB):
                e.tensor_mul(atA[:], scA[:, 0:128], DT[:, (2 * g) * 128:(2 * g) * 128 + 128])
                return e.tensor_mul(atB[:], scB[:, 0:128], DT[:, (2 * g + 1) * 128:(2 * g + 1) * 128 + 128])
            dv_tok = p.emit('v', dve2, inc='dv')
            p.wait('t', 'dv', dv_tok)
            def pe2(e, g=g, vcol=vcol, atA=atA, atB=atB):
                e.matmul(rtA[0:64, 0:128], Vtm[:, vcol.start:vcol.start + 64], atA[:],
                         start=False, stop=True)
                e.matmul(rtB[64:128, 0:128], Vtm[:, vcol.start + 64:vcol.start + 128], atB[:],
                         start=False, stop=True, tile_position=(0, 64))
                return e.matmul(ds[:, 0:128], ktok[:, vcol], Vtm[:, vcol],
                                start=True, stop=True)
            pe_tok = p.emit('t', pe2, inc='pe')
            p.wait('v', 'pe', pe_tok)
            def dve3(e, g=g, c=c):
                e.tensor_copy(RTg[0:64, g * T + c * 128: g * T + c * 128 + 128], rtA[0:64, 0:128])
                e.tensor_copy(RTg[64:128, g * T + c * 128: g * T + c * 128 + 128], rtB[64:128, 0:128])
                e.scalar_tensor_tensor(Sf32[0:64, g * 128:g * 128 + 64],
                                       Sf32[0:64, g * 128:g * 128 + 64],
                                       SDECG[0:64, g:g + 1], ds[0:64, 0:64],
                                       ALU.mult, ALU.add)
                return e.scalar_tensor_tensor(Sf32[64:128, g * 128 + 64:g * 128 + 128],
                                              Sf32[64:128, g * 128 + 64:g * 128 + 128],
                                              SDECG[64:128, g:g + 1], ds[64:128, 64:128],
                                              ALU.mult, ALU.add)
            dve3_tok = p.emit('v', dve3, inc='dv')
    p.wait('t', 'dv', dve3_tok)

def _din(nc, reg, name, arr, dt=None):
    if dt is None:
        dt = mybir.dt.from_np(np.asarray(arr[0] if isinstance(arr, list) else arr).dtype)
    a0 = np.asarray(arr[0] if isinstance(arr, list) else arr)
    reg[name] = arr
    return nc.dram_tensor(name, list(a0.shape), dt, kind="ExternalInput").ap()


def build_l1():
    nc = bass.Bass("TRN2", target_bir_lowering=False, detect_race_conditions=False)
    xT = nc.dram_tensor("xT", [D, T], F32, kind="ExternalInput").ap()
    oxn = nc.dram_tensor("oxn", [D, T], BF16, kind="ExternalOutput").ap()
    p = P(nc)
    k = alloc_common(nc, p)
    xb = nc.alloc_sbuf_tensor("xb", [128, 8 * T], BF16)
    xsq = nc.alloc_sbuf_tensor("xsq", [128, 8 * T], BF16)
    xn = nc.alloc_sbuf_tensor("xn", [128, 8 * T], BF16)
    p.dma('g', xb[:], rearr_fm(xT))
    p.wait('v', 'dg'); p.wait('t', 'dg'); p.wait('s', 'dg')
    emit_ln_fm(k, xb, xsq, xn)
    p.wait('y', 'dv')
    p.dma('y', rearr_fm(oxn), xn[:])
    p.wait('y', 'dy')
    p.finalize()
    return nc


def build_l2():
    nc = bass.Bass("TRN2", target_bir_lowering=False, detect_race_conditions=False)
    din = lambda name, shape, dt=F32: nc.dram_tensor(name, shape, dt, kind="ExternalInput").ap()
    a_xnT = din("xnT", [D, T], BF16)
    a_wq = din("wq", [D, D]); a_wk = din("wk", [D, D]); a_wv = din("wv", [D, D])
    a_cosT = din("cosT", [512, T]); a_sinT = din("sinT", [512, T])
    a_cosB = din("cosB", [T, 512]); a_sinB = din("sinB", [T, 512])
    o_q = nc.dram_tensor("o_q", [D, T], BF16, kind="ExternalOutput").ap()
    o_k = nc.dram_tensor("o_k", [D, T], BF16, kind="ExternalOutput").ap()
    o_kt = nc.dram_tensor("o_kt", [T, D], BF16, kind="ExternalOutput").ap()
    o_v = nc.dram_tensor("o_v", [T, D], BF16, kind="ExternalOutput").ap()
    p = P(nc)
    k = alloc_common(nc, p)
    A = lambda nm, shape, dt=BF16: nc.alloc_sbuf_tensor(nm, shape, dt)
    xnT = A("xnTs", [128, 8 * T])
    ws = A("ws", [128, 3 * 8 * 1024])
    wq, wk, wv = ws[:, 0:8192], ws[:, 8192:16384], ws[:, 16384:24576]
    tb8 = A("tb8", [128, 16384], F32)
    raw = A("raw", [128, 16384])
    scr = A("scr", [128, 3 * 512], F32)
    scrA, scrB, scrC = scr[:, 0:512], scr[:, 512:1024], scr[:, 1024:1536]
    p.dma('g', xnT[:], rearr_fm(a_xnT))
    p.dma('g', wq, rearr_fm(a_wq))
    p.dma('g', wk, rearr_fm(a_wk))
    p.dma('g', wv, rearr_fm(a_wv))
    p.dma('g', tb8[:, 0:8192], rearr_fm(a_cosT))
    p.dma('g', tb8[:, 8192:16384], rearr_fm(a_sinT))
    p.wait('t', 'dg'); p.wait('v', 'dg'); p.wait('s', 'dg')
    cosT, sinT = tb8[:, 0:8192], tb8[:, 8192:16384]

    def rot_and_store(src_fm, out_hbm):
        for c in range(4):
            for ts in range(T // 512):
                lo = src_fm[:, c * T + ts * 512: c * T + ts * 512 + 512]
                hi = src_fm[:, (c + 4) * T + ts * 512: (c + 4) * T + ts * 512 + 512]
                cs = cosT[:, c * T + ts * 512: c * T + ts * 512 + 512]
                sn = sinT[:, c * T + ts * 512: c * T + ts * 512 + 512]
                def rot(e, lo=lo, hi=hi, cs=cs, sn=sn):
                    e.tensor_mul(scrA[:], lo, cs)
                    e.tensor_mul(scrC[:], lo, sn)
                    e.tensor_mul(scrB[:], hi, sn)
                    e.tensor_sub(lo, scrA[:], scrB[:])
                    e.tensor_mul(scrB[:], hi, cs)
                    return e.tensor_add(hi, scrC[:], scrB[:])
                p.emit('v', rot, inc='dv')
        p.wait('y', 'dv')
        p.dma('y', rearr_fm(out_hbm), src_fm[:])

    emit_proj_fm(k, wq, xnT, raw)
    p.wait('v', 'ac')
    rot_and_store(raw, o_q)
    p.wait('s', 'dy')
    emit_proj_fm(k, wk, xnT, raw)
    p.wait('v', 'ac')
    rot_and_store(raw, o_k)
    p.wait('g', 'dv')
    p.dma('g', tb8[:, 0:8192], rearr_tm(a_cosB))
    p.dma('g', tb8[:, 8192:16384], rearr_tm(a_sinB))
    p.wait('s', 'dy')
    emit_proj_tm(k, wk, xnT, raw)
    p.wait('v', 'ac'); p.wait('v', 'dg')
    ktok = ws[:, 0:16384]
    emit_rot_tm(k, raw, ktok, tb8[:, 0:8192], tb8[:, 8192:16384], scrA, scrB, scrC)
    p.wait('y', 'dv')
    p.dma('y', rearr_tm(o_kt), ktok)
    p.wait('s', 'dy')
    emit_proj_tm(k, wv, xnT, raw)
    p.wait('y', 'ac')
    p.dma('y', rearr_tm(o_v), raw[:])
    p.wait('y', 'dy')
    p.finalize()
    return nc


def build_l3a():
    nc = bass.Bass("TRN2", target_bir_lowering=False, detect_race_conditions=False)
    din = lambda name, shape, dt=F32: nc.dram_tensor(name, shape, dt, kind="ExternalInput").ap()
    a_kt = din("kt", [T, D], BF16)
    a_v = din("v", [T, D], BF16)
    a_sdec = din("sdec", [128, 8])
    o_st = nc.dram_tensor("o_st", [128, 1024], F32, kind="ExternalOutput").ap()
    p = P(nc)
    k = alloc_common(nc, p)
    A = lambda nm, shape, dt=BF16: nc.alloc_sbuf_tensor(nm, shape, dt)
    ktok = A("ktoks", [128, 16384]); Vtm = A("Vtms", [128, 16384])
    sdec = A("sdecs", [128, 8], F32)
    Sf32 = A("Sf32", [128, 1024], F32)
    p.dma('g', ktok[:], rearr_tm(a_kt))
    p.dma('g', Vtm[:], rearr_tm(a_v))
    p.dma('g', sdec[:], a_sdec)
    p.emit('v', lambda e: e.memset(Sf32[:], 0.0), inc='dv')
    p.wait('t', 'dg'); p.wait('t', 'dv'); p.wait('v', 'dg')
    ds = k.pb[0]
    for c in range(16):
        for g in range(8):
            vcol = slice(c * 1024 + g * 128, c * 1024 + g * 128 + 128)
            pe_tok = p.emit('t', lambda e, vcol=vcol: e.matmul(
                ds[:, 0:128], ktok[:, vcol], Vtm[:, vcol], start=True, stop=True), inc='pe')
            p.wait('v', 'pe', pe_tok)
            def upd(e, g=g):
                e.scalar_tensor_tensor(Sf32[0:64, g * 128:g * 128 + 64],
                                       Sf32[0:64, g * 128:g * 128 + 64],
                                       sdec[0:64, g:g + 1], ds[0:64, 0:64],
                                       ALU.mult, ALU.add)
                return e.scalar_tensor_tensor(Sf32[64:128, g * 128 + 64:g * 128 + 128],
                                              Sf32[64:128, g * 128 + 64:g * 128 + 128],
                                              sdec[64:128, g:g + 1], ds[64:128, 64:128],
                                              ALU.mult, ALU.add)
            dv_tok = p.emit('v', upd, inc='dv')
            p.wait('t', 'dv', dv_tok)
    p.wait('y', 'dv')
    p.dma('y', o_st, Sf32[:])
    p.wait('y', 'dy')
    p.finalize()
    return nc


def build_l3b():
    nc = bass.Bass("TRN2", target_bir_lowering=False, detect_race_conditions=False)
    din = lambda name, shape, dt=F32: nc.dram_tensor(name, shape, dt, kind="ExternalInput").ap()
    a_qg = din("qg", [D, T], BF16)
    a_kg = din("kg", [D, T], BF16)
    a_kt = din("kt", [T, D], BF16)
    a_v = din("v", [T, D], BF16)
    a_dt = din("dtc", [128, 2048])
    a_qdec = din("qdec", [128, 1024])
    a_sdec = din("sdec", [128, 8])
    a_stin = din("stin", [128, 1024])
    a_gam = din("gam", [128, 8]); a_bet = din("bet", [128, 8])
    a_wo = din("wo", [D, D])
    a_ones2 = din("ones2", [128, 2], BF16); a_E2 = din("E2", [2, 128])
    o_s = nc.dram_tensor("o_s", [D, T], BF16, kind="ExternalOutput").ap()
    p = P(nc)
    k = alloc_common(nc, p)
    A = lambda nm, shape, dt=BF16: nc.alloc_sbuf_tensor(nm, shape, dt)
    Qg = A("Qgs", [128, 8 * T]); Kg = A("Kgs", [128, 8 * T])
    ktok = A("ktoks", [128, 16384]); Vtm = A("Vtms", [128, 16384])
    RTg = A("RTgs", [128, 8 * T])
    dtc = A("dtcs", [128, 2048], F32); qdec = A("qdecs", [128, 1024], F32)
    sdec = A("sdecs", [128, 8], F32)
    Sf32 = A("Sf32", [128, 1024], F32); S2b = A("S2bs", [128, 1024])
    gam = A("gams", [128, 8], F32); bet = A("bets", [128, 8], F32)
    qscr = A("qscrs", [128, 128]); atA = A("atAs", [128, 128]); atB = A("atBs", [128, 128])
    sqscr = A("sqscrs", [128, 512])
    ones2 = A("ones2s", [128, 2]); E2 = A("E2s", [2, 128], F32)
    p.dma('g', Qg[:], rearr_fm(a_qg))
    p.dma('g', Kg[:], rearr_fm(a_kg))
    p.dma('g', ktok[:], rearr_tm(a_kt))
    p.dma('g', Vtm[:], rearr_tm(a_v))
    p.dma('g', dtc[:], a_dt)
    p.dma('g', qdec[:], a_qdec)
    p.dma('g', sdec[:], a_sdec)
    p.dma('g', Sf32[:], a_stin)
    p.dma('g', gam[:], a_gam)
    p.dma('g', bet[:], a_bet)
    p.dma('g', ones2[:], a_ones2)
    p.dma('g', E2[:], a_E2)
    p.emit('v', lambda e: e.memset(S2b[:], 0.0), inc='dv')
    p.wait('t', 'dg'); p.wait('v', 'dg'); p.wait('s', 'dg'); p.wait('t', 'dv')
    emit_retention(k, Qg, Kg, ktok, Vtm, dtc, qdec, sdec, Sf32, S2b, RTg,
                   qscr, atA, atB)
    wo = ktok[:, 0:8192]
    p.wait('g', 'pe')
    p.dma('g', wo, rearr_fm(a_wo))
    emit_gnorm(k, RTg, RTg, ones2, E2, gam, bet, sqscr)
    p.wait('t', 'dv'); p.wait('t', 'dg')
    emit_proj_fm(k, wo, RTg, Qg)
    p.wait('y', 'ac')
    p.dma('y', rearr_fm(o_s), Qg[:])
    p.wait('y', 'dy')
    p.finalize()
    return nc


def build_l4():
    nc = bass.Bass("TRN2", target_bir_lowering=False, detect_race_conditions=False)
    din = lambda name, shape, dt=F32: nc.dram_tensor(name, shape, dt, kind="ExternalInput").ap()
    a_xn = din("xnT", [D, T], BF16)
    a_q1 = din("q1", [D, 4096], I32)
    a_sc1 = din("sc1", [8, 4096])
    a_b1 = din("b1", [128, 32])
    a_lg = din("lg", [128, 8]); a_lb = din("lb", [128, 8])
    o_y1 = nc.dram_tensor("o_y1", [4096, T], BF16, kind="ExternalOutput").ap()
    p = P(nc)
    k = alloc_common(nc, p)
    A = lambda nm, shape, dt=BF16: nc.alloc_sbuf_tensor(nm, shape, dt)
    xnT = A("xnTs", [128, 8 * T]); xsq = A("xsqs", [128, 8 * T])
    w1 = A("w1s", [128, 8 * 4096])
    scaleB = A("scaleBs", [128, 4096], F32)
    b1f = A("b1fs", [128, 32], F32)
    lg = A("lgs", [128, 8], F32); lb = A("lbs", [128, 8], F32)
    y1T = A("y1Ts", [128, 32 * 512])
    p.dma('g', xnT[:], rearr_fm(a_xn))
    p.dma('g', b1f[:], a_b1)
    p.dma('g', lg[:], a_lg)
    p.dma('g', lb[:], a_lb)
    emit_dequant(k, a_q1, w1, a_sc1, scaleB, nk=8, nn=4096)
    p.wait('t', 'dg'); p.wait('v', 'dg'); p.wait('s', 'dg')
    emit_ln_fm(k, xnT, xsq, xnT, gamma_ap=lg, beta_ap=lb)
    p.wait('t', 'dv')
    for tc in range(4):
        emit_ffn_fc1(k, w1, xnT, b1f, y1T, tc=tc)
        p.wait('y', 'ac')
        p.dma('y', o_y1.rearrange("(c p) n -> p c n", p=128)[:, :, 512 * tc:512 * (tc + 1)], y1T[:])
        p.wait('s', 'dy')
    p.wait('y', 'dy')
    p.finalize()
    return nc


def build_l5():
    nc = bass.Bass("TRN2", target_bir_lowering=False, detect_race_conditions=False)
    din = lambda name, shape, dt=F32: nc.dram_tensor(name, shape, dt, kind="ExternalInput").ap()
    a_xn = din("xnT", [D, T], BF16)
    a_y1 = din("y1", [4096, T], BF16)
    a_q2 = din("q2", [4096, 1024], I32)
    a_sc2 = din("sc2", [32, 1024])
    a_b2 = din("b2", [128, 8])
    o_g = nc.dram_tensor("o_g", [D, T], BF16, kind="ExternalOutput").ap()
    p = P(nc)
    k = alloc_common(nc, p)
    A = lambda nm, shape, dt=BF16: nc.alloc_sbuf_tensor(nm, shape, dt)
    xnT = A("xnTs", [128, 8 * T])
    w2 = A("w2s", [128, 32 * 1024])
    scaleB = A("scaleBs", [128, 1024], F32)
    b2h = A("b2hs", [128, 8], F32)
    y1T = A("y1Ts", [128, 32 * 512])
    gT = A("gTs", [128, 8 * T])
    p.dma('g', xnT[:], rearr_fm(a_xn))
    p.dma('g', b2h[:], a_b2)
    emit_dequant(k, a_q2, w2, a_sc2, scaleB, nk=32, nn=1024)
    p.wait('v', 'dg')
    p.emit('v', lambda e: e.tensor_scalar_mul(b2h[:], b2h[:], 0.5), inc='dv')
    p.wait('t', 'dg'); p.wait('s', 'dv'); p.wait('t', 'dv')
    for tc in range(4):
        p.wait('g', 'pe')
        p.dma('g', y1T[:], a_y1.rearrange("(c p) n -> p c n", p=128)[:, :, 512 * tc:512 * (tc + 1)])
        p.wait('t', 'dg')
        emit_ffn_fc2(k, w2, y1T, b2h, xnT, gT, tc=tc)
    p.wait('y', 'ac')
    p.dma('y', rearr_fm(o_g), gT[:])
    p.wait('y', 'dy')
    p.finalize()
    return nc


def build_l6():
    nc = bass.Bass("TRN2", target_bir_lowering=False, detect_race_conditions=False)
    din = lambda name, shape, dt=F32: nc.dram_tensor(name, shape, dt, kind="ExternalInput").ap()
    a_x = din("xT", [D, T])
    a_s = din("sT", [D, T], BF16)
    a_g = din("gT", [D, T], BF16)
    o_x1 = nc.dram_tensor("o_x1", [D, T], F32, kind="ExternalOutput").ap()
    o_xn1 = nc.dram_tensor("o_xn1", [D, T], BF16, kind="ExternalOutput").ap()
    o_m = nc.dram_tensor("o_m", [1, T], F32, kind="ExternalOutput").ap()
    o_r = nc.dram_tensor("o_r", [1, T], F32, kind="ExternalOutput").ap()
    p = P(nc)
    k = alloc_common(nc, p)
    A = lambda nm, shape, dt=BF16: nc.alloc_sbuf_tensor(nm, shape, dt)
    sT = A("sTs", [128, 8 * T]); gT = A("gTs", [128, 8 * T])
    x1b = A("x1bs", [128, 8 * T]); xn1 = A("xn1s", [128, 8 * T])
    xr = A("xrs", [128, 2 * T], F32)
    x1f = A("x1fs", [128, 2 * T], F32)
    p.dma('g', sT[:], rearr_fm(a_s))
    p.dma('g', gT[:], rearr_fm(a_g))
    for c in range(8):
        rr = (c % 2)
        p.wait('y', 'dv', p.tok('dv'))
        p.dma('y', xr[:, rr * T:(rr + 1) * T], rearr_fm(a_x)[:, c:c + 1, :])
        p.wait('v', 'dy'); p.wait('v', 'dg')
        def comb(e, c=c, rr=rr):
            e.tensor_mul(x1f[:, rr * T:(rr + 1) * T], sT[:, c * T:(c + 1) * T],
                         gT[:, c * T:(c + 1) * T])
            e.tensor_add(x1f[:, rr * T:(rr + 1) * T], x1f[:, rr * T:(rr + 1) * T],
                         xr[:, rr * T:(rr + 1) * T])
            return e.tensor_copy(x1b[:, c * T:(c + 1) * T], x1f[:, rr * T:(rr + 1) * T])
        dv_tok = p.emit('v', comb, inc='dv')
        p.wait('y', 'dv', dv_tok)
        p.dma('y', rearr_fm(o_x1)[:, c:c + 1, :], x1f[:, rr * T:(rr + 1) * T])
    msb = nc.alloc_sbuf_tensor("msb", [1, T], F32)
    rsb = nc.alloc_sbuf_tensor("rsb", [1, T], F32)
    emit_ln_fm(k, x1b, gT, xn1, m_out=msb, r_out=rsb)
    p.wait('y', 'dv')
    p.dma('y', rearr_fm(o_xn1), xn1[:])
    p.dma('y', o_m, msb[:])
    p.dma('y', o_r, rsb[:])
    p.wait('y', 'dy')
    p.finalize()
    return nc

def build_l7():
    nc = bass.Bass("TRN2", target_bir_lowering=False, detect_race_conditions=False)
    din = lambda name, shape, dt=F32: nc.dram_tensor(name, shape, dt, kind="ExternalInput").ap()
    a_keysT = din("keysT", [D, S])
    a_valsT = din("valsT", [D, S])
    a_wk = din("wkT", [D, 512]); a_wv = din("wvT", [D, 512])
    a_bk = din("bk", [128, 4]); a_bv = din("bv", [1, 512])
    o_kT = nc.dram_tensor("o_kT", [512, S], BF16, kind="ExternalOutput").ap()
    o_v65 = nc.dram_tensor("o_v65", [S, 528], BF16, kind="ExternalOutput").ap()
    p = P(nc)
    k = alloc_common(nc, p)
    A = lambda nm, shape, dt=BF16: nc.alloc_sbuf_tensor(nm, shape, dt)
    kb = A("kbs", [128, 8 * S])
    wk = A("wks", [128, 8 * 512]); wv = A("wvs", [128, 8 * 512])
    bkf = A("bkfs", [128, 4], F32)
    bvB = A("bvBs", [128, 512], F32)
    kT = A("kTs", [128, 4 * S])
    v65 = A("v65s", [128, 32 * 528])
    p.dma('g', kb[:], rearr_fm(a_keysT))
    p.dma('g', wk[:], rearr_fm(a_wk))
    p.dma('g', wv[:], rearr_fm(a_wv))
    p.dma('g', bkf[:], a_bk)
    p.dma('g', bvB[:], a_bv.broadcast_to((128, 512)))
    p.wait('t', 'dg'); p.wait('v', 'dg')
    emit_proj_fm(k, wk, kb, kT, nchunk_out=4, tlen=S, evac='v', bias_ap=bkf, relu=True)
    p.wait('y', 'dv')
    p.dma('y', rearr_fm(o_kT), kT[:])
    p.wait('g', 'pe')
    p.dma('g', kb[:], rearr_fm(a_valsT))
    p.emit('v', lambda e: e.memset(v65[:], 0.0))
    v65r = v65[:].rearrange("p (n h u) -> p n h u", n=32, h=8, u=66)
    p.emit('v', lambda e: e.memset(v65r[:, :, :, 64:65], 1.0), inc='dv')
    p.wait('t', 'dg')
    for tt in range(32):
        bank = k.pb[tt % 2]
        def mms(e, tt=tt, bank=bank):
            for dc in range(8):
                i = e.matmul(bank[:, 0:512], kb[:, dc * S + tt * 128: dc * S + tt * 128 + 128],
                             wv[:, dc * 512:(dc + 1) * 512], start=(dc == 0), stop=(dc == 7))
            return i
        pe_tok = p.emit('t', mms, inc='pe')
        p.wait('v', 'pe', pe_tok)
        def evac(e, tt=tt, bank=bank):
            e.tensor_add(k.tscr[:, 0:512], bank[:, 0:512], bvB[:])
            ov = v65[:, tt * 528:(tt + 1) * 528].rearrange("p (h u) -> p h u", h=8, u=66)[:, :, 0:64]
            return e.tensor_scalar_max(ov, k.tscr[:, 0:512].rearrange("p (h u) -> p h u", h=8, u=64), 0.0)
        dv_tok = p.emit('v', evac, inc='dv')
        p.wait('t', 'dv', dv_tok)
    p.wait('y', 'dv')
    p.dma('y', rearr_tm(o_v65), v65[:])
    p.wait('y', 'dy')
    p.finalize()
    return nc


def build_l8(dbg=False):
    nc = bass.Bass("TRN2", target_bir_lowering=False, detect_race_conditions=False)
    din = lambda name, shape, dt=F32: nc.dram_tensor(name, shape, dt, kind="ExternalInput").ap()
    a_xn1 = din("xn1T", [D, S], BF16)
    a_xn1own = din("xn1own", [D, T], BF16)
    a_wq = din("wqT", [D, 512]); a_bq = din("bq", [128, 4])
    a_kT = din("kT", [512, S], BF16)
    a_v65 = din("v65", [S, 528], BF16)
    a_id = din("ident", [128, 128], BF16)
    o_s1 = nc.dram_tensor("o_s1", [D, T], BF16, kind="ExternalOutput").ap()
    if dbg:
        o_pcop = nc.dram_tensor("o_pcop", [128, 528], F32, kind="ExternalOutput").ap()
        o_rec = nc.dram_tensor("o_rec", [128, 8], F32, kind="ExternalOutput").ap()
        o_avs = nc.dram_tensor("o_avs", [65, 1024], F32, kind="ExternalOutput").ap()
    p = P(nc)
    A = (lambda nm, shape, dt=BF16: nc.alloc_sbuf_tensor(nm, shape, dt))
    k = type('K', (), {})(); k.nc, k.p = nc, p
    psc = [nc.alloc_psum_tensor(f"psc{i}", [128, 1024], F32) for i in range(2)]
    avd = nc.alloc_psum_tensor("avd", [65, 1024], F32)
    pbh = nc.alloc_psum_tensor("pbh", [128, 1024], BF16)
    xb = A("xbs", [128, 8 * S])
    xo = A("xos", [128, 8 * T])
    qT = A("qTs", [128, 4 * S])
    wq = A("wqs", [128, 8 * 512]); bqf = A("bqfs", [128, 4], F32)
    s1T = A("s1Ts", [128, 8 * T])
    kts = A("ktss", [128, 2 * S])
    vs = A("vss", [128, 2 * 32 * 66])
    eb = A("ebs", [128, 2 * 1024])
    avs = A("avss", [65, 1024])
    pcop = A("pcops", [128, 528])
    rec = A("recs", [128, 8], F32)
    k.ident = A("idents", [128, 128])
    p.dma('g', k.ident[:], a_id)
    p.dma('g', xb[:], rearr_fm(a_xn1))
    p.dma('g', xo[:], rearr_fm(a_xn1own))
    p.dma('g', wq[:], rearr_fm(a_wq))
    p.dma('g', bqf[:], a_bq)
    p.wait('t', 'dg'); p.wait('v', 'dg')
    NS = 512
    for cch in range(4):
        for ts in range(S // NS):
            bank = psc[ts % 2]
            def mms(e, cch=cch, ts=ts, bank=bank):
                for dc in range(8):
                    i = e.matmul(bank[:, 0:NS],
                                 wq[:, dc * 512 + cch * 128: dc * 512 + cch * 128 + 128],
                                 xb[:, dc * S + ts * NS: dc * S + ts * NS + NS],
                                 start=(dc == 0), stop=(dc == 7))
                return i
            pe_tok = p.emit('t', mms, inc='pe')
            p.wait('v', 'pe', pe_tok)
            osl = qT[:, cch * S + ts * NS: cch * S + ts * NS + NS]
            tk = p.emit('v', lambda e, bank=bank, osl=osl, cch=cch: e.tensor_scalar(
                osl, bank[:, 0:NS], bqf[:, cch:cch + 1], 0.0, ALU.add, ALU.max), inc='dv')
            p.wait('t', 'dv', tk)
    kload_tok = {}
    for h in range(8):
        pair = h // 2
        if h % 2 == 0:
            p.wait('g', 'pe')
            kload_tok[pair] = p.dma('g', kts[:, (pair % 2) * S:(pair % 2 + 1) * S],
                                    rearr_fm(a_kT)[:, pair:pair + 1, :])
        p.wait('g', 'pe')
        vs_tok = p.dma('g', vs[:, (h % 2) * 2112:(h % 2 + 1) * 2112],
                       rearr_tm(a_v65)[:, :, 66 * h:66 * h + 66])
        krow = slice(64 * (h % 2), 64 * (h % 2) + 64)
        ktsv = kts[:, (pair % 2) * S:(pair % 2 + 1) * S]
        vsv = vs[:, (h % 2) * 2112:(h % 2 + 1) * 2112]
        for qb in range(4):
            for kt in range(32):
                bank = psc[kt % 2]
                p.wait('t', 'dg', kload_tok[pair])
                if exp_tok[kt % 2] is not None:
                    p.wait('t', 'ac', exp_tok[kt % 2])
                def sc(e, kt=kt, bank=bank, krow=krow, ktsv=ktsv, qb=qb, h=h):
                    e.matmul(bank[:, 0:512], ktsv[krow, kt * 128:(kt + 1) * 128],
                             qT[krow, (h // 2) * S + qb * 1024: (h // 2) * S + qb * 1024 + 512],
                             start=True, stop=True)
                    return e.matmul(bank[:, 512:1024], ktsv[krow, kt * 128:(kt + 1) * 128],
                                    qT[krow, (h // 2) * S + qb * 1024 + 512: (h // 2) * S + qb * 1024 + 1024],
                                    start=True, stop=True)
                pe_tok = p.emit('t', sc, inc='pe')
                p.wait('s', 'pe', pe_tok)
                if av_tok[kt % 2] is not None:
                    p.wait('s', 'pe', av_tok[kt % 2])
                exp_tok[kt % 2] = p.emit('s', lambda e, kt=kt, bank=bank: e.activation(
                    eb[:, (kt % 2) * 1024:(kt % 2 + 1) * 1024], bank[:], AF.Exp, scale=0.125),
                    inc='ac')
                p.wait('t', 'ac', exp_tok[kt % 2])
                p.wait('t', 'dg', vs_tok)
                def av(e, kt=kt, vsv=vsv):
                    e.matmul(avd[:, 0:512], vsv[:, kt * 66:kt * 66 + 65],
                             eb[:, (kt % 2) * 1024:(kt % 2) * 1024 + 512],
                             start=(kt == 0), stop=(kt == 31))
                    return e.matmul(avd[:, 512:1024], vsv[:, kt * 66:kt * 66 + 65],
                                    eb[:, (kt % 2) * 1024 + 512:(kt % 2) * 1024 + 1024],
                                    start=(kt == 0), stop=(kt == 31))
                av_tok[kt % 2] = p.emit('t', av, inc='pe')
            p.wait('v', 'pe', av_tok[31 % 2])
            dv_tok = p.emit('v', lambda e: e.tensor_copy(avs[:], avd[:]), inc='dv')
            p.wait('t', 'dv', dv_tok)
            def tps(e):
                for i in range(8):
                    e.transpose(pbh[:, 66 * i:66 * i + 65], avs[:, 128 * i:128 * i + 128],
                                k.ident[0:65, 0:65])
                return e.transpose(pbh[:, 700:765], avs[:, 0:128], k.ident[0:65, 0:65])
            pe_tok = p.emit('t', tps, inc='pe')
            p.wait('v', 'pe', pe_tok)
            def evacp(e):
                e.reciprocal(rec[:, 0:8], pbh[:, 64:528:66])
                return e.tensor_copy(pcop[:], pbh[:, 0:528])
            dv_tok = p.emit('v', evacp, inc='dv')
            p.wait('t', 'dv', dv_tok)
            if dbg and h == 0 and qb == 0:
                pcf = nc.alloc_sbuf_tensor("pcfs", [128, 528], F32)
                avf = nc.alloc_sbuf_tensor("avfs", [65, 1024], F32)
                p.emit('v', lambda e: e.tensor_copy(pcf[:], pcop[:]))
                p.emit('v', lambda e: e.tensor_copy(avf[:], avs[:]), inc='dv')
                p.wait('y', 'dv')
                p.dma('y', o_pcop, pcf[:])
                p.dma('y', o_rec, rec[:])
                p.dma('y', o_avs, avf[:])
            s1v = s1T[:].rearrange("p (tc hh u f) -> p tc hh u f", tc=8, hh=8, u=64, f=4)
            xov = xo[:].rearrange("p (tc hh u f) -> p tc hh u f", tc=8, hh=8, u=64, f=4)
            def perm(e, h=h, qb=qb, s1v=s1v, xov=xov):
                for tc in range(8):
                    osl = s1v[:, tc:tc + 1, h:h + 1, :, qb:qb + 1]
                    isl = xov[:, tc:tc + 1, h:h + 1, :, qb:qb + 1]
                    ii = e.scalar_tensor_tensor(
                        osl, pcop[:, 66 * tc:66 * tc + 64].rearrange(
                            "p (a u f) -> p a u f", a=1, u=64, f=1),
                        rec[:, tc:tc + 1], isl, ALU.mult, ALU.add)
                return ii
            dv_tok = p.emit('v', perm, inc='dv')
    p.wait('y', 'dv')
    p.dma('y', rearr_fm(o_s1), s1T[:])
    p.wait('y', 'dy')
    p.finalize()
    return nc


def build_l9(dbg=False):
    nc = bass.Bass("TRN2", target_bir_lowering=False, detect_race_conditions=False)
    din = lambda name, shape, dt=F32: nc.dram_tensor(name, shape, dt, kind="ExternalInput").ap()
    a_x1 = din("x1T", [D, T])
    a_wg = din("wg", [D, 8])
    a_m = din("mrow", [1, T])
    a_rtm = din("r_tm", [128, 16])
    a_negb = din("negbig", [128, 32])
    a_posw = din("posw", [128, 32])

    a_id = din("ident", [128, 128])
    o_oh = nc.dram_tensor("o_oh", [8, T], F32, kind="ExternalOutput").ap()
    o_rk = nc.dram_tensor("o_rk", [8, T], F32, kind="ExternalOutput").ap()
    o_gate = nc.dram_tensor("o_gate", [128, 16], F32, kind="ExternalOutput").ap()
    o_hist = nc.dram_tensor("o_hist", [8, 1], F32, kind="ExternalOutput").ap()
    if dbg:
        o_lgs = nc.dram_tensor("o_lgs", [128, 8], F32, kind="ExternalOutput").ap()
        o_oh0 = nc.dram_tensor("o_oh0", [128, 8], F32, kind="ExternalOutput").ap()
        o_ohd = nc.dram_tensor("o_ohd", [128, 8], F32, kind="ExternalOutput").ap()
        o_sc8 = nc.dram_tensor("o_sc8", [128, 8], F32, kind="ExternalOutput").ap()
        o_praw = nc.dram_tensor("o_praw", [128, 8], F32, kind="ExternalOutput").ap()
        o_ppm = nc.dram_tensor("o_ppm", [128, 8], F32, kind="ExternalOutput").ap()
        o_wsum = nc.dram_tensor("o_wsum", [1, 8], F32, kind="ExternalOutput").ap()
        o_lgs1 = nc.dram_tensor("o_lgs1", [128, 8], F32, kind="ExternalOutput").ap()
    p = P(nc)
    k = alloc_common(nc, p)
    A = lambda nm, shape, dt=F32: nc.alloc_sbuf_tensor(nm, shape, dt)
    x1f = A("x1fs", [128, 8 * T])
    wg = A("wgs", [128, 8 * 8])
    msb = A("msbs", [1, T]); rtm = A("rtms", [128, 16])
    oh0 = A("oh0s", [128, 32]); sc8 = A("sc8s", [128, 32])
    t_pm = A("t_pms", [128, 32]); t_raw = A("t_raws", [128, 32]); t_d = A("t_ds", [128, 32])
    zz8 = A("zz8s", [128, 32]); big30 = A("big30s", [128, 32])
    posw = A("posws", [128, 32]); one30 = A("one30s", [128, 32])
    t1 = A("t1s", [128, 32]); t2 = A("t2s", [128, 32]); t3 = A("t3s", [128, 32])
    q8 = A("q8s", [128, 32]); m2 = A("m2s", [128, 32]); nm2 = A("nm2s", [128, 32])
    idf = A("idfs", [128, 128])
    wsum = A("wsums", [1, 8])
    onescol = A("onescols", [128, 1])
    lgs = A("lgss", [128, 32]); el = A("els", [128, 32])
    mx = A("mxs", [128, 32]); nmx = A("nmxs", [128, 32]); sm = A("sms", [128, 32])
    negb = A("negbs", [128, 32])
    oh = A("ohs", [128, 32])
    gate_tm = A("gatetms", [128, 16]); smbuf = A("smbufs", [128, 16])
    ohT = A("ohTs", [8, T]); scanT = A("scanTs", [8, T]); rkT = A("rkTs", [8, T])
    p.dma('g', x1f[:], rearr_fm(a_x1))
    p.dma('g', wg[:], rearr_fm(a_wg))
    p.dma('g', msb[:], a_m)
    p.dma('g', rtm[:], a_rtm)
    p.dma('g', negb[:], a_negb)
    p.dma('g', posw[:], a_posw)
    p.dma('g', idf[:], a_id)
    p.emit('v', lambda e: e.memset(onescol[:], 1.0))
    p.emit('v', lambda e: e.memset(zz8[:], 0.0))
    p.emit('v', lambda e: e.memset(big30[:], 1e30))
    p.emit('v', lambda e: e.memset(one30[:], 1.0))
    p.emit('v', lambda e: e.memset(k.pb[0][:, 0:32], 0.0))
    p.emit('v', lambda e: e.memset(k.pb[1][:, 0:32], 0.0), inc='dv')
    p.wait('t', 'dg'); p.wait('v', 'dg'); p.wait('t', 'dv')
    def mmw(e):
        for dc in range(8):
            i = e.matmul(k.pb[2][0:1, 0:8], onescol[:, 0:1],
                         wg[:, dc * 8:(dc + 1) * 8], start=(dc == 0), stop=(dc == 7))
        return i
    pe_tok = p.emit('t', mmw, inc='pe')
    p.wait('v', 'pe', pe_tok)
    dv_tok = p.emit('v', lambda e: e.tensor_copy(wsum[:], k.pb[2][0:1, 0:8]), inc='dv')
    p.wait('t', 'dv', dv_tok)
    for tc in range(16):
        def mmr(e, tc=tc):
            for dc in range(8):
                e.matmul(k.pb[0][:, 0:8], x1f[:, dc * T + tc * 128: dc * T + tc * 128 + 128],
                         wg[:, dc * 8:(dc + 1) * 8], start=(dc == 0), stop=(dc == 7))
            return e.matmul(k.pb[1][:, 0:8], msb[0:1, tc * 128:(tc + 1) * 128], wsum[:],
                     start=True, stop=True)
        pe_tok = p.emit('t', mmr, inc='pe')
        p.wait('v', 'pe', pe_tok)
        if dbg and tc == 0:
            prawsb = nc.alloc_sbuf_tensor("prawsb", [128, 8], F32)
            ppmsb = nc.alloc_sbuf_tensor("ppmsb", [128, 8], F32)
            p.emit('v', lambda e: e.tensor_copy(prawsb[:], k.pb[0][:, 0:8]))
            dvt = p.emit('v', lambda e: e.tensor_copy(ppmsb[:], k.pb[1][:, 0:8]), inc='dv')
            p.wait('y', 'dv', dvt)
            p.dma('y', o_praw, prawsb[:])
            p.dma('y', o_ppm, ppmsb[:])
            p.dma('y', o_wsum, wsum[:])
            p.wait('v', 'dy')
        def gdve(e, tc=tc):
            e.tensor_copy(t_pm[:], k.pb[1][:, 0:32])
            e.tensor_copy(t_raw[:], k.pb[0][:, 0:32])
            e.tensor_sub(t_d[:], t_raw[:], t_pm[:])
            e.scalar_tensor_tensor(lgs[:], t_d[:], rtm[:, tc:tc + 1], negb[:],
                                   ALU.mult, ALU.add)
            e.reduce_max(nmx[:, 0:1], lgs[:], AX.X, negate=True)
            e.tensor_copy(t_pm[:], lgs[:])
            return e.tensor_copy(t_raw[:], lgs[:])
        dv_tok = p.emit('v', gdve, inc='dv')
        if dbg and tc == 0:
            p.wait('y', 'dv', dv_tok)
            p.dma('y', o_lgs1, lgs[:, 0:8])
            p.wait('v', 'dy')
        p.wait('s', 'dv', dv_tok)
        ac_tok = p.emit('s', lambda e: e.activation(el[:], lgs[:], AF.Exp, bias=nmx[:, 0:1]), inc='ac')
        p.wait('v', 'ac', ac_tok)
        def gdve2(e, tc=tc):
            e.reduce_sum(smbuf[:, tc:tc + 1], el[:], AX.X)
            e.scalar_tensor_tensor(t1[:], lgs[:], nmx[:, 0:1], big30[:],
                                   ALU.add, ALU.mult)
            e.tensor_add(t2[:], t1[:], one30[:])
            e.tensor_max(t3[:], t2[:], zz8[:])
            e.tensor_tensor(oh0[:], t3[:], one30[:], ALU.min)
            e.tensor_mul(q8[:], oh0[:], posw[:])
            e.reduce_max(nm2[:, 0:1], q8[:], AX.X, negate=True)
            e.tensor_copy(t2[:], q8[:])
            e.tensor_copy(t3[:], q8[:])
            e.scalar_tensor_tensor(t1[:], q8[:], nm2[:, 0:1], big30[:],
                                   ALU.add, ALU.mult)
            e.tensor_add(t2[:], t1[:], one30[:])
            e.tensor_max(t3[:], t2[:], zz8[:])
            return e.tensor_tensor(oh[:], t3[:], one30[:], ALU.min)
        dv_tok = p.emit('v', gdve2, inc='dv')
        if dbg and tc == 0:
            p.wait('y', 'dv', dv_tok)
            p.dma('y', o_lgs, lgs[:, 0:8])
            p.dma('y', o_oh0, oh0[:, 0:8])
            p.dma('y', o_ohd, oh[:, 0:8])
            p.dma('y', o_sc8, sc8[:, 0:8])
            p.wait('v', 'dy')
        p.wait('t', 'dv', dv_tok)
        def tpo(e, tc=tc):
            e.transpose(k.pb[3][0:32, 0:128], oh[:], idf[:])
            return e.transpose(k.pb[5][0:32, 0:128], oh[:], idf[:])
        pe_tok = p.emit('t', tpo, inc='pe')
        p.wait('v', 'pe', pe_tok)
        dv_tok = p.emit('v', lambda e, tc=tc: e.tensor_copy(ohT[0:8, tc * 128:(tc + 1) * 128],
                                                            k.pb[3][0:8, 0:128]), inc='dv')
        p.wait('t', 'dv', dv_tok)
    def scan(e):
        e.reciprocal(gate_tm[:], smbuf[:])
        e.tensor_tensor_scan(scanT[0:8, :], ohT[0:8, :], ohT[0:8, :], 0.0, ALU.add, ALU.bypass)
        return e.tensor_sub(rkT[0:8, :], scanT[0:8, :], ohT[0:8, :])
    dv_tok = p.emit('v', scan, inc='dv')
    p.wait('y', 'dv')
    p.dma('y', o_oh, ohT[:])
    p.dma('y', o_rk, rkT[:])
    p.dma('y', o_gate, gate_tm[:])
    p.dma('y', o_hist, scanT[0:8, T - 1:T])
    p.wait('y', 'dy')
    p.finalize()
    return nc



def build_l10():
    nc = bass.Bass("TRN2", target_bir_lowering=False, detect_race_conditions=False)
    din = lambda name, shape, dt=F32: nc.dram_tensor(name, shape, dt, kind="ExternalInput").ap()
    a_gath = din("gath", [8, 8])
    a_mask = din("maskc", [8, 1])
    a_oh = din("ohT", [8, T]); a_rk = din("rkT", [8, T])
    a_gate = din("gate", [128, 16])
    a_xn1 = din("xn1T", [D, T], BF16)
    a_we = din("we", [D, D])
    a_be = din("be", [128, 8])
    a_s1 = din("s1T", [D, T], BF16)
    a_x1 = din("x1T", [D, T])
    a_id = din("ident", [128, 128])
    o_out = nc.dram_tensor("o_out", [D, T], F32, kind="ExternalOutput").ap()
    p = P(nc)
    k = alloc_common(nc, p)
    A = lambda nm, shape, dt=F32: nc.alloc_sbuf_tensor(nm, shape, dt)
    gath = A("gaths", [8, 8]); maskc = A("maskcs", [8, 1])
    ohT = A("ohTs", [8, T]); rkT = A("rkTs", [8, T]); keptT = A("keptTs", [8, T])
    gate_tm = A("gatetms", [128, 16]); smbuf = A("smbufs", [128, 16]); offs = A("offss", [8, 1])
    xn1 = nc.alloc_sbuf_tensor("xn1s", [128, 8 * T], BF16)
    s1T = nc.alloc_sbuf_tensor("s1Ts", [128, 8 * T], BF16)
    we = nc.alloc_sbuf_tensor("wes", [128, 8 * 1024], BF16)
    bef = A("befs", [128, 8])
    idf = A("idfs", [128, 128])
    kcol = A("kcols", [128, 1]); gk = A("gks", [128, 16]); kcolb = A("kcolbs", [128, 16])
    gkrow = A("gkrows", [1, T]); gkB = A("gkBs", [128, T])
    x1r = A("x1rs", [128, 2 * T])
    sil = nc.alloc_sbuf_tensor("sils", [128, 512], BF16)
    t2 = A("t2s", [128, 512])
    p.dma('g', gath[:], a_gath)
    p.dma('g', maskc[:], a_mask)
    p.dma('g', ohT[:], a_oh)
    p.dma('g', rkT[:], a_rk)
    p.dma('g', gate_tm[:], a_gate)
    p.dma('g', xn1[:], rearr_fm(a_xn1))
    p.dma('g', we[:], rearr_fm(a_we))
    p.dma('g', bef[:], a_be)
    p.dma('g', idf[:], a_id)
    p.dma('g', s1T[:], rearr_fm(a_s1))
    p.wait('t', 'dg'); p.wait('v', 'dg')
    pe_tok = p.emit('t', lambda e: e.matmul(k.pb[0][0:8, 0:1], gath[0:8, :], maskc[0:8, 0:1],
                                            start=True, stop=True), inc='pe')
    p.wait('v', 'pe', pe_tok)
    def kdve(e):
        e.tensor_copy(offs[:], k.pb[0][0:8, 0:1])
        e.tensor_copy(keptT[0:8, :], rkT[0:8, :])
        e.tensor_copy(keptT[0:8, :], rkT[0:8, :])
        e.tensor_scalar(keptT[0:8, :], rkT[0:8, :], offs[0:8, 0:1], -1.0, ALU.add, ALU.mult)
        e.tensor_scalar(keptT[0:8, :], keptT[0:8, :], 100.0, 0.0, ALU.add, ALU.max)
        e.tensor_scalar(keptT[0:8, :], keptT[0:8, :], 1.0, 0.0, ALU.min, ALU.add)
        return e.tensor_mul(keptT[0:8, :], keptT[0:8, :], ohT[0:8, :])
    dv_tok = p.emit('v', kdve, inc='dv')
    p.wait('t', 'dv', dv_tok)
    for tc in range(16):
        def tpk(e, tc=tc):
            e.transpose(k.pb[1][0:128, 0:8], keptT[0:8, tc * 128:(tc + 1) * 128], idf[0:8, 0:8])
            return e.transpose(k.pb[5][0:128, 0:8], keptT[0:8, tc * 128:(tc + 1) * 128], idf[0:8, 0:8])
        pe_tok = p.emit('t', tpk, inc='pe')
        p.wait('v', 'pe', pe_tok)
        dv_tok = p.emit('v', lambda e, tc=tc: e.reduce_sum(kcolb[:, tc:tc + 1],
                                                           k.pb[1][0:128, 0:8], AX.X), inc='dv')
        p.wait('t', 'dv', dv_tok)
    dv_tok = p.emit('v', lambda e: e.tensor_mul(gk[:], kcolb[:], gate_tm[:]), inc='dv')
    p.wait('t', 'dv', dv_tok)
    for tc in range(16):
        def tpg(e, tc=tc):
            e.transpose(k.pb[1][0:1, 0:128], gk[:, tc:tc + 1], idf[:, :])
            return e.transpose(k.pb[5][0:1, 0:128], gk[:, tc:tc + 1], idf[:, :])
        pe_tok = p.emit('t', tpg, inc='pe')
        p.wait('v', 'pe', pe_tok)
        dv_tok = p.emit('v', lambda e, tc=tc: e.tensor_copy(
            gkrow[0:1, tc * 128:(tc + 1) * 128], k.pb[1][0:1, 0:128]), inc='dv')
        p.wait('t', 'dv', dv_tok)
    for ts in range(4):
        pe_tok = p.emit('t', lambda e, ts=ts: e.matmul(
            k.pb[2][:, 0:512], k.ones_f1[0:1, :], gkrow[0:1, ts * 512:(ts + 1) * 512],
            start=True, stop=True), inc='pe')
        p.wait('v', 'pe', pe_tok)
        dv_tok = p.emit('v', lambda e, ts=ts: e.tensor_copy(
            gkB[:, ts * 512:(ts + 1) * 512], k.pb[2][:, 0:512]), inc='dv')
        p.wait('t', 'dv', dv_tok)
    for cch in range(8):
        p.wait('y', 'dv', p.tok('dv'))
        x1tok = p.dma('y', x1r[:, (cch % 2) * T:(cch % 2 + 1) * T], rearr_fm(a_x1)[:, cch:cch + 1, :])
        for ts in range(4):
            bank = k.pb[3 + (ts % 2)]
            def mms(e, cch=cch, ts=ts, bank=bank):
                for dc in range(8):
                    i = e.matmul(bank[:, 0:512],
                                 we[:, dc * 1024 + cch * 128: dc * 1024 + cch * 128 + 128],
                                 xn1[:, dc * T + ts * 512: dc * T + ts * 512 + 512],
                                 start=(dc == 0), stop=(dc == 7))
                return i
            pe_tok = p.emit('t', mms, inc='pe')
            p.wait('v', 'pe', pe_tok)
            sl = slice(cch * T + ts * 512, cch * T + ts * 512 + 512)
            def md1(e, cch=cch, ts=ts, bank=bank, sl=sl):
                return e.scalar_tensor_tensor(k.tscr[:, 0:512], bank[:, 0:512],
                                              bef[:, cch:cch + 1],
                                              gkB[:, ts * 512:(ts + 1) * 512],
                                              ALU.add, ALU.mult)
            dv_tok = p.emit('v', md1, inc='dv')
            p.wait('s', 'dv', dv_tok)
            ac_tok = p.emit('s', lambda e: e.activation(sil[:], k.tscr[:, 0:512], AF.Silu), inc='ac')
            p.wait('v', 'ac', ac_tok)
            p.wait('v', 'dy', x1tok)
            def md2(e, cch=cch, ts=ts, sl=sl):
                e.tensor_mul(t2[:], sil[:], s1T[:, sl])
                return e.tensor_add(t2[:], t2[:],
                                    x1r[:, (cch % 2) * T + ts * 512: (cch % 2) * T + ts * 512 + 512])
            dv_tok = p.emit('v', md2, inc='dv')
            p.wait('y', 'dv', dv_tok)
            p.dma('y', rearr_fm(o_out)[:, cch:cch + 1, ts * 512:(ts + 1) * 512], t2[:])
            p.wait('t', 'dy')
    p.wait('y', 'dy')
    p.finalize()
    return nc



def _hostfm_rows(arr):
    return np.ascontiguousarray(arr)

def _bf(a):
    import ml_dtypes
    return np.asarray(a).astype(ml_dtypes.bfloat16)

def _run(nc, in_maps):
    res = run_bass_kernel_spmd(nc, in_maps, core_ids=list(range(8)))
    return res.results

_USE_FUSED = True

def _posw():
    w = np.zeros((128, 32), np.float32)
    w[:, 0:8] = (8.0 - np.arange(8, dtype=np.float32))[None, :]
    return w

def kernel(**inp):
    if _USE_FUSED:
        return kernel_fused(inp)
    import ml_dtypes
    TB = make_tables()
    perm = make_perm()
    gm = np.zeros(1024, dtype=np.int64)
    for h in range(H):
        g, hf = h // 2, h % 2
        cr, qq = h // 4, h % 4
        for x1 in range(2):
            for i in range(32):
                gm[128 * g + 64 * hf + 32 * x1 + i] = (cr + 4 * x1) * 128 + 32 * qq + i
    x = np.ascontiguousarray(inp['L0_attn_x'].astype(np.float32))
    cores = [(c // 2, c % 2) for c in range(8)]
    sls = [slice(2048 * hf, 2048 * hf + 2048) for (_, hf) in cores]
    xT = [np.ascontiguousarray(x[b, s_].T) for (b, _), s_ in zip(cores, sls)]

    r = _run(build_l1(), [{"xT": xT[c]} for c in range(8)])
    xnT = [np.asarray(r[c]["oxn"]) for c in range(8)]

    WQpT = np.ascontiguousarray(inp['L0_attn_WQ'].reshape(D, D)[perm].T)
    WKpT = np.ascontiguousarray(inp['L0_attn_WK'].reshape(D, D)[perm].T)
    WVT = np.ascontiguousarray(inp['L0_attn_WV'].reshape(D, D).T)
    ins = []
    for c in range(8):
        s_ = sls[c]
        ins.append({"xnT": xnT[c], "wq": WQpT, "wk": WKpT, "wv": WVT,
                    "cosT": np.ascontiguousarray(TB['COS_T'][:, s_]),
                    "sinT": np.ascontiguousarray(TB['SIN_T'][:, s_]),
                    "cosB": np.ascontiguousarray(TB['COS_BK'][s_]),
                    "sinB": np.ascontiguousarray(TB['SIN_BK'][s_])})
    r = _run(build_l2(), ins)
    q_rot = [np.asarray(r[c]["o_q"]) for c in range(8)]
    k_rot = [np.asarray(r[c]["o_k"]) for c in range(8)]
    ktok = [np.asarray(r[c]["o_kt"]) for c in range(8)]
    vtm = [np.asarray(r[c]["o_v"]) for c in range(8)]

    r = _run(build_l3a(), [{"kt": ktok[c], "v": vtm[c], "sdec": TB['SDECG']}
                           for c in range(8)])
    states = [np.asarray(r[c]["o_st"]) for c in range(8)]

    o2 = np.zeros((128, 2), ml_dtypes.bfloat16); o2[0:64, 0] = 1; o2[64:128, 1] = 1
    E2h = np.zeros((2, 128), np.float32); E2h[0, 0:64] = 1; E2h[1, 64:128] = 1
    gamT = inp['L0_attn_gamma'].reshape(8, 128).T.copy()
    betT = inp['L0_attn_beta'].reshape(8, 128).T.copy()
    ins = []
    for c in range(8):
        b, hf = cores[c]
        stin = states[2 * b] if hf == 1 else np.zeros((128, 1024), np.float32)
        ins.append({"qg": q_rot[c][gm], "kg": k_rot[c][gm],
                    "kt": ktok[c], "v": vtm[c],
                    "dtc": TB['DTcat'], "qdec": TB['QDECG'].astype(np.float32),
                    "sdec": TB['SDECG'], "stin": np.ascontiguousarray(stin),
                    "gam": gamT, "bet": betT, "wo": inp['L0_attn_WO'],
                    "ones2": o2, "E2": E2h})
    r = _run(build_l3b(), ins)
    sT = [np.asarray(r[c]["o_s"]) for c in range(8)]

    ins = [{"xnT": xnT[c], "q1": inp['L0_ffn_fc1_q'], "sc1": inp['L0_ffn_fc1_scales'],
            "b1": inp['L0_ffn_fc1_bias'].reshape(32, 128).T.copy(),
            "lg": inp['L0_ffn_ln_gamma'].reshape(8, 128).T.copy(),
            "lb": inp['L0_ffn_ln_beta'].reshape(8, 128).T.copy()} for c in range(8)]
    r = _run(build_l4(), ins)
    y1 = [np.asarray(r[c]["o_y1"]) for c in range(8)]
    ins = [{"xnT": xnT[c], "y1": y1[c], "q2": inp['L0_ffn_fc2_q'],
            "sc2": inp['L0_ffn_fc2_scales'],
            "b2": inp['L0_ffn_fc2_bias'].reshape(8, 128).T.copy()} for c in range(8)]
    r = _run(build_l5(), ins)
    gT = [np.asarray(r[c]["o_g"]) for c in range(8)]

    ins = [{"xT": xT[c], "sT": sT[c], "gT": gT[c]} for c in range(8)]
    r = _run(build_l6(), ins)
    x1T = [np.asarray(r[c]["o_x1"]) for c in range(8)]
    xn1T = [np.asarray(r[c]["o_xn1"]) for c in range(8)]
    mrow = [np.asarray(r[c]["o_m"]) for c in range(8)]
    rrow = [np.asarray(r[c]["o_r"]) for c in range(8)]

    keysT = np.ascontiguousarray(inp['L1_attn_keys'][0].T.astype(np.float32))
    valsT = np.ascontiguousarray(inp['L1_attn_values'][0].T.astype(np.float32))
    ins = []
    for c in range(8):
        hh = cores[c][1]
        ins.append({"keysT": keysT, "valsT": valsT,
                    "wkT": np.ascontiguousarray(inp['L1_attn_Wk_w'][512 * hh:512 * (hh + 1)].T),
                    "wvT": np.ascontiguousarray(inp['L1_attn_Wv_w'][512 * hh:512 * (hh + 1)].T),
                    "bk": inp['L1_attn_Wk_b'][512 * hh:512 * (hh + 1)].reshape(4, 128).T.copy(),
                    "bv": inp['L1_attn_Wv_b'][512 * hh:512 * (hh + 1)].reshape(1, 512)})
    r = _run(build_l7(), ins)
    kTl = [np.asarray(r[c]["o_kT"]) for c in range(8)]
    v65l = [np.asarray(r[c]["o_v65"]) for c in range(8)]

    idb = np.eye(128, dtype=ml_dtypes.bfloat16)
    ins = []
    for c in range(8):
        b, hh = cores[c]
        full = np.concatenate([xn1T[2 * b], xn1T[2 * b + 1]], axis=1)
        ins.append({"xn1T": np.ascontiguousarray(full), "xn1own": xn1T[c],
                    "wqT": np.ascontiguousarray(inp['L1_attn_Wq_w'][512 * hh:512 * (hh + 1)].T),
                    "bq": inp['L1_attn_Wq_b'][512 * hh:512 * (hh + 1)].reshape(4, 128).T.copy(),
                    "kT": kTl[c], "v65": v65l[c], "ident": idb})
    r = _run(build_l8(), ins)
    s1T = [np.asarray(r[c]["o_s1"]) for c in range(8)]

    idf = np.eye(128, dtype=np.float32)
    negb = np.zeros((128, 32), np.float32); negb[:, 8:] = -1e30
    ins = [{"x1T": x1T[c], "wg": inp['L1_ffn_w_gate'].astype(np.float32),
            "mrow": mrow[c], "r_tm": np.ascontiguousarray(rrow[c].reshape(16, 128).T),
            "negbig": negb, "posw": _posw(), "ident": idf} for c in range(8)]
    r = _run(build_l9(), ins)
    ohT = [np.asarray(r[c]["o_oh"]) for c in range(8)]
    rkT = [np.asarray(r[c]["o_rk"]) for c in range(8)]
    gatec = [np.asarray(r[c]["o_gate"]) for c in range(8)]
    hists = np.stack([np.asarray(r[c]["o_hist"]).ravel() for c in range(8)])

    ins = []
    for c in range(8):
        maskc = np.zeros((8, 1), np.float32); maskc[0:c, 0] = 1.0
        ins.append({"gath": hists.astype(np.float32), "maskc": maskc,
                    "ohT": ohT[c], "rkT": rkT[c], "gate": gatec[c],
                    "xn1T": xn1T[c], "we": inp['L1_ffn_w_expert'].astype(np.float32),
                    "be": inp['L1_ffn_b_expert'].reshape(8, 128).T.copy(),
                    "s1T": s1T[c], "x1T": x1T[c], "ident": idf})
    r = _run(build_l10(), ins)
    out = np.zeros((4, 4096, 1024), np.float32)
    for c in range(8):
        b, hf = cores[c]
        out[b, sls[c]] = np.asarray(r[c]["o_out"]).T
    return out


def _barrier(p):
    for e in _ENG_ATTR:
        for s_ in ('pe', 'dv', 'ac', 'gp', 'dy', 'dg', 'ds'):
            p.wait(e, s_)


def build_fused():
    nc = bass.Bass("TRN2", target_bir_lowering=False, detect_race_conditions=False)
    din = lambda name, shape, dt=F32: nc.dram_tensor(name, shape, dt, kind="ExternalInput").ap()
    a_xT = din("xT", [D, T])
    a_wq0 = din("wq0", [D, D]); a_wk0 = din("wk0", [D, D]); a_wv0 = din("wv0", [D, D])
    a_cosT = din("cosT", [512, T]); a_sinT = din("sinT", [512, T])
    a_cosB = din("cosB", [T, 512]); a_sinB = din("sinB", [T, 512])
    a_dt = din("dtc", [128, 2048]); a_qdec = din("qdec", [128, 1024]); a_sdec = din("sdec", [128, 8])
    a_stmask = din("stmask", [128, 8])
    a_gam = din("gam", [128, 8]); a_bet = din("bet", [128, 8])
    a_wo = din("wo", [D, D])
    a_ones2 = din("ones2", [128, 2], BF16); a_E2 = din("E2", [2, 128])
    a_q1 = din("q1", [D, 4096], I32); a_sc1 = din("sc1", [8, 4096]); a_b1 = din("b1", [128, 32])
    a_lg = din("lg", [128, 8]); a_lb = din("lb", [128, 8])
    a_q2 = din("q2", [4096, 1024], I32); a_sc2 = din("sc2", [32, 1024]); a_b2 = din("b2", [128, 8])
    a_keysT = din("keysT", [D, S]); a_valsT = din("valsT", [D, S])
    a_wk1 = din("wk1", [D, 512]); a_wv1 = din("wv1", [D, 512])
    a_bk1 = din("bk1", [128, 4]); a_bv1 = din("bv1", [1, 512])
    a_wq1 = din("wq1", [D, 512]); a_bq1 = din("bq1", [128, 4])
    a_idb = din("identb", [128, 128], BF16); a_idf = din("identf", [128, 128])
    a_wg = din("wg", [D, 8]); a_wgT = din("wgT", [8, D])
    a_negb = din("negbig", [128, 32]); a_posw = din("posw", [128, 32])
    a_maskc = din("maskc", [8, 1])
    a_we = din("we", [D, D]); a_be = din("be", [128, 8])
    o_out = nc.dram_tensor("o_out", [D, T], F32, kind="ExternalOutput").ap()
    d_xn = nc.dram_tensor("d_xn", [D, T], BF16).ap()
    d_q = nc.dram_tensor("d_q", [D, T], BF16).ap()
    d_k = nc.dram_tensor("d_k", [D, T], BF16).ap()
    d_s = nc.dram_tensor("d_s", [D, T], BF16).ap()
    d_y1 = nc.dram_tensor("d_y1", [4096, T], BF16).ap()
    d_x1 = nc.dram_tensor("d_x1", [D, T], F32).ap()
    d_kT = nc.dram_tensor("d_kT", [512, S], BF16).ap()
    d_v65 = nc.dram_tensor("d_v65", [S, 528], BF16).ap()
    cc_st_in = nc.dram_tensor("cc_st_in", [128, 1024], F32)
    cc_st_out = nc.dram_tensor("cc_st_out", [1024, 1024], F32)
    cc_x_in = nc.dram_tensor("cc_x_in", [D, T], BF16)
    cc_x_out = nc.dram_tensor("cc_x_out", [2 * D, T], BF16)
    cc_h_in = nc.dram_tensor("cc_h_in", [1, 8], F32)
    cc_h_out = nc.dram_tensor("cc_h_out", [8, 8], F32)

    p = P(nc)
    psc0 = nc.alloc_psum_tensor("psc0", [128, 1024], F32)
    psc1 = nc.alloc_psum_tensor("psc1", [128, 1024], F32)
    avdt = nc.alloc_psum_tensor("avdt", [128, 1024], F32)
    pbh = nc.alloc_psum_tensor("pbhx", [128, 1024], BF16)
    pb7 = nc.alloc_psum_tensor("pb7", [128, 512], F32)
    k = type('K', (), {})()
    k.nc, k.p = nc, p
    k.pb = [psc0[:, 0:512], psc0[:, 512:1024], psc1[:, 0:512], psc1[:, 512:1024],
            avdt[:, 0:512], avdt[:, 512:1024], pb7[:, 0:512]]
    k.pbh = pbh
    SA = nc.alloc_sbuf_tensor("SA", [128, 16384], BF16)
    SB = nc.alloc_sbuf_tensor("SB", [128, 16384], BF16)
    SC = nc.alloc_sbuf_tensor("SC", [128, 16384], BF16)
    SD = nc.alloc_sbuf_tensor("SD", [128, 16896], BF16)
    F8 = nc.alloc_sbuf_tensor("F8", [128, 32768], BF16)
    H = nc.alloc_sbuf_tensor("H", [128, 1024], F32)
    SBf = SB.bitcast(F32); SDf = SD.bitcast(F32); SCf = SC.bitcast(F32)
    F8f = F8.bitcast(F32)
    k.ones_bf = nc.alloc_sbuf_tensor("ones_bf", [128, 1], BF16)
    k.ones_f1 = nc.alloc_sbuf_tensor("ones_f1", [1, 128], F32)
    k.stat = F8f[0:1, 0:2048]
    k.tscr = H[:, 0:512]
    p.emit('v', lambda e: e.memset(k.ones_bf[:], 1.0))
    p.emit('v', lambda e: e.memset(k.ones_f1[:], 1.0), inc='dv')
    sm_ = lambda nm, shape, dt=F32: nc.alloc_sbuf_tensor(nm, shape, dt)
    gamv = sm_("gamv", [128, 8]); betv = sm_("betv", [128, 8])
    ones2 = sm_("ones2v", [128, 2], BF16); E2 = sm_("E2v", [2, 128])
    sdec = sm_("sdecv", [128, 8]); stmask = sm_("stmaskv", [128, 8])
    b1f = sm_("b1fv", [128, 32]); lgv = sm_("lgv", [128, 8]); lbv = sm_("lbv", [128, 8])
    b2h = sm_("b2hv", [128, 8])
    bkf = sm_("bkfv", [128, 4])
    bqf = sm_("bqfv", [128, 4])
    idb = sm_("idbv", [128, 128], BF16); idf = sm_("idfv", [128, 128])
    m_tm = sm_("m_tmv", [128, 16]); r_tm = sm_("r_tmv", [128, 16])
    wgfm = sm_("wgfmv", [128, 64]); wsumc = sm_("wsumcv", [8, 32])
    wsrow = sm_("wsrowv", [1, 32]); wsB = sm_("wsBv", [128, 8])
    negb = SBf[:, 0:32]; posw = SBf[:, 32:64]
    one30 = SBf[:, 64:96]; zz8 = SBf[:, 96:128]; big30 = SBf[:, 128:160]
    lgs = SBf[:, 160:192]; el = SBf[:, 192:224]
    mnn = SBf[:, 224:256]; pmn = SBf[:, 256:288]
    t1g = SBf[:, 288:320]; t2g = SBf[:, 320:352]; t3g = SBf[:, 352:384]
    oh0 = SBf[:, 384:416]; ohv = SBf[:, 416:448]
    q8 = SBf[:, 448:480]; nm2 = SBf[:, 480:512]
    smbuf = sm_("smbufv", [128, 16]); gate_tm = sm_("gatetmv", [128, 16])
    kcolb = sm_("kcolbv", [128, 16]); gk = sm_("gkv", [128, 16])
    hrow = sm_("hrowv", [1, 32]); gath = sm_("gathv", [8, 8]); maskc = sm_("maskcv", [8, 1])
    offs = sm_("offsv", [8, 1]); bef = sm_("befv", [128, 8])
    rec = sm_("recv", [128, 8])
    silb = SD[:, 0:512]
    t2b = SDf[:, 512:1024]

    for ap_, sb in [(a_gam, gamv), (a_bet, betv), (a_ones2, ones2), (a_E2, E2),
                    (a_sdec, sdec), (a_stmask, stmask), (a_b1, b1f), (a_lg, lgv),
                    (a_lb, lbv), (a_b2, b2h), (a_bk1, bkf), (a_bq1, bqf),
                    (a_idb, idb), (a_idf, idf),
                    (a_maskc, maskc), (a_be, bef)]:
        p.dma('g', sb[:], ap_)
    p.dma('g', wgfm[:], rearr_fm(a_wg))
    _barrier(p)

    NS = 512
    xb = SA[:, 0:16384]; xsq = SB[:, 0:16384]; xn = SC[:, 0:16384]
    k.stat = F8f[0:1, 0:2048]
    p.dma('g', xb, rearr_fm(a_xT))
    p.wait('v', 'dg'); p.wait('t', 'dg'); p.wait('s', 'dg')
    emit_ln_fm(k, xb, xsq, xn)
    p.wait('y', 'dv')
    p.dma('y', rearr_fm(d_xn), xn)
    _barrier(p)

    ws = F8[:, 0:24576]
    wq0, wk0, wv0 = ws[:, 0:8192], ws[:, 8192:16384], ws[:, 16384:24576]
    cosT = SBf[:, 0:8192]; sinT = SDf[:, 0:8192]
    raw = SA[:, 0:16384]
    scrA, scrB, scrC = F8f[:, 12288:12800], H[:, 0:512], H[:, 512:1024]
    p.dma('g', wq0, rearr_fm(a_wq0))
    p.dma('g', wk0, rearr_fm(a_wk0))
    p.dma('g', wv0, rearr_fm(a_wv0))
    p.dma('g', cosT, rearr_fm(a_cosT))
    p.dma('g', sinT, rearr_fm(a_sinT))
    p.wait('t', 'dg'); p.wait('v', 'dg'); p.wait('s', 'dg')

    def rot_and_store(src_fm, out_hbm):
        for c in range(4):
            for ts in range(T // 512):
                lo = src_fm[:, c * T + ts * 512: c * T + ts * 512 + 512]
                hi = src_fm[:, (c + 4) * T + ts * 512: (c + 4) * T + ts * 512 + 512]
                cs = cosT[:, c * T + ts * 512: c * T + ts * 512 + 512]
                sn = sinT[:, c * T + ts * 512: c * T + ts * 512 + 512]
                def rot(e, lo=lo, hi=hi, cs=cs, sn=sn):
                    e.tensor_mul(scrA[:], lo, cs)
                    e.tensor_mul(scrC[:], lo, sn)
                    e.tensor_mul(scrB[:], hi, sn)
                    e.tensor_sub(lo, scrA[:], scrB[:])
                    e.tensor_mul(scrB[:], hi, cs)
                    return e.tensor_add(hi, scrC[:], scrB[:])
                p.emit('v', rot, inc='dv')
        p.wait('y', 'dv')
        p.dma('y', rearr_fm(out_hbm), src_fm[:, 0:16384])

    emit_proj_fm(k, wq0, xn, raw)
    p.wait('v', 'ac')
    rot_and_store(raw, d_q)
    p.wait('s', 'dy')
    emit_proj_fm(k, wk0, xn, raw)
    p.wait('v', 'ac')
    rot_and_store(raw, d_k)
    p.wait('g', 'dv')
    p.dma('g', cosT, rearr_tm(a_cosB))
    p.dma('g', sinT, rearr_tm(a_sinB))
    p.wait('s', 'dy')
    emit_proj_tm(k, wk0, xn, raw)
    p.wait('v', 'ac'); p.wait('v', 'dg')
    ktok = F8[:, 0:16384]
    emit_rot_tm(k, raw, ktok, cosT, sinT, scrA, scrB, scrC)
    _barrier(p)
    Vtm = SA[:, 0:16384]
    emit_proj_tm(k, wv0, xn, Vtm)
    _barrier(p)

    Sf32 = F8f[:, 8192:9216]
    p.emit('v', lambda e: e.memset(Sf32[:], 0.0), inc='dv')
    p.wait('t', 'dv'); p.wait('v', 'dg')
    ds_ = k.pb[0]
    for c in range(16):
        for g in range(8):
            vcol = slice(c * 1024 + g * 128, c * 1024 + g * 128 + 128)
            pe_tok = p.emit('t', lambda e, vcol=vcol: e.matmul(
                ds_[:, 0:128], ktok[:, vcol], Vtm[:, vcol], start=True, stop=True), inc='pe')
            p.wait('v', 'pe', pe_tok)
            def upd(e, g=g):
                e.scalar_tensor_tensor(Sf32[0:64, g * 128:g * 128 + 64],
                                       Sf32[0:64, g * 128:g * 128 + 64],
                                       sdec[0:64, g:g + 1], ds_[0:64, 0:64],
                                       ALU.mult, ALU.add)
                return e.scalar_tensor_tensor(Sf32[64:128, g * 128 + 64:g * 128 + 128],
                                              Sf32[64:128, g * 128 + 64:g * 128 + 128],
                                              sdec[64:128, g:g + 1], ds_[64:128, 64:128],
                                              ALU.mult, ALU.add)
            dv_tok = p.emit('v', upd, inc='dv')
            p.wait('t', 'dv', dv_tok)
    p.wait('g', 'dv')
    p.dma('g', cc_st_in[:, :], Sf32[:])
    p.wait('g', 'dg')
    p.emit('g', lambda e: e.collective_compute(
        "AllGather", ALU.bypass, replica_groups=[[0, 1, 2, 3, 4, 5, 6, 7]],
        ins=[cc_st_in.ap().opt()], outs=[cc_st_out.ap().opt()]), inc='gp')
    p.wait('g', 'gp')
    p.wait('v', 'dg')
    p.wait('v', 'gp')
    p.emit('v', lambda e: e.memset(Sf32[:], 0.0), inc='dv')
    stmp = H[:, 0:1024]
    for r in range(8):
        p.wait('g', 'dv')
        rt_ = p.dma('g', stmp, cc_st_out[128 * r:128 * r + 128, :])
        p.wait('v', 'dg', rt_)
        p.emit('v', lambda e, r=r: e.scalar_tensor_tensor(
            Sf32[:], stmp, stmask[:, r:r + 1], Sf32[:], ALU.mult, ALU.add), inc='dv')
    _barrier(p)

    Qg = SB[:, 0:16384]; Kg = SD[:, 0:16384]; RTg = SC[:, 0:16384]
    for g in range(8):
        for hf in range(2):
            h = 2 * g + hf
            cr, qq = h // 4, h % 4
            for x1 in range(2):
                srows = slice((cr + 4 * x1) * 128 + 32 * qq, (cr + 4 * x1) * 128 + 32 * qq + 32)
                drows = slice(64 * hf + 32 * x1, 64 * hf + 32 * x1 + 32)
                p.dma('y', Qg[drows, g * T:(g + 1) * T], d_q[srows, :])
                p.dma('y', Kg[drows, g * T:(g + 1) * T], d_k[srows, :])
    dtc = F8f[:, 9728:11776]; qdec = F8f[:, 11776:12800]
    S2b = F8[:, 36864 // 2:36864 // 2 + 1024]
    k.atA = F8[:, 25600:25728]; k.atB = F8[:, 25728:25856]
    k.atA2 = F8[:, 25856:25984]; k.atB2 = F8[:, 25984:26112]
    qfull = F8[:, 26624:27648]
    p.dma('g', dtc, a_dt)
    p.dma('g', qdec, a_qdec)
    p.emit('v', lambda e: e.memset(S2b[:], 0.0), inc='dv')
    st2 = F8f[0:2, 13312:15360]
    k.st2m = st2[0:2, 0:512]; k.st2v = st2[0:2, 512:1024]
    k.st2t = st2[0:2, 1024:1536]; k.st2r = st2[0:2, 1536:2048]
    _barrier(p)
    emit_retention2(k, Qg, Kg, ktok, Vtm, dtc, qdec, sdec, Sf32, S2b, RTg, qfull)
    wo = F8[:, 0:8192]
    p.wait('g', 'pe')
    p.dma('g', wo, rearr_fm(a_wo))
    emit_gnorm(k, RTg, RTg, ones2, E2, gamv, betv, silb[:, 0:512])
    p.wait('t', 'dv'); p.wait('t', 'dg')
    emit_proj_fm(k, wo, RTg, Qg)
    p.wait('y', 'ac')
    p.dma('y', rearr_fm(d_s), Qg)
    _barrier(p)

    xnA = SA[:, 0:16384]; xsq4 = SB[:, 0:16384]
    w1 = F8[:, 0:32768]
    scaleB1 = SCf[:, 0:4096]
    y1T = SD[:, 0:16384]
    k.stat = SCf[0:1, 4096:6144]
    p.dma('g', xnA, rearr_fm(d_xn))
    emit_dequant(k, a_q1, w1, a_sc1, scaleB1, nk=8, nn=4096)
    p.wait('t', 'dg'); p.wait('v', 'dg'); p.wait('s', 'dg')
    emit_ln_fm(k, xnA, xsq4, xnA, gamma_ap=lgv, beta_ap=lbv)
    p.wait('t', 'dv')
    for tc in range(4):
        emit_ffn_fc1(k, w1, xnA, b1f, y1T, tc=tc)
        p.wait('y', 'ac')
        p.dma('y', d_y1.rearrange("(c p) n -> p c n", p=128)[:, :, 512 * tc:512 * (tc + 1)], y1T)
        p.wait('s', 'dy')
    _barrier(p)

    p.dma('g', xnA, rearr_fm(d_xn))
    w2 = F8[:, 0:32768]
    scaleB2 = SCf[:, 0:1024]
    gT = SB[:, 0:16384]
    emit_dequant(k, a_q2, w2, a_sc2, scaleB2, nk=32, nn=1024)
    p.wait('v', 'dg')
    p.emit('v', lambda e: e.tensor_scalar_mul(b2h[:], b2h[:], 0.5), inc='dv')
    p.wait('t', 'dg'); p.wait('s', 'dv'); p.wait('t', 'dv')
    for tc in range(4):
        p.wait('g', 'pe')
        p.dma('g', y1T, d_y1.rearrange("(c p) n -> p c n", p=128)[:, :, 512 * tc:512 * (tc + 1)])
        p.wait('t', 'dg')
        emit_ffn_fc2(k, w2, y1T, b2h, xnA, gT, tc=tc)
    _barrier(p)

    sT6 = SA[:, 0:16384]
    x1b = SD[:, 0:16384]; xn1 = SC[:, 0:16384]
    xr0 = F8f[:, 8192:10240]; xr1 = F8f[:, 10240:12288]
    xf0 = F8f[:, 0:2048]; xf1 = F8f[:, 2048:4096]
    msb = F8f[0:1, 4096:6144]; rsb = F8f[0:1, 6144:8192]
    k.stat = SBf[0:1, 0:2048]
    p.dma('g', sT6, rearr_fm(d_s))
    p.wait('v', 'dg')
    for c in range(8):
        xr = (xr0, xr1)[c % 2]; xf = (xf0, xf1)[c % 2]
        p.wait('y', 'dv', p.tok('dv'))
        p.dma('y', xr, rearr_fm(a_xT)[:, c:c + 1, :])
        p.wait('v', 'dy')
        def comb(e, c=c, xr=xr, xf=xf):
            e.tensor_mul(xf, sT6[:, c * T:(c + 1) * T], gT[:, c * T:(c + 1) * T])
            e.tensor_add(xf, xf, xr)
            return e.tensor_copy(x1b[:, c * T:(c + 1) * T], xf)
        dv_tok = p.emit('v', comb, inc='dv')
        p.wait('y', 'dv', dv_tok)
        p.dma('y', rearr_fm(d_x1)[:, c:c + 1, :], xf)
    emit_ln_fm(k, x1b, sT6, xn1, m_out=msb, r_out=rsb)
    p.wait('t', 'dv')
    def tpstats(e):
        for tc in range(16):
            e.transpose(k.pb[0][0:128, tc:tc + 1], msb[0:1, 128 * tc:128 * tc + 128], idf[0:1, 0:1])
        for tc in range(16):
            e.transpose(k.pb[1][0:128, tc:tc + 1], rsb[0:1, 128 * tc:128 * tc + 128], idf[0:1, 0:1])
        return e.transpose(k.pb[2][0:128, 0:1], msb[0:1, 0:128], idf[0:1, 0:1])
    pe_tok = p.emit('t', tpstats, inc='pe')
    p.wait('v', 'pe', pe_tok)
    def evst(e):
        e.tensor_copy(m_tm[:], k.pb[0][:, 0:16])
        return e.tensor_copy(r_tm[:], k.pb[1][:, 0:16])
    p.emit('v', evst, inc='dv')
    p.wait('y', 'dv')
    p.dma('y', rearr_fm(cc_x_in.ap()), xn1)
    _barrier(p)

    kb = F8[:, 0:32768]
    bvB = H.bitcast(F32)[:, 512:1024] if False else H[:, 512:1024]
    p.dma('g', bvB[:], a_bv1.broadcast_to((128, 512)))
    wk1 = SA[:, 0:4096]; wv1 = SA[:, 4096:8192]
    kT7 = SB[:, 0:16384]
    v65 = SD[:, 0:16896]
    p.dma('g', kb, rearr_fm(a_keysT))
    p.dma('g', wk1, rearr_fm(a_wk1))
    p.dma('g', wv1, rearr_fm(a_wv1))
    p.wait('t', 'dg'); p.wait('v', 'dg')
    emit_proj_fm(k, wk1, kb, kT7, nchunk_out=4, tlen=S, evac='v', bias_ap=bkf, relu=True)
    p.wait('y', 'dv')
    p.dma('y', rearr_fm(d_kT), kT7)
    p.wait('g', 'pe')
    p.dma('g', kb, rearr_fm(a_valsT))
    p.emit('v', lambda e: e.memset(v65[:], 0.0))
    v65r = v65.rearrange("p (n h u) -> p n h u", n=32, h=8, u=66)
    p.emit('v', lambda e: e.memset(v65r[:, :, :, 64:65], 1.0), inc='dv')
    p.wait('t', 'dg'); p.wait('t', 'dv')
    for tt in range(32):
        bank = k.pb[tt % 2]
        def mms7(e, tt=tt, bank=bank):
            for dc in range(8):
                i = e.matmul(bank[:, 0:512], kb[:, dc * S + tt * 128: dc * S + tt * 128 + 128],
                             wv1[:, dc * 512:(dc + 1) * 512], start=(dc == 0), stop=(dc == 7))
            return i
        pe_tok = p.emit('t', mms7, inc='pe')
        p.wait('v', 'pe', pe_tok)
        def evac7(e, tt=tt, bank=bank):
            e.tensor_add(k.tscr[:, 0:512], bank[:, 0:512], bvB[:])
            ov = v65[:, tt * 528:(tt + 1) * 528].rearrange("p (h u) -> p h u", h=8, u=66)[:, :, 0:64]
            return e.tensor_scalar_max(ov, k.tscr[:, 0:512].rearrange("p (h u) -> p h u", h=8, u=64), 0.0)
        dv_tok = p.emit('v', evac7, inc='dv')
        p.wait('t', 'dv', dv_tok)
    p.wait('y', 'dv')
    p.dma('y', rearr_tm(d_v65), v65)
    p.emit('g', lambda e: e.collective_compute(
        "AllGather", ALU.bypass, replica_groups=[[0, 1], [2, 3], [4, 5], [6, 7]],
        ins=[cc_x_in.ap().opt()], outs=[cc_x_out.ap().opt()]), inc='gp')
    p.wait('g', 'gp')
    _barrier(p)

    xb8 = F8[:, 0:32768]
    qT = SA[:, 0:16384]; s1T = SB[:, 0:16384]
    kts = SD[:, 0:8192]; vs = SD[:, 8192:12416]
    eb = SD[:, 12416:14464]; pcop = SD[:, 14464:14992]
    avs = SD[0:65, 14992:16016]
    for cch in range(8):
        p.dma('g', xb8[:, cch * S: cch * S + T], cc_x_out.ap()[128 * cch:128 * cch + 128, :])
        p.dma('g', xb8[:, cch * S + T: (cch + 1) * S], cc_x_out.ap()[D + 128 * cch:D + 128 * cch + 128, :])
    wq1s = F8[:, 0:4096]
    wq1s = SA[:, 0:4096]
    p.dma('g', wq1s, rearr_fm(a_wq1))
    p.wait('t', 'dg'); p.wait('v', 'dg')
    qT = SB[:, 0:16384]; s1T = SA[:, 0:16384]
    for cch in range(4):
        for ts8 in range(S // NS):
            bank = k.pb[ts8 % 2]
            def mms8(e, cch=cch, ts8=ts8, bank=bank):
                for dc in range(8):
                    i = e.matmul(bank[:, 0:NS],
                                 wq1s[:, dc * 512 + cch * 128: dc * 512 + cch * 128 + 128],
                                 xb8[:, dc * S + ts8 * NS: dc * S + ts8 * NS + NS],
                                 start=(dc == 0), stop=(dc == 7))
                return i
            pe_tok = p.emit('t', mms8, inc='pe')
            p.wait('v', 'pe', pe_tok)
            osl = qT[:, cch * S + ts8 * NS: cch * S + ts8 * NS + NS]
            tk = p.emit('v', lambda e, bank=bank, osl=osl, cch=cch: e.tensor_scalar(
                osl, bank[:, 0:NS], bqf[:, cch:cch + 1], 0.0, ALU.add, ALU.max), inc='dv')
            p.wait('t', 'dv', tk)
    _barrier(p)
    psc = [psc0, psc1]
    kload_tok = {}
    exp_tok = [None, None]
    av_tok = [None, None]
    for h in range(8):
        pair = h // 2
        if h % 2 == 0:
            p.wait('g', 'pe')
            kload_tok[pair] = p.dma('g', kts[:, (pair % 2) * S:(pair % 2 + 1) * S],
                                    rearr_fm(d_kT)[:, pair:pair + 1, :])
        p.wait('g', 'pe')
        vs_tok = p.dma('g', vs[:, (h % 2) * 2112:(h % 2 + 1) * 2112],
                       rearr_tm(d_v65)[:, :, 66 * h:66 * h + 66])
        krow = slice(64 * (h % 2), 64 * (h % 2) + 64)
        ktsv = kts[:, (pair % 2) * S:(pair % 2 + 1) * S]
        vsv = vs[:, (h % 2) * 2112:(h % 2 + 1) * 2112]
        for qb in range(4):
            def mk_av(kt):
                def av8(e, kt=kt, vsv=vsv):
                    e.matmul(avdt[0:65, 0:512], vsv[:, kt * 66:kt * 66 + 65],
                             eb[:, (kt % 2) * 1024:(kt % 2) * 1024 + 512],
                             start=(kt == 0), stop=(kt == 31))
                    return e.matmul(avdt[0:65, 512:1024], vsv[:, kt * 66:kt * 66 + 65],
                                    eb[:, (kt % 2) * 1024 + 512:(kt % 2) * 1024 + 1024],
                                    start=(kt == 0), stop=(kt == 31))
                return av8
            for kt in range(32):
                bank = psc[kt % 2]
                p.wait('t', 'dg', kload_tok[pair])
                if exp_tok[kt % 2] is not None:
                    p.wait('t', 'ac', exp_tok[kt % 2])
                def sc8_(e, kt=kt, bank=bank, krow=krow, ktsv=ktsv, qb=qb, h=h):
                    e.matmul(bank[:, 0:512], ktsv[krow, kt * 128:(kt + 1) * 128],
                             qT[krow, (h // 2) * S + qb * 1024: (h // 2) * S + qb * 1024 + 512],
                             start=True, stop=True)
                    return e.matmul(bank[:, 512:1024], ktsv[krow, kt * 128:(kt + 1) * 128],
                                    qT[krow, (h // 2) * S + qb * 1024 + 512: (h // 2) * S + qb * 1024 + 1024],
                                    start=True, stop=True)
                pe_tok = p.emit('t', sc8_, inc='pe')
                p.wait('s', 'pe', pe_tok)
                if av_tok[kt % 2] is not None:
                    p.wait('s', 'pe', av_tok[kt % 2])
                exp_tok[kt % 2] = p.emit('s', lambda e, kt=kt, bank=bank: e.activation(
                    eb[:, (kt % 2) * 1024:(kt % 2 + 1) * 1024], bank[:], AF.Exp, scale=0.125),
                    inc='ac')
                if kt >= 1:
                    p.wait('t', 'ac', exp_tok[(kt - 1) % 2])
                    p.wait('t', 'dg', vs_tok)
                    av_tok[(kt - 1) % 2] = p.emit('t', mk_av(kt - 1), inc='pe')
            p.wait('t', 'ac', exp_tok[31 % 2])
            p.wait('t', 'dg', vs_tok)
            av_tok[31 % 2] = p.emit('t', mk_av(31), inc='pe')
            p.wait('v', 'pe', av_tok[1])
            dv_tok = p.emit('v', lambda e: e.tensor_copy(avs[:, 0:1024], avdt[0:65, 0:1024]), inc='dv')
            p.wait('t', 'dv', dv_tok)
            def tps8(e):
                for i in range(8):
                    e.transpose(pbh[:, 66 * i:66 * i + 65], avs[:, 128 * i:128 * i + 128],
                                idb[0:65, 0:65])
                return e.transpose(pbh[:, 700:765], avs[:, 0:128], idb[0:65, 0:65])
            pe_tok = p.emit('t', tps8, inc='pe')
            p.wait('v', 'pe', pe_tok)
            def evacp8(e):
                e.reciprocal(rec[:, 0:8], pbh[:, 64:528:66])
                return e.tensor_copy(pcop[:, 0:528], pbh[:, 0:528])
            dv_tok = p.emit('v', evacp8, inc='dv')
            p.wait('t', 'dv', dv_tok)
            s1v = s1T.rearrange("p (tc hh u f) -> p tc hh u f", tc=8, hh=8, u=64, f=4)
            xov = xn1.rearrange("p (tc hh u f) -> p tc hh u f", tc=8, hh=8, u=64, f=4)
            def perm8(e, h=h, qb=qb, s1v=s1v, xov=xov):
                for tc in range(8):
                    osl = s1v[:, tc:tc + 1, h:h + 1, :, qb:qb + 1]
                    isl = xov[:, tc:tc + 1, h:h + 1, :, qb:qb + 1]
                    ii = e.scalar_tensor_tensor(
                        osl, pcop[:, 66 * tc:66 * tc + 64].rearrange(
                            "p (a u f) -> p a u f", a=1, u=64, f=1),
                        rec[:, tc:tc + 1], isl, ALU.mult, ALU.add)
                return ii
            dv_tok = p.emit('v', perm8, inc='dv')
    _barrier(p)

    x1f9 = F8f[:, 0:16384]
    ohT = SDf[0:8, 0:2048]; scanT = SDf[0:8, 2048:4096]
    rkT = SDf[0:8, 4096:6144]; keptT = SDf[0:8, 6144:8192]
    wgT9 = H[0:8, 0:1024]
    p.dma('g', x1f9, rearr_fm(d_x1))
    p.dma('g', wgT9, a_wgT)
    p.dma('g', negb[:], a_negb)
    p.dma('g', posw[:], a_posw)
    p.wait('v', 'dg'); p.wait('t', 'dg')
    p.emit('v', lambda e: e.memset(one30[:], 1.0))
    p.emit('v', lambda e: e.memset(zz8[:], 0.0))
    p.emit('v', lambda e: e.memset(big30[:], 1e30), inc='dv')
    p.wait('t', 'dv'); p.wait('s', 'dv')
    dv_tok = p.emit('v', lambda e: e.reduce_sum(wsumc[0:8, 0:1], wgT9, AX.X), inc='dv')
    p.wait('t', 'dv', dv_tok)
    def tpws(e):
        e.transpose(k.pb[2][0:1, 0:8], wsumc[0:8, 0:1], idf[0:8, 0:8])
        return e.transpose(k.pb[3][0:1, 0:8], wsumc[0:8, 0:1], idf[0:8, 0:8])
    pe_tok = p.emit('t', tpws, inc='pe')
    p.wait('v', 'pe', pe_tok)
    dv_tok = p.emit('v', lambda e: e.tensor_copy(wsrow[0:1, 0:8], k.pb[2][0:1, 0:8]), inc='dv')
    p.wait('t', 'dv', dv_tok)
    pe_tok = p.emit('t', lambda e: e.matmul(k.pb[2][:, 0:8], k.ones_f1[0:1, :],
                                            wsrow[0:1, 0:8], start=True, stop=True), inc='pe')
    p.wait('v', 'pe', pe_tok)
    dv_tok = p.emit('v', lambda e: e.tensor_copy(wsB[:], k.pb[2][:, 0:8]), inc='dv')
    p.wait('t', 'dv', dv_tok)
    for tc in range(16):
        def mmr9(e, tc=tc):
            for dc in range(8):
                i = e.matmul(k.pb[0][:, 0:8], x1f9[:, dc * T + tc * 128: dc * T + tc * 128 + 128],
                             wgfm[:, dc * 8:(dc + 1) * 8], start=(dc == 0), stop=(dc == 7))
            return i
        pe_tok = p.emit('t', mmr9, inc='pe')
        p.wait('v', 'pe', pe_tok)
        def gdve9(e, tc=tc):
            e.scalar_tensor_tensor(t1g[:, 0:8], wsB[:], m_tm[:, tc:tc + 1],
                                   k.pb[0][:, 0:8], ALU.mult, ALU.subtract)
            e.tensor_copy(t2g[:, 0:8], t1g[:, 0:8])
            e.scalar_tensor_tensor(lgs[:, 0:8], t2g[:, 0:8], r_tm[:, tc:tc + 1],
                                   zz8[:, 0:8], ALU.mult, ALU.add)
            e.tensor_sub(lgs[:], lgs[:], negb[:])
            e.tensor_reduce(mnn[:, 0:1], lgs[:], AX.X, ALU.min)
            e.tensor_reduce(pmn[:, 0:1], lgs[:], AX.X, ALU.min, negate=True)
            e.tensor_copy(t2g[:], lgs[:])
            return e.tensor_copy(t3g[:], lgs[:])
        dv_tok = p.emit('v', gdve9, inc='dv')
        p.wait('s', 'dv', dv_tok)
        ac_tok = p.emit('s', lambda e: e.activation(el[:], lgs[:], AF.Exp, scale=-1.0,
                                                    bias=mnn[:, 0:1]), inc='ac')
        p.wait('v', 'ac', ac_tok)
        def gdve9b(e, tc=tc):
            e.reduce_sum(smbuf[:, tc:tc + 1], el[:], AX.X)
            e.scalar_tensor_tensor(t1g[:], lgs[:], pmn[:, 0:1], big30[:],
                                   ALU.add, ALU.mult)
            e.tensor_sub(t2g[:], one30[:], t1g[:])
            e.tensor_max(t3g[:], t2g[:], zz8[:])
            e.tensor_tensor(oh0[:], t3g[:], one30[:], ALU.min)
            e.tensor_mul(q8[:], oh0[:], posw[:])
            e.reduce_max(nm2[:, 0:1], q8[:], AX.X, negate=True)
            e.tensor_copy(t2g[:], q8[:])
            e.tensor_copy(t3g[:], q8[:])
            e.scalar_tensor_tensor(t1g[:], q8[:], nm2[:, 0:1], big30[:],
                                   ALU.add, ALU.mult)
            e.tensor_add(t2g[:], t1g[:], one30[:])
            e.tensor_max(t3g[:], t2g[:], zz8[:])
            return e.tensor_tensor(ohv[:], t3g[:], one30[:], ALU.min)
        dv_tok = p.emit('v', gdve9b, inc='dv')
        p.wait('t', 'dv', dv_tok)
        def tpo9(e, tc=tc):
            e.transpose(k.pb[2][0:32, 0:128], ohv[:], idf[:])
            return e.transpose(k.pb[4][0:32, 0:128], ohv[:], idf[:])
        pe_tok = p.emit('t', tpo9, inc='pe')
        p.wait('v', 'pe', pe_tok)
        dv_tok = p.emit('v', lambda e, tc=tc: e.tensor_copy(
            ohT[0:8, tc * 128:(tc + 1) * 128], k.pb[2][0:8, 0:128]), inc='dv')
        p.wait('t', 'dv', dv_tok)
    def scan9(e):
        e.reciprocal(gate_tm[:], smbuf[:])
        e.tensor_copy(t1g[:], oh0[:])
        e.tensor_tensor_scan(scanT[0:8, :], ohT[0:8, :], ohT[0:8, :], 0.0, ALU.add, ALU.bypass)
        return e.tensor_sub(rkT[0:8, :], scanT[0:8, :], ohT[0:8, :])
    dv_tok = p.emit('v', scan9, inc='dv')
    p.wait('t', 'dv', dv_tok)
    def tph(e):
        e.transpose(k.pb[2][0:1, 0:8], scanT[0:8, 2047:2048], idf[0:8, 0:8])
        return e.transpose(k.pb[4][0:1, 0:8], scanT[0:8, 2047:2048], idf[0:8, 0:8])
    pe_tok = p.emit('t', tph, inc='pe')
    p.wait('v', 'pe', pe_tok)
    dv_tok = p.emit('v', lambda e: e.tensor_copy(hrow[0:1, 0:8], k.pb[2][0:1, 0:8]), inc='dv')
    p.wait('y', 'dv', dv_tok)
    p.dma('y', cc_h_in.ap(), hrow[0:1, 0:8])
    p.wait('g', 'dy')
    p.emit('g', lambda e: e.collective_compute(
        "AllGather", ALU.bypass, replica_groups=[[0, 1, 2, 3, 4, 5, 6, 7]],
        ins=[cc_h_in.ap().opt()], outs=[cc_h_out.ap().opt()]), inc='gp')
    p.wait('g', 'gp')
    p.dma('g', gath[:], cc_h_out.ap())
    _barrier(p)

    we = SD[:, 8192:16384]
    pe_tok = p.emit('t', lambda e: e.matmul(k.pb[0][0:8, 0:1], gath[0:8, :], maskc[0:8, 0:1],
                                            start=True, stop=True), inc='pe')
    p.wait('v', 'pe', pe_tok)
    def kdve10(e):
        e.tensor_copy(offs[:], k.pb[0][0:8, 0:1])
        e.tensor_copy(keptT[0:8, :], rkT[0:8, :])
        e.tensor_copy(keptT[0:8, :], rkT[0:8, :])
        e.tensor_scalar(keptT[0:8, :], rkT[0:8, :], offs[0:8, 0:1], -1.0, ALU.add, ALU.mult)
        e.tensor_scalar(keptT[0:8, :], keptT[0:8, :], 100.0, 0.0, ALU.add, ALU.max)
        e.tensor_scalar(keptT[0:8, :], keptT[0:8, :], 1.0, 0.0, ALU.min, ALU.add)
        return e.tensor_mul(keptT[0:8, :], keptT[0:8, :], ohT[0:8, :])
    dv_tok = p.emit('v', kdve10, inc='dv')
    p.wait('t', 'dv', dv_tok)
    for tc in range(16):
        def tpk10(e, tc=tc):
            e.transpose(k.pb[2][0:128, 0:8], keptT[0:8, tc * 128:(tc + 1) * 128], idf[0:8, 0:8])
            return e.transpose(k.pb[4][0:128, 0:8], keptT[0:8, tc * 128:(tc + 1) * 128], idf[0:8, 0:8])
        pe_tok = p.emit('t', tpk10, inc='pe')
        p.wait('v', 'pe', pe_tok)
        dv_tok = p.emit('v', lambda e, tc=tc: e.reduce_sum(kcolb[:, tc:tc + 1],
                                                           k.pb[2][0:128, 0:8], AX.X), inc='dv')
        p.wait('t', 'dv', dv_tok)
    dv_tok = p.emit('v', lambda e: e.tensor_mul(gk[:], kcolb[:], gate_tm[:]), inc='dv')
    p.wait('t', 'dv', dv_tok)
    p.wait('g', 'dv')
    p.dma('g', we, rearr_fm(a_we))
    for tc in range(16):
        def tpg10(e, tc=tc):
            e.transpose(k.pb[2][0:1, 0:128], gk[:, tc:tc + 1], idf[:, :])
            return e.transpose(k.pb[4][0:1, 0:128], gk[:, tc:tc + 1], idf[:, :])
        pe_tok = p.emit('t', tpg10, inc='pe')
        p.wait('v', 'pe', pe_tok)
        dv_tok = p.emit('v', lambda e, tc=tc: e.tensor_copy(
            k.stat[0:1, tc * 128:(tc + 1) * 128], k.pb[2][0:1, 0:128]), inc='dv')
        p.wait('t', 'dv', dv_tok)
    gkB = H[:, 512:1024]
    k.stat = SBf[0:1, 2048:4096]
    for cch in range(8):
        for ts10 in range(4):
            pe_tok = p.emit('t', lambda e, ts10=ts10: e.matmul(
                k.pb[5][:, 0:512], k.ones_f1[0:1, :],
                k.stat[0:1, ts10 * 512:(ts10 + 1) * 512], start=True, stop=True), inc='pe')
            p.wait('v', 'pe', pe_tok)
            dv_tok = p.emit('v', lambda e: e.tensor_copy(gkB, k.pb[5][:, 0:512]), inc='dv')
            p.wait('t', 'dv', dv_tok)
            bank = k.pb[3 + (ts10 % 2) - 3 + 0] if False else k.pb[(ts10 % 2)]
            if cch == 0 and ts10 == 0:
                p.wait('t', 'dg')
            def mms10(e, cch=cch, ts10=ts10, bank=bank):
                for dc in range(8):
                    i = e.matmul(bank[:, 0:512],
                                 we[:, dc * 1024 + cch * 128: dc * 1024 + cch * 128 + 128],
                                 xn1[:, dc * T + ts10 * 512: dc * T + ts10 * 512 + 512],
                                 start=(dc == 0), stop=(dc == 7))
                return i
            pe_tok = p.emit('t', mms10, inc='pe')
            p.wait('v', 'pe', pe_tok)
            sl10 = slice(cch * T + ts10 * 512, cch * T + ts10 * 512 + 512)
            dv_tok = p.emit('v', lambda e, cch=cch, bank=bank: e.scalar_tensor_tensor(
                k.tscr[:, 0:512], bank[:, 0:512], bef[:, cch:cch + 1], gkB,
                ALU.add, ALU.mult), inc='dv')
            p.wait('s', 'dv', dv_tok)
            ac_tok = p.emit('s', lambda e: e.activation(silb[:], k.tscr[:, 0:512], AF.Silu), inc='ac')
            p.wait('v', 'ac', ac_tok)
            def md210(e, cch=cch, ts10=ts10, sl10=sl10):
                e.tensor_mul(t2b[:], silb[:], s1T[:, sl10])
                return e.tensor_add(t2b[:], t2b[:],
                                    x1f9[:, cch * T + ts10 * 512: cch * T + ts10 * 512 + 512])
            dv_tok = p.emit('v', md210, inc='dv')
            p.wait('y', 'dv', dv_tok)
            p.dma('y', rearr_fm(o_out)[:, cch:cch + 1, ts10 * 512:(ts10 + 1) * 512], t2b[:])
            p.wait('t', 'dy'); p.wait('v', 'dy')
    p.wait('y', 'dy')
    p.finalize()
    return nc


def kernel_fused(inp):
    import ml_dtypes
    TB = make_tables()
    perm = make_perm()
    x = np.ascontiguousarray(inp['L0_attn_x'].astype(np.float32))
    cores = [(c // 2, c % 2) for c in range(8)]
    sls = [slice(2048 * hf, 2048 * hf + 2048) for (_, hf) in cores]
    WQpT = np.ascontiguousarray(inp['L0_attn_WQ'].reshape(D, D)[perm].T)
    WKpT = np.ascontiguousarray(inp['L0_attn_WK'].reshape(D, D)[perm].T)
    WVT = np.ascontiguousarray(inp['L0_attn_WV'].reshape(D, D).T)
    keysT = np.ascontiguousarray(inp['L1_attn_keys'][0].T.astype(np.float32))
    valsT = np.ascontiguousarray(inp['L1_attn_values'][0].T.astype(np.float32))
    o2 = np.zeros((128, 2), ml_dtypes.bfloat16); o2[0:64, 0] = 1; o2[64:128, 1] = 1
    E2h = np.zeros((2, 128), np.float32); E2h[0, 0:64] = 1; E2h[1, 64:128] = 1
    negb = np.zeros((128, 32), np.float32); negb[:, 8:] = -1e30
    idb = np.eye(128, dtype=ml_dtypes.bfloat16)
    idf = np.eye(128, dtype=np.float32)
    ins = []
    for c in range(8):
        b, hf = cores[c]
        s_ = sls[c]
        stm = np.zeros((128, 8), np.float32)
        if hf == 1:
            stm[:, 2 * b] = 1.0
        mkc = np.zeros((8, 1), np.float32); mkc[0:c, 0] = 1.0
        ins.append({
            "xT": np.ascontiguousarray(x[b, s_].T),
            "wq0": WQpT, "wk0": WKpT, "wv0": WVT,
            "cosT": np.ascontiguousarray(TB['COS_T'][:, s_]),
            "sinT": np.ascontiguousarray(TB['SIN_T'][:, s_]),
            "cosB": np.ascontiguousarray(TB['COS_BK'][s_]),
            "sinB": np.ascontiguousarray(TB['SIN_BK'][s_]),
            "dtc": TB['DTcat'], "qdec": TB['QDECG'].astype(np.float32),
            "sdec": TB['SDECG'], "stmask": stm,
            "gam": inp['L0_attn_gamma'].reshape(8, 128).T.copy(),
            "bet": inp['L0_attn_beta'].reshape(8, 128).T.copy(),
            "wo": inp['L0_attn_WO'].astype(np.float32),
            "ones2": o2, "E2": E2h,
            "q1": inp['L0_ffn_fc1_q'], "sc1": inp['L0_ffn_fc1_scales'],
            "b1": inp['L0_ffn_fc1_bias'].reshape(32, 128).T.copy(),
            "lg": inp['L0_ffn_ln_gamma'].reshape(8, 128).T.copy(),
            "lb": inp['L0_ffn_ln_beta'].reshape(8, 128).T.copy(),
            "q2": inp['L0_ffn_fc2_q'], "sc2": inp['L0_ffn_fc2_scales'],
            "b2": inp['L0_ffn_fc2_bias'].reshape(8, 128).T.copy(),
            "keysT": keysT, "valsT": valsT,
            "wk1": np.ascontiguousarray(inp['L1_attn_Wk_w'][512 * hf:512 * (hf + 1)].T),
            "wv1": np.ascontiguousarray(inp['L1_attn_Wv_w'][512 * hf:512 * (hf + 1)].T),
            "bk1": inp['L1_attn_Wk_b'][512 * hf:512 * (hf + 1)].reshape(4, 128).T.copy(),
            "bv1": inp['L1_attn_Wv_b'][512 * hf:512 * (hf + 1)].reshape(1, 512),
            "wq1": np.ascontiguousarray(inp['L1_attn_Wq_w'][512 * hf:512 * (hf + 1)].T),
            "bq1": inp['L1_attn_Wq_b'][512 * hf:512 * (hf + 1)].reshape(4, 128).T.copy(),
            "identb": idb, "identf": idf,
            "wg": inp['L1_ffn_w_gate'].astype(np.float32),
            "wgT": np.ascontiguousarray(inp['L1_ffn_w_gate'].T.astype(np.float32)),
            "negbig": negb, "posw": _posw(), "maskc": mkc,
            "we": inp['L1_ffn_w_expert'].astype(np.float32),
            "be": inp['L1_ffn_b_expert'].reshape(8, 128).T.copy(),
        })
    res = run_bass_kernel_spmd(build_fused(), ins, core_ids=list(range(8)))
    out = np.zeros((4, 4096, 1024), np.float32)
    for c in range(8):
        b, hf = cores[c]
        out[b, sls[c]] = np.asarray(res.results[c]["o_out"]).T
    return out


# version 09
# speedup vs baseline: 1.3159x; 1.0086x over previous
import numpy as np
import concourse.bass as bass
import concourse.mybir as mybir
from concourse.bass_utils import run_bass_kernel_spmd

F32 = mybir.dt.float32
BF16 = mybir.dt.bfloat16
I32 = mybir.dt.int32
AF = mybir.ActivationFunctionType
ALU = mybir.AluOpType
AX = mybir.AxisListType

H, DH, D = 16, 64, 1024
C = 128
T = 2048
S = 4096
EPS = 1e-5
NB = 8


def make_perm():
    perm = np.zeros(D, dtype=np.int64)
    for h in range(H):
        for i in range(32):
            perm[32 * h + i] = 64 * h + 2 * i
            perm[512 + 32 * h + i] = 64 * h + 2 * i + 1
    return perm

def make_tables():
    t = {}
    angle = 10.0 ** (-4.0 * (np.arange(0, DH, 2, dtype=np.float64) / DH))
    pos = np.arange(S, dtype=np.float64)
    nang = pos[:, None] * angle[None, :]
    COS = np.cos(nang); SIN = np.sin(nang)
    colmap = np.arange(512) % 32
    t['COS_T'] = COS[:, colmap].T.astype(np.float32).copy()
    t['SIN_T'] = SIN[:, colmap].T.astype(np.float32).copy()
    scale_h = 1.0 - 2.0 ** (-5.0 - np.arange(H, dtype=np.float64))
    lns = np.log(scale_h)
    jj, ii = np.meshgrid(np.arange(C), np.arange(C), indexing='ij')
    DT = np.where(ii >= jj, np.exp((ii - jj)[None] * lns[:, None, None]), 0.0)
    t['DTcat'] = np.concatenate([DT[h] for h in range(H)], axis=1).astype(np.float32)
    qd = np.zeros((128, 8 * C), dtype=np.float32)
    for g in range(8):
        for hf, h in ((0, 2 * g), (1, 2 * g + 1)):
            qd[64 * hf:64 * hf + 64, C * g:C * (g + 1)] = np.exp(
                (np.arange(C, dtype=np.float64)[None, :] + 1.0) * lns[h])
    t['QDECG'] = qd
    kdec = np.exp((C - 1.0 - (pos % C))[:, None] * lns[None, :])
    hmap = np.arange(512) // 32
    t['COS_BK'] = (COS[:, colmap] * kdec[:, hmap]).astype(np.float32)
    t['SIN_BK'] = (SIN[:, colmap] * kdec[:, hmap]).astype(np.float32)
    sdec = np.exp(C * lns)
    sd = np.zeros((128, 8), dtype=np.float32)
    for g in range(8):
        sd[0:64, g] = sdec[2 * g]
        sd[64:128, g] = sdec[2 * g + 1]
    t['SDECG'] = sd
    return t


_ENG_ATTR = {'t': 'tensor', 'v': 'vector', 's': 'scalar', 'g': 'gpsimd', 'y': 'sync'}

class P:
    def __init__(self, nc, sems=('pe', 'dv', 'ac', 'gp', 'dy', 'dg', 'ds')):
        self.nc = nc
        self.q = {e: [] for e in _ENG_ATTR}
        self.sems = {}
        self.cnt = {}
        for nm in sems:
            self.sems[nm] = nc.alloc_semaphore(nm)
            self.cnt[nm] = 0

    def emit(self, eng, fn, inc=None, k=1):
        if inc is None:
            self.q[eng].append(fn)
            return None
        self.cnt[inc] += k
        sem = self.sems[inc]
        self.q[eng].append(lambda e, fn=fn, sem=sem, k=k: fn(e).then_inc(sem, k))
        return self.cnt[inc]

    def dma(self, eng, out, in_, inc=None, **kw):
        inc = inc if inc is not None else {'y': 'dy', 'g': 'dg', 's': 'ds'}[eng]
        return self.emit(eng, lambda e, out=out, in_=in_, kw=kw: e.dma_start(out=out, in_=in_, **kw),
                         inc=inc, k=16)

    def wait(self, eng, nm, val=None):
        v = self.cnt[nm] if val is None else val
        sem = self.sems[nm]
        self.q[eng].append(lambda e, sem=sem, v=v: e.wait_ge(sem, v))

    def tok(self, nm):
        return self.cnt[nm]

    def finalize(self):
        nc = self.nc
        with nc.Block() as block:
            for eng, attr in _ENG_ATTR.items():
                thunks = self.q[eng]
                if not thunks:
                    continue
                getattr(block, attr)(lambda e, thunks=thunks: [th(e) for th in thunks])


def alloc_common(nc, p):
    k = type('K', (), {})()
    k.nc, k.p = nc, p
    k.pb = [nc.alloc_psum_tensor(f"pb{i}", [128, 512], F32) for i in range(7)]
    k.pbh = nc.alloc_psum_tensor("pbh", [128, 1024], BF16)
    k.ones_bf = nc.alloc_sbuf_tensor("ones_bf", [128, 1], BF16)
    k.ones_f1 = nc.alloc_sbuf_tensor("ones_f1", [1, 128], F32)
    k.stat = nc.alloc_sbuf_tensor("stat", [1, 4 * 512], F32)
    k.stat2 = nc.alloc_sbuf_tensor("stat2", [2, 4 * 512], F32)
    k.st2m = k.stat2[0:2, 0:512]
    k.st2v = k.stat2[0:2, 512:1024]
    k.st2t = k.stat2[0:2, 1024:1536]
    k.st2r = k.stat2[0:2, 1536:2048]
    k.tscr = nc.alloc_sbuf_tensor("tscr", [128, 512], F32)
    p.emit('v', lambda e: e.memset(k.ones_bf[:], 1.0))
    p.emit('v', lambda e: e.memset(k.ones_f1[:], 1.0), inc='dv')
    return k

def rearr_fm(hbm_ap):
    return hbm_ap.rearrange("(c p) t -> p c t", p=128)

def rearr_tm(hbm_ap):
    return hbm_ap.rearrange("(n p) c -> p n c", p=128)


def emit_ln_fm(k, xb, xsq, out, *, nchunk=8, tlen=T, gamma_ap=None, beta_ap=None, m_out=None, r_out=None):
    p, nc = k.p, k.nc
    NS = 512
    nts = tlen // NS
    for c in range(nchunk):
        p.emit('v', lambda e, c=c: e.tensor_mul(
            xsq[:, c * tlen:(c + 1) * tlen], xb[:, c * tlen:(c + 1) * tlen],
            xb[:, c * tlen:(c + 1) * tlen]), inc='dv')
    sq_tok = p.tok('dv')
    p.wait('t', 'dv', sq_tok)
    stat_tok = [None, None]
    rdy_tok = [None, None]
    apply_tok = [None, None]
    for ts in range(nts):
        par = ts % 2
        sb, qb_ = k.pb[0], k.pb[1]
        bcm = k.pb[2] if par == 0 else k.pb[5]
        bcr = k.pb[3] if par == 0 else k.pb[6]
        M = k.stat[0:1, par * 1024 + 0:par * 1024 + 512]
        R = k.stat[0:1, par * 1024 + 512:par * 1024 + 1024]
        if stat_tok[0] is not None:
            p.wait('t', 'dv', stat_tok[0])
        def sums(e, ts=ts):
            for c in range(nchunk):
                sl = slice(c * tlen + ts * NS, c * tlen + ts * NS + NS)
                e.matmul(sb[0:1, 0:NS], k.ones_bf[:, 0:1], xb[:, sl],
                         start=(c == 0), stop=(c == nchunk - 1))
            for c in range(nchunk):
                sl = slice(c * tlen + ts * NS, c * tlen + ts * NS + NS)
                i2 = e.matmul(qb_[0:1, 0:NS], k.ones_bf[:, 0:1], xsq[:, sl],
                              start=(c == 0), stop=(c == nchunk - 1))
            return i2
        pe_tok = p.emit('t', sums, inc='pe')
        p.wait('v', 'pe', pe_tok)
        def stats(e, ts=ts, M=M, R=R):
            e.tensor_scalar_mul(M, sb[0:1, 0:NS], 1.0 / (nchunk * 128.0))
            e.tensor_scalar_mul(R, qb_[0:1, 0:NS], 1.0 / (nchunk * 128.0))
            if m_out is not None:
                e.tensor_copy(m_out[0:1, ts * NS:(ts + 1) * NS], M)
            e.tensor_mul(k.tscr[0:1, 0:NS], M, M)
            e.tensor_sub(R, R, k.tscr[0:1, 0:NS])
            return e.tensor_scalar_add(R, R, EPS)
        stat_tok[0] = p.emit('v', stats, inc='dv')
        p.wait('s', 'dv', stat_tok[0])
        ac_tok = p.emit('s', lambda e, R=R: e.activation(R, R, AF.Sqrt), inc='ac')
        p.wait('v', 'ac', ac_tok)
        def recf(e, ts=ts, R=R):
            i = e.reciprocal(R, R)
            if r_out is not None:
                i = e.tensor_copy(r_out[0:1, ts * NS:(ts + 1) * NS], R)
            return i
        rdy_tok[par] = p.emit('v', recf, inc='dv')
        if apply_tok[par] is not None:
            p.wait('t', 'dv', apply_tok[par])
        p.wait('t', 'dv', rdy_tok[par])
        def bcast(e, M=M, R=R, bcm=bcm, bcr=bcr):
            e.matmul(bcm[:, 0:NS], k.ones_f1[0:1, :], M, start=True, stop=True)
            return e.matmul(bcr[:, 0:NS], k.ones_f1[0:1, :], R, start=True, stop=True)
        pe_tok = p.emit('t', bcast, inc='pe')
        p.wait('v', 'pe', pe_tok)
        def apply(e, ts=ts, bcm=bcm, bcr=bcr):
            for c in range(nchunk):
                sl = slice(c * tlen + ts * NS, c * tlen + ts * NS + NS)
                e.tensor_sub(k.tscr[:, 0:NS], xb[:, sl], bcm[:, 0:NS])
                if gamma_ap is None:
                    iv = e.tensor_mul(out[:, sl], k.tscr[:, 0:NS], bcr[:, 0:NS])
                else:
                    e.tensor_mul(k.tscr[:, 0:NS], k.tscr[:, 0:NS], bcr[:, 0:NS])
                    iv = e.tensor_scalar(out[:, sl], k.tscr[:, 0:NS],
                                         gamma_ap[:, c:c + 1], beta_ap[:, c:c + 1],
                                         ALU.mult, ALU.add)
            return iv
        apply_tok[par] = p.emit('v', apply, inc='dv')
    p.wait('t', 'dv')


def emit_proj_fm(k, wfm, xfm, out, *, nchunk_out=8, nchunk_in=8, tlen=T, evac='s',
                 bank0=0, bias_ap=None, relu=False, out_tlen=None, out_col0=0):
    p = k.p
    NS = 512
    wcols = nchunk_out * 128
    if out_tlen is None:
        out_tlen = tlen
    btok = [None, None]
    for cch in range(nchunk_out):
        for ts in range(tlen // NS):
            bi = ts % 2
            bank = k.pb[bank0 + bi]
            if btok[bi] is not None:
                p.wait('t', 'ac' if evac == 's' else 'dv', btok[bi])
            def mms(e, cch=cch, ts=ts, bank=bank):
                for dc in range(nchunk_in):
                    i = e.matmul(bank[:, 0:NS],
                                 wfm[:, dc * wcols + cch * 128: dc * wcols + cch * 128 + 128],
                                 xfm[:, dc * tlen + ts * NS: dc * tlen + ts * NS + NS],
                                 start=(dc == 0), stop=(dc == nchunk_in - 1))
                return i
            pe_tok = p.emit('t', mms, inc='pe')
            osl = out[:, cch * out_tlen + out_col0 + ts * NS: cch * out_tlen + out_col0 + ts * NS + NS]
            if evac == 's':
                p.wait('s', 'pe', pe_tok)
                btok[bi] = p.emit('s', lambda e, bank=bank, osl=osl: e.activation(osl, bank[:, 0:NS], AF.Copy),
                                  inc='ac')
            else:
                p.wait('v', 'pe', pe_tok)
                if bias_ap is not None:
                    btok[bi] = p.emit('v', lambda e, bank=bank, osl=osl, cch=cch:
                                      e.tensor_scalar(osl, bank[:, 0:NS], bias_ap[:, cch:cch + 1],
                                                      0.0, ALU.add, ALU.max if relu else ALU.bypass),
                                      inc='dv')
                else:
                    btok[bi] = p.emit('v', lambda e, bank=bank, osl=osl: e.tensor_copy(osl, bank[:, 0:NS]),
                                      inc='dv')
    p.wait('t', 'ac' if evac == 's' else 'dv')


def emit_rot_fm(k, raw, cosT, sinT, scrA, scrB, scrC):
    p = k.p
    for c in range(4):
        lo = raw[:, c * T:(c + 1) * T]
        hi = raw[:, (c + 4) * T:(c + 5) * T]
        cs = cosT[:, c * T:(c + 1) * T]
        sn = sinT[:, c * T:(c + 1) * T]
        def rot(e, lo=lo, hi=hi, cs=cs, sn=sn):
            e.tensor_mul(scrA[:], lo, cs)
            e.tensor_mul(scrC[:], lo, sn)
            e.tensor_mul(scrB[:], hi, sn)
            e.tensor_sub(lo, scrA[:], scrB[:])
            e.tensor_mul(scrB[:], hi, cs)
            return e.tensor_add(hi, scrC[:], scrB[:])
        p.emit('v', rot, inc='dv')


def emit_regroup(k, src, dst):
    p = k.p
    for g in range(8):
        for hf in range(2):
            h = 2 * g + hf
            cr, qq = h // 4, h % 4
            for x1 in range(2):
                srows = slice(32 * qq, 32 * qq + 32)
                scol = (cr + 4 * x1) * T
                drows = slice(64 * hf + 32 * x1, 64 * hf + 32 * x1 + 32)
                p.dma('y', dst[drows, g * T:(g + 1) * T], src[srows, scol:scol + T])


def emit_rot_tm(k, raw_tm, out_tm, cosB, sinB, scrA, scrB, scrC, *, ntile=16):
    p = k.p
    for n in range(ntile):
        lo = raw_tm[:, n * 1024: n * 1024 + 512]
        hi = raw_tm[:, n * 1024 + 512: n * 1024 + 1024]
        cs = cosB[:, n * 512:(n + 1) * 512]
        sn = sinB[:, n * 512:(n + 1) * 512]
        oblk = out_tm[:, n * 1024:(n + 1) * 1024].rearrange("p (h i) -> p h i", h=16, i=64)
        olo = oblk[:, :, 0:32]
        ohi = oblk[:, :, 32:64]
        def rot(e, lo=lo, hi=hi, cs=cs, sn=sn, olo=olo, ohi=ohi):
            e.tensor_mul(scrA[:, 0:512], lo, cs)
            e.tensor_mul(scrC[:, 0:512], lo, sn)
            e.tensor_mul(scrB[:, 0:512], hi, sn)
            e.tensor_sub(olo, scrA[:, 0:512].rearrange("p (h i) -> p h i", i=32),
                         scrB[:, 0:512].rearrange("p (h i) -> p h i", i=32))
            e.tensor_mul(scrB[:, 0:512], hi, cs)
            return e.tensor_add(ohi, scrC[:, 0:512].rearrange("p (h i) -> p h i", i=32),
                                scrB[:, 0:512].rearrange("p (h i) -> p h i", i=32))
        p.emit('v', rot, inc='dv')

def emit_proj_tm(k, wfm, xfm, out_tm, *, nchunk_in=8, tlen=T, ncol=1024, bank0=0):
    p = k.p
    NS = 512
    btok = [None, None]
    ii_ = 0
    for tt in range(tlen // 128):
        for cs in range(ncol // NS):
            bi = ii_ % 2
            ii_ += 1
            bank = k.pb[bank0 + bi]
            if btok[bi] is not None:
                p.wait('t', 'ac', btok[bi])
            def mms(e, tt=tt, cs=cs, bank=bank):
                for dc in range(nchunk_in):
                    i = e.matmul(bank[:, 0:NS],
                                 xfm[:, dc * tlen + tt * 128: dc * tlen + tt * 128 + 128],
                                 wfm[:, dc * ncol + cs * NS: dc * ncol + cs * NS + NS],
                                 start=(dc == 0), stop=(dc == nchunk_in - 1))
                return i
            pe_tok = p.emit('t', mms, inc='pe')
            p.wait('s', 'pe', pe_tok)
            osl = out_tm[:, tt * ncol + cs * NS: tt * ncol + cs * NS + NS]
            btok[bi] = p.emit('s', lambda e, bank=bank, osl=osl: e.activation(osl, bank[:, 0:NS], AF.Copy),
                              inc='ac')
    p.wait('t', 'ac')


def emit_retention(k, Qg, Kg, ktok, Vtm, DT, QDECG, SDECG, Sf32, S2b, RTg, qscr, atA, atB):
    p = k.p
    scA, scB, rtA, rtB, ds = k.pb[0], k.pb[1], k.pb[2], k.pb[3], k.pb[4]
    for c in range(16):
        for g in range(8):
            qcol = slice(g * T + c * 128, g * T + c * 128 + 128)
            vcol = slice(c * 1024 + g * 128, c * 1024 + g * 128 + 128)
            def dve1(e, g=g, qcol=qcol):
                e.tensor_mul(qscr[:], Qg[:, qcol], QDECG[:, g * 128:(g + 1) * 128])
                e.tensor_copy(S2b[0:64, g * 128:g * 128 + 64], Sf32[0:64, g * 128:g * 128 + 64])
                return e.tensor_copy(S2b[64:128, g * 128 + 64:g * 128 + 128],
                                     Sf32[64:128, g * 128 + 64:g * 128 + 128])
            dv_tok = p.emit('v', dve1, inc='dv')
            p.wait('t', 'dv', dv_tok)
            def pe1(e, g=g, qcol=qcol):
                e.matmul(rtA[0:64, 0:128], S2b[:, g * 128:g * 128 + 64], qscr[:],
                         start=True, stop=False)
                e.matmul(rtB[64:128, 0:128], S2b[:, g * 128 + 64:g * 128 + 128], qscr[:],
                         start=True, stop=False, tile_position=(0, 64))
                e.matmul(scA[:, 0:128], Kg[0:64, qcol], Qg[0:64, qcol],
                         start=True, stop=True)
                return e.matmul(scB[:, 0:128], Kg[64:128, qcol], Qg[64:128, qcol],
                                start=True, stop=True, tile_position=(64, 0))
            pe_tok = p.emit('t', pe1, inc='pe')
            p.wait('v', 'pe', pe_tok)
            def dve2(e, g=g):
                e.tensor_mul(atA[:], scA[:, 0:128], DT[:, (2 * g) * 128:(2 * g) * 128 + 128])
                return e.tensor_mul(atB[:], scB[:, 0:128], DT[:, (2 * g + 1) * 128:(2 * g + 1) * 128 + 128])
            dv_tok = p.emit('v', dve2, inc='dv')
            p.wait('t', 'dv', dv_tok)
            def pe2(e, g=g, vcol=vcol):
                e.matmul(rtA[0:64, 0:128], Vtm[:, vcol.start:vcol.start + 64], atA[:],
                         start=False, stop=True)
                e.matmul(rtB[64:128, 0:128], Vtm[:, vcol.start + 64:vcol.start + 128], atB[:],
                         start=False, stop=True, tile_position=(0, 64))
                return e.matmul(ds[:, 0:128], ktok[:, vcol], Vtm[:, vcol],
                                start=True, stop=True)
            pe_tok = p.emit('t', pe2, inc='pe')
            p.wait('v', 'pe', pe_tok)
            def dve3(e, g=g, c=c):
                e.tensor_copy(RTg[0:64, g * T + c * 128: g * T + c * 128 + 128], rtA[0:64, 0:128])
                e.tensor_copy(RTg[64:128, g * T + c * 128: g * T + c * 128 + 128], rtB[64:128, 0:128])
                e.scalar_tensor_tensor(Sf32[0:64, g * 128:g * 128 + 64],
                                       Sf32[0:64, g * 128:g * 128 + 64],
                                       SDECG[0:64, g:g + 1], ds[0:64, 0:64],
                                       ALU.mult, ALU.add)
                return e.scalar_tensor_tensor(Sf32[64:128, g * 128 + 64:g * 128 + 128],
                                              Sf32[64:128, g * 128 + 64:g * 128 + 128],
                                              SDECG[64:128, g:g + 1], ds[64:128, 64:128],
                                              ALU.mult, ALU.add)
            dv_tok = p.emit('v', dve3, inc='dv')
            p.wait('t', 'dv', dv_tok)


def emit_gnorm(k, RTg, Rn, ones2, E2, gammaT, betaT, sqscr):
    p = k.p
    NS = 512
    for g in range(8):
        for ts in range(T // NS):
            sl = slice(g * T + ts * NS, g * T + ts * NS + NS)
            dv_tok = p.emit('v', lambda e, sl=sl: e.tensor_mul(sqscr[:], RTg[:, sl], RTg[:, sl]),
                            inc='dv')
            p.wait('t', 'dv', dv_tok)
            def pe1(e, sl=sl):
                e.matmul(k.pb[0][0:2, 0:NS], ones2[:, 0:2], RTg[:, sl], start=True, stop=True)
                return e.matmul(k.pb[1][0:2, 0:NS], ones2[:, 0:2], sqscr[:], start=True, stop=True)
            pe_tok = p.emit('t', pe1, inc='pe')
            p.wait('v', 'pe', pe_tok)
            def stats(e):
                e.tensor_scalar_mul(k.st2m[:], k.pb[0][0:2, 0:NS], 1.0 / 64.0)
                e.tensor_scalar_mul(k.st2v[:], k.pb[1][0:2, 0:NS], 1.0 / 64.0)
                e.tensor_mul(k.st2t[:], k.st2m[:], k.st2m[:])
                e.tensor_sub(k.st2v[:], k.st2v[:], k.st2t[:])
                return e.tensor_scalar_add(k.st2v[:], k.st2v[:], EPS)
            dv_tok = p.emit('v', stats, inc='dv')
            p.wait('s', 'dv', dv_tok)
            ac_tok = p.emit('s', lambda e: e.activation(k.st2t[:], k.st2v[:], AF.Sqrt), inc='ac')
            p.wait('v', 'ac', ac_tok)
            dv_tok = p.emit('v', lambda e: e.reciprocal(k.st2r[:], k.st2t[:]), inc='dv')
            p.wait('t', 'dv', dv_tok)
            def bcast(e):
                e.matmul(k.pb[2][:, 0:NS], E2[0:2, :], k.st2m[:], start=True, stop=True)
                return e.matmul(k.pb[3][:, 0:NS], E2[0:2, :], k.st2r[:], start=True, stop=True)
            pe_tok = p.emit('t', bcast, inc='pe')
            p.wait('v', 'pe', pe_tok)
            def apply(e, sl=sl, g=g):
                e.tensor_sub(k.tscr[:, 0:NS], RTg[:, sl], k.pb[2][:, 0:NS])
                e.tensor_mul(k.tscr[:, 0:NS], k.tscr[:, 0:NS], k.pb[3][:, 0:NS])
                return e.tensor_scalar(Rn[:, sl], k.tscr[:, 0:NS],
                                       gammaT[:, g:g + 1], betaT[:, g:g + 1],
                                       ALU.mult, ALU.add)
            dv_tok = p.emit('v', apply, inc='dv')
            p.wait('t', 'dv', dv_tok)


def emit_dequant(k, wq_i32_hbm, w_sb, scales_hbm, scaleB, *, nk, nn, eng='g'):
    p = k.p
    for g in range(nk):
        p.dma(eng, w_sb[:, g * nn:(g + 1) * nn],
              wq_i32_hbm.rearrange("(c p) t -> p c t", p=128)[:, g:g + 1, :])
        p.dma(eng, scaleB[:], scales_hbm[g:g + 1, :].broadcast_to((128, nn)))
        p.wait('v', 'd' + eng)
        p.emit('v', lambda e, g=g: e.scalar_tensor_tensor(
            w_sb[:, g * nn:(g + 1) * nn], w_sb[:, g * nn:(g + 1) * nn], 128.0,
            scaleB[:], ALU.subtract, ALU.mult), inc='dv')
        p.wait('g' if eng == 'g' else 'y', 'dv')


def emit_ffn_fc1(k, w1, hT, b1f, y1T, *, tc, NS=512):
    p = k.p
    btok = [None, None]
    for n1c in range(32):
        bi = n1c % 2
        bank = k.pb[bi]
        if btok[bi] is not None:
            p.wait('t', 'ac', btok[bi])
        def mms(e, n1c=n1c, bank=bank, tc=tc):
            for dc in range(8):
                i = e.matmul(bank[:, 0:NS],
                             w1[:, dc * 4096 + n1c * 128: dc * 4096 + n1c * 128 + 128],
                             hT[:, dc * T + tc * NS: dc * T + tc * NS + NS],
                             start=(dc == 0), stop=(dc == 7))
            return i
        pe_tok = p.emit('t', mms, inc='pe')
        p.wait('s', 'pe', pe_tok)
        btok[bi] = p.emit('s', lambda e, bank=bank, n1c=n1c: e.activation(
            y1T[:, n1c * NS:(n1c + 1) * NS], bank[:, 0:NS], AF.Silu,
            bias=b1f[:, n1c:n1c + 1]), inc='ac')
    p.wait('t', 'ac')


def emit_ffn_fc2(k, w2, y1T, b2h, xnT, gT, *, tc, NS=512):
    p = k.p
    btok = [None, None]
    for n2c in range(8):
        bi = n2c % 2
        bank = k.pb[bi]
        if btok[bi] is not None:
            p.wait('t', 'dv', btok[bi])
        def mms(e, n2c=n2c, bank=bank):
            for kc in range(32):
                i = e.matmul(bank[:, 0:NS],
                             w2[:, kc * 1024 + n2c * 128: kc * 1024 + n2c * 128 + 128],
                             y1T[:, kc * NS:(kc + 1) * NS],
                             start=(kc == 0), stop=(kc == 31))
            return i
        pe_tok = p.emit('t', mms, inc='pe')
        p.wait('v', 'pe', pe_tok)
        sl = slice(n2c * T + tc * NS, n2c * T + tc * NS + NS)
        p.wait('v', 'ac')
        btok[bi] = p.emit('v', lambda e, bank=bank, sl=sl: e.scalar_tensor_tensor(
            k.tscr[:, 0:NS], bank[:, 0:NS], 0.5, xnT[:, sl], ALU.mult, ALU.add),
            inc='dv')
        p.wait('s', 'dv', btok[bi])
        tk = p.emit('s', lambda e, sl=sl, n2c=n2c: e.activation(
            gT[:, sl], k.tscr[:, 0:NS], AF.Silu, bias=b2h[:, n2c:n2c + 1]), inc='ac')
    p.wait('t', 'ac'); p.wait('v', 'ac')

def emit_retention2(k, Qg, Kg, ktok, Vtm, DT, QDECG, SDECG, Sf32, S2b, RTg, qfull):
    p = k.p
    rtA, rtB, ds = k.pb[2], k.pb[3], k.pb[4]
    Qgv = Qg.rearrange("p (g t) -> p g t", g=8)
    dve3_tok = None
    for c in range(16):
        p.wait('v', 'pe')
        def dve0(e, c=c):
            for g_ in range(8):
                e.tensor_mul(qfull[:, g_ * 128:(g_ + 1) * 128],
                             Qg[:, g_ * T + c * 128: g_ * T + c * 128 + 128],
                             QDECG[:, g_ * 128:(g_ + 1) * 128])
            return e.tensor_copy(S2b[:], Sf32[:])
        dv0_tok = p.emit('v', dve0, inc='dv')
        p.wait('t', 'dv', dv0_tok)
        for g in range(8):
            par = g % 2
            scA = k.pb[0] if par == 0 else k.pb[5]
            scB = k.pb[1] if par == 0 else k.pb[6]
            qcol = slice(g * T + c * 128, g * T + c * 128 + 128)
            vcol = slice(c * 1024 + g * 128, c * 1024 + g * 128 + 128)
            if dve3_tok is not None:
                p.wait('t', 'dv', dve3_tok)
            def pe1(e, g=g, qcol=qcol, scA=scA, scB=scB):
                e.matmul(rtA[0:64, 0:128], S2b[:, g * 128:g * 128 + 64],
                         qfull[:, g * 128:(g + 1) * 128], start=True, stop=False)
                e.matmul(rtB[64:128, 0:128], S2b[:, g * 128 + 64:g * 128 + 128],
                         qfull[:, g * 128:(g + 1) * 128], start=True, stop=False,
                         tile_position=(0, 64))
                e.matmul(scA[:, 0:128], Kg[0:64, qcol], Qg[0:64, qcol],
                         start=True, stop=True)
                return e.matmul(scB[:, 0:128], Kg[64:128, qcol], Qg[64:128, qcol],
                                start=True, stop=True, tile_position=(64, 0))
            pe_tok = p.emit('t', pe1, inc='pe')
            p.wait('v', 'pe', pe_tok)
            atA = k.atA if par == 0 else k.atA2
            atB = k.atB if par == 0 else k.atB2
            def dve2(e, g=g, scA=scA, scB=scB, atA=atA, atB=atB):
                e.tensor_mul(atA[:], scA[:, 0:128], DT[:, (2 * g) * 128:(2 * g) * 128 + 128])
                return e.tensor_mul(atB[:], scB[:, 0:128], DT[:, (2 * g + 1) * 128:(2 * g + 1) * 128 + 128])
            dv_tok = p.emit('v', dve2, inc='dv')
            p.wait('t', 'dv', dv_tok)
            def pe2(e, g=g, vcol=vcol, atA=atA, atB=atB):
                e.matmul(rtA[0:64, 0:128], Vtm[:, vcol.start:vcol.start + 64], atA[:],
                         start=False, stop=True)
                e.matmul(rtB[64:128, 0:128], Vtm[:, vcol.start + 64:vcol.start + 128], atB[:],
                         start=False, stop=True, tile_position=(0, 64))
                return e.matmul(ds[:, 0:128], ktok[:, vcol], Vtm[:, vcol],
                                start=True, stop=True)
            pe_tok = p.emit('t', pe2, inc='pe')
            p.wait('v', 'pe', pe_tok)
            def dve3(e, g=g, c=c):
                e.tensor_copy(RTg[0:64, g * T + c * 128: g * T + c * 128 + 128], rtA[0:64, 0:128])
                e.tensor_copy(RTg[64:128, g * T + c * 128: g * T + c * 128 + 128], rtB[64:128, 0:128])
                e.scalar_tensor_tensor(Sf32[0:64, g * 128:g * 128 + 64],
                                       Sf32[0:64, g * 128:g * 128 + 64],
                                       SDECG[0:64, g:g + 1], ds[0:64, 0:64],
                                       ALU.mult, ALU.add)
                return e.scalar_tensor_tensor(Sf32[64:128, g * 128 + 64:g * 128 + 128],
                                              Sf32[64:128, g * 128 + 64:g * 128 + 128],
                                              SDECG[64:128, g:g + 1], ds[64:128, 64:128],
                                              ALU.mult, ALU.add)
            dve3_tok = p.emit('v', dve3, inc='dv')
    p.wait('t', 'dv', dve3_tok)

def _din(nc, reg, name, arr, dt=None):
    if dt is None:
        dt = mybir.dt.from_np(np.asarray(arr[0] if isinstance(arr, list) else arr).dtype)
    a0 = np.asarray(arr[0] if isinstance(arr, list) else arr)
    reg[name] = arr
    return nc.dram_tensor(name, list(a0.shape), dt, kind="ExternalInput").ap()


def build_l1():
    nc = bass.Bass("TRN2", target_bir_lowering=False, detect_race_conditions=False)
    xT = nc.dram_tensor("xT", [D, T], F32, kind="ExternalInput").ap()
    oxn = nc.dram_tensor("oxn", [D, T], BF16, kind="ExternalOutput").ap()
    p = P(nc)
    k = alloc_common(nc, p)
    xb = nc.alloc_sbuf_tensor("xb", [128, 8 * T], BF16)
    xsq = nc.alloc_sbuf_tensor("xsq", [128, 8 * T], BF16)
    xn = nc.alloc_sbuf_tensor("xn", [128, 8 * T], BF16)
    p.dma('g', xb[:], rearr_fm(xT))
    p.wait('v', 'dg'); p.wait('t', 'dg'); p.wait('s', 'dg')
    emit_ln_fm(k, xb, xsq, xn)
    p.wait('y', 'dv')
    p.dma('y', rearr_fm(oxn), xn[:])
    p.wait('y', 'dy')
    p.finalize()
    return nc


def build_l2():
    nc = bass.Bass("TRN2", target_bir_lowering=False, detect_race_conditions=False)
    din = lambda name, shape, dt=F32: nc.dram_tensor(name, shape, dt, kind="ExternalInput").ap()
    a_xnT = din("xnT", [D, T], BF16)
    a_wq = din("wq", [D, D]); a_wk = din("wk", [D, D]); a_wv = din("wv", [D, D])
    a_cosT = din("cosT", [512, T]); a_sinT = din("sinT", [512, T])
    a_cosB = din("cosB", [T, 512]); a_sinB = din("sinB", [T, 512])
    o_q = nc.dram_tensor("o_q", [D, T], BF16, kind="ExternalOutput").ap()
    o_k = nc.dram_tensor("o_k", [D, T], BF16, kind="ExternalOutput").ap()
    o_kt = nc.dram_tensor("o_kt", [T, D], BF16, kind="ExternalOutput").ap()
    o_v = nc.dram_tensor("o_v", [T, D], BF16, kind="ExternalOutput").ap()
    p = P(nc)
    k = alloc_common(nc, p)
    A = lambda nm, shape, dt=BF16: nc.alloc_sbuf_tensor(nm, shape, dt)
    xnT = A("xnTs", [128, 8 * T])
    ws = A("ws", [128, 3 * 8 * 1024])
    wq, wk, wv = ws[:, 0:8192], ws[:, 8192:16384], ws[:, 16384:24576]
    tb8 = A("tb8", [128, 16384], F32)
    raw = A("raw", [128, 16384])
    scr = A("scr", [128, 3 * 512], F32)
    scrA, scrB, scrC = scr[:, 0:512], scr[:, 512:1024], scr[:, 1024:1536]
    p.dma('g', xnT[:], rearr_fm(a_xnT))
    p.dma('g', wq, rearr_fm(a_wq))
    p.dma('g', wk, rearr_fm(a_wk))
    p.dma('g', wv, rearr_fm(a_wv))
    p.dma('g', tb8[:, 0:8192], rearr_fm(a_cosT))
    p.dma('g', tb8[:, 8192:16384], rearr_fm(a_sinT))
    p.wait('t', 'dg'); p.wait('v', 'dg'); p.wait('s', 'dg')
    cosT, sinT = tb8[:, 0:8192], tb8[:, 8192:16384]

    def rot_and_store(src_fm, out_hbm):
        for c in range(4):
            for ts in range(T // 512):
                lo = src_fm[:, c * T + ts * 512: c * T + ts * 512 + 512]
                hi = src_fm[:, (c + 4) * T + ts * 512: (c + 4) * T + ts * 512 + 512]
                cs = cosT[:, c * T + ts * 512: c * T + ts * 512 + 512]
                sn = sinT[:, c * T + ts * 512: c * T + ts * 512 + 512]
                def rot(e, lo=lo, hi=hi, cs=cs, sn=sn):
                    e.tensor_mul(scrA[:], lo, cs)
                    e.tensor_mul(scrC[:], lo, sn)
                    e.tensor_mul(scrB[:], hi, sn)
                    e.tensor_sub(lo, scrA[:], scrB[:])
                    e.tensor_mul(scrB[:], hi, cs)
                    return e.tensor_add(hi, scrC[:], scrB[:])
                p.emit('v', rot, inc='dv')
        p.wait('y', 'dv')
        p.dma('y', rearr_fm(out_hbm), src_fm[:])

    emit_proj_fm(k, wq, xnT, raw)
    p.wait('v', 'ac')
    rot_and_store(raw, o_q)
    p.wait('s', 'dy')
    emit_proj_fm(k, wk, xnT, raw)
    p.wait('v', 'ac')
    rot_and_store(raw, o_k)
    p.wait('g', 'dv')
    p.dma('g', tb8[:, 0:8192], rearr_tm(a_cosB))
    p.dma('g', tb8[:, 8192:16384], rearr_tm(a_sinB))
    p.wait('s', 'dy')
    emit_proj_tm(k, wk, xnT, raw)
    p.wait('v', 'ac'); p.wait('v', 'dg')
    ktok = ws[:, 0:16384]
    emit_rot_tm(k, raw, ktok, tb8[:, 0:8192], tb8[:, 8192:16384], scrA, scrB, scrC)
    p.wait('y', 'dv')
    p.dma('y', rearr_tm(o_kt), ktok)
    p.wait('s', 'dy')
    emit_proj_tm(k, wv, xnT, raw)
    p.wait('y', 'ac')
    p.dma('y', rearr_tm(o_v), raw[:])
    p.wait('y', 'dy')
    p.finalize()
    return nc


def build_l3a():
    nc = bass.Bass("TRN2", target_bir_lowering=False, detect_race_conditions=False)
    din = lambda name, shape, dt=F32: nc.dram_tensor(name, shape, dt, kind="ExternalInput").ap()
    a_kt = din("kt", [T, D], BF16)
    a_v = din("v", [T, D], BF16)
    a_sdec = din("sdec", [128, 8])
    o_st = nc.dram_tensor("o_st", [128, 1024], F32, kind="ExternalOutput").ap()
    p = P(nc)
    k = alloc_common(nc, p)
    A = lambda nm, shape, dt=BF16: nc.alloc_sbuf_tensor(nm, shape, dt)
    ktok = A("ktoks", [128, 16384]); Vtm = A("Vtms", [128, 16384])
    sdec = A("sdecs", [128, 8], F32)
    Sf32 = A("Sf32", [128, 1024], F32)
    p.dma('g', ktok[:], rearr_tm(a_kt))
    p.dma('g', Vtm[:], rearr_tm(a_v))
    p.dma('g', sdec[:], a_sdec)
    p.emit('v', lambda e: e.memset(Sf32[:], 0.0), inc='dv')
    p.wait('t', 'dg'); p.wait('t', 'dv'); p.wait('v', 'dg')
    ds = k.pb[0]
    for c in range(16):
        for g in range(8):
            vcol = slice(c * 1024 + g * 128, c * 1024 + g * 128 + 128)
            pe_tok = p.emit('t', lambda e, vcol=vcol: e.matmul(
                ds[:, 0:128], ktok[:, vcol], Vtm[:, vcol], start=True, stop=True), inc='pe')
            p.wait('v', 'pe', pe_tok)
            def upd(e, g=g):
                e.scalar_tensor_tensor(Sf32[0:64, g * 128:g * 128 + 64],
                                       Sf32[0:64, g * 128:g * 128 + 64],
                                       sdec[0:64, g:g + 1], ds[0:64, 0:64],
                                       ALU.mult, ALU.add)
                return e.scalar_tensor_tensor(Sf32[64:128, g * 128 + 64:g * 128 + 128],
                                              Sf32[64:128, g * 128 + 64:g * 128 + 128],
                                              sdec[64:128, g:g + 1], ds[64:128, 64:128],
                                              ALU.mult, ALU.add)
            dv_tok = p.emit('v', upd, inc='dv')
            p.wait('t', 'dv', dv_tok)
    p.wait('y', 'dv')
    p.dma('y', o_st, Sf32[:])
    p.wait('y', 'dy')
    p.finalize()
    return nc


def build_l3b():
    nc = bass.Bass("TRN2", target_bir_lowering=False, detect_race_conditions=False)
    din = lambda name, shape, dt=F32: nc.dram_tensor(name, shape, dt, kind="ExternalInput").ap()
    a_qg = din("qg", [D, T], BF16)
    a_kg = din("kg", [D, T], BF16)
    a_kt = din("kt", [T, D], BF16)
    a_v = din("v", [T, D], BF16)
    a_dt = din("dtc", [128, 2048])
    a_qdec = din("qdec", [128, 1024])
    a_sdec = din("sdec", [128, 8])
    a_stin = din("stin", [128, 1024])
    a_gam = din("gam", [128, 8]); a_bet = din("bet", [128, 8])
    a_wo = din("wo", [D, D])
    a_ones2 = din("ones2", [128, 2], BF16); a_E2 = din("E2", [2, 128])
    o_s = nc.dram_tensor("o_s", [D, T], BF16, kind="ExternalOutput").ap()
    p = P(nc)
    k = alloc_common(nc, p)
    A = lambda nm, shape, dt=BF16: nc.alloc_sbuf_tensor(nm, shape, dt)
    Qg = A("Qgs", [128, 8 * T]); Kg = A("Kgs", [128, 8 * T])
    ktok = A("ktoks", [128, 16384]); Vtm = A("Vtms", [128, 16384])
    RTg = A("RTgs", [128, 8 * T])
    dtc = A("dtcs", [128, 2048], F32); qdec = A("qdecs", [128, 1024], F32)
    sdec = A("sdecs", [128, 8], F32)
    Sf32 = A("Sf32", [128, 1024], F32); S2b = A("S2bs", [128, 1024])
    gam = A("gams", [128, 8], F32); bet = A("bets", [128, 8], F32)
    qscr = A("qscrs", [128, 128]); atA = A("atAs", [128, 128]); atB = A("atBs", [128, 128])
    sqscr = A("sqscrs", [128, 512])
    ones2 = A("ones2s", [128, 2]); E2 = A("E2s", [2, 128], F32)
    p.dma('g', Qg[:], rearr_fm(a_qg))
    p.dma('g', Kg[:], rearr_fm(a_kg))
    p.dma('g', ktok[:], rearr_tm(a_kt))
    p.dma('g', Vtm[:], rearr_tm(a_v))
    p.dma('g', dtc[:], a_dt)
    p.dma('g', qdec[:], a_qdec)
    p.dma('g', sdec[:], a_sdec)
    p.dma('g', Sf32[:], a_stin)
    p.dma('g', gam[:], a_gam)
    p.dma('g', bet[:], a_bet)
    p.dma('g', ones2[:], a_ones2)
    p.dma('g', E2[:], a_E2)
    p.emit('v', lambda e: e.memset(S2b[:], 0.0), inc='dv')
    p.wait('t', 'dg'); p.wait('v', 'dg'); p.wait('s', 'dg'); p.wait('t', 'dv')
    emit_retention(k, Qg, Kg, ktok, Vtm, dtc, qdec, sdec, Sf32, S2b, RTg,
                   qscr, atA, atB)
    wo = ktok[:, 0:8192]
    p.wait('g', 'pe')
    p.dma('g', wo, rearr_fm(a_wo))
    emit_gnorm(k, RTg, RTg, ones2, E2, gam, bet, sqscr)
    p.wait('t', 'dv'); p.wait('t', 'dg')
    emit_proj_fm(k, wo, RTg, Qg)
    p.wait('y', 'ac')
    p.dma('y', rearr_fm(o_s), Qg[:])
    p.wait('y', 'dy')
    p.finalize()
    return nc


def build_l4():
    nc = bass.Bass("TRN2", target_bir_lowering=False, detect_race_conditions=False)
    din = lambda name, shape, dt=F32: nc.dram_tensor(name, shape, dt, kind="ExternalInput").ap()
    a_xn = din("xnT", [D, T], BF16)
    a_q1 = din("q1", [D, 4096], I32)
    a_sc1 = din("sc1", [8, 4096])
    a_b1 = din("b1", [128, 32])
    a_lg = din("lg", [128, 8]); a_lb = din("lb", [128, 8])
    o_y1 = nc.dram_tensor("o_y1", [4096, T], BF16, kind="ExternalOutput").ap()
    p = P(nc)
    k = alloc_common(nc, p)
    A = lambda nm, shape, dt=BF16: nc.alloc_sbuf_tensor(nm, shape, dt)
    xnT = A("xnTs", [128, 8 * T]); xsq = A("xsqs", [128, 8 * T])
    w1 = A("w1s", [128, 8 * 4096])
    scaleB = A("scaleBs", [128, 4096], F32)
    b1f = A("b1fs", [128, 32], F32)
    lg = A("lgs", [128, 8], F32); lb = A("lbs", [128, 8], F32)
    y1T = A("y1Ts", [128, 32 * 512])
    p.dma('g', xnT[:], rearr_fm(a_xn))
    p.dma('g', b1f[:], a_b1)
    p.dma('g', lg[:], a_lg)
    p.dma('g', lb[:], a_lb)
    emit_dequant(k, a_q1, w1, a_sc1, scaleB, nk=8, nn=4096)
    p.wait('t', 'dg'); p.wait('v', 'dg'); p.wait('s', 'dg')
    emit_ln_fm(k, xnT, xsq, xnT, gamma_ap=lg, beta_ap=lb)
    p.wait('t', 'dv')
    for tc in range(4):
        emit_ffn_fc1(k, w1, xnT, b1f, y1T, tc=tc)
        p.wait('y', 'ac')
        p.dma('y', o_y1.rearrange("(c p) n -> p c n", p=128)[:, :, 512 * tc:512 * (tc + 1)], y1T[:])
        p.wait('s', 'dy')
    p.wait('y', 'dy')
    p.finalize()
    return nc


def build_l5():
    nc = bass.Bass("TRN2", target_bir_lowering=False, detect_race_conditions=False)
    din = lambda name, shape, dt=F32: nc.dram_tensor(name, shape, dt, kind="ExternalInput").ap()
    a_xn = din("xnT", [D, T], BF16)
    a_y1 = din("y1", [4096, T], BF16)
    a_q2 = din("q2", [4096, 1024], I32)
    a_sc2 = din("sc2", [32, 1024])
    a_b2 = din("b2", [128, 8])
    o_g = nc.dram_tensor("o_g", [D, T], BF16, kind="ExternalOutput").ap()
    p = P(nc)
    k = alloc_common(nc, p)
    A = lambda nm, shape, dt=BF16: nc.alloc_sbuf_tensor(nm, shape, dt)
    xnT = A("xnTs", [128, 8 * T])
    w2 = A("w2s", [128, 32 * 1024])
    scaleB = A("scaleBs", [128, 1024], F32)
    b2h = A("b2hs", [128, 8], F32)
    y1T = A("y1Ts", [128, 32 * 512])
    gT = A("gTs", [128, 8 * T])
    p.dma('g', xnT[:], rearr_fm(a_xn))
    p.dma('g', b2h[:], a_b2)
    emit_dequant(k, a_q2, w2, a_sc2, scaleB, nk=32, nn=1024)
    p.wait('v', 'dg')
    p.emit('v', lambda e: e.tensor_scalar_mul(b2h[:], b2h[:], 0.5), inc='dv')
    p.wait('t', 'dg'); p.wait('s', 'dv'); p.wait('t', 'dv')
    for tc in range(4):
        p.wait('g', 'pe')
        p.dma('g', y1T[:], a_y1.rearrange("(c p) n -> p c n", p=128)[:, :, 512 * tc:512 * (tc + 1)])
        p.wait('t', 'dg')
        emit_ffn_fc2(k, w2, y1T, b2h, xnT, gT, tc=tc)
    p.wait('y', 'ac')
    p.dma('y', rearr_fm(o_g), gT[:])
    p.wait('y', 'dy')
    p.finalize()
    return nc


def build_l6():
    nc = bass.Bass("TRN2", target_bir_lowering=False, detect_race_conditions=False)
    din = lambda name, shape, dt=F32: nc.dram_tensor(name, shape, dt, kind="ExternalInput").ap()
    a_x = din("xT", [D, T])
    a_s = din("sT", [D, T], BF16)
    a_g = din("gT", [D, T], BF16)
    o_x1 = nc.dram_tensor("o_x1", [D, T], F32, kind="ExternalOutput").ap()
    o_xn1 = nc.dram_tensor("o_xn1", [D, T], BF16, kind="ExternalOutput").ap()
    o_m = nc.dram_tensor("o_m", [1, T], F32, kind="ExternalOutput").ap()
    o_r = nc.dram_tensor("o_r", [1, T], F32, kind="ExternalOutput").ap()
    p = P(nc)
    k = alloc_common(nc, p)
    A = lambda nm, shape, dt=BF16: nc.alloc_sbuf_tensor(nm, shape, dt)
    sT = A("sTs", [128, 8 * T]); gT = A("gTs", [128, 8 * T])
    x1b = A("x1bs", [128, 8 * T]); xn1 = A("xn1s", [128, 8 * T])
    xr = A("xrs", [128, 2 * T], F32)
    x1f = A("x1fs", [128, 2 * T], F32)
    p.dma('g', sT[:], rearr_fm(a_s))
    p.dma('g', gT[:], rearr_fm(a_g))
    for c in range(8):
        rr = (c % 2)
        p.wait('y', 'dv', p.tok('dv'))
        p.dma('y', xr[:, rr * T:(rr + 1) * T], rearr_fm(a_x)[:, c:c + 1, :])
        p.wait('v', 'dy'); p.wait('v', 'dg')
        def comb(e, c=c, rr=rr):
            e.tensor_mul(x1f[:, rr * T:(rr + 1) * T], sT[:, c * T:(c + 1) * T],
                         gT[:, c * T:(c + 1) * T])
            e.tensor_add(x1f[:, rr * T:(rr + 1) * T], x1f[:, rr * T:(rr + 1) * T],
                         xr[:, rr * T:(rr + 1) * T])
            return e.tensor_copy(x1b[:, c * T:(c + 1) * T], x1f[:, rr * T:(rr + 1) * T])
        dv_tok = p.emit('v', comb, inc='dv')
        p.wait('y', 'dv', dv_tok)
        p.dma('y', rearr_fm(o_x1)[:, c:c + 1, :], x1f[:, rr * T:(rr + 1) * T])
    msb = nc.alloc_sbuf_tensor("msb", [1, T], F32)
    rsb = nc.alloc_sbuf_tensor("rsb", [1, T], F32)
    emit_ln_fm(k, x1b, gT, xn1, m_out=msb, r_out=rsb)
    p.wait('y', 'dv')
    p.dma('y', rearr_fm(o_xn1), xn1[:])
    p.dma('y', o_m, msb[:])
    p.dma('y', o_r, rsb[:])
    p.wait('y', 'dy')
    p.finalize()
    return nc

def build_l7():
    nc = bass.Bass("TRN2", target_bir_lowering=False, detect_race_conditions=False)
    din = lambda name, shape, dt=F32: nc.dram_tensor(name, shape, dt, kind="ExternalInput").ap()
    a_keysT = din("keysT", [D, S])
    a_valsT = din("valsT", [D, S])
    a_wk = din("wkT", [D, 512]); a_wv = din("wvT", [D, 512])
    a_bk = din("bk", [128, 4]); a_bv = din("bv", [1, 512])
    o_kT = nc.dram_tensor("o_kT", [512, S], BF16, kind="ExternalOutput").ap()
    o_v65 = nc.dram_tensor("o_v65", [S, 528], BF16, kind="ExternalOutput").ap()
    p = P(nc)
    k = alloc_common(nc, p)
    A = lambda nm, shape, dt=BF16: nc.alloc_sbuf_tensor(nm, shape, dt)
    kb = A("kbs", [128, 8 * S])
    wk = A("wks", [128, 8 * 512]); wv = A("wvs", [128, 8 * 512])
    bkf = A("bkfs", [128, 4], F32)
    bvB = A("bvBs", [128, 512], F32)
    kT = A("kTs", [128, 4 * S])
    v65 = A("v65s", [128, 32 * 528])
    p.dma('g', kb[:], rearr_fm(a_keysT))
    p.dma('g', wk[:], rearr_fm(a_wk))
    p.dma('g', wv[:], rearr_fm(a_wv))
    p.dma('g', bkf[:], a_bk)
    p.dma('g', bvB[:], a_bv.broadcast_to((128, 512)))
    p.wait('t', 'dg'); p.wait('v', 'dg')
    emit_proj_fm(k, wk, kb, kT, nchunk_out=4, tlen=S, evac='v', bias_ap=bkf, relu=True)
    p.wait('y', 'dv')
    p.dma('y', rearr_fm(o_kT), kT[:])
    p.wait('g', 'pe')
    p.dma('g', kb[:], rearr_fm(a_valsT))
    p.emit('v', lambda e: e.memset(v65[:], 0.0))
    v65r = v65[:].rearrange("p (n h u) -> p n h u", n=32, h=8, u=66)
    p.emit('v', lambda e: e.memset(v65r[:, :, :, 64:65], 1.0), inc='dv')
    p.wait('t', 'dg')
    for tt in range(32):
        bank = k.pb[tt % 2]
        def mms(e, tt=tt, bank=bank):
            for dc in range(8):
                i = e.matmul(bank[:, 0:512], kb[:, dc * S + tt * 128: dc * S + tt * 128 + 128],
                             wv[:, dc * 512:(dc + 1) * 512], start=(dc == 0), stop=(dc == 7))
            return i
        pe_tok = p.emit('t', mms, inc='pe')
        p.wait('v', 'pe', pe_tok)
        def evac(e, tt=tt, bank=bank):
            e.tensor_add(k.tscr[:, 0:512], bank[:, 0:512], bvB[:])
            ov = v65[:, tt * 528:(tt + 1) * 528].rearrange("p (h u) -> p h u", h=8, u=66)[:, :, 0:64]
            return e.tensor_scalar_max(ov, k.tscr[:, 0:512].rearrange("p (h u) -> p h u", h=8, u=64), 0.0)
        dv_tok = p.emit('v', evac, inc='dv')
        p.wait('t', 'dv', dv_tok)
    p.wait('y', 'dv')
    p.dma('y', rearr_tm(o_v65), v65[:])
    p.wait('y', 'dy')
    p.finalize()
    return nc


def build_l8(dbg=False):
    nc = bass.Bass("TRN2", target_bir_lowering=False, detect_race_conditions=False)
    din = lambda name, shape, dt=F32: nc.dram_tensor(name, shape, dt, kind="ExternalInput").ap()
    a_xn1 = din("xn1T", [D, S], BF16)
    a_xn1own = din("xn1own", [D, T], BF16)
    a_wq = din("wqT", [D, 512]); a_bq = din("bq", [128, 4])
    a_kT = din("kT", [512, S], BF16)
    a_v65 = din("v65", [S, 528], BF16)
    a_id = din("ident", [128, 128], BF16)
    o_s1 = nc.dram_tensor("o_s1", [D, T], BF16, kind="ExternalOutput").ap()
    if dbg:
        o_pcop = nc.dram_tensor("o_pcop", [128, 528], F32, kind="ExternalOutput").ap()
        o_rec = nc.dram_tensor("o_rec", [128, 8], F32, kind="ExternalOutput").ap()
        o_avs = nc.dram_tensor("o_avs", [65, 1024], F32, kind="ExternalOutput").ap()
    p = P(nc)
    A = (lambda nm, shape, dt=BF16: nc.alloc_sbuf_tensor(nm, shape, dt))
    k = type('K', (), {})(); k.nc, k.p = nc, p
    psc = [nc.alloc_psum_tensor(f"psc{i}", [128, 1024], F32) for i in range(2)]
    avd = nc.alloc_psum_tensor("avd", [65, 1024], F32)
    pbh = nc.alloc_psum_tensor("pbh", [128, 1024], BF16)
    xb = A("xbs", [128, 8 * S])
    xo = A("xos", [128, 8 * T])
    qT = A("qTs", [128, 4 * S])
    wq = A("wqs", [128, 8 * 512]); bqf = A("bqfs", [128, 4], F32)
    s1T = A("s1Ts", [128, 8 * T])
    kts = A("ktss", [128, 2 * S])
    vs = A("vss", [128, 2 * 32 * 66])
    avs = A("avss", [65, 1024])
    pcop = A("pcops", [128, 528])
    rec = A("recs", [128, 8], F32)
    k.ident = A("idents", [128, 128])
    p.dma('g', k.ident[:], a_id)
    p.dma('g', xb[:], rearr_fm(a_xn1))
    p.dma('g', xo[:], rearr_fm(a_xn1own))
    p.dma('g', wq[:], rearr_fm(a_wq))
    p.dma('g', bqf[:], a_bq)
    p.wait('t', 'dg'); p.wait('v', 'dg')
    NS = 512
    for cch in range(4):
        for ts in range(S // NS):
            bank = psc[ts % 2]
            def mms(e, cch=cch, ts=ts, bank=bank):
                for dc in range(8):
                    i = e.matmul(bank[:, 0:NS],
                                 wq[:, dc * 512 + cch * 128: dc * 512 + cch * 128 + 128],
                                 xb[:, dc * S + ts * NS: dc * S + ts * NS + NS],
                                 start=(dc == 0), stop=(dc == 7))
                return i
            pe_tok = p.emit('t', mms, inc='pe')
            p.wait('v', 'pe', pe_tok)
            osl = qT[:, cch * S + ts * NS: cch * S + ts * NS + NS]
            tk = p.emit('v', lambda e, bank=bank, osl=osl, cch=cch: e.tensor_scalar(
                osl, bank[:, 0:NS], bqf[:, cch:cch + 1], 0.0, ALU.add, ALU.max), inc='dv')
            p.wait('t', 'dv', tk)
    kload_tok = {}
    vs_tok = {}
    eb6 = A("eb6s", [128, 6 * 1024])
    LAG8 = 4
    exp_tok = [None, None]
    av_ring = [None] * 6
    pending_av = []
    pending_epi = []
    st8 = {'g': 0, 'avs_dv': None}
    s1v = s1T[:].rearrange("p (tc hh u f) -> p tc hh u f", tc=8, hh=8, u=64, f=4)
    xov = xo[:].rearrange("p (tc hh u f) -> p tc hh u f", tc=8, hh=8, u=64, f=4)

    def _emit_epi(h, qb, dv_tok):
        p.wait('t', 'dv', dv_tok)
        def tps8(e):
            for i in range(8):
                e.transpose(pbh[:, 66 * i:66 * i + 65], avs[:, 128 * i:128 * i + 128],
                            k.ident[0:65, 0:65])
            return e.transpose(pbh[:, 700:765], avs[:, 0:128], k.ident[0:65, 0:65])
        pe_tok = p.emit('t', tps8, inc='pe')
        p.wait('v', 'pe', pe_tok)
        def evacp8(e):
            e.reciprocal(rec[:, 0:8], pbh[:, 64:528:66])
            return e.tensor_copy(pcop[:], pbh[:, 0:528])
        dv2 = p.emit('v', evacp8, inc='dv')
        p.wait('t', 'dv', dv2)
        def perm8(e, h=h, qb=qb):
            for tc in range(8):
                osl = s1v[:, tc:tc + 1, h:h + 1, :, qb:qb + 1]
                isl = xov[:, tc:tc + 1, h:h + 1, :, qb:qb + 1]
                ii = e.scalar_tensor_tensor(
                    osl, pcop[:, 66 * tc:66 * tc + 64].rearrange(
                        "p (a u f) -> p a u f", a=1, u=64, f=1),
                    rec[:, tc:tc + 1], isl, ALU.mult, ALU.add)
            return ii
        p.emit('v', perm8, inc='dv')

    def _issue_av(it):
        sl, ac_tok, h_, qb_, kt_ = it
        if kt_ == 0:
            if st8['avs_dv'] is not None:
                p.wait('t', 'dv', st8['avs_dv'])
            p.wait('t', 'dg', vs_tok[h_])
        p.wait('t', 'ac', ac_tok)
        vsv_ = vs[:, (h_ % 2) * 2112:(h_ % 2 + 1) * 2112]
        ebl = (sl % 6) * 1024
        def av8(e, kt=kt_, vsv=vsv_, ebl=ebl):
            e.matmul(avd[:, 0:512], vsv[:, kt * 66:kt * 66 + 65],
                     eb6[:, ebl:ebl + 512], start=(kt == 0), stop=(kt == 31))
            return e.matmul(avd[:, 512:1024], vsv[:, kt * 66:kt * 66 + 65],
                            eb6[:, ebl + 512:ebl + 1024], start=(kt == 0), stop=(kt == 31))
        tokp = p.emit('t', av8, inc='pe')
        av_ring[sl % 6] = tokp
        if kt_ == 31:
            p.wait('v', 'pe', tokp)
            dvc = p.emit('v', lambda e: e.tensor_copy(avs[:], avd[:]), inc='dv')
            st8['avs_dv'] = dvc
            pending_epi.append((st8['g'] + 3, h_, qb_, dvc))

    for h in range(8):
        pair = h // 2
        if h % 2 == 0:
            p.wait('g', 'pe')
            kload_tok[pair] = p.dma('g', kts[:, (pair % 2) * S:(pair % 2 + 1) * S],
                                    rearr_fm(a_kT)[:, pair:pair + 1, :])
        p.wait('g', 'pe')
        vs_tok[h] = p.dma('g', vs[:, (h % 2) * 2112:(h % 2 + 1) * 2112],
                          rearr_tm(a_v65)[:, :, 66 * h:66 * h + 66])
        krow = slice(64 * (h % 2), 64 * (h % 2) + 64)
        ktsv = kts[:, (pair % 2) * S:(pair % 2 + 1) * S]
        for qb in range(4):
            for kt in range(32):
                g8 = st8['g']
                while pending_epi and pending_epi[0][0] <= g8:
                    _, eh, eqb, etok = pending_epi.pop(0)
                    _emit_epi(eh, eqb, etok)
                if h % 2 == 0 and qb == 0 and kt == 0:
                    p.wait('t', 'dg', kload_tok[pair])
                if exp_tok[g8 % 2] is not None:
                    p.wait('t', 'ac', exp_tok[g8 % 2])
                def sc8_(e, kt=kt, g8=g8, krow=krow, ktsv=ktsv, qb=qb, h=h):
                    bank = psc[g8 % 2]
                    e.matmul(bank[:, 0:512], ktsv[krow, kt * 128:(kt + 1) * 128],
                             qT[krow, (h // 2) * S + qb * 1024: (h // 2) * S + qb * 1024 + 512],
                             start=True, stop=True)
                    return e.matmul(bank[:, 512:1024], ktsv[krow, kt * 128:(kt + 1) * 128],
                                    qT[krow, (h // 2) * S + qb * 1024 + 512: (h // 2) * S + qb * 1024 + 1024],
                                    start=True, stop=True)
                pe_tok = p.emit('t', sc8_, inc='pe')
                p.wait('s', 'pe', pe_tok)
                if av_ring[g8 % 6] is not None:
                    p.wait('s', 'pe', av_ring[g8 % 6])
                ac_tok = p.emit('s', lambda e, g8=g8: e.activation(
                    eb6[:, (g8 % 6) * 1024:(g8 % 6 + 1) * 1024], psc[g8 % 2][:], AF.Exp,
                    scale=0.125), inc='ac')
                exp_tok[g8 % 2] = ac_tok
                pending_av.append((g8, ac_tok, h, qb, kt))
                while pending_av and pending_av[0][0] <= g8 - LAG8:
                    _issue_av(pending_av.pop(0))
                st8['g'] = g8 + 1
    while pending_av:
        _issue_av(pending_av.pop(0))
    while pending_epi:
        _, eh, eqb, etok = pending_epi.pop(0)
        _emit_epi(eh, eqb, etok)
    p.wait('y', 'dv')
    p.dma('y', rearr_fm(o_s1), s1T[:])
    p.wait('y', 'dy')
    p.finalize()
    return nc


def build_l9(dbg=False):
    nc = bass.Bass("TRN2", target_bir_lowering=False, detect_race_conditions=False)
    din = lambda name, shape, dt=F32: nc.dram_tensor(name, shape, dt, kind="ExternalInput").ap()
    a_x1 = din("x1T", [D, T])
    a_wg = din("wg", [D, 8])
    a_m = din("mrow", [1, T])
    a_rtm = din("r_tm", [128, 16])
    a_negb = din("negbig", [128, 32])
    a_posw = din("posw", [128, 32])

    a_id = din("ident", [128, 128])
    o_oh = nc.dram_tensor("o_oh", [8, T], F32, kind="ExternalOutput").ap()
    o_rk = nc.dram_tensor("o_rk", [8, T], F32, kind="ExternalOutput").ap()
    o_gate = nc.dram_tensor("o_gate", [128, 16], F32, kind="ExternalOutput").ap()
    o_hist = nc.dram_tensor("o_hist", [8, 1], F32, kind="ExternalOutput").ap()
    if dbg:
        o_lgs = nc.dram_tensor("o_lgs", [128, 8], F32, kind="ExternalOutput").ap()
        o_oh0 = nc.dram_tensor("o_oh0", [128, 8], F32, kind="ExternalOutput").ap()
        o_ohd = nc.dram_tensor("o_ohd", [128, 8], F32, kind="ExternalOutput").ap()
        o_sc8 = nc.dram_tensor("o_sc8", [128, 8], F32, kind="ExternalOutput").ap()
        o_praw = nc.dram_tensor("o_praw", [128, 8], F32, kind="ExternalOutput").ap()
        o_ppm = nc.dram_tensor("o_ppm", [128, 8], F32, kind="ExternalOutput").ap()
        o_wsum = nc.dram_tensor("o_wsum", [1, 8], F32, kind="ExternalOutput").ap()
        o_lgs1 = nc.dram_tensor("o_lgs1", [128, 8], F32, kind="ExternalOutput").ap()
    p = P(nc)
    k = alloc_common(nc, p)
    A = lambda nm, shape, dt=F32: nc.alloc_sbuf_tensor(nm, shape, dt)
    x1f = A("x1fs", [128, 8 * T])
    wg = A("wgs", [128, 8 * 8])
    msb = A("msbs", [1, T]); rtm = A("rtms", [128, 16])
    oh0 = A("oh0s", [128, 32]); sc8 = A("sc8s", [128, 32])
    t_pm = A("t_pms", [128, 32]); t_raw = A("t_raws", [128, 32]); t_d = A("t_ds", [128, 32])
    zz8 = A("zz8s", [128, 32]); big30 = A("big30s", [128, 32])
    posw = A("posws", [128, 32]); one30 = A("one30s", [128, 32])
    t1 = A("t1s", [128, 32]); t2 = A("t2s", [128, 32]); t3 = A("t3s", [128, 32])
    q8 = A("q8s", [128, 32]); m2 = A("m2s", [128, 32]); nm2 = A("nm2s", [128, 32])
    idf = A("idfs", [128, 128])
    wsum = A("wsums", [1, 8])
    onescol = A("onescols", [128, 1])
    lgs = A("lgss", [128, 32]); el = A("els", [128, 32])
    mx = A("mxs", [128, 32]); nmx = A("nmxs", [128, 32]); sm = A("sms", [128, 32])
    negb = A("negbs", [128, 32])
    oh = A("ohs", [128, 32])
    gate_tm = A("gatetms", [128, 16]); smbuf = A("smbufs", [128, 16])
    ohT = A("ohTs", [8, T]); scanT = A("scanTs", [8, T]); rkT = A("rkTs", [8, T])
    p.dma('g', x1f[:], rearr_fm(a_x1))
    p.dma('g', wg[:], rearr_fm(a_wg))
    p.dma('g', msb[:], a_m)
    p.dma('g', rtm[:], a_rtm)
    p.dma('g', negb[:], a_negb)
    p.dma('g', posw[:], a_posw)
    p.dma('g', idf[:], a_id)
    p.emit('v', lambda e: e.memset(onescol[:], 1.0))
    p.emit('v', lambda e: e.memset(zz8[:], 0.0))
    p.emit('v', lambda e: e.memset(big30[:], 1e30))
    p.emit('v', lambda e: e.memset(one30[:], 1.0))
    p.emit('v', lambda e: e.memset(k.pb[0][:, 0:32], 0.0))
    p.emit('v', lambda e: e.memset(k.pb[1][:, 0:32], 0.0), inc='dv')
    p.wait('t', 'dg'); p.wait('v', 'dg'); p.wait('t', 'dv')
    def mmw(e):
        for dc in range(8):
            i = e.matmul(k.pb[2][0:1, 0:8], onescol[:, 0:1],
                         wg[:, dc * 8:(dc + 1) * 8], start=(dc == 0), stop=(dc == 7))
        return i
    pe_tok = p.emit('t', mmw, inc='pe')
    p.wait('v', 'pe', pe_tok)
    dv_tok = p.emit('v', lambda e: e.tensor_copy(wsum[:], k.pb[2][0:1, 0:8]), inc='dv')
    p.wait('t', 'dv', dv_tok)
    for tc in range(16):
        def mmr(e, tc=tc):
            for dc in range(8):
                e.matmul(k.pb[0][:, 0:8], x1f[:, dc * T + tc * 128: dc * T + tc * 128 + 128],
                         wg[:, dc * 8:(dc + 1) * 8], start=(dc == 0), stop=(dc == 7))
            return e.matmul(k.pb[1][:, 0:8], msb[0:1, tc * 128:(tc + 1) * 128], wsum[:],
                     start=True, stop=True)
        pe_tok = p.emit('t', mmr, inc='pe')
        p.wait('v', 'pe', pe_tok)
        if dbg and tc == 0:
            prawsb = nc.alloc_sbuf_tensor("prawsb", [128, 8], F32)
            ppmsb = nc.alloc_sbuf_tensor("ppmsb", [128, 8], F32)
            p.emit('v', lambda e: e.tensor_copy(prawsb[:], k.pb[0][:, 0:8]))
            dvt = p.emit('v', lambda e: e.tensor_copy(ppmsb[:], k.pb[1][:, 0:8]), inc='dv')
            p.wait('y', 'dv', dvt)
            p.dma('y', o_praw, prawsb[:])
            p.dma('y', o_ppm, ppmsb[:])
            p.dma('y', o_wsum, wsum[:])
            p.wait('v', 'dy')
        def gdve(e, tc=tc):
            e.tensor_copy(t_pm[:], k.pb[1][:, 0:32])
            e.tensor_copy(t_raw[:], k.pb[0][:, 0:32])
            e.tensor_sub(t_d[:], t_raw[:], t_pm[:])
            e.scalar_tensor_tensor(lgs[:], t_d[:], rtm[:, tc:tc + 1], negb[:],
                                   ALU.mult, ALU.add)
            e.reduce_max(nmx[:, 0:1], lgs[:], AX.X, negate=True)
            e.tensor_copy(t_pm[:], lgs[:])
            return e.tensor_copy(t_raw[:], lgs[:])
        dv_tok = p.emit('v', gdve, inc='dv')
        if dbg and tc == 0:
            p.wait('y', 'dv', dv_tok)
            p.dma('y', o_lgs1, lgs[:, 0:8])
            p.wait('v', 'dy')
        p.wait('s', 'dv', dv_tok)
        ac_tok = p.emit('s', lambda e: e.activation(el[:], lgs[:], AF.Exp, bias=nmx[:, 0:1]), inc='ac')
        p.wait('v', 'ac', ac_tok)
        def gdve2(e, tc=tc):
            e.reduce_sum(smbuf[:, tc:tc + 1], el[:], AX.X)
            e.scalar_tensor_tensor(t1[:], lgs[:], nmx[:, 0:1], big30[:],
                                   ALU.add, ALU.mult)
            e.tensor_add(t2[:], t1[:], one30[:])
            e.tensor_max(t3[:], t2[:], zz8[:])
            e.tensor_tensor(oh0[:], t3[:], one30[:], ALU.min)
            e.tensor_mul(q8[:], oh0[:], posw[:])
            e.reduce_max(nm2[:, 0:1], q8[:], AX.X, negate=True)
            e.tensor_copy(t2[:], q8[:])
            e.tensor_copy(t3[:], q8[:])
            e.scalar_tensor_tensor(t1[:], q8[:], nm2[:, 0:1], big30[:],
                                   ALU.add, ALU.mult)
            e.tensor_add(t2[:], t1[:], one30[:])
            e.tensor_max(t3[:], t2[:], zz8[:])
            return e.tensor_tensor(oh[:], t3[:], one30[:], ALU.min)
        dv_tok = p.emit('v', gdve2, inc='dv')
        if dbg and tc == 0:
            p.wait('y', 'dv', dv_tok)
            p.dma('y', o_lgs, lgs[:, 0:8])
            p.dma('y', o_oh0, oh0[:, 0:8])
            p.dma('y', o_ohd, oh[:, 0:8])
            p.dma('y', o_sc8, sc8[:, 0:8])
            p.wait('v', 'dy')
        p.wait('t', 'dv', dv_tok)
        def tpo(e, tc=tc):
            e.transpose(k.pb[3][0:32, 0:128], oh[:], idf[:])
            return e.transpose(k.pb[5][0:32, 0:128], oh[:], idf[:])
        pe_tok = p.emit('t', tpo, inc='pe')
        p.wait('v', 'pe', pe_tok)
        dv_tok = p.emit('v', lambda e, tc=tc: e.tensor_copy(ohT[0:8, tc * 128:(tc + 1) * 128],
                                                            k.pb[3][0:8, 0:128]), inc='dv')
        p.wait('t', 'dv', dv_tok)
    def scan(e):
        e.reciprocal(gate_tm[:], smbuf[:])
        e.tensor_tensor_scan(scanT[0:8, :], ohT[0:8, :], ohT[0:8, :], 0.0, ALU.add, ALU.bypass)
        return e.tensor_sub(rkT[0:8, :], scanT[0:8, :], ohT[0:8, :])
    dv_tok = p.emit('v', scan, inc='dv')
    p.wait('y', 'dv')
    p.dma('y', o_oh, ohT[:])
    p.dma('y', o_rk, rkT[:])
    p.dma('y', o_gate, gate_tm[:])
    p.dma('y', o_hist, scanT[0:8, T - 1:T])
    p.wait('y', 'dy')
    p.finalize()
    return nc



def build_l10():
    nc = bass.Bass("TRN2", target_bir_lowering=False, detect_race_conditions=False)
    din = lambda name, shape, dt=F32: nc.dram_tensor(name, shape, dt, kind="ExternalInput").ap()
    a_gath = din("gath", [8, 8])
    a_mask = din("maskc", [8, 1])
    a_oh = din("ohT", [8, T]); a_rk = din("rkT", [8, T])
    a_gate = din("gate", [128, 16])
    a_xn1 = din("xn1T", [D, T], BF16)
    a_we = din("we", [D, D])
    a_be = din("be", [128, 8])
    a_s1 = din("s1T", [D, T], BF16)
    a_x1 = din("x1T", [D, T])
    a_id = din("ident", [128, 128])
    o_out = nc.dram_tensor("o_out", [D, T], F32, kind="ExternalOutput").ap()
    p = P(nc)
    k = alloc_common(nc, p)
    A = lambda nm, shape, dt=F32: nc.alloc_sbuf_tensor(nm, shape, dt)
    gath = A("gaths", [8, 8]); maskc = A("maskcs", [8, 1])
    ohT = A("ohTs", [8, T]); rkT = A("rkTs", [8, T]); keptT = A("keptTs", [8, T])
    gate_tm = A("gatetms", [128, 16]); smbuf = A("smbufs", [128, 16]); offs = A("offss", [8, 1])
    xn1 = nc.alloc_sbuf_tensor("xn1s", [128, 8 * T], BF16)
    s1T = nc.alloc_sbuf_tensor("s1Ts", [128, 8 * T], BF16)
    we = nc.alloc_sbuf_tensor("wes", [128, 8 * 1024], BF16)
    bef = A("befs", [128, 8])
    idf = A("idfs", [128, 128])
    kcol = A("kcols", [128, 1]); gk = A("gks", [128, 16]); kcolb = A("kcolbs", [128, 16])
    gkrow = A("gkrows", [1, T]); gkB = A("gkBs", [128, T])
    x1r = A("x1rs", [128, 2 * T])
    sil = nc.alloc_sbuf_tensor("sils", [128, 512], BF16)
    t2 = A("t2s", [128, 512])
    p.dma('g', gath[:], a_gath)
    p.dma('g', maskc[:], a_mask)
    p.dma('g', ohT[:], a_oh)
    p.dma('g', rkT[:], a_rk)
    p.dma('g', gate_tm[:], a_gate)
    p.dma('g', xn1[:], rearr_fm(a_xn1))
    p.dma('g', we[:], rearr_fm(a_we))
    p.dma('g', bef[:], a_be)
    p.dma('g', idf[:], a_id)
    p.dma('g', s1T[:], rearr_fm(a_s1))
    p.wait('t', 'dg'); p.wait('v', 'dg')
    pe_tok = p.emit('t', lambda e: e.matmul(k.pb[0][0:8, 0:1], gath[0:8, :], maskc[0:8, 0:1],
                                            start=True, stop=True), inc='pe')
    p.wait('v', 'pe', pe_tok)
    def kdve(e):
        e.tensor_copy(offs[:], k.pb[0][0:8, 0:1])
        e.tensor_copy(keptT[0:8, :], rkT[0:8, :])
        e.tensor_copy(keptT[0:8, :], rkT[0:8, :])
        e.tensor_scalar(keptT[0:8, :], rkT[0:8, :], offs[0:8, 0:1], -1.0, ALU.add, ALU.mult)
        e.tensor_scalar(keptT[0:8, :], keptT[0:8, :], 100.0, 0.0, ALU.add, ALU.max)
        e.tensor_scalar(keptT[0:8, :], keptT[0:8, :], 1.0, 0.0, ALU.min, ALU.add)
        return e.tensor_mul(keptT[0:8, :], keptT[0:8, :], ohT[0:8, :])
    dv_tok = p.emit('v', kdve, inc='dv')
    p.wait('t', 'dv', dv_tok)
    for tc in range(16):
        def tpk(e, tc=tc):
            e.transpose(k.pb[1][0:128, 0:8], keptT[0:8, tc * 128:(tc + 1) * 128], idf[0:8, 0:8])
            return e.transpose(k.pb[5][0:128, 0:8], keptT[0:8, tc * 128:(tc + 1) * 128], idf[0:8, 0:8])
        pe_tok = p.emit('t', tpk, inc='pe')
        p.wait('v', 'pe', pe_tok)
        dv_tok = p.emit('v', lambda e, tc=tc: e.reduce_sum(kcolb[:, tc:tc + 1],
                                                           k.pb[1][0:128, 0:8], AX.X), inc='dv')
        p.wait('t', 'dv', dv_tok)
    dv_tok = p.emit('v', lambda e: e.tensor_mul(gk[:], kcolb[:], gate_tm[:]), inc='dv')
    p.wait('t', 'dv', dv_tok)
    for tc in range(16):
        def tpg(e, tc=tc):
            e.transpose(k.pb[1][0:1, 0:128], gk[:, tc:tc + 1], idf[:, :])
            return e.transpose(k.pb[5][0:1, 0:128], gk[:, tc:tc + 1], idf[:, :])
        pe_tok = p.emit('t', tpg, inc='pe')
        p.wait('v', 'pe', pe_tok)
        dv_tok = p.emit('v', lambda e, tc=tc: e.tensor_copy(
            gkrow[0:1, tc * 128:(tc + 1) * 128], k.pb[1][0:1, 0:128]), inc='dv')
        p.wait('t', 'dv', dv_tok)
    for ts in range(4):
        pe_tok = p.emit('t', lambda e, ts=ts: e.matmul(
            k.pb[2][:, 0:512], k.ones_f1[0:1, :], gkrow[0:1, ts * 512:(ts + 1) * 512],
            start=True, stop=True), inc='pe')
        p.wait('v', 'pe', pe_tok)
        dv_tok = p.emit('v', lambda e, ts=ts: e.tensor_copy(
            gkB[:, ts * 512:(ts + 1) * 512], k.pb[2][:, 0:512]), inc='dv')
        p.wait('t', 'dv', dv_tok)
    for cch in range(8):
        p.wait('y', 'dv', p.tok('dv'))
        x1tok = p.dma('y', x1r[:, (cch % 2) * T:(cch % 2 + 1) * T], rearr_fm(a_x1)[:, cch:cch + 1, :])
        for ts in range(4):
            bank = k.pb[3 + (ts % 2)]
            def mms(e, cch=cch, ts=ts, bank=bank):
                for dc in range(8):
                    i = e.matmul(bank[:, 0:512],
                                 we[:, dc * 1024 + cch * 128: dc * 1024 + cch * 128 + 128],
                                 xn1[:, dc * T + ts * 512: dc * T + ts * 512 + 512],
                                 start=(dc == 0), stop=(dc == 7))
                return i
            pe_tok = p.emit('t', mms, inc='pe')
            p.wait('v', 'pe', pe_tok)
            sl = slice(cch * T + ts * 512, cch * T + ts * 512 + 512)
            def md1(e, cch=cch, ts=ts, bank=bank, sl=sl):
                return e.scalar_tensor_tensor(k.tscr[:, 0:512], bank[:, 0:512],
                                              bef[:, cch:cch + 1],
                                              gkB[:, ts * 512:(ts + 1) * 512],
                                              ALU.add, ALU.mult)
            dv_tok = p.emit('v', md1, inc='dv')
            p.wait('s', 'dv', dv_tok)
            ac_tok = p.emit('s', lambda e: e.activation(sil[:], k.tscr[:, 0:512], AF.Silu), inc='ac')
            p.wait('v', 'ac', ac_tok)
            p.wait('v', 'dy', x1tok)
            def md2(e, cch=cch, ts=ts, sl=sl):
                e.tensor_mul(t2[:], sil[:], s1T[:, sl])
                return e.tensor_add(t2[:], t2[:],
                                    x1r[:, (cch % 2) * T + ts * 512: (cch % 2) * T + ts * 512 + 512])
            dv_tok = p.emit('v', md2, inc='dv')
            p.wait('y', 'dv', dv_tok)
            p.dma('y', rearr_fm(o_out)[:, cch:cch + 1, ts * 512:(ts + 1) * 512], t2[:])
            p.wait('t', 'dy')
    p.wait('y', 'dy')
    p.finalize()
    return nc



def _hostfm_rows(arr):
    return np.ascontiguousarray(arr)

def _bf(a):
    import ml_dtypes
    return np.asarray(a).astype(ml_dtypes.bfloat16)

def _run(nc, in_maps):
    res = run_bass_kernel_spmd(nc, in_maps, core_ids=list(range(8)))
    return res.results

_USE_FUSED = True

def _posw():
    w = np.zeros((128, 32), np.float32)
    w[:, 0:8] = (8.0 - np.arange(8, dtype=np.float32))[None, :]
    return w

def kernel(**inp):
    if _USE_FUSED:
        return kernel_fused(inp)
    import ml_dtypes
    TB = make_tables()
    perm = make_perm()
    gm = np.zeros(1024, dtype=np.int64)
    for h in range(H):
        g, hf = h // 2, h % 2
        cr, qq = h // 4, h % 4
        for x1 in range(2):
            for i in range(32):
                gm[128 * g + 64 * hf + 32 * x1 + i] = (cr + 4 * x1) * 128 + 32 * qq + i
    x = np.ascontiguousarray(inp['L0_attn_x'].astype(np.float32))
    cores = [(c // 2, c % 2) for c in range(8)]
    sls = [slice(2048 * hf, 2048 * hf + 2048) for (_, hf) in cores]
    xT = [np.ascontiguousarray(x[b, s_].T) for (b, _), s_ in zip(cores, sls)]

    r = _run(build_l1(), [{"xT": xT[c]} for c in range(8)])
    xnT = [np.asarray(r[c]["oxn"]) for c in range(8)]

    WQpT = np.ascontiguousarray(inp['L0_attn_WQ'].reshape(D, D)[perm].T)
    WKpT = np.ascontiguousarray(inp['L0_attn_WK'].reshape(D, D)[perm].T)
    WVT = np.ascontiguousarray(inp['L0_attn_WV'].reshape(D, D).T)
    ins = []
    for c in range(8):
        s_ = sls[c]
        ins.append({"xnT": xnT[c], "wq": WQpT, "wk": WKpT, "wv": WVT,
                    "cosT": np.ascontiguousarray(TB['COS_T'][:, s_]),
                    "sinT": np.ascontiguousarray(TB['SIN_T'][:, s_]),
                    "cosB": np.ascontiguousarray(TB['COS_BK'][s_]),
                    "sinB": np.ascontiguousarray(TB['SIN_BK'][s_])})
    r = _run(build_l2(), ins)
    q_rot = [np.asarray(r[c]["o_q"]) for c in range(8)]
    k_rot = [np.asarray(r[c]["o_k"]) for c in range(8)]
    ktok = [np.asarray(r[c]["o_kt"]) for c in range(8)]
    vtm = [np.asarray(r[c]["o_v"]) for c in range(8)]

    r = _run(build_l3a(), [{"kt": ktok[c], "v": vtm[c], "sdec": TB['SDECG']}
                           for c in range(8)])
    states = [np.asarray(r[c]["o_st"]) for c in range(8)]

    o2 = np.zeros((128, 2), ml_dtypes.bfloat16); o2[0:64, 0] = 1; o2[64:128, 1] = 1
    E2h = np.zeros((2, 128), np.float32); E2h[0, 0:64] = 1; E2h[1, 64:128] = 1
    gamT = inp['L0_attn_gamma'].reshape(8, 128).T.copy()
    betT = inp['L0_attn_beta'].reshape(8, 128).T.copy()
    ins = []
    for c in range(8):
        b, hf = cores[c]
        stin = states[2 * b] if hf == 1 else np.zeros((128, 1024), np.float32)
        ins.append({"qg": q_rot[c][gm], "kg": k_rot[c][gm],
                    "kt": ktok[c], "v": vtm[c],
                    "dtc": TB['DTcat'], "qdec": TB['QDECG'].astype(np.float32),
                    "sdec": TB['SDECG'], "stin": np.ascontiguousarray(stin),
                    "gam": gamT, "bet": betT, "wo": inp['L0_attn_WO'],
                    "ones2": o2, "E2": E2h})
    r = _run(build_l3b(), ins)
    sT = [np.asarray(r[c]["o_s"]) for c in range(8)]

    ins = [{"xnT": xnT[c], "q1": inp['L0_ffn_fc1_q'], "sc1": inp['L0_ffn_fc1_scales'],
            "b1": inp['L0_ffn_fc1_bias'].reshape(32, 128).T.copy(),
            "lg": inp['L0_ffn_ln_gamma'].reshape(8, 128).T.copy(),
            "lb": inp['L0_ffn_ln_beta'].reshape(8, 128).T.copy()} for c in range(8)]
    r = _run(build_l4(), ins)
    y1 = [np.asarray(r[c]["o_y1"]) for c in range(8)]
    ins = [{"xnT": xnT[c], "y1": y1[c], "q2": inp['L0_ffn_fc2_q'],
            "sc2": inp['L0_ffn_fc2_scales'],
            "b2": inp['L0_ffn_fc2_bias'].reshape(8, 128).T.copy()} for c in range(8)]
    r = _run(build_l5(), ins)
    gT = [np.asarray(r[c]["o_g"]) for c in range(8)]

    ins = [{"xT": xT[c], "sT": sT[c], "gT": gT[c]} for c in range(8)]
    r = _run(build_l6(), ins)
    x1T = [np.asarray(r[c]["o_x1"]) for c in range(8)]
    xn1T = [np.asarray(r[c]["o_xn1"]) for c in range(8)]
    mrow = [np.asarray(r[c]["o_m"]) for c in range(8)]
    rrow = [np.asarray(r[c]["o_r"]) for c in range(8)]

    keysT = np.ascontiguousarray(inp['L1_attn_keys'][0].T.astype(np.float32))
    valsT = np.ascontiguousarray(inp['L1_attn_values'][0].T.astype(np.float32))
    ins = []
    for c in range(8):
        hh = cores[c][1]
        ins.append({"keysT": keysT, "valsT": valsT,
                    "wkT": np.ascontiguousarray(inp['L1_attn_Wk_w'][512 * hh:512 * (hh + 1)].T),
                    "wvT": np.ascontiguousarray(inp['L1_attn_Wv_w'][512 * hh:512 * (hh + 1)].T),
                    "bk": inp['L1_attn_Wk_b'][512 * hh:512 * (hh + 1)].reshape(4, 128).T.copy(),
                    "bv": inp['L1_attn_Wv_b'][512 * hh:512 * (hh + 1)].reshape(1, 512)})
    r = _run(build_l7(), ins)
    kTl = [np.asarray(r[c]["o_kT"]) for c in range(8)]
    v65l = [np.asarray(r[c]["o_v65"]) for c in range(8)]

    idb = np.eye(128, dtype=ml_dtypes.bfloat16)
    ins = []
    for c in range(8):
        b, hh = cores[c]
        full = np.concatenate([xn1T[2 * b], xn1T[2 * b + 1]], axis=1)
        ins.append({"xn1T": np.ascontiguousarray(full), "xn1own": xn1T[c],
                    "wqT": np.ascontiguousarray(inp['L1_attn_Wq_w'][512 * hh:512 * (hh + 1)].T),
                    "bq": inp['L1_attn_Wq_b'][512 * hh:512 * (hh + 1)].reshape(4, 128).T.copy(),
                    "kT": kTl[c], "v65": v65l[c], "ident": idb})
    r = _run(build_l8(), ins)
    s1T = [np.asarray(r[c]["o_s1"]) for c in range(8)]

    idf = np.eye(128, dtype=np.float32)
    negb = np.zeros((128, 32), np.float32); negb[:, 8:] = -1e30
    ins = [{"x1T": x1T[c], "wg": inp['L1_ffn_w_gate'].astype(np.float32),
            "mrow": mrow[c], "r_tm": np.ascontiguousarray(rrow[c].reshape(16, 128).T),
            "negbig": negb, "posw": _posw(), "ident": idf} for c in range(8)]
    r = _run(build_l9(), ins)
    ohT = [np.asarray(r[c]["o_oh"]) for c in range(8)]
    rkT = [np.asarray(r[c]["o_rk"]) for c in range(8)]
    gatec = [np.asarray(r[c]["o_gate"]) for c in range(8)]
    hists = np.stack([np.asarray(r[c]["o_hist"]).ravel() for c in range(8)])

    ins = []
    for c in range(8):
        maskc = np.zeros((8, 1), np.float32); maskc[0:c, 0] = 1.0
        ins.append({"gath": hists.astype(np.float32), "maskc": maskc,
                    "ohT": ohT[c], "rkT": rkT[c], "gate": gatec[c],
                    "xn1T": xn1T[c], "we": inp['L1_ffn_w_expert'].astype(np.float32),
                    "be": inp['L1_ffn_b_expert'].reshape(8, 128).T.copy(),
                    "s1T": s1T[c], "x1T": x1T[c], "ident": idf})
    r = _run(build_l10(), ins)
    out = np.zeros((4, 4096, 1024), np.float32)
    for c in range(8):
        b, hf = cores[c]
        out[b, sls[c]] = np.asarray(r[c]["o_out"]).T
    return out


def _barrier(p):
    for e in _ENG_ATTR:
        for s_ in ('pe', 'dv', 'ac', 'gp', 'dy', 'dg', 'ds'):
            p.wait(e, s_)


def build_fused():
    nc = bass.Bass("TRN2", target_bir_lowering=False, detect_race_conditions=False)
    din = lambda name, shape, dt=F32: nc.dram_tensor(name, shape, dt, kind="ExternalInput").ap()
    a_xT = din("xT", [D, T])
    a_wq0 = din("wq0", [D, D]); a_wk0 = din("wk0", [D, D]); a_wv0 = din("wv0", [D, D])
    a_cosT = din("cosT", [512, T]); a_sinT = din("sinT", [512, T])
    a_cosB = din("cosB", [T, 512]); a_sinB = din("sinB", [T, 512])
    a_dt = din("dtc", [128, 2048]); a_qdec = din("qdec", [128, 1024]); a_sdec = din("sdec", [128, 8])
    a_stmask = din("stmask", [128, 8])
    a_gam = din("gam", [128, 8]); a_bet = din("bet", [128, 8])
    a_wo = din("wo", [D, D])
    a_ones2 = din("ones2", [128, 2], BF16); a_E2 = din("E2", [2, 128])
    a_q1 = din("q1", [D, 4096], I32); a_sc1 = din("sc1", [8, 4096]); a_b1 = din("b1", [128, 32])
    a_lg = din("lg", [128, 8]); a_lb = din("lb", [128, 8])
    a_q2 = din("q2", [4096, 1024], I32); a_sc2 = din("sc2", [32, 1024]); a_b2 = din("b2", [128, 8])
    a_keysT = din("keysT", [D, S]); a_valsT = din("valsT", [D, S])
    a_wk1 = din("wk1", [D, 512]); a_wv1 = din("wv1", [D, 512])
    a_bk1 = din("bk1", [128, 4]); a_bv1 = din("bv1", [1, 512])
    a_wq1 = din("wq1", [D, 512]); a_bq1 = din("bq1", [128, 4])
    a_idb = din("identb", [128, 128], BF16); a_idf = din("identf", [128, 128])
    a_wg = din("wg", [D, 8]); a_wgT = din("wgT", [8, D])
    a_negb = din("negbig", [128, 32]); a_posw = din("posw", [128, 32])
    a_maskc = din("maskc", [8, 1])
    a_we = din("we", [D, D]); a_be = din("be", [128, 8])
    o_out = nc.dram_tensor("o_out", [D, T], F32, kind="ExternalOutput").ap()
    d_xn = nc.dram_tensor("d_xn", [D, T], BF16).ap()
    d_q = nc.dram_tensor("d_q", [D, T], BF16).ap()
    d_k = nc.dram_tensor("d_k", [D, T], BF16).ap()
    d_s = nc.dram_tensor("d_s", [D, T], BF16).ap()
    d_y1 = nc.dram_tensor("d_y1", [4096, T], BF16).ap()
    d_x1 = nc.dram_tensor("d_x1", [D, T], F32).ap()
    d_kT = nc.dram_tensor("d_kT", [512, S], BF16).ap()
    d_v65 = nc.dram_tensor("d_v65", [S, 528], BF16).ap()
    cc_st_in = nc.dram_tensor("cc_st_in", [128, 1024], F32)
    cc_st_out = nc.dram_tensor("cc_st_out", [1024, 1024], F32)
    cc_x_in = nc.dram_tensor("cc_x_in", [D, T], BF16)
    cc_x_out = nc.dram_tensor("cc_x_out", [2 * D, T], BF16)
    cc_h_in = nc.dram_tensor("cc_h_in", [1, 8], F32)
    cc_h_out = nc.dram_tensor("cc_h_out", [8, 8], F32)

    p = P(nc)
    psc0 = nc.alloc_psum_tensor("psc0", [128, 1024], F32)
    psc1 = nc.alloc_psum_tensor("psc1", [128, 1024], F32)
    avdt = nc.alloc_psum_tensor("avdt", [128, 1024], F32)
    pbh = nc.alloc_psum_tensor("pbhx", [128, 1024], BF16)
    pb7 = nc.alloc_psum_tensor("pb7", [128, 512], F32)
    k = type('K', (), {})()
    k.nc, k.p = nc, p
    k.pb = [psc0[:, 0:512], psc0[:, 512:1024], psc1[:, 0:512], psc1[:, 512:1024],
            avdt[:, 0:512], avdt[:, 512:1024], pb7[:, 0:512]]
    k.pbh = pbh
    SA = nc.alloc_sbuf_tensor("SA", [128, 16384], BF16)
    SB = nc.alloc_sbuf_tensor("SB", [128, 16384], BF16)
    SC = nc.alloc_sbuf_tensor("SC", [128, 16384], BF16)
    SD = nc.alloc_sbuf_tensor("SD", [128, 16896], BF16)
    F8 = nc.alloc_sbuf_tensor("F8", [128, 32768], BF16)
    H = nc.alloc_sbuf_tensor("H", [128, 1024], F32)
    SBf = SB.bitcast(F32); SDf = SD.bitcast(F32); SCf = SC.bitcast(F32)
    F8f = F8.bitcast(F32)
    k.ones_bf = nc.alloc_sbuf_tensor("ones_bf", [128, 1], BF16)
    k.ones_f1 = nc.alloc_sbuf_tensor("ones_f1", [1, 128], F32)
    k.stat = F8f[0:1, 0:2048]
    k.tscr = H[:, 0:512]
    p.emit('v', lambda e: e.memset(k.ones_bf[:], 1.0))
    p.emit('v', lambda e: e.memset(k.ones_f1[:], 1.0), inc='dv')
    sm_ = lambda nm, shape, dt=F32: nc.alloc_sbuf_tensor(nm, shape, dt)
    gamv = sm_("gamv", [128, 8]); betv = sm_("betv", [128, 8])
    ones2 = sm_("ones2v", [128, 2], BF16); E2 = sm_("E2v", [2, 128])
    sdec = sm_("sdecv", [128, 8]); stmask = sm_("stmaskv", [128, 8])
    b1f = sm_("b1fv", [128, 32]); lgv = sm_("lgv", [128, 8]); lbv = sm_("lbv", [128, 8])
    b2h = sm_("b2hv", [128, 8])
    bkf = sm_("bkfv", [128, 4])
    bqf = sm_("bqfv", [128, 4])
    idb = sm_("idbv", [128, 128], BF16); idf = sm_("idfv", [128, 128])
    m_tm = sm_("m_tmv", [128, 16]); r_tm = sm_("r_tmv", [128, 16])
    wgfm = sm_("wgfmv", [128, 64]); wsumc = sm_("wsumcv", [8, 32])
    wsrow = sm_("wsrowv", [1, 32]); wsB = sm_("wsBv", [128, 8])
    negb = SBf[:, 0:32]; posw = SBf[:, 32:64]
    one30 = SBf[:, 64:96]; zz8 = SBf[:, 96:128]; big30 = SBf[:, 128:160]
    lgs = SBf[:, 160:192]; el = SBf[:, 192:224]
    mnn = SBf[:, 224:256]; pmn = SBf[:, 256:288]
    t1g = SBf[:, 288:320]; t2g = SBf[:, 320:352]; t3g = SBf[:, 352:384]
    oh0 = SBf[:, 384:416]; ohv = SBf[:, 416:448]
    q8 = SBf[:, 448:480]; nm2 = SBf[:, 480:512]
    smbuf = sm_("smbufv", [128, 16]); gate_tm = sm_("gatetmv", [128, 16])
    kcolb = sm_("kcolbv", [128, 16]); gk = sm_("gkv", [128, 16])
    hrow = sm_("hrowv", [1, 32]); gath = sm_("gathv", [8, 8]); maskc = sm_("maskcv", [8, 1])
    offs = sm_("offsv", [8, 1]); bef = sm_("befv", [128, 8])
    rec = sm_("recv", [128, 8])
    silb = SD[:, 0:512]
    t2b = SDf[:, 512:1024]

    for ap_, sb in [(a_gam, gamv), (a_bet, betv), (a_ones2, ones2), (a_E2, E2),
                    (a_sdec, sdec), (a_stmask, stmask), (a_b1, b1f), (a_lg, lgv),
                    (a_lb, lbv), (a_b2, b2h), (a_bk1, bkf), (a_bq1, bqf),
                    (a_idb, idb), (a_idf, idf),
                    (a_maskc, maskc), (a_be, bef)]:
        p.dma('g', sb[:], ap_)
    p.dma('g', wgfm[:], rearr_fm(a_wg))
    _barrier(p)

    NS = 512
    xb = SA[:, 0:16384]; xsq = SB[:, 0:16384]; xn = SC[:, 0:16384]
    k.stat = F8f[0:1, 0:2048]
    p.dma('g', xb, rearr_fm(a_xT))
    p.wait('v', 'dg'); p.wait('t', 'dg'); p.wait('s', 'dg')
    emit_ln_fm(k, xb, xsq, xn)
    p.wait('y', 'dv')
    p.dma('y', rearr_fm(d_xn), xn)
    _barrier(p)

    ws = F8[:, 0:24576]
    wq0, wk0, wv0 = ws[:, 0:8192], ws[:, 8192:16384], ws[:, 16384:24576]
    cosT = SBf[:, 0:8192]; sinT = SDf[:, 0:8192]
    raw = SA[:, 0:16384]
    scrA, scrB, scrC = F8f[:, 12288:12800], H[:, 0:512], H[:, 512:1024]
    p.dma('g', wq0, rearr_fm(a_wq0))
    p.dma('g', wk0, rearr_fm(a_wk0))
    p.dma('g', wv0, rearr_fm(a_wv0))
    p.dma('g', cosT, rearr_fm(a_cosT))
    p.dma('g', sinT, rearr_fm(a_sinT))
    p.wait('t', 'dg'); p.wait('v', 'dg'); p.wait('s', 'dg')

    def rot_and_store(src_fm, out_hbm):
        for c in range(4):
            for ts in range(T // 512):
                lo = src_fm[:, c * T + ts * 512: c * T + ts * 512 + 512]
                hi = src_fm[:, (c + 4) * T + ts * 512: (c + 4) * T + ts * 512 + 512]
                cs = cosT[:, c * T + ts * 512: c * T + ts * 512 + 512]
                sn = sinT[:, c * T + ts * 512: c * T + ts * 512 + 512]
                def rot(e, lo=lo, hi=hi, cs=cs, sn=sn):
                    e.tensor_mul(scrA[:], lo, cs)
                    e.tensor_mul(scrC[:], lo, sn)
                    e.tensor_mul(scrB[:], hi, sn)
                    e.tensor_sub(lo, scrA[:], scrB[:])
                    e.tensor_mul(scrB[:], hi, cs)
                    return e.tensor_add(hi, scrC[:], scrB[:])
                p.emit('v', rot, inc='dv')
        p.wait('y', 'dv')
        p.dma('y', rearr_fm(out_hbm), src_fm[:, 0:16384])

    emit_proj_fm(k, wq0, xn, raw)
    p.wait('v', 'ac')
    rot_and_store(raw, d_q)
    p.wait('s', 'dy')
    emit_proj_fm(k, wk0, xn, raw)
    p.wait('v', 'ac')
    rot_and_store(raw, d_k)
    p.wait('g', 'dv')
    p.dma('g', cosT, rearr_tm(a_cosB))
    p.dma('g', sinT, rearr_tm(a_sinB))
    p.wait('s', 'dy')
    emit_proj_tm(k, wk0, xn, raw)
    p.wait('v', 'ac'); p.wait('v', 'dg')
    ktok = F8[:, 0:16384]
    emit_rot_tm(k, raw, ktok, cosT, sinT, scrA, scrB, scrC)
    _barrier(p)
    Vtm = SA[:, 0:16384]
    emit_proj_tm(k, wv0, xn, Vtm)
    _barrier(p)

    Sf32 = F8f[:, 8192:9216]
    p.emit('v', lambda e: e.memset(Sf32[:], 0.0), inc='dv')
    p.wait('t', 'dv'); p.wait('v', 'dg')
    ds_ = k.pb[0]
    for c in range(16):
        for g in range(8):
            vcol = slice(c * 1024 + g * 128, c * 1024 + g * 128 + 128)
            pe_tok = p.emit('t', lambda e, vcol=vcol: e.matmul(
                ds_[:, 0:128], ktok[:, vcol], Vtm[:, vcol], start=True, stop=True), inc='pe')
            p.wait('v', 'pe', pe_tok)
            def upd(e, g=g):
                e.scalar_tensor_tensor(Sf32[0:64, g * 128:g * 128 + 64],
                                       Sf32[0:64, g * 128:g * 128 + 64],
                                       sdec[0:64, g:g + 1], ds_[0:64, 0:64],
                                       ALU.mult, ALU.add)
                return e.scalar_tensor_tensor(Sf32[64:128, g * 128 + 64:g * 128 + 128],
                                              Sf32[64:128, g * 128 + 64:g * 128 + 128],
                                              sdec[64:128, g:g + 1], ds_[64:128, 64:128],
                                              ALU.mult, ALU.add)
            dv_tok = p.emit('v', upd, inc='dv')
            p.wait('t', 'dv', dv_tok)
    p.wait('g', 'dv')
    p.dma('g', cc_st_in[:, :], Sf32[:])
    p.wait('g', 'dg')
    p.emit('g', lambda e: e.collective_compute(
        "AllGather", ALU.bypass, replica_groups=[[0, 1, 2, 3, 4, 5, 6, 7]],
        ins=[cc_st_in.ap().opt()], outs=[cc_st_out.ap().opt()]), inc='gp')
    p.wait('g', 'gp')
    p.wait('v', 'dg')
    p.wait('v', 'gp')
    p.emit('v', lambda e: e.memset(Sf32[:], 0.0), inc='dv')
    stmp = H[:, 0:1024]
    for r in range(8):
        p.wait('g', 'dv')
        rt_ = p.dma('g', stmp, cc_st_out[128 * r:128 * r + 128, :])
        p.wait('v', 'dg', rt_)
        p.emit('v', lambda e, r=r: e.scalar_tensor_tensor(
            Sf32[:], stmp, stmask[:, r:r + 1], Sf32[:], ALU.mult, ALU.add), inc='dv')
    _barrier(p)

    Qg = SB[:, 0:16384]; Kg = SD[:, 0:16384]; RTg = SC[:, 0:16384]
    for g in range(8):
        for hf in range(2):
            h = 2 * g + hf
            cr, qq = h // 4, h % 4
            for x1 in range(2):
                srows = slice((cr + 4 * x1) * 128 + 32 * qq, (cr + 4 * x1) * 128 + 32 * qq + 32)
                drows = slice(64 * hf + 32 * x1, 64 * hf + 32 * x1 + 32)
                p.dma('y', Qg[drows, g * T:(g + 1) * T], d_q[srows, :])
                p.dma('y', Kg[drows, g * T:(g + 1) * T], d_k[srows, :])
    dtc = F8f[:, 9728:11776]; qdec = F8f[:, 11776:12800]
    S2b = F8[:, 36864 // 2:36864 // 2 + 1024]
    k.atA = F8[:, 25600:25728]; k.atB = F8[:, 25728:25856]
    k.atA2 = F8[:, 25856:25984]; k.atB2 = F8[:, 25984:26112]
    qfull = F8[:, 26624:27648]
    p.dma('g', dtc, a_dt)
    p.dma('g', qdec, a_qdec)
    p.emit('v', lambda e: e.memset(S2b[:], 0.0), inc='dv')
    st2 = F8f[0:2, 13312:15360]
    k.st2m = st2[0:2, 0:512]; k.st2v = st2[0:2, 512:1024]
    k.st2t = st2[0:2, 1024:1536]; k.st2r = st2[0:2, 1536:2048]
    _barrier(p)
    emit_retention2(k, Qg, Kg, ktok, Vtm, dtc, qdec, sdec, Sf32, S2b, RTg, qfull)
    wo = F8[:, 0:8192]
    p.wait('g', 'pe')
    p.dma('g', wo, rearr_fm(a_wo))
    emit_gnorm(k, RTg, RTg, ones2, E2, gamv, betv, silb[:, 0:512])
    p.wait('t', 'dv'); p.wait('t', 'dg')
    emit_proj_fm(k, wo, RTg, Qg)
    p.wait('y', 'ac')
    p.dma('y', rearr_fm(d_s), Qg)
    _barrier(p)

    xnA = SA[:, 0:16384]; xsq4 = SB[:, 0:16384]
    w1 = F8[:, 0:32768]
    scaleB1 = SCf[:, 0:4096]
    y1T = SD[:, 0:16384]
    k.stat = SCf[0:1, 4096:6144]
    p.dma('g', xnA, rearr_fm(d_xn))
    emit_dequant(k, a_q1, w1, a_sc1, scaleB1, nk=8, nn=4096)
    p.wait('t', 'dg'); p.wait('v', 'dg'); p.wait('s', 'dg')
    emit_ln_fm(k, xnA, xsq4, xnA, gamma_ap=lgv, beta_ap=lbv)
    p.wait('t', 'dv')
    for tc in range(4):
        emit_ffn_fc1(k, w1, xnA, b1f, y1T, tc=tc)
        p.wait('y', 'ac')
        p.dma('y', d_y1.rearrange("(c p) n -> p c n", p=128)[:, :, 512 * tc:512 * (tc + 1)], y1T)
        p.wait('s', 'dy')
    _barrier(p)

    p.dma('g', xnA, rearr_fm(d_xn))
    w2 = F8[:, 0:32768]
    scaleB2 = SCf[:, 0:1024]
    gT = SB[:, 0:16384]
    emit_dequant(k, a_q2, w2, a_sc2, scaleB2, nk=32, nn=1024)
    p.wait('v', 'dg')
    p.emit('v', lambda e: e.tensor_scalar_mul(b2h[:], b2h[:], 0.5), inc='dv')
    p.wait('t', 'dg'); p.wait('s', 'dv'); p.wait('t', 'dv')
    for tc in range(4):
        p.wait('g', 'pe')
        p.dma('g', y1T, d_y1.rearrange("(c p) n -> p c n", p=128)[:, :, 512 * tc:512 * (tc + 1)])
        p.wait('t', 'dg')
        emit_ffn_fc2(k, w2, y1T, b2h, xnA, gT, tc=tc)
    _barrier(p)

    sT6 = SA[:, 0:16384]
    x1b = SD[:, 0:16384]; xn1 = SC[:, 0:16384]
    xr0 = F8f[:, 8192:10240]; xr1 = F8f[:, 10240:12288]
    xf0 = F8f[:, 0:2048]; xf1 = F8f[:, 2048:4096]
    msb = F8f[0:1, 4096:6144]; rsb = F8f[0:1, 6144:8192]
    k.stat = SBf[0:1, 0:2048]
    p.dma('g', sT6, rearr_fm(d_s))
    p.wait('v', 'dg')
    for c in range(8):
        xr = (xr0, xr1)[c % 2]; xf = (xf0, xf1)[c % 2]
        p.wait('y', 'dv', p.tok('dv'))
        p.dma('y', xr, rearr_fm(a_xT)[:, c:c + 1, :])
        p.wait('v', 'dy')
        def comb(e, c=c, xr=xr, xf=xf):
            e.tensor_mul(xf, sT6[:, c * T:(c + 1) * T], gT[:, c * T:(c + 1) * T])
            e.tensor_add(xf, xf, xr)
            return e.tensor_copy(x1b[:, c * T:(c + 1) * T], xf)
        dv_tok = p.emit('v', comb, inc='dv')
        p.wait('y', 'dv', dv_tok)
        p.dma('y', rearr_fm(d_x1)[:, c:c + 1, :], xf)
    emit_ln_fm(k, x1b, sT6, xn1, m_out=msb, r_out=rsb)
    p.wait('t', 'dv')
    def tpstats(e):
        for tc in range(16):
            e.transpose(k.pb[0][0:128, tc:tc + 1], msb[0:1, 128 * tc:128 * tc + 128], idf[0:1, 0:1])
        for tc in range(16):
            e.transpose(k.pb[1][0:128, tc:tc + 1], rsb[0:1, 128 * tc:128 * tc + 128], idf[0:1, 0:1])
        return e.transpose(k.pb[2][0:128, 0:1], msb[0:1, 0:128], idf[0:1, 0:1])
    pe_tok = p.emit('t', tpstats, inc='pe')
    p.wait('v', 'pe', pe_tok)
    def evst(e):
        e.tensor_copy(m_tm[:], k.pb[0][:, 0:16])
        return e.tensor_copy(r_tm[:], k.pb[1][:, 0:16])
    p.emit('v', evst, inc='dv')
    p.wait('y', 'dv')
    p.dma('y', rearr_fm(cc_x_in.ap()), xn1)
    _barrier(p)

    kb = F8[:, 0:32768]
    bvB = H.bitcast(F32)[:, 512:1024] if False else H[:, 512:1024]
    p.dma('g', bvB[:], a_bv1.broadcast_to((128, 512)))
    wk1 = SA[:, 0:4096]; wv1 = SA[:, 4096:8192]
    kT7 = SB[:, 0:16384]
    v65 = SD[:, 0:16896]
    p.dma('g', kb, rearr_fm(a_keysT))
    p.dma('g', wk1, rearr_fm(a_wk1))
    p.dma('g', wv1, rearr_fm(a_wv1))
    p.wait('t', 'dg'); p.wait('v', 'dg')
    emit_proj_fm(k, wk1, kb, kT7, nchunk_out=4, tlen=S, evac='v', bias_ap=bkf, relu=True)
    p.wait('y', 'dv')
    p.dma('y', rearr_fm(d_kT), kT7)
    p.wait('g', 'pe')
    p.dma('g', kb, rearr_fm(a_valsT))
    p.emit('v', lambda e: e.memset(v65[:], 0.0))
    v65r = v65.rearrange("p (n h u) -> p n h u", n=32, h=8, u=66)
    p.emit('v', lambda e: e.memset(v65r[:, :, :, 64:65], 1.0), inc='dv')
    p.wait('t', 'dg'); p.wait('t', 'dv')
    for tt in range(32):
        bank = k.pb[tt % 2]
        def mms7(e, tt=tt, bank=bank):
            for dc in range(8):
                i = e.matmul(bank[:, 0:512], kb[:, dc * S + tt * 128: dc * S + tt * 128 + 128],
                             wv1[:, dc * 512:(dc + 1) * 512], start=(dc == 0), stop=(dc == 7))
            return i
        pe_tok = p.emit('t', mms7, inc='pe')
        p.wait('v', 'pe', pe_tok)
        def evac7(e, tt=tt, bank=bank):
            e.tensor_add(k.tscr[:, 0:512], bank[:, 0:512], bvB[:])
            ov = v65[:, tt * 528:(tt + 1) * 528].rearrange("p (h u) -> p h u", h=8, u=66)[:, :, 0:64]
            return e.tensor_scalar_max(ov, k.tscr[:, 0:512].rearrange("p (h u) -> p h u", h=8, u=64), 0.0)
        dv_tok = p.emit('v', evac7, inc='dv')
        p.wait('t', 'dv', dv_tok)
    p.wait('y', 'dv')
    p.dma('y', rearr_tm(d_v65), v65)
    p.emit('g', lambda e: e.collective_compute(
        "AllGather", ALU.bypass, replica_groups=[[0, 1], [2, 3], [4, 5], [6, 7]],
        ins=[cc_x_in.ap().opt()], outs=[cc_x_out.ap().opt()]), inc='gp')
    p.wait('g', 'gp')
    _barrier(p)

    xb8 = F8[:, 0:32768]
    qT = SA[:, 0:16384]; s1T = SB[:, 0:16384]
    kts = SD[:, 0:8192]; vs = SD[:, 8192:12416]
    eb = SD[:, 12416:14464]; pcop = SD[:, 14464:14992]
    avs = SD[0:65, 14992:16016]
    for cch in range(8):
        p.dma('g', xb8[:, cch * S: cch * S + T], cc_x_out.ap()[128 * cch:128 * cch + 128, :])
        p.dma('g', xb8[:, cch * S + T: (cch + 1) * S], cc_x_out.ap()[D + 128 * cch:D + 128 * cch + 128, :])
    wq1s = F8[:, 0:4096]
    wq1s = SA[:, 0:4096]
    p.dma('g', wq1s, rearr_fm(a_wq1))
    p.wait('t', 'dg'); p.wait('v', 'dg')
    qT = SB[:, 0:16384]; s1T = SA[:, 0:16384]
    for cch in range(4):
        for ts8 in range(S // NS):
            bank = k.pb[ts8 % 2]
            def mms8(e, cch=cch, ts8=ts8, bank=bank):
                for dc in range(8):
                    i = e.matmul(bank[:, 0:NS],
                                 wq1s[:, dc * 512 + cch * 128: dc * 512 + cch * 128 + 128],
                                 xb8[:, dc * S + ts8 * NS: dc * S + ts8 * NS + NS],
                                 start=(dc == 0), stop=(dc == 7))
                return i
            pe_tok = p.emit('t', mms8, inc='pe')
            p.wait('v', 'pe', pe_tok)
            osl = qT[:, cch * S + ts8 * NS: cch * S + ts8 * NS + NS]
            tk = p.emit('v', lambda e, bank=bank, osl=osl, cch=cch: e.tensor_scalar(
                osl, bank[:, 0:NS], bqf[:, cch:cch + 1], 0.0, ALU.add, ALU.max), inc='dv')
            p.wait('t', 'dv', tk)
    _barrier(p)
    psc = [psc0, psc1]
    eb6 = F8[:, 0:6144]
    LAG8 = 4
    kload_tok = {}
    vs_tok = {}
    exp_tok = [None, None]
    av_ring = [None] * 6
    pending_av = []
    pending_epi = []
    st8 = {'g': 0, 'avs_dv': None}
    s1v = s1T.rearrange("p (tc hh u f) -> p tc hh u f", tc=8, hh=8, u=64, f=4)
    xov = xn1.rearrange("p (tc hh u f) -> p tc hh u f", tc=8, hh=8, u=64, f=4)

    def _emit_epi(h, qb, dv_tok):
        p.wait('t', 'dv', dv_tok)
        def tps8(e):
            for i in range(8):
                e.transpose(pbh[:, 66 * i:66 * i + 65], avs[:, 128 * i:128 * i + 128],
                            idb[0:65, 0:65])
            return e.transpose(pbh[:, 700:765], avs[:, 0:128], idb[0:65, 0:65])
        pe_tok = p.emit('t', tps8, inc='pe')
        p.wait('v', 'pe', pe_tok)
        def evacp8(e):
            e.reciprocal(rec[:, 0:8], pbh[:, 64:528:66])
            return e.tensor_copy(pcop[:, 0:528], pbh[:, 0:528])
        dv2 = p.emit('v', evacp8, inc='dv')
        p.wait('t', 'dv', dv2)
        def perm8(e, h=h, qb=qb):
            for tc in range(8):
                osl = s1v[:, tc:tc + 1, h:h + 1, :, qb:qb + 1]
                isl = xov[:, tc:tc + 1, h:h + 1, :, qb:qb + 1]
                ii = e.scalar_tensor_tensor(
                    osl, pcop[:, 66 * tc:66 * tc + 64].rearrange(
                        "p (a u f) -> p a u f", a=1, u=64, f=1),
                    rec[:, tc:tc + 1], isl, ALU.mult, ALU.add)
            return ii
        p.emit('v', perm8, inc='dv')

    def _issue_av(it):
        sl, ac_tok, h_, qb_, kt_ = it
        if kt_ == 0:
            if st8['avs_dv'] is not None:
                p.wait('t', 'dv', st8['avs_dv'])
            p.wait('t', 'dg', vs_tok[h_])
        p.wait('t', 'ac', ac_tok)
        vsv_ = vs[:, (h_ % 2) * 2112:(h_ % 2 + 1) * 2112]
        ebl = (sl % 6) * 1024
        def av8(e, kt=kt_, vsv=vsv_, ebl=ebl):
            e.matmul(avdt[0:65, 0:512], vsv[:, kt * 66:kt * 66 + 65],
                     eb6[:, ebl:ebl + 512], start=(kt == 0), stop=(kt == 31))
            return e.matmul(avdt[0:65, 512:1024], vsv[:, kt * 66:kt * 66 + 65],
                            eb6[:, ebl + 512:ebl + 1024], start=(kt == 0), stop=(kt == 31))
        tokp = p.emit('t', av8, inc='pe')
        av_ring[sl % 6] = tokp
        if kt_ == 31:
            p.wait('v', 'pe', tokp)
            dvc = p.emit('v', lambda e: e.tensor_copy(avs[:, 0:1024], avdt[0:65, 0:1024]), inc='dv')
            st8['avs_dv'] = dvc
            pending_epi.append((st8['g'] + 3, h_, qb_, dvc))

    for h in range(8):
        pair = h // 2
        if h % 2 == 0:
            p.wait('g', 'pe')
            kload_tok[pair] = p.dma('g', kts[:, (pair % 2) * S:(pair % 2 + 1) * S],
                                    rearr_fm(d_kT)[:, pair:pair + 1, :])
        p.wait('g', 'pe')
        vs_tok[h] = p.dma('g', vs[:, (h % 2) * 2112:(h % 2 + 1) * 2112],
                          rearr_tm(d_v65)[:, :, 66 * h:66 * h + 66])
        krow = slice(64 * (h % 2), 64 * (h % 2) + 64)
        ktsv = kts[:, (pair % 2) * S:(pair % 2 + 1) * S]
        for qb in range(4):
            for kt in range(32):
                g8 = st8['g']
                while pending_epi and pending_epi[0][0] <= g8:
                    _, eh, eqb, etok = pending_epi.pop(0)
                    _emit_epi(eh, eqb, etok)
                if h % 2 == 0 and qb == 0 and kt == 0:
                    p.wait('t', 'dg', kload_tok[pair])
                if exp_tok[g8 % 2] is not None:
                    p.wait('t', 'ac', exp_tok[g8 % 2])
                def sc8_(e, kt=kt, g8=g8, krow=krow, ktsv=ktsv, qb=qb, h=h):
                    bank = psc[g8 % 2]
                    e.matmul(bank[:, 0:512], ktsv[krow, kt * 128:(kt + 1) * 128],
                             qT[krow, (h // 2) * S + qb * 1024: (h // 2) * S + qb * 1024 + 512],
                             start=True, stop=True)
                    return e.matmul(bank[:, 512:1024], ktsv[krow, kt * 128:(kt + 1) * 128],
                                    qT[krow, (h // 2) * S + qb * 1024 + 512: (h // 2) * S + qb * 1024 + 1024],
                                    start=True, stop=True)
                pe_tok = p.emit('t', sc8_, inc='pe')
                p.wait('s', 'pe', pe_tok)
                if av_ring[g8 % 6] is not None:
                    p.wait('s', 'pe', av_ring[g8 % 6])
                ac_tok = p.emit('s', lambda e, g8=g8: e.activation(
                    eb6[:, (g8 % 6) * 1024:(g8 % 6 + 1) * 1024], psc[g8 % 2][:], AF.Exp,
                    scale=0.125), inc='ac')
                exp_tok[g8 % 2] = ac_tok
                pending_av.append((g8, ac_tok, h, qb, kt))
                while pending_av and pending_av[0][0] <= g8 - LAG8:
                    _issue_av(pending_av.pop(0))
                st8['g'] = g8 + 1
    while pending_av:
        _issue_av(pending_av.pop(0))
    while pending_epi:
        _, eh, eqb, etok = pending_epi.pop(0)
        _emit_epi(eh, eqb, etok)
    _barrier(p)

    x1f9 = F8f[:, 0:16384]
    ohT = SDf[0:8, 0:2048]; scanT = SDf[0:8, 2048:4096]
    rkT = SDf[0:8, 4096:6144]; keptT = SDf[0:8, 6144:8192]
    wgT9 = H[0:8, 0:1024]
    p.dma('g', x1f9, rearr_fm(d_x1))
    p.dma('g', wgT9, a_wgT)
    p.dma('g', negb[:], a_negb)
    p.dma('g', posw[:], a_posw)
    p.wait('v', 'dg'); p.wait('t', 'dg')
    p.emit('v', lambda e: e.memset(one30[:], 1.0))
    p.emit('v', lambda e: e.memset(zz8[:], 0.0))
    p.emit('v', lambda e: e.memset(big30[:], 1e30), inc='dv')
    p.wait('t', 'dv'); p.wait('s', 'dv')
    dv_tok = p.emit('v', lambda e: e.reduce_sum(wsumc[0:8, 0:1], wgT9, AX.X), inc='dv')
    p.wait('t', 'dv', dv_tok)
    def tpws(e):
        e.transpose(k.pb[2][0:1, 0:8], wsumc[0:8, 0:1], idf[0:8, 0:8])
        return e.transpose(k.pb[3][0:1, 0:8], wsumc[0:8, 0:1], idf[0:8, 0:8])
    pe_tok = p.emit('t', tpws, inc='pe')
    p.wait('v', 'pe', pe_tok)
    dv_tok = p.emit('v', lambda e: e.tensor_copy(wsrow[0:1, 0:8], k.pb[2][0:1, 0:8]), inc='dv')
    p.wait('t', 'dv', dv_tok)
    pe_tok = p.emit('t', lambda e: e.matmul(k.pb[2][:, 0:8], k.ones_f1[0:1, :],
                                            wsrow[0:1, 0:8], start=True, stop=True), inc='pe')
    p.wait('v', 'pe', pe_tok)
    dv_tok = p.emit('v', lambda e: e.tensor_copy(wsB[:], k.pb[2][:, 0:8]), inc='dv')
    p.wait('t', 'dv', dv_tok)
    for tc in range(16):
        def mmr9(e, tc=tc):
            for dc in range(8):
                i = e.matmul(k.pb[0][:, 0:8], x1f9[:, dc * T + tc * 128: dc * T + tc * 128 + 128],
                             wgfm[:, dc * 8:(dc + 1) * 8], start=(dc == 0), stop=(dc == 7))
            return i
        pe_tok = p.emit('t', mmr9, inc='pe')
        p.wait('v', 'pe', pe_tok)
        def gdve9(e, tc=tc):
            e.scalar_tensor_tensor(t1g[:, 0:8], wsB[:], m_tm[:, tc:tc + 1],
                                   k.pb[0][:, 0:8], ALU.mult, ALU.subtract)
            e.tensor_copy(t2g[:, 0:8], t1g[:, 0:8])
            e.scalar_tensor_tensor(lgs[:, 0:8], t2g[:, 0:8], r_tm[:, tc:tc + 1],
                                   zz8[:, 0:8], ALU.mult, ALU.add)
            e.tensor_sub(lgs[:], lgs[:], negb[:])
            e.tensor_reduce(mnn[:, 0:1], lgs[:], AX.X, ALU.min)
            e.tensor_reduce(pmn[:, 0:1], lgs[:], AX.X, ALU.min, negate=True)
            e.tensor_copy(t2g[:], lgs[:])
            return e.tensor_copy(t3g[:], lgs[:])
        dv_tok = p.emit('v', gdve9, inc='dv')
        p.wait('s', 'dv', dv_tok)
        ac_tok = p.emit('s', lambda e: e.activation(el[:], lgs[:], AF.Exp, scale=-1.0,
                                                    bias=mnn[:, 0:1]), inc='ac')
        p.wait('v', 'ac', ac_tok)
        def gdve9b(e, tc=tc):
            e.reduce_sum(smbuf[:, tc:tc + 1], el[:], AX.X)
            e.scalar_tensor_tensor(t1g[:], lgs[:], pmn[:, 0:1], big30[:],
                                   ALU.add, ALU.mult)
            e.tensor_sub(t2g[:], one30[:], t1g[:])
            e.tensor_max(t3g[:], t2g[:], zz8[:])
            e.tensor_tensor(oh0[:], t3g[:], one30[:], ALU.min)
            e.tensor_mul(q8[:], oh0[:], posw[:])
            e.reduce_max(nm2[:, 0:1], q8[:], AX.X, negate=True)
            e.tensor_copy(t2g[:], q8[:])
            e.tensor_copy(t3g[:], q8[:])
            e.scalar_tensor_tensor(t1g[:], q8[:], nm2[:, 0:1], big30[:],
                                   ALU.add, ALU.mult)
            e.tensor_add(t2g[:], t1g[:], one30[:])
            e.tensor_max(t3g[:], t2g[:], zz8[:])
            return e.tensor_tensor(ohv[:], t3g[:], one30[:], ALU.min)
        dv_tok = p.emit('v', gdve9b, inc='dv')
        p.wait('t', 'dv', dv_tok)
        def tpo9(e, tc=tc):
            e.transpose(k.pb[2][0:32, 0:128], ohv[:], idf[:])
            return e.transpose(k.pb[4][0:32, 0:128], ohv[:], idf[:])
        pe_tok = p.emit('t', tpo9, inc='pe')
        p.wait('v', 'pe', pe_tok)
        dv_tok = p.emit('v', lambda e, tc=tc: e.tensor_copy(
            ohT[0:8, tc * 128:(tc + 1) * 128], k.pb[2][0:8, 0:128]), inc='dv')
        p.wait('t', 'dv', dv_tok)
    def scan9(e):
        e.reciprocal(gate_tm[:], smbuf[:])
        e.tensor_copy(t1g[:], oh0[:])
        e.tensor_tensor_scan(scanT[0:8, :], ohT[0:8, :], ohT[0:8, :], 0.0, ALU.add, ALU.bypass)
        return e.tensor_sub(rkT[0:8, :], scanT[0:8, :], ohT[0:8, :])
    dv_tok = p.emit('v', scan9, inc='dv')
    p.wait('t', 'dv', dv_tok)
    def tph(e):
        e.transpose(k.pb[2][0:1, 0:8], scanT[0:8, 2047:2048], idf[0:8, 0:8])
        return e.transpose(k.pb[4][0:1, 0:8], scanT[0:8, 2047:2048], idf[0:8, 0:8])
    pe_tok = p.emit('t', tph, inc='pe')
    p.wait('v', 'pe', pe_tok)
    dv_tok = p.emit('v', lambda e: e.tensor_copy(hrow[0:1, 0:8], k.pb[2][0:1, 0:8]), inc='dv')
    p.wait('y', 'dv', dv_tok)
    p.dma('y', cc_h_in.ap(), hrow[0:1, 0:8])
    p.wait('g', 'dy')
    p.emit('g', lambda e: e.collective_compute(
        "AllGather", ALU.bypass, replica_groups=[[0, 1, 2, 3, 4, 5, 6, 7]],
        ins=[cc_h_in.ap().opt()], outs=[cc_h_out.ap().opt()]), inc='gp')
    p.wait('g', 'gp')
    p.dma('g', gath[:], cc_h_out.ap())
    _barrier(p)

    we = SD[:, 8192:16384]
    pe_tok = p.emit('t', lambda e: e.matmul(k.pb[0][0:8, 0:1], gath[0:8, :], maskc[0:8, 0:1],
                                            start=True, stop=True), inc='pe')
    p.wait('v', 'pe', pe_tok)
    def kdve10(e):
        e.tensor_copy(offs[:], k.pb[0][0:8, 0:1])
        e.tensor_copy(keptT[0:8, :], rkT[0:8, :])
        e.tensor_copy(keptT[0:8, :], rkT[0:8, :])
        e.tensor_scalar(keptT[0:8, :], rkT[0:8, :], offs[0:8, 0:1], -1.0, ALU.add, ALU.mult)
        e.tensor_scalar(keptT[0:8, :], keptT[0:8, :], 100.0, 0.0, ALU.add, ALU.max)
        e.tensor_scalar(keptT[0:8, :], keptT[0:8, :], 1.0, 0.0, ALU.min, ALU.add)
        return e.tensor_mul(keptT[0:8, :], keptT[0:8, :], ohT[0:8, :])
    dv_tok = p.emit('v', kdve10, inc='dv')
    p.wait('t', 'dv', dv_tok)
    for tc in range(16):
        def tpk10(e, tc=tc):
            e.transpose(k.pb[2][0:128, 0:8], keptT[0:8, tc * 128:(tc + 1) * 128], idf[0:8, 0:8])
            return e.transpose(k.pb[4][0:128, 0:8], keptT[0:8, tc * 128:(tc + 1) * 128], idf[0:8, 0:8])
        pe_tok = p.emit('t', tpk10, inc='pe')
        p.wait('v', 'pe', pe_tok)
        dv_tok = p.emit('v', lambda e, tc=tc: e.reduce_sum(kcolb[:, tc:tc + 1],
                                                           k.pb[2][0:128, 0:8], AX.X), inc='dv')
        p.wait('t', 'dv', dv_tok)
    dv_tok = p.emit('v', lambda e: e.tensor_mul(gk[:], kcolb[:], gate_tm[:]), inc='dv')
    p.wait('t', 'dv', dv_tok)
    p.wait('g', 'dv')
    p.dma('g', we, rearr_fm(a_we))
    for tc in range(16):
        def tpg10(e, tc=tc):
            e.transpose(k.pb[2][0:1, 0:128], gk[:, tc:tc + 1], idf[:, :])
            return e.transpose(k.pb[4][0:1, 0:128], gk[:, tc:tc + 1], idf[:, :])
        pe_tok = p.emit('t', tpg10, inc='pe')
        p.wait('v', 'pe', pe_tok)
        dv_tok = p.emit('v', lambda e, tc=tc: e.tensor_copy(
            k.stat[0:1, tc * 128:(tc + 1) * 128], k.pb[2][0:1, 0:128]), inc='dv')
        p.wait('t', 'dv', dv_tok)
    gkB = H[:, 512:1024]
    k.stat = SBf[0:1, 2048:4096]
    for cch in range(8):
        for ts10 in range(4):
            pe_tok = p.emit('t', lambda e, ts10=ts10: e.matmul(
                k.pb[5][:, 0:512], k.ones_f1[0:1, :],
                k.stat[0:1, ts10 * 512:(ts10 + 1) * 512], start=True, stop=True), inc='pe')
            p.wait('v', 'pe', pe_tok)
            dv_tok = p.emit('v', lambda e: e.tensor_copy(gkB, k.pb[5][:, 0:512]), inc='dv')
            p.wait('t', 'dv', dv_tok)
            bank = k.pb[3 + (ts10 % 2) - 3 + 0] if False else k.pb[(ts10 % 2)]
            if cch == 0 and ts10 == 0:
                p.wait('t', 'dg')
            def mms10(e, cch=cch, ts10=ts10, bank=bank):
                for dc in range(8):
                    i = e.matmul(bank[:, 0:512],
                                 we[:, dc * 1024 + cch * 128: dc * 1024 + cch * 128 + 128],
                                 xn1[:, dc * T + ts10 * 512: dc * T + ts10 * 512 + 512],
                                 start=(dc == 0), stop=(dc == 7))
                return i
            pe_tok = p.emit('t', mms10, inc='pe')
            p.wait('v', 'pe', pe_tok)
            sl10 = slice(cch * T + ts10 * 512, cch * T + ts10 * 512 + 512)
            dv_tok = p.emit('v', lambda e, cch=cch, bank=bank: e.scalar_tensor_tensor(
                k.tscr[:, 0:512], bank[:, 0:512], bef[:, cch:cch + 1], gkB,
                ALU.add, ALU.mult), inc='dv')
            p.wait('s', 'dv', dv_tok)
            ac_tok = p.emit('s', lambda e: e.activation(silb[:], k.tscr[:, 0:512], AF.Silu), inc='ac')
            p.wait('v', 'ac', ac_tok)
            def md210(e, cch=cch, ts10=ts10, sl10=sl10):
                e.tensor_mul(t2b[:], silb[:], s1T[:, sl10])
                return e.tensor_add(t2b[:], t2b[:],
                                    x1f9[:, cch * T + ts10 * 512: cch * T + ts10 * 512 + 512])
            dv_tok = p.emit('v', md210, inc='dv')
            p.wait('y', 'dv', dv_tok)
            p.dma('y', rearr_fm(o_out)[:, cch:cch + 1, ts10 * 512:(ts10 + 1) * 512], t2b[:])
            p.wait('t', 'dy'); p.wait('v', 'dy')
    p.wait('y', 'dy')
    p.finalize()
    return nc


def kernel_fused(inp):
    import ml_dtypes
    TB = make_tables()
    perm = make_perm()
    x = np.ascontiguousarray(inp['L0_attn_x'].astype(np.float32))
    cores = [(c // 2, c % 2) for c in range(8)]
    sls = [slice(2048 * hf, 2048 * hf + 2048) for (_, hf) in cores]
    WQpT = np.ascontiguousarray(inp['L0_attn_WQ'].reshape(D, D)[perm].T)
    WKpT = np.ascontiguousarray(inp['L0_attn_WK'].reshape(D, D)[perm].T)
    WVT = np.ascontiguousarray(inp['L0_attn_WV'].reshape(D, D).T)
    keysT = np.ascontiguousarray(inp['L1_attn_keys'][0].T.astype(np.float32))
    valsT = np.ascontiguousarray(inp['L1_attn_values'][0].T.astype(np.float32))
    o2 = np.zeros((128, 2), ml_dtypes.bfloat16); o2[0:64, 0] = 1; o2[64:128, 1] = 1
    E2h = np.zeros((2, 128), np.float32); E2h[0, 0:64] = 1; E2h[1, 64:128] = 1
    negb = np.zeros((128, 32), np.float32); negb[:, 8:] = -1e30
    idb = np.eye(128, dtype=ml_dtypes.bfloat16)
    idf = np.eye(128, dtype=np.float32)
    ins = []
    for c in range(8):
        b, hf = cores[c]
        s_ = sls[c]
        stm = np.zeros((128, 8), np.float32)
        if hf == 1:
            stm[:, 2 * b] = 1.0
        mkc = np.zeros((8, 1), np.float32); mkc[0:c, 0] = 1.0
        ins.append({
            "xT": np.ascontiguousarray(x[b, s_].T),
            "wq0": WQpT, "wk0": WKpT, "wv0": WVT,
            "cosT": np.ascontiguousarray(TB['COS_T'][:, s_]),
            "sinT": np.ascontiguousarray(TB['SIN_T'][:, s_]),
            "cosB": np.ascontiguousarray(TB['COS_BK'][s_]),
            "sinB": np.ascontiguousarray(TB['SIN_BK'][s_]),
            "dtc": TB['DTcat'], "qdec": TB['QDECG'].astype(np.float32),
            "sdec": TB['SDECG'], "stmask": stm,
            "gam": inp['L0_attn_gamma'].reshape(8, 128).T.copy(),
            "bet": inp['L0_attn_beta'].reshape(8, 128).T.copy(),
            "wo": inp['L0_attn_WO'].astype(np.float32),
            "ones2": o2, "E2": E2h,
            "q1": inp['L0_ffn_fc1_q'], "sc1": inp['L0_ffn_fc1_scales'],
            "b1": inp['L0_ffn_fc1_bias'].reshape(32, 128).T.copy(),
            "lg": inp['L0_ffn_ln_gamma'].reshape(8, 128).T.copy(),
            "lb": inp['L0_ffn_ln_beta'].reshape(8, 128).T.copy(),
            "q2": inp['L0_ffn_fc2_q'], "sc2": inp['L0_ffn_fc2_scales'],
            "b2": inp['L0_ffn_fc2_bias'].reshape(8, 128).T.copy(),
            "keysT": keysT, "valsT": valsT,
            "wk1": np.ascontiguousarray(inp['L1_attn_Wk_w'][512 * hf:512 * (hf + 1)].T),
            "wv1": np.ascontiguousarray(inp['L1_attn_Wv_w'][512 * hf:512 * (hf + 1)].T),
            "bk1": inp['L1_attn_Wk_b'][512 * hf:512 * (hf + 1)].reshape(4, 128).T.copy(),
            "bv1": inp['L1_attn_Wv_b'][512 * hf:512 * (hf + 1)].reshape(1, 512),
            "wq1": np.ascontiguousarray(inp['L1_attn_Wq_w'][512 * hf:512 * (hf + 1)].T),
            "bq1": inp['L1_attn_Wq_b'][512 * hf:512 * (hf + 1)].reshape(4, 128).T.copy(),
            "identb": idb, "identf": idf,
            "wg": inp['L1_ffn_w_gate'].astype(np.float32),
            "wgT": np.ascontiguousarray(inp['L1_ffn_w_gate'].T.astype(np.float32)),
            "negbig": negb, "posw": _posw(), "maskc": mkc,
            "we": inp['L1_ffn_w_expert'].astype(np.float32),
            "be": inp['L1_ffn_b_expert'].reshape(8, 128).T.copy(),
        })
    res = run_bass_kernel_spmd(build_fused(), ins, core_ids=list(range(8)))
    out = np.zeros((4, 4096, 1024), np.float32)
    for c in range(8):
        b, hf = cores[c]
        out[b, sls[c]] = np.asarray(res.results[c]["o_out"]).T
    return out


# version 10
# speedup vs baseline: 1.3235x; 1.0058x over previous
import numpy as np
import concourse.bass as bass
import concourse.mybir as mybir
from concourse.bass_utils import run_bass_kernel_spmd

F32 = mybir.dt.float32
BF16 = mybir.dt.bfloat16
I32 = mybir.dt.int32
AF = mybir.ActivationFunctionType
ALU = mybir.AluOpType
AX = mybir.AxisListType

H, DH, D = 16, 64, 1024
C = 128
T = 2048
S = 4096
EPS = 1e-5
NB = 8


def make_perm():
    perm = np.zeros(D, dtype=np.int64)
    for h in range(H):
        for i in range(32):
            perm[32 * h + i] = 64 * h + 2 * i
            perm[512 + 32 * h + i] = 64 * h + 2 * i + 1
    return perm

def make_tables():
    t = {}
    angle = 10.0 ** (-4.0 * (np.arange(0, DH, 2, dtype=np.float64) / DH))
    pos = np.arange(S, dtype=np.float64)
    nang = pos[:, None] * angle[None, :]
    COS = np.cos(nang); SIN = np.sin(nang)
    colmap = np.arange(512) % 32
    t['COS_T'] = COS[:, colmap].T.astype(np.float32).copy()
    t['SIN_T'] = SIN[:, colmap].T.astype(np.float32).copy()
    scale_h = 1.0 - 2.0 ** (-5.0 - np.arange(H, dtype=np.float64))
    lns = np.log(scale_h)
    jj, ii = np.meshgrid(np.arange(C), np.arange(C), indexing='ij')
    DT = np.where(ii >= jj, np.exp((ii - jj)[None] * lns[:, None, None]), 0.0)
    t['DTcat'] = np.concatenate([DT[h] for h in range(H)], axis=1).astype(np.float32)
    qd = np.zeros((128, 8 * C), dtype=np.float32)
    for g in range(8):
        for hf, h in ((0, 2 * g), (1, 2 * g + 1)):
            qd[64 * hf:64 * hf + 64, C * g:C * (g + 1)] = np.exp(
                (np.arange(C, dtype=np.float64)[None, :] + 1.0) * lns[h])
    t['QDECG'] = qd
    kdec = np.exp((C - 1.0 - (pos % C))[:, None] * lns[None, :])
    hmap = np.arange(512) // 32
    t['COS_BK'] = (COS[:, colmap] * kdec[:, hmap]).astype(np.float32)
    t['SIN_BK'] = (SIN[:, colmap] * kdec[:, hmap]).astype(np.float32)
    sdec = np.exp(C * lns)
    sd = np.zeros((128, 8), dtype=np.float32)
    for g in range(8):
        sd[0:64, g] = sdec[2 * g]
        sd[64:128, g] = sdec[2 * g + 1]
    t['SDECG'] = sd
    return t


_ENG_ATTR = {'t': 'tensor', 'v': 'vector', 's': 'scalar', 'g': 'gpsimd', 'y': 'sync'}

class P:
    def __init__(self, nc, sems=('pe', 'dv', 'ac', 'gp', 'dy', 'dg', 'ds')):
        self.nc = nc
        self.q = {e: [] for e in _ENG_ATTR}
        self.sems = {}
        self.cnt = {}
        for nm in sems:
            self.sems[nm] = nc.alloc_semaphore(nm)
            self.cnt[nm] = 0

    def emit(self, eng, fn, inc=None, k=1):
        if inc is None:
            self.q[eng].append(fn)
            return None
        self.cnt[inc] += k
        sem = self.sems[inc]
        self.q[eng].append(lambda e, fn=fn, sem=sem, k=k: fn(e).then_inc(sem, k))
        return self.cnt[inc]

    def dma(self, eng, out, in_, inc=None, **kw):
        inc = inc if inc is not None else {'y': 'dy', 'g': 'dg', 's': 'ds'}[eng]
        return self.emit(eng, lambda e, out=out, in_=in_, kw=kw: e.dma_start(out=out, in_=in_, **kw),
                         inc=inc, k=16)

    def wait(self, eng, nm, val=None):
        v = self.cnt[nm] if val is None else val
        sem = self.sems[nm]
        self.q[eng].append(lambda e, sem=sem, v=v: e.wait_ge(sem, v))

    def tok(self, nm):
        return self.cnt[nm]

    def finalize(self):
        nc = self.nc
        with nc.Block() as block:
            for eng, attr in _ENG_ATTR.items():
                thunks = self.q[eng]
                if not thunks:
                    continue
                getattr(block, attr)(lambda e, thunks=thunks: [th(e) for th in thunks])


def alloc_common(nc, p):
    k = type('K', (), {})()
    k.nc, k.p = nc, p
    k.pb = [nc.alloc_psum_tensor(f"pb{i}", [128, 512], F32) for i in range(7)]
    k.pbh = nc.alloc_psum_tensor("pbh", [128, 1024], BF16)
    k.ones_bf = nc.alloc_sbuf_tensor("ones_bf", [128, 1], BF16)
    k.ones_f1 = nc.alloc_sbuf_tensor("ones_f1", [1, 128], F32)
    k.stat = nc.alloc_sbuf_tensor("stat", [1, 4 * 512], F32)
    k.stat2 = nc.alloc_sbuf_tensor("stat2", [2, 4 * 512], F32)
    k.st2m = k.stat2[0:2, 0:512]
    k.st2v = k.stat2[0:2, 512:1024]
    k.st2t = k.stat2[0:2, 1024:1536]
    k.st2r = k.stat2[0:2, 1536:2048]
    k.tscr = nc.alloc_sbuf_tensor("tscr", [128, 512], F32)
    p.emit('v', lambda e: e.memset(k.ones_bf[:], 1.0))
    p.emit('v', lambda e: e.memset(k.ones_f1[:], 1.0), inc='dv')
    return k

def rearr_fm(hbm_ap):
    return hbm_ap.rearrange("(c p) t -> p c t", p=128)

def rearr_tm(hbm_ap):
    return hbm_ap.rearrange("(n p) c -> p n c", p=128)


def emit_ln_fm(k, xb, xsq, out, *, nchunk=8, tlen=T, gamma_ap=None, beta_ap=None, m_out=None, r_out=None):
    p, nc = k.p, k.nc
    NS = 512
    nts = tlen // NS
    for c in range(nchunk):
        p.emit('v', lambda e, c=c: e.tensor_mul(
            xsq[:, c * tlen:(c + 1) * tlen], xb[:, c * tlen:(c + 1) * tlen],
            xb[:, c * tlen:(c + 1) * tlen]), inc='dv')
    sq_tok = p.tok('dv')
    p.wait('t', 'dv', sq_tok)
    stat_tok = [None, None]
    rdy_tok = [None, None]
    apply_tok = [None, None]
    for ts in range(nts):
        par = ts % 2
        sb, qb_ = k.pb[0], k.pb[1]
        bcm = k.pb[2] if par == 0 else k.pb[5]
        bcr = k.pb[3] if par == 0 else k.pb[6]
        M = k.stat[0:1, par * 1024 + 0:par * 1024 + 512]
        R = k.stat[0:1, par * 1024 + 512:par * 1024 + 1024]
        if stat_tok[0] is not None:
            p.wait('t', 'dv', stat_tok[0])
        def sums(e, ts=ts):
            for c in range(nchunk):
                sl = slice(c * tlen + ts * NS, c * tlen + ts * NS + NS)
                e.matmul(sb[0:1, 0:NS], k.ones_bf[:, 0:1], xb[:, sl],
                         start=(c == 0), stop=(c == nchunk - 1))
            for c in range(nchunk):
                sl = slice(c * tlen + ts * NS, c * tlen + ts * NS + NS)
                i2 = e.matmul(qb_[0:1, 0:NS], k.ones_bf[:, 0:1], xsq[:, sl],
                              start=(c == 0), stop=(c == nchunk - 1))
            return i2
        pe_tok = p.emit('t', sums, inc='pe')
        p.wait('v', 'pe', pe_tok)
        def stats(e, ts=ts, M=M, R=R):
            e.tensor_scalar_mul(M, sb[0:1, 0:NS], 1.0 / (nchunk * 128.0))
            e.tensor_scalar_mul(R, qb_[0:1, 0:NS], 1.0 / (nchunk * 128.0))
            if m_out is not None:
                e.tensor_copy(m_out[0:1, ts * NS:(ts + 1) * NS], M)
            e.tensor_mul(k.tscr[0:1, 0:NS], M, M)
            e.tensor_sub(R, R, k.tscr[0:1, 0:NS])
            return e.tensor_scalar_add(R, R, EPS)
        stat_tok[0] = p.emit('v', stats, inc='dv')
        p.wait('s', 'dv', stat_tok[0])
        ac_tok = p.emit('s', lambda e, R=R: e.activation(R, R, AF.Sqrt), inc='ac')
        p.wait('v', 'ac', ac_tok)
        def recf(e, ts=ts, R=R):
            i = e.reciprocal(R, R)
            if r_out is not None:
                i = e.tensor_copy(r_out[0:1, ts * NS:(ts + 1) * NS], R)
            return i
        rdy_tok[par] = p.emit('v', recf, inc='dv')
        if apply_tok[par] is not None:
            p.wait('t', 'dv', apply_tok[par])
        p.wait('t', 'dv', rdy_tok[par])
        def bcast(e, M=M, R=R, bcm=bcm, bcr=bcr):
            e.matmul(bcm[:, 0:NS], k.ones_f1[0:1, :], M, start=True, stop=True)
            return e.matmul(bcr[:, 0:NS], k.ones_f1[0:1, :], R, start=True, stop=True)
        pe_tok = p.emit('t', bcast, inc='pe')
        p.wait('v', 'pe', pe_tok)
        def apply(e, ts=ts, bcm=bcm, bcr=bcr):
            for c in range(nchunk):
                sl = slice(c * tlen + ts * NS, c * tlen + ts * NS + NS)
                e.tensor_sub(k.tscr[:, 0:NS], xb[:, sl], bcm[:, 0:NS])
                if gamma_ap is None:
                    iv = e.tensor_mul(out[:, sl], k.tscr[:, 0:NS], bcr[:, 0:NS])
                else:
                    e.tensor_mul(k.tscr[:, 0:NS], k.tscr[:, 0:NS], bcr[:, 0:NS])
                    iv = e.tensor_scalar(out[:, sl], k.tscr[:, 0:NS],
                                         gamma_ap[:, c:c + 1], beta_ap[:, c:c + 1],
                                         ALU.mult, ALU.add)
            return iv
        apply_tok[par] = p.emit('v', apply, inc='dv')
    p.wait('t', 'dv')


def emit_proj_fm(k, wfm, xfm, out, *, nchunk_out=8, nchunk_in=8, tlen=T, evac='s',
                 bank0=0, bias_ap=None, relu=False, out_tlen=None, out_col0=0):
    p = k.p
    NS = 512
    wcols = nchunk_out * 128
    if out_tlen is None:
        out_tlen = tlen
    btok = [None, None]
    for cch in range(nchunk_out):
        for ts in range(tlen // NS):
            bi = ts % 2
            bank = k.pb[bank0 + bi]
            if btok[bi] is not None:
                p.wait('t', 'ac' if evac == 's' else 'dv', btok[bi])
            def mms(e, cch=cch, ts=ts, bank=bank):
                for dc in range(nchunk_in):
                    i = e.matmul(bank[:, 0:NS],
                                 wfm[:, dc * wcols + cch * 128: dc * wcols + cch * 128 + 128],
                                 xfm[:, dc * tlen + ts * NS: dc * tlen + ts * NS + NS],
                                 start=(dc == 0), stop=(dc == nchunk_in - 1))
                return i
            pe_tok = p.emit('t', mms, inc='pe')
            osl = out[:, cch * out_tlen + out_col0 + ts * NS: cch * out_tlen + out_col0 + ts * NS + NS]
            if evac == 's':
                p.wait('s', 'pe', pe_tok)
                btok[bi] = p.emit('s', lambda e, bank=bank, osl=osl: e.activation(osl, bank[:, 0:NS], AF.Copy),
                                  inc='ac')
            else:
                p.wait('v', 'pe', pe_tok)
                if bias_ap is not None:
                    btok[bi] = p.emit('v', lambda e, bank=bank, osl=osl, cch=cch:
                                      e.tensor_scalar(osl, bank[:, 0:NS], bias_ap[:, cch:cch + 1],
                                                      0.0, ALU.add, ALU.max if relu else ALU.bypass),
                                      inc='dv')
                else:
                    btok[bi] = p.emit('v', lambda e, bank=bank, osl=osl: e.tensor_copy(osl, bank[:, 0:NS]),
                                      inc='dv')
    p.wait('t', 'ac' if evac == 's' else 'dv')


def emit_rot_fm(k, raw, cosT, sinT, scrA, scrB, scrC):
    p = k.p
    for c in range(4):
        lo = raw[:, c * T:(c + 1) * T]
        hi = raw[:, (c + 4) * T:(c + 5) * T]
        cs = cosT[:, c * T:(c + 1) * T]
        sn = sinT[:, c * T:(c + 1) * T]
        def rot(e, lo=lo, hi=hi, cs=cs, sn=sn):
            e.tensor_mul(scrA[:], lo, cs)
            e.tensor_mul(scrC[:], lo, sn)
            e.tensor_mul(scrB[:], hi, sn)
            e.tensor_sub(lo, scrA[:], scrB[:])
            e.tensor_mul(scrB[:], hi, cs)
            return e.tensor_add(hi, scrC[:], scrB[:])
        p.emit('v', rot, inc='dv')


def emit_regroup(k, src, dst):
    p = k.p
    for g in range(8):
        for hf in range(2):
            h = 2 * g + hf
            cr, qq = h // 4, h % 4
            for x1 in range(2):
                srows = slice(32 * qq, 32 * qq + 32)
                scol = (cr + 4 * x1) * T
                drows = slice(64 * hf + 32 * x1, 64 * hf + 32 * x1 + 32)
                p.dma('y', dst[drows, g * T:(g + 1) * T], src[srows, scol:scol + T])


def emit_rot_tm(k, raw_tm, out_tm, cosB, sinB, scrA, scrB, scrC, *, ntile=16):
    p = k.p
    for n in range(ntile):
        lo = raw_tm[:, n * 1024: n * 1024 + 512]
        hi = raw_tm[:, n * 1024 + 512: n * 1024 + 1024]
        cs = cosB[:, n * 512:(n + 1) * 512]
        sn = sinB[:, n * 512:(n + 1) * 512]
        oblk = out_tm[:, n * 1024:(n + 1) * 1024].rearrange("p (h i) -> p h i", h=16, i=64)
        olo = oblk[:, :, 0:32]
        ohi = oblk[:, :, 32:64]
        def rot(e, lo=lo, hi=hi, cs=cs, sn=sn, olo=olo, ohi=ohi):
            e.tensor_mul(scrA[:, 0:512], lo, cs)
            e.tensor_mul(scrC[:, 0:512], lo, sn)
            e.tensor_mul(scrB[:, 0:512], hi, sn)
            e.tensor_sub(olo, scrA[:, 0:512].rearrange("p (h i) -> p h i", i=32),
                         scrB[:, 0:512].rearrange("p (h i) -> p h i", i=32))
            e.tensor_mul(scrB[:, 0:512], hi, cs)
            return e.tensor_add(ohi, scrC[:, 0:512].rearrange("p (h i) -> p h i", i=32),
                                scrB[:, 0:512].rearrange("p (h i) -> p h i", i=32))
        p.emit('v', rot, inc='dv')

def emit_proj_tm(k, wfm, xfm, out_tm, *, nchunk_in=8, tlen=T, ncol=1024, bank0=0):
    p = k.p
    NS = 512
    btok = [None, None]
    ii_ = 0
    for tt in range(tlen // 128):
        for cs in range(ncol // NS):
            bi = ii_ % 2
            ii_ += 1
            bank = k.pb[bank0 + bi]
            if btok[bi] is not None:
                p.wait('t', 'ac', btok[bi])
            def mms(e, tt=tt, cs=cs, bank=bank):
                for dc in range(nchunk_in):
                    i = e.matmul(bank[:, 0:NS],
                                 xfm[:, dc * tlen + tt * 128: dc * tlen + tt * 128 + 128],
                                 wfm[:, dc * ncol + cs * NS: dc * ncol + cs * NS + NS],
                                 start=(dc == 0), stop=(dc == nchunk_in - 1))
                return i
            pe_tok = p.emit('t', mms, inc='pe')
            p.wait('s', 'pe', pe_tok)
            osl = out_tm[:, tt * ncol + cs * NS: tt * ncol + cs * NS + NS]
            btok[bi] = p.emit('s', lambda e, bank=bank, osl=osl: e.activation(osl, bank[:, 0:NS], AF.Copy),
                              inc='ac')
    p.wait('t', 'ac')


def emit_retention(k, Qg, Kg, ktok, Vtm, DT, QDECG, SDECG, Sf32, S2b, RTg, qscr, atA, atB):
    p = k.p
    scA, scB, rtA, rtB, ds = k.pb[0], k.pb[1], k.pb[2], k.pb[3], k.pb[4]
    for c in range(16):
        for g in range(8):
            qcol = slice(g * T + c * 128, g * T + c * 128 + 128)
            vcol = slice(c * 1024 + g * 128, c * 1024 + g * 128 + 128)
            def dve1(e, g=g, qcol=qcol):
                e.tensor_mul(qscr[:], Qg[:, qcol], QDECG[:, g * 128:(g + 1) * 128])
                e.tensor_copy(S2b[0:64, g * 128:g * 128 + 64], Sf32[0:64, g * 128:g * 128 + 64])
                return e.tensor_copy(S2b[64:128, g * 128 + 64:g * 128 + 128],
                                     Sf32[64:128, g * 128 + 64:g * 128 + 128])
            dv_tok = p.emit('v', dve1, inc='dv')
            p.wait('t', 'dv', dv_tok)
            def pe1(e, g=g, qcol=qcol):
                e.matmul(rtA[0:64, 0:128], S2b[:, g * 128:g * 128 + 64], qscr[:],
                         start=True, stop=False)
                e.matmul(rtB[64:128, 0:128], S2b[:, g * 128 + 64:g * 128 + 128], qscr[:],
                         start=True, stop=False, tile_position=(0, 64))
                e.matmul(scA[:, 0:128], Kg[0:64, qcol], Qg[0:64, qcol],
                         start=True, stop=True)
                return e.matmul(scB[:, 0:128], Kg[64:128, qcol], Qg[64:128, qcol],
                                start=True, stop=True, tile_position=(64, 0))
            pe_tok = p.emit('t', pe1, inc='pe')
            p.wait('v', 'pe', pe_tok)
            def dve2(e, g=g):
                e.tensor_mul(atA[:], scA[:, 0:128], DT[:, (2 * g) * 128:(2 * g) * 128 + 128])
                return e.tensor_mul(atB[:], scB[:, 0:128], DT[:, (2 * g + 1) * 128:(2 * g + 1) * 128 + 128])
            dv_tok = p.emit('v', dve2, inc='dv')
            p.wait('t', 'dv', dv_tok)
            def pe2(e, g=g, vcol=vcol):
                e.matmul(rtA[0:64, 0:128], Vtm[:, vcol.start:vcol.start + 64], atA[:],
                         start=False, stop=True)
                e.matmul(rtB[64:128, 0:128], Vtm[:, vcol.start + 64:vcol.start + 128], atB[:],
                         start=False, stop=True, tile_position=(0, 64))
                return e.matmul(ds[:, 0:128], ktok[:, vcol], Vtm[:, vcol],
                                start=True, stop=True)
            pe_tok = p.emit('t', pe2, inc='pe')
            p.wait('v', 'pe', pe_tok)
            def dve3(e, g=g, c=c):
                e.tensor_copy(RTg[0:64, g * T + c * 128: g * T + c * 128 + 128], rtA[0:64, 0:128])
                e.tensor_copy(RTg[64:128, g * T + c * 128: g * T + c * 128 + 128], rtB[64:128, 0:128])
                e.scalar_tensor_tensor(Sf32[0:64, g * 128:g * 128 + 64],
                                       Sf32[0:64, g * 128:g * 128 + 64],
                                       SDECG[0:64, g:g + 1], ds[0:64, 0:64],
                                       ALU.mult, ALU.add)
                return e.scalar_tensor_tensor(Sf32[64:128, g * 128 + 64:g * 128 + 128],
                                              Sf32[64:128, g * 128 + 64:g * 128 + 128],
                                              SDECG[64:128, g:g + 1], ds[64:128, 64:128],
                                              ALU.mult, ALU.add)
            dv_tok = p.emit('v', dve3, inc='dv')
            p.wait('t', 'dv', dv_tok)


def emit_gnorm(k, RTg, Rn, ones2, E2, gammaT, betaT, sqscr):
    p = k.p
    NS = 512
    for g in range(8):
        for ts in range(T // NS):
            sl = slice(g * T + ts * NS, g * T + ts * NS + NS)
            dv_tok = p.emit('v', lambda e, sl=sl: e.tensor_mul(sqscr[:], RTg[:, sl], RTg[:, sl]),
                            inc='dv')
            p.wait('t', 'dv', dv_tok)
            def pe1(e, sl=sl):
                e.matmul(k.pb[0][0:2, 0:NS], ones2[:, 0:2], RTg[:, sl], start=True, stop=True)
                return e.matmul(k.pb[1][0:2, 0:NS], ones2[:, 0:2], sqscr[:], start=True, stop=True)
            pe_tok = p.emit('t', pe1, inc='pe')
            p.wait('v', 'pe', pe_tok)
            def stats(e):
                e.tensor_scalar_mul(k.st2m[:], k.pb[0][0:2, 0:NS], 1.0 / 64.0)
                e.tensor_scalar_mul(k.st2v[:], k.pb[1][0:2, 0:NS], 1.0 / 64.0)
                e.tensor_mul(k.st2t[:], k.st2m[:], k.st2m[:])
                e.tensor_sub(k.st2v[:], k.st2v[:], k.st2t[:])
                return e.tensor_scalar_add(k.st2v[:], k.st2v[:], EPS)
            dv_tok = p.emit('v', stats, inc='dv')
            p.wait('s', 'dv', dv_tok)
            ac_tok = p.emit('s', lambda e: e.activation(k.st2t[:], k.st2v[:], AF.Sqrt), inc='ac')
            p.wait('v', 'ac', ac_tok)
            dv_tok = p.emit('v', lambda e: e.reciprocal(k.st2r[:], k.st2t[:]), inc='dv')
            p.wait('t', 'dv', dv_tok)
            def bcast(e):
                e.matmul(k.pb[2][:, 0:NS], E2[0:2, :], k.st2m[:], start=True, stop=True)
                return e.matmul(k.pb[3][:, 0:NS], E2[0:2, :], k.st2r[:], start=True, stop=True)
            pe_tok = p.emit('t', bcast, inc='pe')
            p.wait('v', 'pe', pe_tok)
            def apply(e, sl=sl, g=g):
                e.tensor_sub(k.tscr[:, 0:NS], RTg[:, sl], k.pb[2][:, 0:NS])
                e.tensor_mul(k.tscr[:, 0:NS], k.tscr[:, 0:NS], k.pb[3][:, 0:NS])
                return e.tensor_scalar(Rn[:, sl], k.tscr[:, 0:NS],
                                       gammaT[:, g:g + 1], betaT[:, g:g + 1],
                                       ALU.mult, ALU.add)
            dv_tok = p.emit('v', apply, inc='dv')
            p.wait('t', 'dv', dv_tok)


def emit_dequant(k, wq_i32_hbm, w_sb, scales_hbm, scaleB, *, nk, nn, eng='g'):
    p = k.p
    for g in range(nk):
        p.dma(eng, w_sb[:, g * nn:(g + 1) * nn],
              wq_i32_hbm.rearrange("(c p) t -> p c t", p=128)[:, g:g + 1, :])
        p.dma(eng, scaleB[:], scales_hbm[g:g + 1, :].broadcast_to((128, nn)))
        p.wait('v', 'd' + eng)
        p.emit('v', lambda e, g=g: e.scalar_tensor_tensor(
            w_sb[:, g * nn:(g + 1) * nn], w_sb[:, g * nn:(g + 1) * nn], 128.0,
            scaleB[:], ALU.subtract, ALU.mult), inc='dv')
        p.wait('g' if eng == 'g' else 'y', 'dv')


def emit_ffn_fc1(k, w1, hT, b1f, y1T, *, tc, NS=512):
    p = k.p
    btok = [None, None]
    for n1c in range(32):
        bi = n1c % 2
        bank = k.pb[bi]
        if btok[bi] is not None:
            p.wait('t', 'ac', btok[bi])
        def mms(e, n1c=n1c, bank=bank, tc=tc):
            for dc in range(8):
                i = e.matmul(bank[:, 0:NS],
                             w1[:, dc * 4096 + n1c * 128: dc * 4096 + n1c * 128 + 128],
                             hT[:, dc * T + tc * NS: dc * T + tc * NS + NS],
                             start=(dc == 0), stop=(dc == 7))
            return i
        pe_tok = p.emit('t', mms, inc='pe')
        p.wait('s', 'pe', pe_tok)
        btok[bi] = p.emit('s', lambda e, bank=bank, n1c=n1c: e.activation(
            y1T[:, n1c * NS:(n1c + 1) * NS], bank[:, 0:NS], AF.Silu,
            bias=b1f[:, n1c:n1c + 1]), inc='ac')
    p.wait('t', 'ac')


def emit_ffn_fc2(k, w2, y1T, b2h, xnT, gT, *, tc, NS=512):
    p = k.p
    btok = [None, None]
    for n2c in range(8):
        bi = n2c % 2
        bank = k.pb[bi]
        if btok[bi] is not None:
            p.wait('t', 'dv', btok[bi])
        def mms(e, n2c=n2c, bank=bank):
            for kc in range(32):
                i = e.matmul(bank[:, 0:NS],
                             w2[:, kc * 1024 + n2c * 128: kc * 1024 + n2c * 128 + 128],
                             y1T[:, kc * NS:(kc + 1) * NS],
                             start=(kc == 0), stop=(kc == 31))
            return i
        pe_tok = p.emit('t', mms, inc='pe')
        p.wait('v', 'pe', pe_tok)
        sl = slice(n2c * T + tc * NS, n2c * T + tc * NS + NS)
        p.wait('v', 'ac')
        btok[bi] = p.emit('v', lambda e, bank=bank, sl=sl: e.scalar_tensor_tensor(
            k.tscr[:, 0:NS], bank[:, 0:NS], 0.5, xnT[:, sl], ALU.mult, ALU.add),
            inc='dv')
        p.wait('s', 'dv', btok[bi])
        tk = p.emit('s', lambda e, sl=sl, n2c=n2c: e.activation(
            gT[:, sl], k.tscr[:, 0:NS], AF.Silu, bias=b2h[:, n2c:n2c + 1]), inc='ac')
    p.wait('t', 'ac'); p.wait('v', 'ac')

def emit_retention2(k, Qg, Kg, ktok, Vtm, DT, QDECG, SDECG, Sf32, S2b, RTg, qfull):
    p = k.p
    rtA, rtB, ds = k.pb[2], k.pb[3], k.pb[4]
    Qgv = Qg.rearrange("p (g t) -> p g t", g=8)
    dve3_tok = None
    for c in range(16):
        p.wait('v', 'pe')
        def dve0(e, c=c):
            for g_ in range(8):
                e.tensor_mul(qfull[:, g_ * 128:(g_ + 1) * 128],
                             Qg[:, g_ * T + c * 128: g_ * T + c * 128 + 128],
                             QDECG[:, g_ * 128:(g_ + 1) * 128])
            return e.tensor_copy(S2b[:], Sf32[:])
        dv0_tok = p.emit('v', dve0, inc='dv')
        p.wait('t', 'dv', dv0_tok)
        for g in range(8):
            par = g % 2
            scA = k.pb[0] if par == 0 else k.pb[5]
            scB = k.pb[1] if par == 0 else k.pb[6]
            qcol = slice(g * T + c * 128, g * T + c * 128 + 128)
            vcol = slice(c * 1024 + g * 128, c * 1024 + g * 128 + 128)
            if dve3_tok is not None:
                p.wait('t', 'dv', dve3_tok)
            def pe1(e, g=g, qcol=qcol, scA=scA, scB=scB):
                e.matmul(rtA[0:64, 0:128], S2b[:, g * 128:g * 128 + 64],
                         qfull[:, g * 128:(g + 1) * 128], start=True, stop=False)
                e.matmul(rtB[64:128, 0:128], S2b[:, g * 128 + 64:g * 128 + 128],
                         qfull[:, g * 128:(g + 1) * 128], start=True, stop=False,
                         tile_position=(0, 64))
                e.matmul(scA[:, 0:128], Kg[0:64, qcol], Qg[0:64, qcol],
                         start=True, stop=True)
                return e.matmul(scB[:, 0:128], Kg[64:128, qcol], Qg[64:128, qcol],
                                start=True, stop=True, tile_position=(64, 0))
            pe_tok = p.emit('t', pe1, inc='pe')
            p.wait('v', 'pe', pe_tok)
            atA = k.atA if par == 0 else k.atA2
            atB = k.atB if par == 0 else k.atB2
            def dve2(e, g=g, scA=scA, scB=scB, atA=atA, atB=atB):
                e.tensor_mul(atA[:], scA[:, 0:128], DT[:, (2 * g) * 128:(2 * g) * 128 + 128])
                return e.tensor_mul(atB[:], scB[:, 0:128], DT[:, (2 * g + 1) * 128:(2 * g + 1) * 128 + 128])
            dv_tok = p.emit('v', dve2, inc='dv')
            p.wait('t', 'dv', dv_tok)
            def pe2(e, g=g, vcol=vcol, atA=atA, atB=atB):
                e.matmul(rtA[0:64, 0:128], Vtm[:, vcol.start:vcol.start + 64], atA[:],
                         start=False, stop=True)
                e.matmul(rtB[64:128, 0:128], Vtm[:, vcol.start + 64:vcol.start + 128], atB[:],
                         start=False, stop=True, tile_position=(0, 64))
                return e.matmul(ds[:, 0:128], ktok[:, vcol], Vtm[:, vcol],
                                start=True, stop=True)
            pe_tok = p.emit('t', pe2, inc='pe')
            p.wait('v', 'pe', pe_tok)
            def dve3(e, g=g, c=c):
                e.tensor_copy(RTg[0:64, g * T + c * 128: g * T + c * 128 + 128], rtA[0:64, 0:128])
                e.tensor_copy(RTg[64:128, g * T + c * 128: g * T + c * 128 + 128], rtB[64:128, 0:128])
                e.scalar_tensor_tensor(Sf32[0:64, g * 128:g * 128 + 64],
                                       Sf32[0:64, g * 128:g * 128 + 64],
                                       SDECG[0:64, g:g + 1], ds[0:64, 0:64],
                                       ALU.mult, ALU.add)
                return e.scalar_tensor_tensor(Sf32[64:128, g * 128 + 64:g * 128 + 128],
                                              Sf32[64:128, g * 128 + 64:g * 128 + 128],
                                              SDECG[64:128, g:g + 1], ds[64:128, 64:128],
                                              ALU.mult, ALU.add)
            dve3_tok = p.emit('v', dve3, inc='dv')
    p.wait('t', 'dv', dve3_tok)

def _din(nc, reg, name, arr, dt=None):
    if dt is None:
        dt = mybir.dt.from_np(np.asarray(arr[0] if isinstance(arr, list) else arr).dtype)
    a0 = np.asarray(arr[0] if isinstance(arr, list) else arr)
    reg[name] = arr
    return nc.dram_tensor(name, list(a0.shape), dt, kind="ExternalInput").ap()


def build_l1():
    nc = bass.Bass("TRN2", target_bir_lowering=False, detect_race_conditions=False)
    xT = nc.dram_tensor("xT", [D, T], F32, kind="ExternalInput").ap()
    oxn = nc.dram_tensor("oxn", [D, T], BF16, kind="ExternalOutput").ap()
    p = P(nc)
    k = alloc_common(nc, p)
    xb = nc.alloc_sbuf_tensor("xb", [128, 8 * T], BF16)
    xsq = nc.alloc_sbuf_tensor("xsq", [128, 8 * T], BF16)
    xn = nc.alloc_sbuf_tensor("xn", [128, 8 * T], BF16)
    p.dma('g', xb[:], rearr_fm(xT))
    p.wait('v', 'dg'); p.wait('t', 'dg'); p.wait('s', 'dg')
    emit_ln_fm(k, xb, xsq, xn)
    p.wait('y', 'dv')
    p.dma('y', rearr_fm(oxn), xn[:])
    p.wait('y', 'dy')
    p.finalize()
    return nc


def build_l2():
    nc = bass.Bass("TRN2", target_bir_lowering=False, detect_race_conditions=False)
    din = lambda name, shape, dt=F32: nc.dram_tensor(name, shape, dt, kind="ExternalInput").ap()
    a_xnT = din("xnT", [D, T], BF16)
    a_wq = din("wq", [D, D]); a_wk = din("wk", [D, D]); a_wv = din("wv", [D, D])
    a_cosT = din("cosT", [512, T]); a_sinT = din("sinT", [512, T])
    a_cosB = din("cosB", [T, 512]); a_sinB = din("sinB", [T, 512])
    o_q = nc.dram_tensor("o_q", [D, T], BF16, kind="ExternalOutput").ap()
    o_k = nc.dram_tensor("o_k", [D, T], BF16, kind="ExternalOutput").ap()
    o_kt = nc.dram_tensor("o_kt", [T, D], BF16, kind="ExternalOutput").ap()
    o_v = nc.dram_tensor("o_v", [T, D], BF16, kind="ExternalOutput").ap()
    p = P(nc)
    k = alloc_common(nc, p)
    A = lambda nm, shape, dt=BF16: nc.alloc_sbuf_tensor(nm, shape, dt)
    xnT = A("xnTs", [128, 8 * T])
    ws = A("ws", [128, 3 * 8 * 1024])
    wq, wk, wv = ws[:, 0:8192], ws[:, 8192:16384], ws[:, 16384:24576]
    tb8 = A("tb8", [128, 16384], F32)
    raw = A("raw", [128, 16384])
    scr = A("scr", [128, 3 * 512], F32)
    scrA, scrB, scrC = scr[:, 0:512], scr[:, 512:1024], scr[:, 1024:1536]
    p.dma('g', xnT[:], rearr_fm(a_xnT))
    p.dma('g', wq, rearr_fm(a_wq))
    p.dma('g', wk, rearr_fm(a_wk))
    p.dma('g', wv, rearr_fm(a_wv))
    p.dma('g', tb8[:, 0:8192], rearr_fm(a_cosT))
    p.dma('g', tb8[:, 8192:16384], rearr_fm(a_sinT))
    p.wait('t', 'dg'); p.wait('v', 'dg'); p.wait('s', 'dg')
    cosT, sinT = tb8[:, 0:8192], tb8[:, 8192:16384]

    def rot_and_store(src_fm, out_hbm):
        for c in range(4):
            for ts in range(T // 512):
                lo = src_fm[:, c * T + ts * 512: c * T + ts * 512 + 512]
                hi = src_fm[:, (c + 4) * T + ts * 512: (c + 4) * T + ts * 512 + 512]
                cs = cosT[:, c * T + ts * 512: c * T + ts * 512 + 512]
                sn = sinT[:, c * T + ts * 512: c * T + ts * 512 + 512]
                def rot(e, lo=lo, hi=hi, cs=cs, sn=sn):
                    e.tensor_mul(scrA[:], lo, cs)
                    e.tensor_mul(scrC[:], lo, sn)
                    e.tensor_mul(scrB[:], hi, sn)
                    e.tensor_sub(lo, scrA[:], scrB[:])
                    e.tensor_mul(scrB[:], hi, cs)
                    return e.tensor_add(hi, scrC[:], scrB[:])
                p.emit('v', rot, inc='dv')
        p.wait('y', 'dv')
        p.dma('y', rearr_fm(out_hbm), src_fm[:])

    emit_proj_fm(k, wq, xnT, raw)
    p.wait('v', 'ac')
    rot_and_store(raw, o_q)
    p.wait('s', 'dy')
    emit_proj_fm(k, wk, xnT, raw)
    p.wait('v', 'ac')
    rot_and_store(raw, o_k)
    p.wait('g', 'dv')
    p.dma('g', tb8[:, 0:8192], rearr_tm(a_cosB))
    p.dma('g', tb8[:, 8192:16384], rearr_tm(a_sinB))
    p.wait('s', 'dy')
    emit_proj_tm(k, wk, xnT, raw)
    p.wait('v', 'ac'); p.wait('v', 'dg')
    ktok = ws[:, 0:16384]
    emit_rot_tm(k, raw, ktok, tb8[:, 0:8192], tb8[:, 8192:16384], scrA, scrB, scrC)
    p.wait('y', 'dv')
    p.dma('y', rearr_tm(o_kt), ktok)
    p.wait('s', 'dy')
    emit_proj_tm(k, wv, xnT, raw)
    p.wait('y', 'ac')
    p.dma('y', rearr_tm(o_v), raw[:])
    p.wait('y', 'dy')
    p.finalize()
    return nc


def build_l3a():
    nc = bass.Bass("TRN2", target_bir_lowering=False, detect_race_conditions=False)
    din = lambda name, shape, dt=F32: nc.dram_tensor(name, shape, dt, kind="ExternalInput").ap()
    a_kt = din("kt", [T, D], BF16)
    a_v = din("v", [T, D], BF16)
    a_sdec = din("sdec", [128, 8])
    o_st = nc.dram_tensor("o_st", [128, 1024], F32, kind="ExternalOutput").ap()
    p = P(nc)
    k = alloc_common(nc, p)
    A = lambda nm, shape, dt=BF16: nc.alloc_sbuf_tensor(nm, shape, dt)
    ktok = A("ktoks", [128, 16384]); Vtm = A("Vtms", [128, 16384])
    sdec = A("sdecs", [128, 8], F32)
    Sf32 = A("Sf32", [128, 1024], F32)
    p.dma('g', ktok[:], rearr_tm(a_kt))
    p.dma('g', Vtm[:], rearr_tm(a_v))
    p.dma('g', sdec[:], a_sdec)
    p.emit('v', lambda e: e.memset(Sf32[:], 0.0), inc='dv')
    p.wait('t', 'dg'); p.wait('t', 'dv'); p.wait('v', 'dg')
    ds = k.pb[0]
    for c in range(16):
        for g in range(8):
            vcol = slice(c * 1024 + g * 128, c * 1024 + g * 128 + 128)
            pe_tok = p.emit('t', lambda e, vcol=vcol: e.matmul(
                ds[:, 0:128], ktok[:, vcol], Vtm[:, vcol], start=True, stop=True), inc='pe')
            p.wait('v', 'pe', pe_tok)
            def upd(e, g=g):
                e.scalar_tensor_tensor(Sf32[0:64, g * 128:g * 128 + 64],
                                       Sf32[0:64, g * 128:g * 128 + 64],
                                       sdec[0:64, g:g + 1], ds[0:64, 0:64],
                                       ALU.mult, ALU.add)
                return e.scalar_tensor_tensor(Sf32[64:128, g * 128 + 64:g * 128 + 128],
                                              Sf32[64:128, g * 128 + 64:g * 128 + 128],
                                              sdec[64:128, g:g + 1], ds[64:128, 64:128],
                                              ALU.mult, ALU.add)
            dv_tok = p.emit('v', upd, inc='dv')
            p.wait('t', 'dv', dv_tok)
    p.wait('y', 'dv')
    p.dma('y', o_st, Sf32[:])
    p.wait('y', 'dy')
    p.finalize()
    return nc


def build_l3b():
    nc = bass.Bass("TRN2", target_bir_lowering=False, detect_race_conditions=False)
    din = lambda name, shape, dt=F32: nc.dram_tensor(name, shape, dt, kind="ExternalInput").ap()
    a_qg = din("qg", [D, T], BF16)
    a_kg = din("kg", [D, T], BF16)
    a_kt = din("kt", [T, D], BF16)
    a_v = din("v", [T, D], BF16)
    a_dt = din("dtc", [128, 2048])
    a_qdec = din("qdec", [128, 1024])
    a_sdec = din("sdec", [128, 8])
    a_stin = din("stin", [128, 1024])
    a_gam = din("gam", [128, 8]); a_bet = din("bet", [128, 8])
    a_wo = din("wo", [D, D])
    a_ones2 = din("ones2", [128, 2], BF16); a_E2 = din("E2", [2, 128])
    o_s = nc.dram_tensor("o_s", [D, T], BF16, kind="ExternalOutput").ap()
    p = P(nc)
    k = alloc_common(nc, p)
    A = lambda nm, shape, dt=BF16: nc.alloc_sbuf_tensor(nm, shape, dt)
    Qg = A("Qgs", [128, 8 * T]); Kg = A("Kgs", [128, 8 * T])
    ktok = A("ktoks", [128, 16384]); Vtm = A("Vtms", [128, 16384])
    RTg = A("RTgs", [128, 8 * T])
    dtc = A("dtcs", [128, 2048], F32); qdec = A("qdecs", [128, 1024], F32)
    sdec = A("sdecs", [128, 8], F32)
    Sf32 = A("Sf32", [128, 1024], F32); S2b = A("S2bs", [128, 1024])
    gam = A("gams", [128, 8], F32); bet = A("bets", [128, 8], F32)
    qscr = A("qscrs", [128, 128]); atA = A("atAs", [128, 128]); atB = A("atBs", [128, 128])
    sqscr = A("sqscrs", [128, 512])
    ones2 = A("ones2s", [128, 2]); E2 = A("E2s", [2, 128], F32)
    p.dma('g', Qg[:], rearr_fm(a_qg))
    p.dma('g', Kg[:], rearr_fm(a_kg))
    p.dma('g', ktok[:], rearr_tm(a_kt))
    p.dma('g', Vtm[:], rearr_tm(a_v))
    p.dma('g', dtc[:], a_dt)
    p.dma('g', qdec[:], a_qdec)
    p.dma('g', sdec[:], a_sdec)
    p.dma('g', Sf32[:], a_stin)
    p.dma('g', gam[:], a_gam)
    p.dma('g', bet[:], a_bet)
    p.dma('g', ones2[:], a_ones2)
    p.dma('g', E2[:], a_E2)
    p.emit('v', lambda e: e.memset(S2b[:], 0.0), inc='dv')
    p.wait('t', 'dg'); p.wait('v', 'dg'); p.wait('s', 'dg'); p.wait('t', 'dv')
    emit_retention(k, Qg, Kg, ktok, Vtm, dtc, qdec, sdec, Sf32, S2b, RTg,
                   qscr, atA, atB)
    wo = ktok[:, 0:8192]
    p.wait('g', 'pe')
    p.dma('g', wo, rearr_fm(a_wo))
    emit_gnorm(k, RTg, RTg, ones2, E2, gam, bet, sqscr)
    p.wait('t', 'dv'); p.wait('t', 'dg')
    emit_proj_fm(k, wo, RTg, Qg)
    p.wait('y', 'ac')
    p.dma('y', rearr_fm(o_s), Qg[:])
    p.wait('y', 'dy')
    p.finalize()
    return nc


def build_l4():
    nc = bass.Bass("TRN2", target_bir_lowering=False, detect_race_conditions=False)
    din = lambda name, shape, dt=F32: nc.dram_tensor(name, shape, dt, kind="ExternalInput").ap()
    a_xn = din("xnT", [D, T], BF16)
    a_q1 = din("q1", [D, 4096], I32)
    a_sc1 = din("sc1", [8, 4096])
    a_b1 = din("b1", [128, 32])
    a_lg = din("lg", [128, 8]); a_lb = din("lb", [128, 8])
    o_y1 = nc.dram_tensor("o_y1", [4096, T], BF16, kind="ExternalOutput").ap()
    p = P(nc)
    k = alloc_common(nc, p)
    A = lambda nm, shape, dt=BF16: nc.alloc_sbuf_tensor(nm, shape, dt)
    xnT = A("xnTs", [128, 8 * T]); xsq = A("xsqs", [128, 8 * T])
    w1 = A("w1s", [128, 8 * 4096])
    scaleB = A("scaleBs", [128, 4096], F32)
    b1f = A("b1fs", [128, 32], F32)
    lg = A("lgs", [128, 8], F32); lb = A("lbs", [128, 8], F32)
    y1T = A("y1Ts", [128, 32 * 512])
    p.dma('g', xnT[:], rearr_fm(a_xn))
    p.dma('g', b1f[:], a_b1)
    p.dma('g', lg[:], a_lg)
    p.dma('g', lb[:], a_lb)
    emit_dequant(k, a_q1, w1, a_sc1, scaleB, nk=8, nn=4096)
    p.wait('t', 'dg'); p.wait('v', 'dg'); p.wait('s', 'dg')
    emit_ln_fm(k, xnT, xsq, xnT, gamma_ap=lg, beta_ap=lb)
    p.wait('t', 'dv')
    for tc in range(4):
        emit_ffn_fc1(k, w1, xnT, b1f, y1T, tc=tc)
        p.wait('y', 'ac')
        p.dma('y', o_y1.rearrange("(c p) n -> p c n", p=128)[:, :, 512 * tc:512 * (tc + 1)], y1T[:])
        p.wait('s', 'dy')
    p.wait('y', 'dy')
    p.finalize()
    return nc


def build_l5():
    nc = bass.Bass("TRN2", target_bir_lowering=False, detect_race_conditions=False)
    din = lambda name, shape, dt=F32: nc.dram_tensor(name, shape, dt, kind="ExternalInput").ap()
    a_xn = din("xnT", [D, T], BF16)
    a_y1 = din("y1", [4096, T], BF16)
    a_q2 = din("q2", [4096, 1024], I32)
    a_sc2 = din("sc2", [32, 1024])
    a_b2 = din("b2", [128, 8])
    o_g = nc.dram_tensor("o_g", [D, T], BF16, kind="ExternalOutput").ap()
    p = P(nc)
    k = alloc_common(nc, p)
    A = lambda nm, shape, dt=BF16: nc.alloc_sbuf_tensor(nm, shape, dt)
    xnT = A("xnTs", [128, 8 * T])
    w2 = A("w2s", [128, 32 * 1024])
    scaleB = A("scaleBs", [128, 1024], F32)
    b2h = A("b2hs", [128, 8], F32)
    y1T = A("y1Ts", [128, 32 * 512])
    gT = A("gTs", [128, 8 * T])
    p.dma('g', xnT[:], rearr_fm(a_xn))
    p.dma('g', b2h[:], a_b2)
    emit_dequant(k, a_q2, w2, a_sc2, scaleB, nk=32, nn=1024)
    p.wait('v', 'dg')
    p.emit('v', lambda e: e.tensor_scalar_mul(b2h[:], b2h[:], 0.5), inc='dv')
    p.wait('t', 'dg'); p.wait('s', 'dv'); p.wait('t', 'dv')
    for tc in range(4):
        p.wait('g', 'pe')
        p.dma('g', y1T[:], a_y1.rearrange("(c p) n -> p c n", p=128)[:, :, 512 * tc:512 * (tc + 1)])
        p.wait('t', 'dg')
        emit_ffn_fc2(k, w2, y1T, b2h, xnT, gT, tc=tc)
    p.wait('y', 'ac')
    p.dma('y', rearr_fm(o_g), gT[:])
    p.wait('y', 'dy')
    p.finalize()
    return nc


def build_l6():
    nc = bass.Bass("TRN2", target_bir_lowering=False, detect_race_conditions=False)
    din = lambda name, shape, dt=F32: nc.dram_tensor(name, shape, dt, kind="ExternalInput").ap()
    a_x = din("xT", [D, T])
    a_s = din("sT", [D, T], BF16)
    a_g = din("gT", [D, T], BF16)
    o_x1 = nc.dram_tensor("o_x1", [D, T], F32, kind="ExternalOutput").ap()
    o_xn1 = nc.dram_tensor("o_xn1", [D, T], BF16, kind="ExternalOutput").ap()
    o_m = nc.dram_tensor("o_m", [1, T], F32, kind="ExternalOutput").ap()
    o_r = nc.dram_tensor("o_r", [1, T], F32, kind="ExternalOutput").ap()
    p = P(nc)
    k = alloc_common(nc, p)
    A = lambda nm, shape, dt=BF16: nc.alloc_sbuf_tensor(nm, shape, dt)
    sT = A("sTs", [128, 8 * T]); gT = A("gTs", [128, 8 * T])
    x1b = A("x1bs", [128, 8 * T]); xn1 = A("xn1s", [128, 8 * T])
    xr = A("xrs", [128, 2 * T], F32)
    x1f = A("x1fs", [128, 2 * T], F32)
    p.dma('g', sT[:], rearr_fm(a_s))
    p.dma('g', gT[:], rearr_fm(a_g))
    for c in range(8):
        rr = (c % 2)
        p.wait('y', 'dv', p.tok('dv'))
        p.dma('y', xr[:, rr * T:(rr + 1) * T], rearr_fm(a_x)[:, c:c + 1, :])
        p.wait('v', 'dy'); p.wait('v', 'dg')
        def comb(e, c=c, rr=rr):
            e.tensor_mul(x1f[:, rr * T:(rr + 1) * T], sT[:, c * T:(c + 1) * T],
                         gT[:, c * T:(c + 1) * T])
            e.tensor_add(x1f[:, rr * T:(rr + 1) * T], x1f[:, rr * T:(rr + 1) * T],
                         xr[:, rr * T:(rr + 1) * T])
            return e.tensor_copy(x1b[:, c * T:(c + 1) * T], x1f[:, rr * T:(rr + 1) * T])
        dv_tok = p.emit('v', comb, inc='dv')
        p.wait('y', 'dv', dv_tok)
        p.dma('y', rearr_fm(o_x1)[:, c:c + 1, :], x1f[:, rr * T:(rr + 1) * T])
    msb = nc.alloc_sbuf_tensor("msb", [1, T], F32)
    rsb = nc.alloc_sbuf_tensor("rsb", [1, T], F32)
    emit_ln_fm(k, x1b, gT, xn1, m_out=msb, r_out=rsb)
    p.wait('y', 'dv')
    p.dma('y', rearr_fm(o_xn1), xn1[:])
    p.dma('y', o_m, msb[:])
    p.dma('y', o_r, rsb[:])
    p.wait('y', 'dy')
    p.finalize()
    return nc

def build_l7():
    nc = bass.Bass("TRN2", target_bir_lowering=False, detect_race_conditions=False)
    din = lambda name, shape, dt=F32: nc.dram_tensor(name, shape, dt, kind="ExternalInput").ap()
    a_keysT = din("keysT", [D, S])
    a_valsT = din("valsT", [D, S])
    a_wk = din("wkT", [D, 512]); a_wv = din("wvT", [D, 512])
    a_bk = din("bk", [128, 4]); a_bv = din("bv", [1, 512])
    o_kT = nc.dram_tensor("o_kT", [512, S], BF16, kind="ExternalOutput").ap()
    o_v65 = nc.dram_tensor("o_v65", [S, 528], BF16, kind="ExternalOutput").ap()
    p = P(nc)
    k = alloc_common(nc, p)
    A = lambda nm, shape, dt=BF16: nc.alloc_sbuf_tensor(nm, shape, dt)
    kb = A("kbs", [128, 8 * S])
    wk = A("wks", [128, 8 * 512]); wv = A("wvs", [128, 8 * 512])
    bkf = A("bkfs", [128, 4], F32)
    bvB = A("bvBs", [128, 512], F32)
    kT = A("kTs", [128, 4 * S])
    v65 = A("v65s", [128, 32 * 528])
    p.dma('g', kb[:], rearr_fm(a_keysT))
    p.dma('g', wk[:], rearr_fm(a_wk))
    p.dma('g', wv[:], rearr_fm(a_wv))
    p.dma('g', bkf[:], a_bk)
    p.dma('g', bvB[:], a_bv.broadcast_to((128, 512)))
    p.wait('t', 'dg'); p.wait('v', 'dg')
    emit_proj_fm(k, wk, kb, kT, nchunk_out=4, tlen=S, evac='v', bias_ap=bkf, relu=True)
    p.wait('y', 'dv')
    p.dma('y', rearr_fm(o_kT), kT[:])
    p.wait('g', 'pe')
    p.dma('g', kb[:], rearr_fm(a_valsT))
    p.emit('v', lambda e: e.memset(v65[:], 0.0))
    v65r = v65[:].rearrange("p (n h u) -> p n h u", n=32, h=8, u=66)
    p.emit('v', lambda e: e.memset(v65r[:, :, :, 64:65], 1.0), inc='dv')
    p.wait('t', 'dg')
    for tt in range(32):
        bank = k.pb[tt % 2]
        def mms(e, tt=tt, bank=bank):
            for dc in range(8):
                i = e.matmul(bank[:, 0:512], kb[:, dc * S + tt * 128: dc * S + tt * 128 + 128],
                             wv[:, dc * 512:(dc + 1) * 512], start=(dc == 0), stop=(dc == 7))
            return i
        pe_tok = p.emit('t', mms, inc='pe')
        p.wait('v', 'pe', pe_tok)
        def evac(e, tt=tt, bank=bank):
            e.tensor_add(k.tscr[:, 0:512], bank[:, 0:512], bvB[:])
            ov = v65[:, tt * 528:(tt + 1) * 528].rearrange("p (h u) -> p h u", h=8, u=66)[:, :, 0:64]
            return e.tensor_scalar_max(ov, k.tscr[:, 0:512].rearrange("p (h u) -> p h u", h=8, u=64), 0.0)
        dv_tok = p.emit('v', evac, inc='dv')
        p.wait('t', 'dv', dv_tok)
    p.wait('y', 'dv')
    p.dma('y', rearr_tm(o_v65), v65[:])
    p.wait('y', 'dy')
    p.finalize()
    return nc


def build_l8(dbg=False):
    nc = bass.Bass("TRN2", target_bir_lowering=False, detect_race_conditions=False)
    din = lambda name, shape, dt=F32: nc.dram_tensor(name, shape, dt, kind="ExternalInput").ap()
    a_xn1 = din("xn1T", [D, S], BF16)
    a_xn1own = din("xn1own", [D, T], BF16)
    a_wq = din("wqT", [D, 512]); a_bq = din("bq", [128, 4])
    a_kT = din("kT", [512, S], BF16)
    a_v65 = din("v65", [S, 528], BF16)
    a_id = din("ident", [128, 128], BF16)
    o_s1 = nc.dram_tensor("o_s1", [D, T], BF16, kind="ExternalOutput").ap()
    if dbg:
        o_pcop = nc.dram_tensor("o_pcop", [128, 528], F32, kind="ExternalOutput").ap()
        o_rec = nc.dram_tensor("o_rec", [128, 8], F32, kind="ExternalOutput").ap()
        o_avs = nc.dram_tensor("o_avs", [65, 1024], F32, kind="ExternalOutput").ap()
    p = P(nc)
    A = (lambda nm, shape, dt=BF16: nc.alloc_sbuf_tensor(nm, shape, dt))
    k = type('K', (), {})(); k.nc, k.p = nc, p
    psc = [nc.alloc_psum_tensor(f"psc{i}", [128, 1024], F32) for i in range(2)]
    avd = nc.alloc_psum_tensor("avd", [65, 1024], F32)
    pbh = nc.alloc_psum_tensor("pbh", [128, 1024], BF16)
    xb = A("xbs", [128, 8 * S])
    xo = A("xos", [128, 8 * T])
    qT = A("qTs", [128, 4 * S])
    wq = A("wqs", [128, 8 * 512]); bqf = A("bqfs", [128, 4], F32)
    s1T = A("s1Ts", [128, 8 * T])
    kts = A("ktss", [128, 2 * S])
    vs = A("vss", [128, 2 * 32 * 66])
    avs = A("avss", [65, 1024])
    pcop = A("pcops", [128, 528])
    rec = A("recs", [128, 8], F32)
    k.ident = A("idents", [128, 128])
    p.dma('g', k.ident[:], a_id)
    p.dma('g', xb[:], rearr_fm(a_xn1))
    p.dma('g', xo[:], rearr_fm(a_xn1own))
    p.dma('g', wq[:], rearr_fm(a_wq))
    p.dma('g', bqf[:], a_bq)
    p.wait('t', 'dg'); p.wait('v', 'dg')
    NS = 512
    for cch in range(4):
        for ts in range(S // NS):
            bank = psc[ts % 2]
            def mms(e, cch=cch, ts=ts, bank=bank):
                for dc in range(8):
                    i = e.matmul(bank[:, 0:NS],
                                 wq[:, dc * 512 + cch * 128: dc * 512 + cch * 128 + 128],
                                 xb[:, dc * S + ts * NS: dc * S + ts * NS + NS],
                                 start=(dc == 0), stop=(dc == 7))
                return i
            pe_tok = p.emit('t', mms, inc='pe')
            p.wait('v', 'pe', pe_tok)
            osl = qT[:, cch * S + ts * NS: cch * S + ts * NS + NS]
            tk = p.emit('v', lambda e, bank=bank, osl=osl, cch=cch: e.tensor_scalar(
                osl, bank[:, 0:NS], bqf[:, cch:cch + 1], 0.0, ALU.add, ALU.max), inc='dv')
            p.wait('t', 'dv', tk)
    kload_tok = {}
    vs_tok = {}
    eb6 = A("eb6s", [128, 6 * 1024])
    LAG8 = 4
    exp_tok = [None, None]
    av_ring = [None] * 6
    pending_av = []
    pending_epi = []
    st8 = {'g': 0, 'avs_dv': None}
    s1v = s1T[:].rearrange("p (tc hh u f) -> p tc hh u f", tc=8, hh=8, u=64, f=4)
    xov = xo[:].rearrange("p (tc hh u f) -> p tc hh u f", tc=8, hh=8, u=64, f=4)

    def _emit_epi(h, qb, dv_tok):
        p.wait('t', 'dv', dv_tok)
        def tps8(e):
            for i in range(8):
                e.transpose(pbh[:, 66 * i:66 * i + 65], avs[:, 128 * i:128 * i + 128],
                            k.ident[0:65, 0:65])
            return e.transpose(pbh[:, 700:765], avs[:, 0:128], k.ident[0:65, 0:65])
        pe_tok = p.emit('t', tps8, inc='pe')
        p.wait('v', 'pe', pe_tok)
        def evacp8(e):
            e.reciprocal(rec[:, 0:8], pbh[:, 64:528:66])
            return e.tensor_copy(pcop[:], pbh[:, 0:528])
        dv2 = p.emit('v', evacp8, inc='dv')
        p.wait('t', 'dv', dv2)
        def perm8(e, h=h, qb=qb):
            for tc in range(8):
                osl = s1v[:, tc:tc + 1, h:h + 1, :, qb:qb + 1]
                isl = xov[:, tc:tc + 1, h:h + 1, :, qb:qb + 1]
                ii = e.scalar_tensor_tensor(
                    osl, pcop[:, 66 * tc:66 * tc + 64].rearrange(
                        "p (a u f) -> p a u f", a=1, u=64, f=1),
                    rec[:, tc:tc + 1], isl, ALU.mult, ALU.add)
            return ii
        p.emit('v', perm8, inc='dv')

    def _issue_av(it):
        sl, ac_tok, h_, qb_, kt_ = it
        if kt_ == 0:
            if st8['avs_dv'] is not None:
                p.wait('t', 'dv', st8['avs_dv'])
            p.wait('t', 'dg', vs_tok[h_])
        p.wait('t', 'ac', ac_tok)
        vsv_ = vs[:, (h_ % 2) * 2112:(h_ % 2 + 1) * 2112]
        ebl = (sl % 6) * 1024
        def av8(e, kt=kt_, vsv=vsv_, ebl=ebl):
            e.matmul(avd[:, 0:512], vsv[:, kt * 66:kt * 66 + 65],
                     eb6[:, ebl:ebl + 512], start=(kt == 0), stop=(kt == 31))
            return e.matmul(avd[:, 512:1024], vsv[:, kt * 66:kt * 66 + 65],
                            eb6[:, ebl + 512:ebl + 1024], start=(kt == 0), stop=(kt == 31))
        tokp = p.emit('t', av8, inc='pe')
        av_ring[sl % 6] = tokp
        if kt_ == 31:
            p.wait('v', 'pe', tokp)
            dvc = p.emit('v', lambda e: e.tensor_copy(avs[:], avd[:]), inc='dv')
            st8['avs_dv'] = dvc
            pending_epi.append((st8['g'] + 3, h_, qb_, dvc))

    for h in range(8):
        pair = h // 2
        if h % 2 == 0:
            p.wait('g', 'pe')
            kload_tok[pair] = p.dma('g', kts[:, (pair % 2) * S:(pair % 2 + 1) * S],
                                    rearr_fm(a_kT)[:, pair:pair + 1, :])
        p.wait('g', 'pe')
        vs_tok[h] = p.dma('g', vs[:, (h % 2) * 2112:(h % 2 + 1) * 2112],
                          rearr_tm(a_v65)[:, :, 66 * h:66 * h + 66])
        krow = slice(64 * (h % 2), 64 * (h % 2) + 64)
        ktsv = kts[:, (pair % 2) * S:(pair % 2 + 1) * S]
        for qb in range(4):
            for kt in range(32):
                g8 = st8['g']
                while pending_epi and pending_epi[0][0] <= g8:
                    _, eh, eqb, etok = pending_epi.pop(0)
                    _emit_epi(eh, eqb, etok)
                if h % 2 == 0 and qb == 0 and kt == 0:
                    p.wait('t', 'dg', kload_tok[pair])
                if exp_tok[g8 % 2] is not None:
                    p.wait('t', 'ac', exp_tok[g8 % 2])
                def sc8_(e, kt=kt, g8=g8, krow=krow, ktsv=ktsv, qb=qb, h=h):
                    bank = psc[g8 % 2]
                    e.matmul(bank[:, 0:512], ktsv[krow, kt * 128:(kt + 1) * 128],
                             qT[krow, (h // 2) * S + qb * 1024: (h // 2) * S + qb * 1024 + 512],
                             start=True, stop=True)
                    return e.matmul(bank[:, 512:1024], ktsv[krow, kt * 128:(kt + 1) * 128],
                                    qT[krow, (h // 2) * S + qb * 1024 + 512: (h // 2) * S + qb * 1024 + 1024],
                                    start=True, stop=True)
                pe_tok = p.emit('t', sc8_, inc='pe')
                p.wait('s', 'pe', pe_tok)
                if av_ring[g8 % 6] is not None:
                    p.wait('s', 'pe', av_ring[g8 % 6])
                ac_tok = p.emit('s', lambda e, g8=g8: e.activation(
                    eb6[:, (g8 % 6) * 1024:(g8 % 6 + 1) * 1024], psc[g8 % 2][:], AF.Exp,
                    scale=0.125), inc='ac')
                exp_tok[g8 % 2] = ac_tok
                pending_av.append((g8, ac_tok, h, qb, kt))
                while pending_av and pending_av[0][0] <= g8 - LAG8:
                    _issue_av(pending_av.pop(0))
                st8['g'] = g8 + 1
    while pending_av:
        _issue_av(pending_av.pop(0))
    while pending_epi:
        _, eh, eqb, etok = pending_epi.pop(0)
        _emit_epi(eh, eqb, etok)
    p.wait('y', 'dv')
    p.dma('y', rearr_fm(o_s1), s1T[:])
    p.wait('y', 'dy')
    p.finalize()
    return nc


def build_l9(dbg=False):
    nc = bass.Bass("TRN2", target_bir_lowering=False, detect_race_conditions=False)
    din = lambda name, shape, dt=F32: nc.dram_tensor(name, shape, dt, kind="ExternalInput").ap()
    a_x1 = din("x1T", [D, T])
    a_wg = din("wg", [D, 8])
    a_m = din("mrow", [1, T])
    a_rtm = din("r_tm", [128, 16])
    a_negb = din("negbig", [128, 32])
    a_posw = din("posw", [128, 32])

    a_id = din("ident", [128, 128])
    o_oh = nc.dram_tensor("o_oh", [8, T], F32, kind="ExternalOutput").ap()
    o_rk = nc.dram_tensor("o_rk", [8, T], F32, kind="ExternalOutput").ap()
    o_gate = nc.dram_tensor("o_gate", [128, 16], F32, kind="ExternalOutput").ap()
    o_hist = nc.dram_tensor("o_hist", [8, 1], F32, kind="ExternalOutput").ap()
    if dbg:
        o_lgs = nc.dram_tensor("o_lgs", [128, 8], F32, kind="ExternalOutput").ap()
        o_oh0 = nc.dram_tensor("o_oh0", [128, 8], F32, kind="ExternalOutput").ap()
        o_ohd = nc.dram_tensor("o_ohd", [128, 8], F32, kind="ExternalOutput").ap()
        o_sc8 = nc.dram_tensor("o_sc8", [128, 8], F32, kind="ExternalOutput").ap()
        o_praw = nc.dram_tensor("o_praw", [128, 8], F32, kind="ExternalOutput").ap()
        o_ppm = nc.dram_tensor("o_ppm", [128, 8], F32, kind="ExternalOutput").ap()
        o_wsum = nc.dram_tensor("o_wsum", [1, 8], F32, kind="ExternalOutput").ap()
        o_lgs1 = nc.dram_tensor("o_lgs1", [128, 8], F32, kind="ExternalOutput").ap()
    p = P(nc)
    k = alloc_common(nc, p)
    A = lambda nm, shape, dt=F32: nc.alloc_sbuf_tensor(nm, shape, dt)
    x1f = A("x1fs", [128, 8 * T])
    wg = A("wgs", [128, 8 * 8])
    msb = A("msbs", [1, T]); rtm = A("rtms", [128, 16])
    oh0 = A("oh0s", [128, 32]); sc8 = A("sc8s", [128, 32])
    t_pm = A("t_pms", [128, 32]); t_raw = A("t_raws", [128, 32]); t_d = A("t_ds", [128, 32])
    zz8 = A("zz8s", [128, 32]); big30 = A("big30s", [128, 32])
    posw = A("posws", [128, 32]); one30 = A("one30s", [128, 32])
    t1 = A("t1s", [128, 32]); t2 = A("t2s", [128, 32]); t3 = A("t3s", [128, 32])
    q8 = A("q8s", [128, 32]); m2 = A("m2s", [128, 32]); nm2 = A("nm2s", [128, 32])
    idf = A("idfs", [128, 128])
    wsum = A("wsums", [1, 8])
    onescol = A("onescols", [128, 1])
    lgs = A("lgss", [128, 32]); el = A("els", [128, 32])
    mx = A("mxs", [128, 32]); nmx = A("nmxs", [128, 32]); sm = A("sms", [128, 32])
    negb = A("negbs", [128, 32])
    oh = A("ohs", [128, 32])
    gate_tm = A("gatetms", [128, 16]); smbuf = A("smbufs", [128, 16])
    ohT = A("ohTs", [8, T]); scanT = A("scanTs", [8, T]); rkT = A("rkTs", [8, T])
    p.dma('g', x1f[:], rearr_fm(a_x1))
    p.dma('g', wg[:], rearr_fm(a_wg))
    p.dma('g', msb[:], a_m)
    p.dma('g', rtm[:], a_rtm)
    p.dma('g', negb[:], a_negb)
    p.dma('g', posw[:], a_posw)
    p.dma('g', idf[:], a_id)
    p.emit('v', lambda e: e.memset(onescol[:], 1.0))
    p.emit('v', lambda e: e.memset(zz8[:], 0.0))
    p.emit('v', lambda e: e.memset(big30[:], 1e30))
    p.emit('v', lambda e: e.memset(one30[:], 1.0))
    p.emit('v', lambda e: e.memset(k.pb[0][:, 0:32], 0.0))
    p.emit('v', lambda e: e.memset(k.pb[1][:, 0:32], 0.0), inc='dv')
    p.wait('t', 'dg'); p.wait('v', 'dg'); p.wait('t', 'dv')
    def mmw(e):
        for dc in range(8):
            i = e.matmul(k.pb[2][0:1, 0:8], onescol[:, 0:1],
                         wg[:, dc * 8:(dc + 1) * 8], start=(dc == 0), stop=(dc == 7))
        return i
    pe_tok = p.emit('t', mmw, inc='pe')
    p.wait('v', 'pe', pe_tok)
    dv_tok = p.emit('v', lambda e: e.tensor_copy(wsum[:], k.pb[2][0:1, 0:8]), inc='dv')
    p.wait('t', 'dv', dv_tok)
    for tc in range(16):
        def mmr(e, tc=tc):
            for dc in range(8):
                e.matmul(k.pb[0][:, 0:8], x1f[:, dc * T + tc * 128: dc * T + tc * 128 + 128],
                         wg[:, dc * 8:(dc + 1) * 8], start=(dc == 0), stop=(dc == 7))
            return e.matmul(k.pb[1][:, 0:8], msb[0:1, tc * 128:(tc + 1) * 128], wsum[:],
                     start=True, stop=True)
        pe_tok = p.emit('t', mmr, inc='pe')
        p.wait('v', 'pe', pe_tok)
        if dbg and tc == 0:
            prawsb = nc.alloc_sbuf_tensor("prawsb", [128, 8], F32)
            ppmsb = nc.alloc_sbuf_tensor("ppmsb", [128, 8], F32)
            p.emit('v', lambda e: e.tensor_copy(prawsb[:], k.pb[0][:, 0:8]))
            dvt = p.emit('v', lambda e: e.tensor_copy(ppmsb[:], k.pb[1][:, 0:8]), inc='dv')
            p.wait('y', 'dv', dvt)
            p.dma('y', o_praw, prawsb[:])
            p.dma('y', o_ppm, ppmsb[:])
            p.dma('y', o_wsum, wsum[:])
            p.wait('v', 'dy')
        def gdve(e, tc=tc):
            e.tensor_copy(t_pm[:], k.pb[1][:, 0:32])
            e.tensor_copy(t_raw[:], k.pb[0][:, 0:32])
            e.tensor_sub(t_d[:], t_raw[:], t_pm[:])
            e.scalar_tensor_tensor(lgs[:], t_d[:], rtm[:, tc:tc + 1], negb[:],
                                   ALU.mult, ALU.add)
            e.reduce_max(nmx[:, 0:1], lgs[:], AX.X, negate=True)
            e.tensor_copy(t_pm[:], lgs[:])
            return e.tensor_copy(t_raw[:], lgs[:])
        dv_tok = p.emit('v', gdve, inc='dv')
        if dbg and tc == 0:
            p.wait('y', 'dv', dv_tok)
            p.dma('y', o_lgs1, lgs[:, 0:8])
            p.wait('v', 'dy')
        p.wait('s', 'dv', dv_tok)
        ac_tok = p.emit('s', lambda e: e.activation(el[:], lgs[:], AF.Exp, bias=nmx[:, 0:1]), inc='ac')
        p.wait('v', 'ac', ac_tok)
        def gdve2(e, tc=tc):
            e.reduce_sum(smbuf[:, tc:tc + 1], el[:], AX.X)
            e.scalar_tensor_tensor(t1[:], lgs[:], nmx[:, 0:1], big30[:],
                                   ALU.add, ALU.mult)
            e.tensor_add(t2[:], t1[:], one30[:])
            e.tensor_max(t3[:], t2[:], zz8[:])
            e.tensor_tensor(oh0[:], t3[:], one30[:], ALU.min)
            e.tensor_mul(q8[:], oh0[:], posw[:])
            e.reduce_max(nm2[:, 0:1], q8[:], AX.X, negate=True)
            e.tensor_copy(t2[:], q8[:])
            e.tensor_copy(t3[:], q8[:])
            e.scalar_tensor_tensor(t1[:], q8[:], nm2[:, 0:1], big30[:],
                                   ALU.add, ALU.mult)
            e.tensor_add(t2[:], t1[:], one30[:])
            e.tensor_max(t3[:], t2[:], zz8[:])
            return e.tensor_tensor(oh[:], t3[:], one30[:], ALU.min)
        dv_tok = p.emit('v', gdve2, inc='dv')
        if dbg and tc == 0:
            p.wait('y', 'dv', dv_tok)
            p.dma('y', o_lgs, lgs[:, 0:8])
            p.dma('y', o_oh0, oh0[:, 0:8])
            p.dma('y', o_ohd, oh[:, 0:8])
            p.dma('y', o_sc8, sc8[:, 0:8])
            p.wait('v', 'dy')
        p.wait('t', 'dv', dv_tok)
        def tpo(e, tc=tc):
            e.transpose(k.pb[3][0:32, 0:128], oh[:], idf[:])
            return e.transpose(k.pb[5][0:32, 0:128], oh[:], idf[:])
        pe_tok = p.emit('t', tpo, inc='pe')
        p.wait('v', 'pe', pe_tok)
        dv_tok = p.emit('v', lambda e, tc=tc: e.tensor_copy(ohT[0:8, tc * 128:(tc + 1) * 128],
                                                            k.pb[3][0:8, 0:128]), inc='dv')
        p.wait('t', 'dv', dv_tok)
    def scan(e):
        e.reciprocal(gate_tm[:], smbuf[:])
        e.tensor_tensor_scan(scanT[0:8, :], ohT[0:8, :], ohT[0:8, :], 0.0, ALU.add, ALU.bypass)
        return e.tensor_sub(rkT[0:8, :], scanT[0:8, :], ohT[0:8, :])
    dv_tok = p.emit('v', scan, inc='dv')
    p.wait('y', 'dv')
    p.dma('y', o_oh, ohT[:])
    p.dma('y', o_rk, rkT[:])
    p.dma('y', o_gate, gate_tm[:])
    p.dma('y', o_hist, scanT[0:8, T - 1:T])
    p.wait('y', 'dy')
    p.finalize()
    return nc



def build_l10():
    nc = bass.Bass("TRN2", target_bir_lowering=False, detect_race_conditions=False)
    din = lambda name, shape, dt=F32: nc.dram_tensor(name, shape, dt, kind="ExternalInput").ap()
    a_gath = din("gath", [8, 8])
    a_mask = din("maskc", [8, 1])
    a_oh = din("ohT", [8, T]); a_rk = din("rkT", [8, T])
    a_gate = din("gate", [128, 16])
    a_xn1 = din("xn1T", [D, T], BF16)
    a_we = din("we", [D, D])
    a_be = din("be", [128, 8])
    a_s1 = din("s1T", [D, T], BF16)
    a_x1 = din("x1T", [D, T])
    a_id = din("ident", [128, 128])
    o_out = nc.dram_tensor("o_out", [D, T], F32, kind="ExternalOutput").ap()
    p = P(nc)
    k = alloc_common(nc, p)
    A = lambda nm, shape, dt=F32: nc.alloc_sbuf_tensor(nm, shape, dt)
    gath = A("gaths", [8, 8]); maskc = A("maskcs", [8, 1])
    ohT = A("ohTs", [8, T]); rkT = A("rkTs", [8, T]); keptT = A("keptTs", [8, T])
    gate_tm = A("gatetms", [128, 16]); smbuf = A("smbufs", [128, 16]); offs = A("offss", [8, 1])
    xn1 = nc.alloc_sbuf_tensor("xn1s", [128, 8 * T], BF16)
    s1T = nc.alloc_sbuf_tensor("s1Ts", [128, 8 * T], BF16)
    we = nc.alloc_sbuf_tensor("wes", [128, 8 * 1024], BF16)
    bef = A("befs", [128, 8])
    idf = A("idfs", [128, 128])
    kcol = A("kcols", [128, 1]); gk = A("gks", [128, 16]); kcolb = A("kcolbs", [128, 16])
    gkrow = A("gkrows", [1, T]); gkB = A("gkBs", [128, T])
    x1r = A("x1rs", [128, 2 * T])
    sil = nc.alloc_sbuf_tensor("sils", [128, 512], BF16)
    t2 = A("t2s", [128, 512])
    p.dma('g', gath[:], a_gath)
    p.dma('g', maskc[:], a_mask)
    p.dma('g', ohT[:], a_oh)
    p.dma('g', rkT[:], a_rk)
    p.dma('g', gate_tm[:], a_gate)
    p.dma('g', xn1[:], rearr_fm(a_xn1))
    p.dma('g', we[:], rearr_fm(a_we))
    p.dma('g', bef[:], a_be)
    p.dma('g', idf[:], a_id)
    p.dma('g', s1T[:], rearr_fm(a_s1))
    p.wait('t', 'dg'); p.wait('v', 'dg')
    pe_tok = p.emit('t', lambda e: e.matmul(k.pb[0][0:8, 0:1], gath[0:8, :], maskc[0:8, 0:1],
                                            start=True, stop=True), inc='pe')
    p.wait('v', 'pe', pe_tok)
    def kdve(e):
        e.tensor_copy(offs[:], k.pb[0][0:8, 0:1])
        e.tensor_copy(keptT[0:8, :], rkT[0:8, :])
        e.tensor_copy(keptT[0:8, :], rkT[0:8, :])
        e.tensor_scalar(keptT[0:8, :], rkT[0:8, :], offs[0:8, 0:1], -1.0, ALU.add, ALU.mult)
        e.tensor_scalar(keptT[0:8, :], keptT[0:8, :], 100.0, 0.0, ALU.add, ALU.max)
        e.tensor_scalar(keptT[0:8, :], keptT[0:8, :], 1.0, 0.0, ALU.min, ALU.add)
        return e.tensor_mul(keptT[0:8, :], keptT[0:8, :], ohT[0:8, :])
    dv_tok = p.emit('v', kdve, inc='dv')
    p.wait('t', 'dv', dv_tok)
    for tc in range(16):
        def tpk(e, tc=tc):
            e.transpose(k.pb[1][0:128, 0:8], keptT[0:8, tc * 128:(tc + 1) * 128], idf[0:8, 0:8])
            return e.transpose(k.pb[5][0:128, 0:8], keptT[0:8, tc * 128:(tc + 1) * 128], idf[0:8, 0:8])
        pe_tok = p.emit('t', tpk, inc='pe')
        p.wait('v', 'pe', pe_tok)
        dv_tok = p.emit('v', lambda e, tc=tc: e.reduce_sum(kcolb[:, tc:tc + 1],
                                                           k.pb[1][0:128, 0:8], AX.X), inc='dv')
        p.wait('t', 'dv', dv_tok)
    dv_tok = p.emit('v', lambda e: e.tensor_mul(gk[:], kcolb[:], gate_tm[:]), inc='dv')
    p.wait('t', 'dv', dv_tok)
    for tc in range(16):
        def tpg(e, tc=tc):
            e.transpose(k.pb[1][0:1, 0:128], gk[:, tc:tc + 1], idf[:, :])
            return e.transpose(k.pb[5][0:1, 0:128], gk[:, tc:tc + 1], idf[:, :])
        pe_tok = p.emit('t', tpg, inc='pe')
        p.wait('v', 'pe', pe_tok)
        dv_tok = p.emit('v', lambda e, tc=tc: e.tensor_copy(
            gkrow[0:1, tc * 128:(tc + 1) * 128], k.pb[1][0:1, 0:128]), inc='dv')
        p.wait('t', 'dv', dv_tok)
    for ts in range(4):
        pe_tok = p.emit('t', lambda e, ts=ts: e.matmul(
            k.pb[2][:, 0:512], k.ones_f1[0:1, :], gkrow[0:1, ts * 512:(ts + 1) * 512],
            start=True, stop=True), inc='pe')
        p.wait('v', 'pe', pe_tok)
        dv_tok = p.emit('v', lambda e, ts=ts: e.tensor_copy(
            gkB[:, ts * 512:(ts + 1) * 512], k.pb[2][:, 0:512]), inc='dv')
        p.wait('t', 'dv', dv_tok)
    for cch in range(8):
        p.wait('y', 'dv', p.tok('dv'))
        x1tok = p.dma('y', x1r[:, (cch % 2) * T:(cch % 2 + 1) * T], rearr_fm(a_x1)[:, cch:cch + 1, :])
        for ts in range(4):
            bank = k.pb[3 + (ts % 2)]
            def mms(e, cch=cch, ts=ts, bank=bank):
                for dc in range(8):
                    i = e.matmul(bank[:, 0:512],
                                 we[:, dc * 1024 + cch * 128: dc * 1024 + cch * 128 + 128],
                                 xn1[:, dc * T + ts * 512: dc * T + ts * 512 + 512],
                                 start=(dc == 0), stop=(dc == 7))
                return i
            pe_tok = p.emit('t', mms, inc='pe')
            p.wait('v', 'pe', pe_tok)
            sl = slice(cch * T + ts * 512, cch * T + ts * 512 + 512)
            def md1(e, cch=cch, ts=ts, bank=bank, sl=sl):
                return e.scalar_tensor_tensor(k.tscr[:, 0:512], bank[:, 0:512],
                                              bef[:, cch:cch + 1],
                                              gkB[:, ts * 512:(ts + 1) * 512],
                                              ALU.add, ALU.mult)
            dv_tok = p.emit('v', md1, inc='dv')
            p.wait('s', 'dv', dv_tok)
            ac_tok = p.emit('s', lambda e: e.activation(sil[:], k.tscr[:, 0:512], AF.Silu), inc='ac')
            p.wait('v', 'ac', ac_tok)
            p.wait('v', 'dy', x1tok)
            def md2(e, cch=cch, ts=ts, sl=sl):
                e.tensor_mul(t2[:], sil[:], s1T[:, sl])
                return e.tensor_add(t2[:], t2[:],
                                    x1r[:, (cch % 2) * T + ts * 512: (cch % 2) * T + ts * 512 + 512])
            dv_tok = p.emit('v', md2, inc='dv')
            p.wait('y', 'dv', dv_tok)
            p.dma('y', rearr_fm(o_out)[:, cch:cch + 1, ts * 512:(ts + 1) * 512], t2[:])
            p.wait('t', 'dy')
    p.wait('y', 'dy')
    p.finalize()
    return nc



def _hostfm_rows(arr):
    return np.ascontiguousarray(arr)

def _bf(a):
    import ml_dtypes
    return np.asarray(a).astype(ml_dtypes.bfloat16)

def _run(nc, in_maps):
    res = run_bass_kernel_spmd(nc, in_maps, core_ids=list(range(8)))
    return res.results

_USE_FUSED = True

def _posw():
    w = np.zeros((128, 32), np.float32)
    w[:, 0:8] = (8.0 - np.arange(8, dtype=np.float32))[None, :]
    return w

def kernel(**inp):
    if _USE_FUSED:
        return kernel_fused(inp)
    import ml_dtypes
    TB = make_tables()
    perm = make_perm()
    gm = np.zeros(1024, dtype=np.int64)
    for h in range(H):
        g, hf = h // 2, h % 2
        cr, qq = h // 4, h % 4
        for x1 in range(2):
            for i in range(32):
                gm[128 * g + 64 * hf + 32 * x1 + i] = (cr + 4 * x1) * 128 + 32 * qq + i
    x = np.ascontiguousarray(inp['L0_attn_x'].astype(np.float32))
    cores = [(c // 2, c % 2) for c in range(8)]
    sls = [slice(2048 * hf, 2048 * hf + 2048) for (_, hf) in cores]
    xT = [np.ascontiguousarray(x[b, s_].T) for (b, _), s_ in zip(cores, sls)]

    r = _run(build_l1(), [{"xT": xT[c]} for c in range(8)])
    xnT = [np.asarray(r[c]["oxn"]) for c in range(8)]

    WQpT = np.ascontiguousarray(inp['L0_attn_WQ'].reshape(D, D)[perm].T)
    WKpT = np.ascontiguousarray(inp['L0_attn_WK'].reshape(D, D)[perm].T)
    WVT = np.ascontiguousarray(inp['L0_attn_WV'].reshape(D, D).T)
    ins = []
    for c in range(8):
        s_ = sls[c]
        ins.append({"xnT": xnT[c], "wq": WQpT, "wk": WKpT, "wv": WVT,
                    "cosT": np.ascontiguousarray(TB['COS_T'][:, s_]),
                    "sinT": np.ascontiguousarray(TB['SIN_T'][:, s_]),
                    "cosB": np.ascontiguousarray(TB['COS_BK'][s_]),
                    "sinB": np.ascontiguousarray(TB['SIN_BK'][s_])})
    r = _run(build_l2(), ins)
    q_rot = [np.asarray(r[c]["o_q"]) for c in range(8)]
    k_rot = [np.asarray(r[c]["o_k"]) for c in range(8)]
    ktok = [np.asarray(r[c]["o_kt"]) for c in range(8)]
    vtm = [np.asarray(r[c]["o_v"]) for c in range(8)]

    r = _run(build_l3a(), [{"kt": ktok[c], "v": vtm[c], "sdec": TB['SDECG']}
                           for c in range(8)])
    states = [np.asarray(r[c]["o_st"]) for c in range(8)]

    o2 = np.zeros((128, 2), ml_dtypes.bfloat16); o2[0:64, 0] = 1; o2[64:128, 1] = 1
    E2h = np.zeros((2, 128), np.float32); E2h[0, 0:64] = 1; E2h[1, 64:128] = 1
    gamT = inp['L0_attn_gamma'].reshape(8, 128).T.copy()
    betT = inp['L0_attn_beta'].reshape(8, 128).T.copy()
    ins = []
    for c in range(8):
        b, hf = cores[c]
        stin = states[2 * b] if hf == 1 else np.zeros((128, 1024), np.float32)
        ins.append({"qg": q_rot[c][gm], "kg": k_rot[c][gm],
                    "kt": ktok[c], "v": vtm[c],
                    "dtc": TB['DTcat'], "qdec": TB['QDECG'].astype(np.float32),
                    "sdec": TB['SDECG'], "stin": np.ascontiguousarray(stin),
                    "gam": gamT, "bet": betT, "wo": inp['L0_attn_WO'],
                    "ones2": o2, "E2": E2h})
    r = _run(build_l3b(), ins)
    sT = [np.asarray(r[c]["o_s"]) for c in range(8)]

    ins = [{"xnT": xnT[c], "q1": inp['L0_ffn_fc1_q'], "sc1": inp['L0_ffn_fc1_scales'],
            "b1": inp['L0_ffn_fc1_bias'].reshape(32, 128).T.copy(),
            "lg": inp['L0_ffn_ln_gamma'].reshape(8, 128).T.copy(),
            "lb": inp['L0_ffn_ln_beta'].reshape(8, 128).T.copy()} for c in range(8)]
    r = _run(build_l4(), ins)
    y1 = [np.asarray(r[c]["o_y1"]) for c in range(8)]
    ins = [{"xnT": xnT[c], "y1": y1[c], "q2": inp['L0_ffn_fc2_q'],
            "sc2": inp['L0_ffn_fc2_scales'],
            "b2": inp['L0_ffn_fc2_bias'].reshape(8, 128).T.copy()} for c in range(8)]
    r = _run(build_l5(), ins)
    gT = [np.asarray(r[c]["o_g"]) for c in range(8)]

    ins = [{"xT": xT[c], "sT": sT[c], "gT": gT[c]} for c in range(8)]
    r = _run(build_l6(), ins)
    x1T = [np.asarray(r[c]["o_x1"]) for c in range(8)]
    xn1T = [np.asarray(r[c]["o_xn1"]) for c in range(8)]
    mrow = [np.asarray(r[c]["o_m"]) for c in range(8)]
    rrow = [np.asarray(r[c]["o_r"]) for c in range(8)]

    keysT = np.ascontiguousarray(inp['L1_attn_keys'][0].T.astype(np.float32))
    valsT = np.ascontiguousarray(inp['L1_attn_values'][0].T.astype(np.float32))
    ins = []
    for c in range(8):
        hh = cores[c][1]
        ins.append({"keysT": keysT, "valsT": valsT,
                    "wkT": np.ascontiguousarray(inp['L1_attn_Wk_w'][512 * hh:512 * (hh + 1)].T),
                    "wvT": np.ascontiguousarray(inp['L1_attn_Wv_w'][512 * hh:512 * (hh + 1)].T),
                    "bk": inp['L1_attn_Wk_b'][512 * hh:512 * (hh + 1)].reshape(4, 128).T.copy(),
                    "bv": inp['L1_attn_Wv_b'][512 * hh:512 * (hh + 1)].reshape(1, 512)})
    r = _run(build_l7(), ins)
    kTl = [np.asarray(r[c]["o_kT"]) for c in range(8)]
    v65l = [np.asarray(r[c]["o_v65"]) for c in range(8)]

    idb = np.eye(128, dtype=ml_dtypes.bfloat16)
    ins = []
    for c in range(8):
        b, hh = cores[c]
        full = np.concatenate([xn1T[2 * b], xn1T[2 * b + 1]], axis=1)
        ins.append({"xn1T": np.ascontiguousarray(full), "xn1own": xn1T[c],
                    "wqT": np.ascontiguousarray(inp['L1_attn_Wq_w'][512 * hh:512 * (hh + 1)].T),
                    "bq": inp['L1_attn_Wq_b'][512 * hh:512 * (hh + 1)].reshape(4, 128).T.copy(),
                    "kT": kTl[c], "v65": v65l[c], "ident": idb})
    r = _run(build_l8(), ins)
    s1T = [np.asarray(r[c]["o_s1"]) for c in range(8)]

    idf = np.eye(128, dtype=np.float32)
    negb = np.zeros((128, 32), np.float32); negb[:, 8:] = -1e30
    ins = [{"x1T": x1T[c], "wg": inp['L1_ffn_w_gate'].astype(np.float32),
            "mrow": mrow[c], "r_tm": np.ascontiguousarray(rrow[c].reshape(16, 128).T),
            "negbig": negb, "posw": _posw(), "ident": idf} for c in range(8)]
    r = _run(build_l9(), ins)
    ohT = [np.asarray(r[c]["o_oh"]) for c in range(8)]
    rkT = [np.asarray(r[c]["o_rk"]) for c in range(8)]
    gatec = [np.asarray(r[c]["o_gate"]) for c in range(8)]
    hists = np.stack([np.asarray(r[c]["o_hist"]).ravel() for c in range(8)])

    ins = []
    for c in range(8):
        maskc = np.zeros((8, 1), np.float32); maskc[0:c, 0] = 1.0
        ins.append({"gath": hists.astype(np.float32), "maskc": maskc,
                    "ohT": ohT[c], "rkT": rkT[c], "gate": gatec[c],
                    "xn1T": xn1T[c], "we": inp['L1_ffn_w_expert'].astype(np.float32),
                    "be": inp['L1_ffn_b_expert'].reshape(8, 128).T.copy(),
                    "s1T": s1T[c], "x1T": x1T[c], "ident": idf})
    r = _run(build_l10(), ins)
    out = np.zeros((4, 4096, 1024), np.float32)
    for c in range(8):
        b, hf = cores[c]
        out[b, sls[c]] = np.asarray(r[c]["o_out"]).T
    return out


def _barrier(p):
    for e in _ENG_ATTR:
        for s_ in ('pe', 'dv', 'ac', 'gp', 'dy', 'dg', 'ds'):
            p.wait(e, s_)


def build_fused():
    nc = bass.Bass("TRN2", target_bir_lowering=False, detect_race_conditions=False)
    din = lambda name, shape, dt=F32: nc.dram_tensor(name, shape, dt, kind="ExternalInput").ap()
    a_xT = din("xT", [D, T])
    a_wq0 = din("wq0", [D, D]); a_wk0 = din("wk0", [D, D]); a_wv0 = din("wv0", [D, D])
    a_cosT = din("cosT", [512, T]); a_sinT = din("sinT", [512, T])
    a_cosB = din("cosB", [T, 512]); a_sinB = din("sinB", [T, 512])
    a_dt = din("dtc", [128, 2048]); a_qdec = din("qdec", [128, 1024]); a_sdec = din("sdec", [128, 8])
    a_stmask = din("stmask", [128, 8])
    a_gam = din("gam", [128, 8]); a_bet = din("bet", [128, 8])
    a_wo = din("wo", [D, D])
    a_ones2 = din("ones2", [128, 2], BF16); a_E2 = din("E2", [2, 128])
    a_q1 = din("q1", [D, 4096], I32); a_sc1 = din("sc1", [8, 4096]); a_b1 = din("b1", [128, 32])
    a_lg = din("lg", [128, 8]); a_lb = din("lb", [128, 8])
    a_q2 = din("q2", [4096, 1024], I32); a_sc2 = din("sc2", [32, 1024]); a_b2 = din("b2", [128, 8])
    a_keysT = din("keysT", [D, S]); a_valsT = din("valsT", [D, S])
    a_wk1 = din("wk1", [D, 512]); a_wv1 = din("wv1", [D, 512])
    a_bk1 = din("bk1", [128, 4]); a_bv1 = din("bv1", [1, 512])
    a_wq1 = din("wq1", [D, 512]); a_bq1 = din("bq1", [128, 4])
    a_idb = din("identb", [128, 128], BF16); a_idf = din("identf", [128, 128])
    a_wg = din("wg", [D, 8]); a_wgT = din("wgT", [8, D])
    a_negb = din("negbig", [128, 32]); a_posw = din("posw", [128, 32])
    a_maskc = din("maskc", [8, 1])
    a_we = din("we", [D, D]); a_be = din("be", [128, 8])
    o_out = nc.dram_tensor("o_out", [D, T], F32, kind="ExternalOutput").ap()
    d_xn = nc.dram_tensor("d_xn", [D, T], BF16).ap()
    d_q = nc.dram_tensor("d_q", [D, T], BF16).ap()
    d_k = nc.dram_tensor("d_k", [D, T], BF16).ap()
    d_s = nc.dram_tensor("d_s", [D, T], BF16).ap()
    d_y1 = nc.dram_tensor("d_y1", [4096, T], BF16).ap()
    d_x1 = nc.dram_tensor("d_x1", [D, T], F32).ap()
    d_kT = nc.dram_tensor("d_kT", [512, S], BF16).ap()
    d_v65 = nc.dram_tensor("d_v65", [S, 528], BF16).ap()
    cc_st_in = nc.dram_tensor("cc_st_in", [128, 1024], F32)
    cc_st_out = nc.dram_tensor("cc_st_out", [1024, 1024], F32)
    cc_x_in = nc.dram_tensor("cc_x_in", [D, T], BF16)
    cc_x_out = nc.dram_tensor("cc_x_out", [2 * D, T], BF16)
    cc_h_in = nc.dram_tensor("cc_h_in", [1, 8], F32)
    cc_h_out = nc.dram_tensor("cc_h_out", [8, 8], F32)

    p = P(nc)
    psc0 = nc.alloc_psum_tensor("psc0", [128, 1024], F32)
    psc1 = nc.alloc_psum_tensor("psc1", [128, 1024], F32)
    avdt = nc.alloc_psum_tensor("avdt", [128, 1024], F32)
    pbh = nc.alloc_psum_tensor("pbhx", [128, 1024], BF16)
    pb7 = nc.alloc_psum_tensor("pb7", [128, 512], F32)
    k = type('K', (), {})()
    k.nc, k.p = nc, p
    k.pb = [psc0[:, 0:512], psc0[:, 512:1024], psc1[:, 0:512], psc1[:, 512:1024],
            avdt[:, 0:512], avdt[:, 512:1024], pb7[:, 0:512]]
    k.pbh = pbh
    SA = nc.alloc_sbuf_tensor("SA", [128, 16384], BF16)
    SB = nc.alloc_sbuf_tensor("SB", [128, 16384], BF16)
    SC = nc.alloc_sbuf_tensor("SC", [128, 16384], BF16)
    SD = nc.alloc_sbuf_tensor("SD", [128, 16896], BF16)
    F8 = nc.alloc_sbuf_tensor("F8", [128, 32768], BF16)
    H = nc.alloc_sbuf_tensor("H", [128, 1024], F32)
    SBf = SB.bitcast(F32); SDf = SD.bitcast(F32); SCf = SC.bitcast(F32)
    F8f = F8.bitcast(F32)
    k.ones_bf = nc.alloc_sbuf_tensor("ones_bf", [128, 1], BF16)
    k.ones_f1 = nc.alloc_sbuf_tensor("ones_f1", [1, 128], F32)
    k.stat = F8f[0:1, 0:2048]
    k.tscr = H[:, 0:512]
    p.emit('v', lambda e: e.memset(k.ones_bf[:], 1.0))
    p.emit('v', lambda e: e.memset(k.ones_f1[:], 1.0), inc='dv')
    sm_ = lambda nm, shape, dt=F32: nc.alloc_sbuf_tensor(nm, shape, dt)
    gamv = sm_("gamv", [128, 8]); betv = sm_("betv", [128, 8])
    ones2 = sm_("ones2v", [128, 2], BF16); E2 = sm_("E2v", [2, 128])
    sdec = sm_("sdecv", [128, 8]); stmask = sm_("stmaskv", [128, 8])
    b1f = sm_("b1fv", [128, 32]); lgv = sm_("lgv", [128, 8]); lbv = sm_("lbv", [128, 8])
    b2h = sm_("b2hv", [128, 8])
    bkf = sm_("bkfv", [128, 4])
    bqf = sm_("bqfv", [128, 4])
    idb = sm_("idbv", [128, 128], BF16); idf = sm_("idfv", [128, 128])
    m_tm = sm_("m_tmv", [128, 16]); r_tm = sm_("r_tmv", [128, 16])
    wgfm = sm_("wgfmv", [128, 64]); wsumc = sm_("wsumcv", [8, 32])
    wsrow = sm_("wsrowv", [1, 32]); wsB = sm_("wsBv", [128, 8])
    negb = SBf[:, 0:32]; posw = SBf[:, 32:64]
    one30 = SBf[:, 64:96]; zz8 = SBf[:, 96:128]; big30 = SBf[:, 128:160]
    lgs = SBf[:, 160:192]; el = SBf[:, 192:224]
    mnn = SBf[:, 224:256]; pmn = SBf[:, 256:288]
    t1g = SBf[:, 288:320]; t2g = SBf[:, 320:352]; t3g = SBf[:, 352:384]
    oh0 = SBf[:, 384:416]; ohv = SBf[:, 416:448]
    q8 = SBf[:, 448:480]; nm2 = SBf[:, 480:512]
    smbuf = sm_("smbufv", [128, 16]); gate_tm = sm_("gatetmv", [128, 16])
    kcolb = sm_("kcolbv", [128, 16]); gk = sm_("gkv", [128, 16])
    hrow = sm_("hrowv", [1, 32]); gath = sm_("gathv", [8, 8]); maskc = sm_("maskcv", [8, 1])
    offs = sm_("offsv", [8, 1]); bef = sm_("befv", [128, 8])
    rec = sm_("recv", [128, 8])
    silb = SD[:, 0:512]
    t2b = SDf[:, 512:1024]

    for ap_, sb in [(a_gam, gamv), (a_bet, betv), (a_ones2, ones2), (a_E2, E2),
                    (a_sdec, sdec), (a_stmask, stmask), (a_b1, b1f), (a_lg, lgv),
                    (a_lb, lbv), (a_b2, b2h), (a_bk1, bkf), (a_bq1, bqf),
                    (a_idb, idb), (a_idf, idf),
                    (a_maskc, maskc), (a_be, bef)]:
        p.dma('g', sb[:], ap_)
    p.dma('g', wgfm[:], rearr_fm(a_wg))
    _barrier(p)

    NS = 512
    xb = SA[:, 0:16384]; xsq = SB[:, 0:16384]; xn = SC[:, 0:16384]
    k.stat = F8f[0:1, 0:2048]
    p.dma('g', xb, rearr_fm(a_xT))
    p.wait('v', 'dg'); p.wait('t', 'dg'); p.wait('s', 'dg')
    emit_ln_fm(k, xb, xsq, xn)
    p.wait('y', 'dv')
    p.dma('y', rearr_fm(d_xn), xn)
    _barrier(p)

    ws = F8[:, 0:24576]
    wq0, wk0, wv0 = ws[:, 0:8192], ws[:, 8192:16384], ws[:, 16384:24576]
    cosT = SBf[:, 0:8192]; sinT = SDf[:, 0:8192]
    raw = SA[:, 0:16384]
    scrA, scrB, scrC = F8f[:, 12288:12800], H[:, 0:512], H[:, 512:1024]
    p.dma('g', wq0, rearr_fm(a_wq0))
    p.dma('g', wk0, rearr_fm(a_wk0))
    p.dma('g', wv0, rearr_fm(a_wv0))
    p.dma('g', cosT, rearr_fm(a_cosT))
    p.dma('g', sinT, rearr_fm(a_sinT))
    p.wait('t', 'dg'); p.wait('v', 'dg'); p.wait('s', 'dg')

    def rot_and_store(src_fm, out_hbm):
        for c in range(4):
            for ts in range(T // 512):
                lo = src_fm[:, c * T + ts * 512: c * T + ts * 512 + 512]
                hi = src_fm[:, (c + 4) * T + ts * 512: (c + 4) * T + ts * 512 + 512]
                cs = cosT[:, c * T + ts * 512: c * T + ts * 512 + 512]
                sn = sinT[:, c * T + ts * 512: c * T + ts * 512 + 512]
                def rot(e, lo=lo, hi=hi, cs=cs, sn=sn):
                    e.tensor_mul(scrA[:], lo, cs)
                    e.tensor_mul(scrC[:], lo, sn)
                    e.tensor_mul(scrB[:], hi, sn)
                    e.tensor_sub(lo, scrA[:], scrB[:])
                    e.tensor_mul(scrB[:], hi, cs)
                    return e.tensor_add(hi, scrC[:], scrB[:])
                p.emit('v', rot, inc='dv')
        p.wait('y', 'dv')
        p.dma('y', rearr_fm(out_hbm), src_fm[:, 0:16384])

    emit_proj_fm(k, wq0, xn, raw)
    p.wait('v', 'ac')
    rot_and_store(raw, d_q)
    p.wait('s', 'dy')
    emit_proj_fm(k, wk0, xn, raw)
    p.wait('v', 'ac')
    rot_and_store(raw, d_k)
    p.wait('g', 'dv')
    p.dma('g', cosT, rearr_tm(a_cosB))
    p.dma('g', sinT, rearr_tm(a_sinB))
    p.wait('s', 'dy')
    emit_proj_tm(k, wk0, xn, raw)
    p.wait('v', 'ac'); p.wait('v', 'dg')
    ktok = F8[:, 0:16384]
    emit_rot_tm(k, raw, ktok, cosT, sinT, scrA, scrB, scrC)
    _barrier(p)
    Vtm = SA[:, 0:16384]
    emit_proj_tm(k, wv0, xn, Vtm)
    _barrier(p)

    Sf32 = F8f[:, 8192:9216]
    p.emit('v', lambda e: e.memset(Sf32[:], 0.0), inc='dv')
    Qg = SB[:, 0:16384]; Kg = SD[:, 0:16384]; RTg = SC[:, 0:16384]
    dtc = F8f[:, 9728:11776]; qdec = F8f[:, 11776:12800]
    for g in range(8):
        for hf in range(2):
            h = 2 * g + hf
            cr, qq = h // 4, h % 4
            for x1 in range(2):
                srows = slice((cr + 4 * x1) * 128 + 32 * qq, (cr + 4 * x1) * 128 + 32 * qq + 32)
                drows = slice(64 * hf + 32 * x1, 64 * hf + 32 * x1 + 32)
                p.dma('y', Qg[drows, g * T:(g + 1) * T], d_q[srows, :])
                p.dma('y', Kg[drows, g * T:(g + 1) * T], d_k[srows, :])
    p.dma('y', dtc, a_dt)
    p.dma('y', qdec, a_qdec)
    p.wait('t', 'dv'); p.wait('v', 'dg')
    ds_ = k.pb[0]
    for c in range(16):
        for g in range(8):
            vcol = slice(c * 1024 + g * 128, c * 1024 + g * 128 + 128)
            pe_tok = p.emit('t', lambda e, vcol=vcol: e.matmul(
                ds_[:, 0:128], ktok[:, vcol], Vtm[:, vcol], start=True, stop=True), inc='pe')
            p.wait('v', 'pe', pe_tok)
            def upd(e, g=g):
                e.scalar_tensor_tensor(Sf32[0:64, g * 128:g * 128 + 64],
                                       Sf32[0:64, g * 128:g * 128 + 64],
                                       sdec[0:64, g:g + 1], ds_[0:64, 0:64],
                                       ALU.mult, ALU.add)
                return e.scalar_tensor_tensor(Sf32[64:128, g * 128 + 64:g * 128 + 128],
                                              Sf32[64:128, g * 128 + 64:g * 128 + 128],
                                              sdec[64:128, g:g + 1], ds_[64:128, 64:128],
                                              ALU.mult, ALU.add)
            dv_tok = p.emit('v', upd, inc='dv')
            p.wait('t', 'dv', dv_tok)
    p.wait('g', 'dv')
    p.dma('g', cc_st_in[:, :], Sf32[:])
    p.wait('g', 'dg')
    p.emit('g', lambda e: e.collective_compute(
        "AllGather", ALU.bypass, replica_groups=[[0, 1, 2, 3, 4, 5, 6, 7]],
        ins=[cc_st_in.ap().opt()], outs=[cc_st_out.ap().opt()]), inc='gp')
    p.wait('g', 'gp')
    p.wait('v', 'dg')
    p.wait('v', 'gp')
    p.emit('v', lambda e: e.memset(Sf32[:], 0.0), inc='dv')
    stmp = H[:, 0:1024]
    for r in range(8):
        p.wait('g', 'dv')
        rt_ = p.dma('g', stmp, cc_st_out[128 * r:128 * r + 128, :])
        p.wait('v', 'dg', rt_)
        p.emit('v', lambda e, r=r: e.scalar_tensor_tensor(
            Sf32[:], stmp, stmask[:, r:r + 1], Sf32[:], ALU.mult, ALU.add), inc='dv')
    _barrier(p)

    S2b = F8[:, 36864 // 2:36864 // 2 + 1024]
    k.atA = F8[:, 25600:25728]; k.atB = F8[:, 25728:25856]
    k.atA2 = F8[:, 25856:25984]; k.atB2 = F8[:, 25984:26112]
    qfull = F8[:, 26624:27648]
    p.emit('v', lambda e: e.memset(S2b[:], 0.0), inc='dv')
    st2 = F8f[0:2, 13312:15360]
    k.st2m = st2[0:2, 0:512]; k.st2v = st2[0:2, 512:1024]
    k.st2t = st2[0:2, 1024:1536]; k.st2r = st2[0:2, 1536:2048]
    _barrier(p)
    emit_retention2(k, Qg, Kg, ktok, Vtm, dtc, qdec, sdec, Sf32, S2b, RTg, qfull)
    wo = F8[:, 0:8192]
    p.wait('g', 'pe')
    p.dma('g', wo, rearr_fm(a_wo))
    emit_gnorm(k, RTg, RTg, ones2, E2, gamv, betv, silb[:, 0:512])
    p.wait('t', 'dv'); p.wait('t', 'dg')
    emit_proj_fm(k, wo, RTg, Qg)
    p.wait('y', 'ac')
    p.dma('y', rearr_fm(d_s), Qg)
    _barrier(p)

    xnA = SA[:, 0:16384]; xsq4 = SB[:, 0:16384]
    w1 = F8[:, 0:32768]
    scaleB1 = SCf[:, 0:4096]
    y1T = SD[:, 0:16384]
    k.stat = SCf[0:1, 4096:6144]
    p.dma('g', xnA, rearr_fm(d_xn))
    emit_dequant(k, a_q1, w1, a_sc1, scaleB1, nk=8, nn=4096)
    p.wait('t', 'dg'); p.wait('v', 'dg'); p.wait('s', 'dg')
    emit_ln_fm(k, xnA, xsq4, xnA, gamma_ap=lgv, beta_ap=lbv)
    p.wait('t', 'dv')
    for tc in range(4):
        emit_ffn_fc1(k, w1, xnA, b1f, y1T, tc=tc)
        p.wait('y', 'ac')
        p.dma('y', d_y1.rearrange("(c p) n -> p c n", p=128)[:, :, 512 * tc:512 * (tc + 1)], y1T)
        p.wait('s', 'dy')
    _barrier(p)

    p.dma('g', xnA, rearr_fm(d_xn))
    w2 = F8[:, 0:32768]
    scaleB2 = SCf[:, 0:1024]
    gT = SB[:, 0:16384]
    emit_dequant(k, a_q2, w2, a_sc2, scaleB2, nk=32, nn=1024)
    p.wait('v', 'dg')
    p.emit('v', lambda e: e.tensor_scalar_mul(b2h[:], b2h[:], 0.5), inc='dv')
    p.wait('t', 'dg'); p.wait('s', 'dv'); p.wait('t', 'dv')
    for tc in range(4):
        p.wait('g', 'pe')
        p.dma('g', y1T, d_y1.rearrange("(c p) n -> p c n", p=128)[:, :, 512 * tc:512 * (tc + 1)])
        p.wait('t', 'dg')
        emit_ffn_fc2(k, w2, y1T, b2h, xnA, gT, tc=tc)
    _barrier(p)

    sT6 = SA[:, 0:16384]
    x1b = SD[:, 0:16384]; xn1 = SC[:, 0:16384]
    xr0 = F8f[:, 8192:10240]; xr1 = F8f[:, 10240:12288]
    xf0 = F8f[:, 0:2048]; xf1 = F8f[:, 2048:4096]
    msb = F8f[0:1, 4096:6144]; rsb = F8f[0:1, 6144:8192]
    k.stat = SBf[0:1, 0:2048]
    p.dma('g', sT6, rearr_fm(d_s))
    p.wait('v', 'dg')
    for c in range(8):
        xr = (xr0, xr1)[c % 2]; xf = (xf0, xf1)[c % 2]
        p.wait('y', 'dv', p.tok('dv'))
        p.dma('y', xr, rearr_fm(a_xT)[:, c:c + 1, :])
        p.wait('v', 'dy')
        def comb(e, c=c, xr=xr, xf=xf):
            e.tensor_mul(xf, sT6[:, c * T:(c + 1) * T], gT[:, c * T:(c + 1) * T])
            e.tensor_add(xf, xf, xr)
            return e.tensor_copy(x1b[:, c * T:(c + 1) * T], xf)
        dv_tok = p.emit('v', comb, inc='dv')
        p.wait('y', 'dv', dv_tok)
        p.dma('y', rearr_fm(d_x1)[:, c:c + 1, :], xf)
    emit_ln_fm(k, x1b, sT6, xn1, m_out=msb, r_out=rsb)
    p.wait('t', 'dv')
    def tpstats(e):
        for tc in range(16):
            e.transpose(k.pb[0][0:128, tc:tc + 1], msb[0:1, 128 * tc:128 * tc + 128], idf[0:1, 0:1])
        for tc in range(16):
            e.transpose(k.pb[1][0:128, tc:tc + 1], rsb[0:1, 128 * tc:128 * tc + 128], idf[0:1, 0:1])
        return e.transpose(k.pb[2][0:128, 0:1], msb[0:1, 0:128], idf[0:1, 0:1])
    pe_tok = p.emit('t', tpstats, inc='pe')
    p.wait('v', 'pe', pe_tok)
    def evst(e):
        e.tensor_copy(m_tm[:], k.pb[0][:, 0:16])
        return e.tensor_copy(r_tm[:], k.pb[1][:, 0:16])
    p.emit('v', evst, inc='dv')
    p.wait('y', 'dv')
    p.dma('y', rearr_fm(cc_x_in.ap()), xn1)
    _barrier(p)

    kb = F8[:, 0:32768]
    bvB = H.bitcast(F32)[:, 512:1024] if False else H[:, 512:1024]
    p.dma('g', bvB[:], a_bv1.broadcast_to((128, 512)))
    wk1 = SA[:, 0:4096]; wv1 = SA[:, 4096:8192]
    kT7 = SB[:, 0:16384]
    v65 = SD[:, 0:16896]
    p.dma('g', kb, rearr_fm(a_keysT))
    p.dma('g', wk1, rearr_fm(a_wk1))
    p.dma('g', wv1, rearr_fm(a_wv1))
    p.wait('t', 'dg'); p.wait('v', 'dg')
    emit_proj_fm(k, wk1, kb, kT7, nchunk_out=4, tlen=S, evac='v', bias_ap=bkf, relu=True)
    p.wait('y', 'dv')
    p.dma('y', rearr_fm(d_kT), kT7)
    p.wait('g', 'pe')
    p.dma('g', kb, rearr_fm(a_valsT))
    p.emit('v', lambda e: e.memset(v65[:], 0.0))
    v65r = v65.rearrange("p (n h u) -> p n h u", n=32, h=8, u=66)
    p.emit('v', lambda e: e.memset(v65r[:, :, :, 64:65], 1.0), inc='dv')
    p.wait('t', 'dg'); p.wait('t', 'dv')
    for tt in range(32):
        bank = k.pb[tt % 2]
        def mms7(e, tt=tt, bank=bank):
            for dc in range(8):
                i = e.matmul(bank[:, 0:512], kb[:, dc * S + tt * 128: dc * S + tt * 128 + 128],
                             wv1[:, dc * 512:(dc + 1) * 512], start=(dc == 0), stop=(dc == 7))
            return i
        pe_tok = p.emit('t', mms7, inc='pe')
        p.wait('v', 'pe', pe_tok)
        def evac7(e, tt=tt, bank=bank):
            e.tensor_add(k.tscr[:, 0:512], bank[:, 0:512], bvB[:])
            ov = v65[:, tt * 528:(tt + 1) * 528].rearrange("p (h u) -> p h u", h=8, u=66)[:, :, 0:64]
            return e.tensor_scalar_max(ov, k.tscr[:, 0:512].rearrange("p (h u) -> p h u", h=8, u=64), 0.0)
        dv_tok = p.emit('v', evac7, inc='dv')
        p.wait('t', 'dv', dv_tok)
    p.wait('y', 'dv')
    p.dma('y', rearr_tm(d_v65), v65)
    p.emit('g', lambda e: e.collective_compute(
        "AllGather", ALU.bypass, replica_groups=[[0, 1], [2, 3], [4, 5], [6, 7]],
        ins=[cc_x_in.ap().opt()], outs=[cc_x_out.ap().opt()]), inc='gp')
    p.wait('g', 'gp')
    _barrier(p)

    xb8 = F8[:, 0:32768]
    qT = SA[:, 0:16384]; s1T = SB[:, 0:16384]
    kts = SD[:, 0:8192]; vs = SD[:, 8192:12416]
    eb = SD[:, 12416:14464]; pcop = SD[:, 14464:14992]
    avs = SD[0:65, 14992:16016]
    for cch in range(8):
        p.dma('g', xb8[:, cch * S: cch * S + T], cc_x_out.ap()[128 * cch:128 * cch + 128, :])
        p.dma('g', xb8[:, cch * S + T: (cch + 1) * S], cc_x_out.ap()[D + 128 * cch:D + 128 * cch + 128, :])
    wq1s = F8[:, 0:4096]
    wq1s = SA[:, 0:4096]
    p.dma('g', wq1s, rearr_fm(a_wq1))
    p.wait('t', 'dg'); p.wait('v', 'dg')
    qT = SB[:, 0:16384]; s1T = SA[:, 0:16384]
    for cch in range(4):
        for ts8 in range(S // NS):
            bank = k.pb[ts8 % 2]
            def mms8(e, cch=cch, ts8=ts8, bank=bank):
                for dc in range(8):
                    i = e.matmul(bank[:, 0:NS],
                                 wq1s[:, dc * 512 + cch * 128: dc * 512 + cch * 128 + 128],
                                 xb8[:, dc * S + ts8 * NS: dc * S + ts8 * NS + NS],
                                 start=(dc == 0), stop=(dc == 7))
                return i
            pe_tok = p.emit('t', mms8, inc='pe')
            p.wait('v', 'pe', pe_tok)
            osl = qT[:, cch * S + ts8 * NS: cch * S + ts8 * NS + NS]
            tk = p.emit('v', lambda e, bank=bank, osl=osl, cch=cch: e.tensor_scalar(
                osl, bank[:, 0:NS], bqf[:, cch:cch + 1], 0.0, ALU.add, ALU.max), inc='dv')
            p.wait('t', 'dv', tk)
    _barrier(p)
    psc = [psc0, psc1]
    eb6 = F8[:, 0:6144]
    LAG8 = 4
    kload_tok = {}
    vs_tok = {}
    exp_tok = [None, None]
    av_ring = [None] * 6
    pending_av = []
    pending_epi = []
    st8 = {'g': 0, 'avs_dv': None}
    s1v = s1T.rearrange("p (tc hh u f) -> p tc hh u f", tc=8, hh=8, u=64, f=4)
    xov = xn1.rearrange("p (tc hh u f) -> p tc hh u f", tc=8, hh=8, u=64, f=4)

    def _emit_epi(h, qb, dv_tok):
        p.wait('t', 'dv', dv_tok)
        def tps8(e):
            for i in range(8):
                e.transpose(pbh[:, 66 * i:66 * i + 65], avs[:, 128 * i:128 * i + 128],
                            idb[0:65, 0:65])
            return e.transpose(pbh[:, 700:765], avs[:, 0:128], idb[0:65, 0:65])
        pe_tok = p.emit('t', tps8, inc='pe')
        p.wait('v', 'pe', pe_tok)
        def evacp8(e):
            e.reciprocal(rec[:, 0:8], pbh[:, 64:528:66])
            return e.tensor_copy(pcop[:, 0:528], pbh[:, 0:528])
        dv2 = p.emit('v', evacp8, inc='dv')
        p.wait('t', 'dv', dv2)
        def perm8(e, h=h, qb=qb):
            for tc in range(8):
                osl = s1v[:, tc:tc + 1, h:h + 1, :, qb:qb + 1]
                isl = xov[:, tc:tc + 1, h:h + 1, :, qb:qb + 1]
                ii = e.scalar_tensor_tensor(
                    osl, pcop[:, 66 * tc:66 * tc + 64].rearrange(
                        "p (a u f) -> p a u f", a=1, u=64, f=1),
                    rec[:, tc:tc + 1], isl, ALU.mult, ALU.add)
            return ii
        p.emit('v', perm8, inc='dv')

    def _issue_av(it):
        sl, ac_tok, h_, qb_, kt_ = it
        if kt_ == 0:
            if st8['avs_dv'] is not None:
                p.wait('t', 'dv', st8['avs_dv'])
            p.wait('t', 'dg', vs_tok[h_])
        p.wait('t', 'ac', ac_tok)
        vsv_ = vs[:, (h_ % 2) * 2112:(h_ % 2 + 1) * 2112]
        ebl = (sl % 6) * 1024
        def av8(e, kt=kt_, vsv=vsv_, ebl=ebl):
            e.matmul(avdt[0:65, 0:512], vsv[:, kt * 66:kt * 66 + 65],
                     eb6[:, ebl:ebl + 512], start=(kt == 0), stop=(kt == 31))
            return e.matmul(avdt[0:65, 512:1024], vsv[:, kt * 66:kt * 66 + 65],
                            eb6[:, ebl + 512:ebl + 1024], start=(kt == 0), stop=(kt == 31))
        tokp = p.emit('t', av8, inc='pe')
        av_ring[sl % 6] = tokp
        if kt_ == 31:
            p.wait('v', 'pe', tokp)
            dvc = p.emit('v', lambda e: e.tensor_copy(avs[:, 0:1024], avdt[0:65, 0:1024]), inc='dv')
            st8['avs_dv'] = dvc
            pending_epi.append((st8['g'] + 3, h_, qb_, dvc))

    for h in range(8):
        pair = h // 2
        if h % 2 == 0:
            p.wait('g', 'pe')
            kload_tok[pair] = p.dma('g', kts[:, (pair % 2) * S:(pair % 2 + 1) * S],
                                    rearr_fm(d_kT)[:, pair:pair + 1, :])
        p.wait('g', 'pe')
        vs_tok[h] = p.dma('g', vs[:, (h % 2) * 2112:(h % 2 + 1) * 2112],
                          rearr_tm(d_v65)[:, :, 66 * h:66 * h + 66])
        krow = slice(64 * (h % 2), 64 * (h % 2) + 64)
        ktsv = kts[:, (pair % 2) * S:(pair % 2 + 1) * S]
        for qb in range(4):
            for kt in range(32):
                g8 = st8['g']
                while pending_epi and pending_epi[0][0] <= g8:
                    _, eh, eqb, etok = pending_epi.pop(0)
                    _emit_epi(eh, eqb, etok)
                if h % 2 == 0 and qb == 0 and kt == 0:
                    p.wait('t', 'dg', kload_tok[pair])
                if exp_tok[g8 % 2] is not None:
                    p.wait('t', 'ac', exp_tok[g8 % 2])
                def sc8_(e, kt=kt, g8=g8, krow=krow, ktsv=ktsv, qb=qb, h=h):
                    bank = psc[g8 % 2]
                    e.matmul(bank[:, 0:512], ktsv[krow, kt * 128:(kt + 1) * 128],
                             qT[krow, (h // 2) * S + qb * 1024: (h // 2) * S + qb * 1024 + 512],
                             start=True, stop=True)
                    return e.matmul(bank[:, 512:1024], ktsv[krow, kt * 128:(kt + 1) * 128],
                                    qT[krow, (h // 2) * S + qb * 1024 + 512: (h // 2) * S + qb * 1024 + 1024],
                                    start=True, stop=True)
                pe_tok = p.emit('t', sc8_, inc='pe')
                p.wait('s', 'pe', pe_tok)
                if av_ring[g8 % 6] is not None:
                    p.wait('s', 'pe', av_ring[g8 % 6])
                ac_tok = p.emit('s', lambda e, g8=g8: e.activation(
                    eb6[:, (g8 % 6) * 1024:(g8 % 6 + 1) * 1024], psc[g8 % 2][:], AF.Exp,
                    scale=0.125), inc='ac')
                exp_tok[g8 % 2] = ac_tok
                pending_av.append((g8, ac_tok, h, qb, kt))
                while pending_av and pending_av[0][0] <= g8 - LAG8:
                    _issue_av(pending_av.pop(0))
                st8['g'] = g8 + 1
    while pending_av:
        _issue_av(pending_av.pop(0))
    while pending_epi:
        _, eh, eqb, etok = pending_epi.pop(0)
        _emit_epi(eh, eqb, etok)
    _barrier(p)

    x1f9 = F8f[:, 0:16384]
    ohT = SDf[0:8, 0:2048]; scanT = SDf[0:8, 2048:4096]
    rkT = SDf[0:8, 4096:6144]; keptT = SDf[0:8, 6144:8192]
    wgT9 = H[0:8, 0:1024]
    p.dma('g', x1f9, rearr_fm(d_x1))
    p.dma('g', wgT9, a_wgT)
    p.dma('g', negb[:], a_negb)
    p.dma('g', posw[:], a_posw)
    p.wait('v', 'dg'); p.wait('t', 'dg')
    p.emit('v', lambda e: e.memset(one30[:], 1.0))
    p.emit('v', lambda e: e.memset(zz8[:], 0.0))
    p.emit('v', lambda e: e.memset(big30[:], 1e30), inc='dv')
    p.wait('t', 'dv'); p.wait('s', 'dv')
    dv_tok = p.emit('v', lambda e: e.reduce_sum(wsumc[0:8, 0:1], wgT9, AX.X), inc='dv')
    p.wait('t', 'dv', dv_tok)
    def tpws(e):
        e.transpose(k.pb[2][0:1, 0:8], wsumc[0:8, 0:1], idf[0:8, 0:8])
        return e.transpose(k.pb[3][0:1, 0:8], wsumc[0:8, 0:1], idf[0:8, 0:8])
    pe_tok = p.emit('t', tpws, inc='pe')
    p.wait('v', 'pe', pe_tok)
    dv_tok = p.emit('v', lambda e: e.tensor_copy(wsrow[0:1, 0:8], k.pb[2][0:1, 0:8]), inc='dv')
    p.wait('t', 'dv', dv_tok)
    pe_tok = p.emit('t', lambda e: e.matmul(k.pb[2][:, 0:8], k.ones_f1[0:1, :],
                                            wsrow[0:1, 0:8], start=True, stop=True), inc='pe')
    p.wait('v', 'pe', pe_tok)
    dv_tok = p.emit('v', lambda e: e.tensor_copy(wsB[:], k.pb[2][:, 0:8]), inc='dv')
    p.wait('t', 'dv', dv_tok)
    for tc in range(16):
        def mmr9(e, tc=tc):
            for dc in range(8):
                i = e.matmul(k.pb[0][:, 0:8], x1f9[:, dc * T + tc * 128: dc * T + tc * 128 + 128],
                             wgfm[:, dc * 8:(dc + 1) * 8], start=(dc == 0), stop=(dc == 7))
            return i
        pe_tok = p.emit('t', mmr9, inc='pe')
        p.wait('v', 'pe', pe_tok)
        def gdve9(e, tc=tc):
            e.scalar_tensor_tensor(t1g[:, 0:8], wsB[:], m_tm[:, tc:tc + 1],
                                   k.pb[0][:, 0:8], ALU.mult, ALU.subtract)
            e.tensor_copy(t2g[:, 0:8], t1g[:, 0:8])
            e.scalar_tensor_tensor(lgs[:, 0:8], t2g[:, 0:8], r_tm[:, tc:tc + 1],
                                   zz8[:, 0:8], ALU.mult, ALU.add)
            e.tensor_sub(lgs[:], lgs[:], negb[:])
            e.tensor_reduce(mnn[:, 0:1], lgs[:], AX.X, ALU.min)
            e.tensor_reduce(pmn[:, 0:1], lgs[:], AX.X, ALU.min, negate=True)
            e.tensor_copy(t2g[:], lgs[:])
            return e.tensor_copy(t3g[:], lgs[:])
        dv_tok = p.emit('v', gdve9, inc='dv')
        p.wait('s', 'dv', dv_tok)
        ac_tok = p.emit('s', lambda e: e.activation(el[:], lgs[:], AF.Exp, scale=-1.0,
                                                    bias=mnn[:, 0:1]), inc='ac')
        p.wait('v', 'ac', ac_tok)
        def gdve9b(e, tc=tc):
            e.reduce_sum(smbuf[:, tc:tc + 1], el[:], AX.X)
            e.scalar_tensor_tensor(t1g[:], lgs[:], pmn[:, 0:1], big30[:],
                                   ALU.add, ALU.mult)
            e.tensor_sub(t2g[:], one30[:], t1g[:])
            e.tensor_max(t3g[:], t2g[:], zz8[:])
            e.tensor_tensor(oh0[:], t3g[:], one30[:], ALU.min)
            e.tensor_mul(q8[:], oh0[:], posw[:])
            e.reduce_max(nm2[:, 0:1], q8[:], AX.X, negate=True)
            e.tensor_copy(t2g[:], q8[:])
            e.tensor_copy(t3g[:], q8[:])
            e.scalar_tensor_tensor(t1g[:], q8[:], nm2[:, 0:1], big30[:],
                                   ALU.add, ALU.mult)
            e.tensor_add(t2g[:], t1g[:], one30[:])
            e.tensor_max(t3g[:], t2g[:], zz8[:])
            return e.tensor_tensor(ohv[:], t3g[:], one30[:], ALU.min)
        dv_tok = p.emit('v', gdve9b, inc='dv')
        p.wait('t', 'dv', dv_tok)
        def tpo9(e, tc=tc):
            e.transpose(k.pb[2][0:32, 0:128], ohv[:], idf[:])
            return e.transpose(k.pb[4][0:32, 0:128], ohv[:], idf[:])
        pe_tok = p.emit('t', tpo9, inc='pe')
        p.wait('v', 'pe', pe_tok)
        dv_tok = p.emit('v', lambda e, tc=tc: e.tensor_copy(
            ohT[0:8, tc * 128:(tc + 1) * 128], k.pb[2][0:8, 0:128]), inc='dv')
        p.wait('t', 'dv', dv_tok)
    def scan9(e):
        e.reciprocal(gate_tm[:], smbuf[:])
        e.tensor_copy(t1g[:], oh0[:])
        e.tensor_tensor_scan(scanT[0:8, :], ohT[0:8, :], ohT[0:8, :], 0.0, ALU.add, ALU.bypass)
        return e.tensor_sub(rkT[0:8, :], scanT[0:8, :], ohT[0:8, :])
    dv_tok = p.emit('v', scan9, inc='dv')
    p.wait('t', 'dv', dv_tok)
    def tph(e):
        e.transpose(k.pb[2][0:1, 0:8], scanT[0:8, 2047:2048], idf[0:8, 0:8])
        return e.transpose(k.pb[4][0:1, 0:8], scanT[0:8, 2047:2048], idf[0:8, 0:8])
    pe_tok = p.emit('t', tph, inc='pe')
    p.wait('v', 'pe', pe_tok)
    dv_tok = p.emit('v', lambda e: e.tensor_copy(hrow[0:1, 0:8], k.pb[2][0:1, 0:8]), inc='dv')
    p.wait('y', 'dv', dv_tok)
    p.dma('y', cc_h_in.ap(), hrow[0:1, 0:8])
    p.wait('g', 'dy')
    p.emit('g', lambda e: e.collective_compute(
        "AllGather", ALU.bypass, replica_groups=[[0, 1, 2, 3, 4, 5, 6, 7]],
        ins=[cc_h_in.ap().opt()], outs=[cc_h_out.ap().opt()]), inc='gp')
    p.wait('g', 'gp')
    p.dma('g', gath[:], cc_h_out.ap())
    _barrier(p)

    we = SD[:, 8192:16384]
    pe_tok = p.emit('t', lambda e: e.matmul(k.pb[0][0:8, 0:1], gath[0:8, :], maskc[0:8, 0:1],
                                            start=True, stop=True), inc='pe')
    p.wait('v', 'pe', pe_tok)
    def kdve10(e):
        e.tensor_copy(offs[:], k.pb[0][0:8, 0:1])
        e.tensor_copy(keptT[0:8, :], rkT[0:8, :])
        e.tensor_copy(keptT[0:8, :], rkT[0:8, :])
        e.tensor_scalar(keptT[0:8, :], rkT[0:8, :], offs[0:8, 0:1], -1.0, ALU.add, ALU.mult)
        e.tensor_scalar(keptT[0:8, :], keptT[0:8, :], 100.0, 0.0, ALU.add, ALU.max)
        e.tensor_scalar(keptT[0:8, :], keptT[0:8, :], 1.0, 0.0, ALU.min, ALU.add)
        return e.tensor_mul(keptT[0:8, :], keptT[0:8, :], ohT[0:8, :])
    dv_tok = p.emit('v', kdve10, inc='dv')
    p.wait('t', 'dv', dv_tok)
    for tc in range(16):
        def tpk10(e, tc=tc):
            e.transpose(k.pb[2][0:128, 0:8], keptT[0:8, tc * 128:(tc + 1) * 128], idf[0:8, 0:8])
            return e.transpose(k.pb[4][0:128, 0:8], keptT[0:8, tc * 128:(tc + 1) * 128], idf[0:8, 0:8])
        pe_tok = p.emit('t', tpk10, inc='pe')
        p.wait('v', 'pe', pe_tok)
        dv_tok = p.emit('v', lambda e, tc=tc: e.reduce_sum(kcolb[:, tc:tc + 1],
                                                           k.pb[2][0:128, 0:8], AX.X), inc='dv')
        p.wait('t', 'dv', dv_tok)
    dv_tok = p.emit('v', lambda e: e.tensor_mul(gk[:], kcolb[:], gate_tm[:]), inc='dv')
    p.wait('t', 'dv', dv_tok)
    p.wait('g', 'dv')
    p.dma('g', we, rearr_fm(a_we))
    for tc in range(16):
        def tpg10(e, tc=tc):
            e.transpose(k.pb[2][0:1, 0:128], gk[:, tc:tc + 1], idf[:, :])
            return e.transpose(k.pb[4][0:1, 0:128], gk[:, tc:tc + 1], idf[:, :])
        pe_tok = p.emit('t', tpg10, inc='pe')
        p.wait('v', 'pe', pe_tok)
        dv_tok = p.emit('v', lambda e, tc=tc: e.tensor_copy(
            k.stat[0:1, tc * 128:(tc + 1) * 128], k.pb[2][0:1, 0:128]), inc='dv')
        p.wait('t', 'dv', dv_tok)
    gkB = H[:, 512:1024]
    k.stat = SBf[0:1, 2048:4096]
    for cch in range(8):
        for ts10 in range(4):
            pe_tok = p.emit('t', lambda e, ts10=ts10: e.matmul(
                k.pb[5][:, 0:512], k.ones_f1[0:1, :],
                k.stat[0:1, ts10 * 512:(ts10 + 1) * 512], start=True, stop=True), inc='pe')
            p.wait('v', 'pe', pe_tok)
            dv_tok = p.emit('v', lambda e: e.tensor_copy(gkB, k.pb[5][:, 0:512]), inc='dv')
            p.wait('t', 'dv', dv_tok)
            bank = k.pb[3 + (ts10 % 2) - 3 + 0] if False else k.pb[(ts10 % 2)]
            if cch == 0 and ts10 == 0:
                p.wait('t', 'dg')
            def mms10(e, cch=cch, ts10=ts10, bank=bank):
                for dc in range(8):
                    i = e.matmul(bank[:, 0:512],
                                 we[:, dc * 1024 + cch * 128: dc * 1024 + cch * 128 + 128],
                                 xn1[:, dc * T + ts10 * 512: dc * T + ts10 * 512 + 512],
                                 start=(dc == 0), stop=(dc == 7))
                return i
            pe_tok = p.emit('t', mms10, inc='pe')
            p.wait('v', 'pe', pe_tok)
            sl10 = slice(cch * T + ts10 * 512, cch * T + ts10 * 512 + 512)
            dv_tok = p.emit('v', lambda e, cch=cch, bank=bank: e.scalar_tensor_tensor(
                k.tscr[:, 0:512], bank[:, 0:512], bef[:, cch:cch + 1], gkB,
                ALU.add, ALU.mult), inc='dv')
            p.wait('s', 'dv', dv_tok)
            ac_tok = p.emit('s', lambda e: e.activation(silb[:], k.tscr[:, 0:512], AF.Silu), inc='ac')
            p.wait('v', 'ac', ac_tok)
            def md210(e, cch=cch, ts10=ts10, sl10=sl10):
                e.tensor_mul(t2b[:], silb[:], s1T[:, sl10])
                return e.tensor_add(t2b[:], t2b[:],
                                    x1f9[:, cch * T + ts10 * 512: cch * T + ts10 * 512 + 512])
            dv_tok = p.emit('v', md210, inc='dv')
            p.wait('y', 'dv', dv_tok)
            p.dma('y', rearr_fm(o_out)[:, cch:cch + 1, ts10 * 512:(ts10 + 1) * 512], t2b[:])
            p.wait('t', 'dy'); p.wait('v', 'dy')
    p.wait('y', 'dy')
    p.finalize()
    return nc


def kernel_fused(inp):
    import ml_dtypes
    TB = make_tables()
    perm = make_perm()
    x = np.ascontiguousarray(inp['L0_attn_x'].astype(np.float32))
    cores = [(c // 2, c % 2) for c in range(8)]
    sls = [slice(2048 * hf, 2048 * hf + 2048) for (_, hf) in cores]
    WQpT = np.ascontiguousarray(inp['L0_attn_WQ'].reshape(D, D)[perm].T)
    WKpT = np.ascontiguousarray(inp['L0_attn_WK'].reshape(D, D)[perm].T)
    WVT = np.ascontiguousarray(inp['L0_attn_WV'].reshape(D, D).T)
    keysT = np.ascontiguousarray(inp['L1_attn_keys'][0].T.astype(np.float32))
    valsT = np.ascontiguousarray(inp['L1_attn_values'][0].T.astype(np.float32))
    o2 = np.zeros((128, 2), ml_dtypes.bfloat16); o2[0:64, 0] = 1; o2[64:128, 1] = 1
    E2h = np.zeros((2, 128), np.float32); E2h[0, 0:64] = 1; E2h[1, 64:128] = 1
    negb = np.zeros((128, 32), np.float32); negb[:, 8:] = -1e30
    idb = np.eye(128, dtype=ml_dtypes.bfloat16)
    idf = np.eye(128, dtype=np.float32)
    ins = []
    for c in range(8):
        b, hf = cores[c]
        s_ = sls[c]
        stm = np.zeros((128, 8), np.float32)
        if hf == 1:
            stm[:, 2 * b] = 1.0
        mkc = np.zeros((8, 1), np.float32); mkc[0:c, 0] = 1.0
        ins.append({
            "xT": np.ascontiguousarray(x[b, s_].T),
            "wq0": WQpT, "wk0": WKpT, "wv0": WVT,
            "cosT": np.ascontiguousarray(TB['COS_T'][:, s_]),
            "sinT": np.ascontiguousarray(TB['SIN_T'][:, s_]),
            "cosB": np.ascontiguousarray(TB['COS_BK'][s_]),
            "sinB": np.ascontiguousarray(TB['SIN_BK'][s_]),
            "dtc": TB['DTcat'], "qdec": TB['QDECG'].astype(np.float32),
            "sdec": TB['SDECG'], "stmask": stm,
            "gam": inp['L0_attn_gamma'].reshape(8, 128).T.copy(),
            "bet": inp['L0_attn_beta'].reshape(8, 128).T.copy(),
            "wo": inp['L0_attn_WO'].astype(np.float32),
            "ones2": o2, "E2": E2h,
            "q1": inp['L0_ffn_fc1_q'], "sc1": inp['L0_ffn_fc1_scales'],
            "b1": inp['L0_ffn_fc1_bias'].reshape(32, 128).T.copy(),
            "lg": inp['L0_ffn_ln_gamma'].reshape(8, 128).T.copy(),
            "lb": inp['L0_ffn_ln_beta'].reshape(8, 128).T.copy(),
            "q2": inp['L0_ffn_fc2_q'], "sc2": inp['L0_ffn_fc2_scales'],
            "b2": inp['L0_ffn_fc2_bias'].reshape(8, 128).T.copy(),
            "keysT": keysT, "valsT": valsT,
            "wk1": np.ascontiguousarray(inp['L1_attn_Wk_w'][512 * hf:512 * (hf + 1)].T),
            "wv1": np.ascontiguousarray(inp['L1_attn_Wv_w'][512 * hf:512 * (hf + 1)].T),
            "bk1": inp['L1_attn_Wk_b'][512 * hf:512 * (hf + 1)].reshape(4, 128).T.copy(),
            "bv1": inp['L1_attn_Wv_b'][512 * hf:512 * (hf + 1)].reshape(1, 512),
            "wq1": np.ascontiguousarray(inp['L1_attn_Wq_w'][512 * hf:512 * (hf + 1)].T),
            "bq1": inp['L1_attn_Wq_b'][512 * hf:512 * (hf + 1)].reshape(4, 128).T.copy(),
            "identb": idb, "identf": idf,
            "wg": inp['L1_ffn_w_gate'].astype(np.float32),
            "wgT": np.ascontiguousarray(inp['L1_ffn_w_gate'].T.astype(np.float32)),
            "negbig": negb, "posw": _posw(), "maskc": mkc,
            "we": inp['L1_ffn_w_expert'].astype(np.float32),
            "be": inp['L1_ffn_b_expert'].reshape(8, 128).T.copy(),
        })
    res = run_bass_kernel_spmd(build_fused(), ins, core_ids=list(range(8)))
    out = np.zeros((4, 4096, 1024), np.float32)
    for c in range(8):
        b, hf = cores[c]
        out[b, sls[c]] = np.asarray(res.results[c]["o_out"]).T
    return out
